# version 37
# baseline (speedup 1.0000x reference)
import numpy as np
from contextlib import ExitStack

import concourse.bass as bass
import concourse.mybir as mybir
from concourse.bass_utils import run_bass_kernel_spmd

F32 = mybir.dt.float32
BF16 = mybir.dt.bfloat16
I32 = mybir.dt.int32
AF = mybir.ActivationFunctionType
ALU = mybir.AluOpType

D = 1024
NK = 8
DF = 4096
NF = 32
PI = float(np.pi)
TWO_PI = 2.0 * PI
INV_2PI = 1.0 / TWO_PI
EPS_MOD = 1e-5
EPS_IN = 1e-6

N_CORES = 8
NB = 4
S = 2048
T = 256
NSLOT = 12

C_EPS_MOD, C_EPS_IN, C_HALFPI, C_NEG1, C_ONE, C_NEGH, C_HALF = 0, 1, 2, 3, 4, 5, 6
V_LN1S, V_LN1B, V_LN2S, V_LN2B, V_B2, V_B1 = 0, 8, 16, 24, 32, 40
V_B2C = 72


class Buf:
    __slots__ = ("name", "writer", "readers")

    def __init__(self, name):
        self.name = name
        self.writer = None
        self.readers = []


class Chan:
    __slots__ = ("sem", "step", "cnt", "last_node")

    def __init__(self, sem, step):
        self.sem = sem
        self.step = step
        self.cnt = 0
        self.last_node = None


class Eng:
    __slots__ = ("raw", "name", "chan", "sat")

    def __init__(self, raw, name, chan):
        self.raw = raw
        self.name = name
        self.chan = chan
        self.sat = {}


class Node:
    __slots__ = ("nid", "ename", "fn", "reads", "writes", "chan", "note",
                 "cost", "table", "deps", "prio", "finish", "alts")

    def __init__(self, nid, ename, fn, reads, writes, chan, note, cost, table):
        self.nid = nid
        self.ename = ename
        self.alts = None
        self.fn = fn
        self.reads = reads
        self.writes = writes
        self.chan = chan
        self.note = note
        self.cost = cost
        self.table = table
        self.deps = set()
        self.prio = 0.0
        self.finish = 0.0


ACT_TABLE_NS = 5000.0
RSQRT_MAGIC = 0x5f3759df


class Prog:

    def __init__(self, loose_sync=False):
        self.engs = {}
        self.nodes = []
        self.capture = True
        self.loose_sync = loose_sync

    def add_eng(self, name, raw, chan):
        self.engs[name] = Eng(raw, name, chan)

    def emit(self, ename, reads, writes, fn, chan=None, note="", cost=None,
             table=None, flex=False, alts=None):
        if not self.capture:
            return self._emit_real(ename, reads, writes, fn, chan, note)
        if cost is None:
            cost = {"pe": 1000.0, "dve": 450.0, "act": 450.0, "gps": 5000.0}.get(
                ename, 2500.0)
        n = Node(len(self.nodes), ename, fn, list(reads), list(writes), chan,
                 note, float(cost), table)
        import os as _os
        if not _os.environ.get("K_FLEX"):
            flex = False
        if alts is not None:
            n.alts = dict(alts)
        elif flex == "copy3":
            n.alts = {"dve": float(cost), "gps": float(cost) * 2.0,
                      "act": float(cost) * 1.4}
        elif flex:
            n.alts = {"dve": float(cost), "gps": float(cost) * 2.0}
        if chan is not None and getattr(chan, "last_node", None) is not None:
            n.deps.add(chan.last_node)
        for b in reads:
            if b.writer is not None:
                n.deps.add(b.writer)
        for b in writes:
            if b.writer is not None:
                n.deps.add(b.writer)
            for r in b.readers:
                n.deps.add(r)
        n.deps.discard(n.nid)
        for b in writes:
            b.writer = n.nid
            b.readers = []
        for b in reads:
            b.readers.append(n.nid)
        if chan is not None:
            chan.last_node = n.nid
        self.nodes.append(n)
        return n

    def schedule_and_replay(self):
        nodes = self.nodes
        nn = len(nodes)
        succ = [[] for _ in range(nn)]
        indeg = [0] * nn
        for n in nodes:
            for d in n.deps:
                succ[d].append(n.nid)
                indeg[n.nid] += 1
        import heapq
        import os
        simkey = lambda e: "gps" if e == "qstore" else e
        keys = {simkey(e) for e in self.engs}
        rev = []
        rem = [len(succ[i]) for i in range(nn)]
        stack = [i for i in range(nn) if rem[i] == 0]
        while stack:
            i = stack.pop()
            rev.append(i)
            for p in nodes[i].deps:
                rem[p] -= 1
                if rem[p] == 0:
                    stack.append(p)

        def run_pass(wc, costs):
            prio = [0.0] * nn
            for i in rev:
                best = 0.0
                for s in succ[i]:
                    if prio[s] > best:
                        best = prio[s]
                prio[i] = costs[i] + best
            eng_time = {e: 0.0 for e in keys}
            act_tab = None
            ready = {e: [] for e in keys}
            dready = [0.0] * nn
            ind = indeg[:]
            scheduled = [False] * nn
            assign = [None] * nn

            def push(i):
                n3 = nodes[i]
                if n3.alts:
                    for e3 in n3.alts:
                        heapq.heappush(ready[e3], (-prio[i], i))
                else:
                    heapq.heappush(ready[simkey(n3.ename)], (-prio[i], i))

            for i in range(nn):
                if ind[i] == 0:
                    push(i)
            sched = []
            nsched = 0
            while nsched < nn:
                best = None
                for e, rl in ready.items():
                    while rl and scheduled[rl[0][1]]:
                        heapq.heappop(rl)
                    if not rl:
                        continue
                    p, i = rl[0]
                    start = max(eng_time[e], dready[i])
                    n3 = nodes[i]
                    ecost = n3.alts.get(e, costs[i]) if n3.alts else costs[i]
                    key = (start + wc * ecost, p)
                    if best is None or key < best[0]:
                        best = (key, e)
                key, e = best
                _, i = heapq.heappop(ready[e])
                n = nodes[i]
                scheduled[i] = True
                if n.alts:
                    assign[i] = e
                    cost = n.alts.get(e, costs[i])
                else:
                    assign[i] = n.ename
                    cost = costs[i]
                start = max(eng_time[e], dready[i])
                if e == "act" and n.table is not None and act_tab != n.table:
                    cost += ACT_TABLE_NS
                    act_tab = n.table
                fin = start + cost
                eng_time[e] = fin
                sched.append(i)
                nsched += 1
                for s in succ[i]:
                    if dready[s] < fin:
                        dready[s] = fin
                    ind[s] -= 1
                    if ind[s] == 0:
                        push(s)
            return sched, assign

        def evaluate(sched, assign):
            eng_time = {e: 0.0 for e in keys}
            act_tab = None
            dready = [0.0] * nn
            fin = [0.0] * nn
            for i in sched:
                n = nodes[i]
                e = simkey(assign[i])
                cost = n.alts.get(e, n.cost) if n.alts else n.cost
                start = max(eng_time[e], dready[i])
                if e == "act" and n.table is not None and act_tab != n.table:
                    cost += ACT_TABLE_NS
                    act_tab = n.table
                fin[i] = start + cost
                eng_time[e] = fin[i]
                for s in succ[i]:
                    if dready[s] < fin[i]:
                        dready[s] = fin[i]
            return max(fin), fin

        base_costs = [n.cost for n in nodes]
        trials = [(0.5, base_costs)]
        for wc in (0.0, 0.15, 0.25, 0.35, 0.42, 0.58, 0.65, 0.8, 1.0, 1.3):
            trials.append((wc, base_costs))
        n_extra = int(os.environ.get("K_SCHED_TRIALS", "0"))
        rngs = np.random.default_rng(1234)
        for tix in range(n_extra):
            wc = (0.35, 0.5, 0.65)[tix % 3]
            amp = 0.25 if tix % 2 else 0.12
            jit = rngs.uniform(1.0 - amp, 1.0 + amp, nn)
            trials.append((wc, [base_costs[i] * jit[i] for i in range(nn)]))
        best_pick = None
        for wc, costs in trials:
            sched, assign = run_pass(wc, costs)
            mk, fin = evaluate(sched, assign)
            if os.environ.get("K_PRINT_TRIALS"):
                print("[sched] trial wc=%s mk=%.3f" % (wc, mk / 1e6))
            if best_pick is None or mk < best_pick[0]:
                best_pick = (mk, sched, assign, fin, wc)
        mk_best, sched, assign, fin, wc_best = best_pick
        for i in range(nn):
            nodes[i].finish = fin[i]
            if nodes[i].alts:
                nodes[i].ename = assign[i] if assign[i] != "qstore" else "gps"
        if os.environ.get("K_PRINT_EST"):
            print("[sched] portfolio best wc=%s makespan %.3f ms"
                  % (wc_best, mk_best / 1e6))
        if os.environ.get("K_GAPS"):
            from collections import Counter
            gape = os.environ.get("K_GAPS")
            seq = [i for i in sched if simkey(nodes[i].ename) == gape]
            gap_c = Counter()
            gap_t = Counter()
            prev_end = 0.0
            for i in seq:
                n = nodes[i]
                stt_ = fin[i] - (n.alts.get(simkey(n.ename), n.cost)
                                 if n.alts else n.cost)
                if stt_ > prev_end + 500.0:
                    dmax, dn = -1.0, None
                    for d in n.deps:
                        if fin[d] > dmax:
                            dmax, dn = fin[d], d
                    if dn is not None:
                        k = nodes[dn]
                        nm = (k.note or "L%d" % k.fn.__code__.co_firstlineno)
                        me = (n.note or "L%d" % n.fn.__code__.co_firstlineno)
                        gap_c[(me.split(" ")[0], "<-", simkey(k.ename),
                               nm.split(" ")[0])] += 1
                        gap_t[(me.split(" ")[0], "<-", simkey(k.ename),
                               nm.split(" ")[0])] += stt_ - prev_end
                prev_end = max(prev_end, fin[i])
            tot = sum(gap_t.values())
            if os.environ.get("K_GAP1"):
                want = os.environ.get("K_GAP1")
                prev_end2 = 0.0
                shown = 0
                for i in seq:
                    n = nodes[i]
                    cst = n.alts.get(simkey(n.ename), n.cost) if n.alts else n.cost
                    stt_ = fin[i] - cst
                    nm = (n.note or "L%d" % n.fn.__code__.co_firstlineno)
                    if (stt_ > prev_end2 + 3000.0 and nm.startswith(want)
                            and shown < 2 and fin[i] > 1.5e6):
                        shown += 1
                        print("[gap1] %s start=%.1fus gap=%.2fus deps:"
                              % (nm, stt_ / 1e3, (stt_ - prev_end2) / 1e3))
                        dl = sorted(n.deps, key=lambda d: -fin[d])[:6]
                        for d in dl:
                            k = nodes[d]
                            dn = (k.note or "L%d" % k.fn.__code__.co_firstlineno)
                            dc = k.alts.get(simkey(k.ename), k.cost) if k.alts else k.cost
                            print("    dep fin=%.1fus start=%.1fus %s %s"
                                  % (fin[d] / 1e3, (fin[d] - dc) / 1e3,
                                     simkey(k.ename), dn))
                    prev_end2 = max(prev_end2, fin[i])
            print("[sched] %s gaps total %.3f ms; top releasers:" % (gape, tot / 1e6))
            for k2, v in gap_t.most_common(18):
                print("    %.3f ms n=%d  %s" % (v / 1e6, gap_c[k2], k2))
        if os.environ.get("K_PRINT_EST"):
            busy = {}
            for n2 in nodes:
                k = simkey(n2.ename)
                busy[k] = busy.get(k, 0.0) + n2.cost
            mk = max(nd.finish for nd in nodes)
            print("[sched] busy:", {k: round(v / 1e6, 2) for k, v in busy.items()},
                  "makespan", round(mk / 1e6, 2))
            eng_prev = {}
            prev_on_eng = {}
            start_t = {}
            for idx2 in sched:
                n2 = nodes[idx2]
                k = simkey(n2.ename)
                eng_prev[idx2] = prev_on_eng.get(k)
                prev_on_eng[k] = idx2
            cur = max(range(nn), key=lambda i2: nodes[i2].finish)
            from collections import Counter
            busy_c = Counter()
            stall_c = Counter()
            hops = 0
            while cur is not None and hops < 20000:
                hops += 1
                n2 = nodes[cur]
                st2 = n2.finish - n2.cost
                ep = eng_prev.get(cur)
                dmax, dnode = -1.0, None
                for d in n2.deps:
                    if nodes[d].finish > dmax:
                        dmax, dnode = nodes[d].finish, d
                epf = nodes[ep].finish if ep is not None else -1.0
                key2 = (simkey(n2.ename), (n2.note or "?").split(" ")[0])
                busy_c[key2] += n2.cost
                if epf >= dmax and ep is not None:
                    cur = ep
                elif dnode is not None:
                    if dmax < st2 - 1.0:
                        stall_c[("idle-before", key2[0])] += st2 - dmax
                    cur = dnode
                else:
                    break
            print("[sched] walk busy by (eng, op):")
            for k2, v in busy_c.most_common(14):
                print("   ", k2, round(v / 1e6, 2), "ms")
            print("[sched] stalls:", {k2: round(v / 1e6, 2) for k2, v in stall_c.most_common(8)})
        for n in nodes:
            for b in n.reads + n.writes:
                b.writer = None
                b.readers = []
            if n.chan is not None:
                n.chan.last_node = None
        self.capture = False
        for i in sched:
            n = nodes[i]
            self._emit_real(n.ename, n.reads, n.writes, n.fn, n.chan, n.note)
        return max(nd.finish for nd in nodes)

    def _emit_real(self, ename, reads, writes, fn, chan=None, note=""):
        eng = self.engs[ename]
        own = chan is None
        ch = eng.chan if own else chan
        deps = {}

        def add(mark):
            if mark is None:
                return
            sem, val, owner = mark
            k = id(sem)
            if k not in deps or deps[k][1] < val:
                deps[k] = mark

        if not own and ch.cnt > 0:
            add((ch.sem, ch.cnt, None))
        for b in reads:
            add(b.writer)
        for b in writes:
            add(b.writer)
            for m in b.readers:
                add(m)
        for sem, val, _owner in deps.values():
            if self.loose_sync and _owner is eng:
                continue
            cur = eng.sat.get(id(sem), 0)
            if val > cur:
                eng.raw.wait_ge(sem, val)
                eng.sat[id(sem)] = val
        last = fn(eng.raw)
        ch.cnt += ch.step
        last.then_inc(ch.sem, ch.step)
        mark = (ch.sem, ch.cnt, eng if own else None)
        for b in writes:
            b.writer = mark
            b.readers = []
        for b in reads:
            b.readers.append(mark)
        return mark



def _kslabs(w_eff):
    M = w_eff.shape[1]
    return np.ascontiguousarray(w_eff.reshape(NK, 128, M))


def _pack_resident(w_eff):
    M = w_eff.shape[1]
    return np.ascontiguousarray(
        w_eff.reshape(NK, 128, M).transpose(1, 0, 2).reshape(128, NK * M))


def _col(v):
    return np.ascontiguousarray(v.reshape(8, 128).T)


def _dmaj(w_eff, nm):
    return np.ascontiguousarray(
        w_eff.reshape(NK, 128, nm, 128).transpose(2, 1, 0, 3).reshape(nm, 128, 1024))


def host_pack_weights(I, t_tile=T):
    f32 = lambda a: np.ascontiguousarray(np.asarray(a, np.float32))
    w = {}
    w["wq"] = _dmaj(f32(I["L0_attn_Wq"]), 8)
    w["wk"] = _dmaj(f32(I["L0_attn_Wk"]), 8)
    w["wv"] = _dmaj(f32(I["L0_attn_Wv"]), 8)
    w["wg"] = _dmaj(f32(I["L0_attn_Wg"]), 8)
    w["wo"] = _dmaj(f32(I["L0_attn_Wo"]), 8)
    w["wa"] = _dmaj(f32(I["L0_attn_Wa"]), 16)
    w["w1p"] = _dmaj(f32(I["L0_attn_W1"]), NF)
    w2 = f32(I["L0_attn_W2"])
    w["w2p"] = np.ascontiguousarray(
        w2.reshape(NF, 128, NK, 128).transpose(2, 0, 1, 3)
        .reshape(NK, 4, NK, 128, 128).transpose(0, 1, 3, 2, 4)
        .reshape(NF, 128, 1024))
    w["fw"] = _dmaj(f32(I["L0_ffn_weight"]).T, 8)
    w["uall"] = _dmaj(f32(I["L0_ffn_lora_U"]).reshape(128, 1024).T, 1)
    w["v2"] = np.ascontiguousarray(f32(I["L0_ffn_lora_V"]).transpose(0, 2, 1).reshape(128, 1024))
    w["gw"] = _pack_resident(f32(I["L0_ffn_gate_w"]).T)
    vecs = np.zeros((128, 80), np.float32)
    vecs[:, V_LN1S:V_LN1S + 8] = _col(f32(I["L0_attn_ln1_scale"]))
    vecs[:, V_LN1B:V_LN1B + 8] = _col(f32(I["L0_attn_ln1_bias"]))
    vecs[:, V_LN2S:V_LN2S + 8] = _col(f32(I["L0_attn_ln2_scale"]))
    vecs[:, V_LN2B:V_LN2B + 8] = _col(f32(I["L0_attn_ln2_bias"]))
    vecs[:, V_B2:V_B2 + 8] = _col(f32(I["L0_attn_b2"]))
    vecs[:, V_B2C:V_B2C + 8] = _col(f32(I["L0_attn_b2"]))
    vecs[:, V_B1:V_B1 + 32] = np.ascontiguousarray(f32(I["L0_attn_b1"]).reshape(32, 128).T)
    w["vecs"] = vecs
    cvec = np.zeros((128, 8), np.float32)
    cvec[:, C_EPS_MOD] = EPS_MOD
    cvec[:, C_EPS_IN] = EPS_IN
    cvec[:, C_HALFPI] = PI / 2
    cvec[:, C_NEG1] = -1.0
    cvec[:, C_ONE] = 1.0
    cvec[:, C_NEGH] = -0.5
    cvec[:, C_HALF] = 0.5
    w["cvec"] = cvec
    w["onesrow"] = np.ones((1, 128), np.float32)
    bsel = np.zeros((8, 128), np.float32)
    for e in range(8):
        bsel[e, 16 * e:16 * (e + 1)] = 1.0
    w["bsel"] = bsel
    import ml_dtypes
    w["zer"] = np.zeros((128, t_tile), ml_dtypes.bfloat16)
    return w




def build_core(nb=NB, s=S, t=T, sim_safe_wrap=False, loose_sync=True):
    CH = NK * t
    nt = s // t
    total = nb * nt
    nc = bass.Bass("TRN2", target_bir_lowering=False)

    xh = nc.dram_tensor("x", [nb, D, s], F32, kind="ExternalInput").ap()
    oh = nc.dram_tensor("out", [nb, D, s], F32, kind="ExternalOutput").ap()
    wslab = {}
    for name, n in [("wq", 8), ("wk", 8), ("wv", 8), ("wg", 8), ("wo", 8),
                    ("wa", 16), ("w1p", 32), ("w2p", 32), ("fw", 8)]:
        wslab[name] = nc.dram_tensor(name, [n, 128, 1024], F32, kind="ExternalInput").ap()
    wslab["uall"] = nc.dram_tensor("uall", [1, 128, 1024], F32, kind="ExternalInput").ap()
    v2h = nc.dram_tensor("v2", [128, 1024], F32, kind="ExternalInput").ap()
    W_NS = [("wa", 16), ("wk", 8), ("wv", 8), ("wq", 8), ("wg", 8), ("wo", 8),
            ("w1p", 32), ("w2p", 32), ("fw", 8), ("uall", 1)]
    scr = {name: nc.dram_tensor(f"scr_{name}", [n, 128, 1024], BF16, kind="Internal").ap()
           for name, n in W_NS}
    scr["v2"] = nc.dram_tensor("scr_v2", [1, 128, 1024], BF16, kind="Internal").ap()
    gwh = nc.dram_tensor("gw", [128, 64], F32, kind="ExternalInput").ap()
    vecsh = nc.dram_tensor("vecs", [128, 80], F32, kind="ExternalInput").ap()
    cvech = nc.dram_tensor("cvec", [128, 8], F32, kind="ExternalInput").ap()
    onesrowh = nc.dram_tensor("onesrow", [1, 128], F32, kind="ExternalInput").ap()
    bselh = nc.dram_tensor("bsel", [8, 128], F32, kind="ExternalInput").ap()
    zerh = nc.dram_tensor("zer", [128, t], BF16, kind="ExternalInput").ap()

    es = ExitStack()
    sb = lambda name, shape, dt=F32: es.enter_context(nc.sbuf_tensor(name, shape, dt))

    bx = [sb("bx0", [128, CH]), sb("bx1", [128, CH])]
    bzbD = [sb("bzb0", [128, CH], BF16), sb("bzb1", [128, CH], BF16)]
    bk = sb("bk", [128, CH], BF16); bv = sb("bv", [128, CH], BF16)
    cs_in = [sb("cs_in0", [128, 1024]), sb("cs_in1", [128, 1024])]
    cs_out = [sb("cs_out0", [128, 1024], BF16), sb("cs_out1", [128, 1024], BF16)]
    bar = sb("bar", [128, CH]); bai = sb("bai", [128, CH])
    bq = sb("bq", [128, CH], BF16); bg = sb("bg", [128, CH], BF16)
    bs_ = sb("bs", [128, CH], BF16)
    bcos = sb("bcos", [128, CH], BF16); bnsin = sb("bnsin", [128, CH], BF16)
    bhr = sb("bhr", [128, CH], BF16); bhi = sb("bhi", [128, CH], BF16)
    bt1 = sb("bt1", [128, CH]); bt2 = sb("bt2", [128, CH])
    bmix = [sb("bmix0", [128, CH], BF16), sb("bmix1", [128, CH], BF16)]
    bki = bar[:, 0:CH].bitcast(I32)
    bw = sb("bw", [128, CH])
    bwb = sb("bwb", [128, CH], BF16)
    bh1 = sb("bh1", [128, NF * t], BF16)
    bst = [sb("bst0", [128, t]), sb("bst1", [128, t])]
    sp = bst[0][0:8, :]
    bcAt = sb("bcAt", [128, 2 * t])
    bcBt = sb("bcBt", [128, 2 * t])
    bcA = [bcAt[:, 0:t], bcAt[:, t:2 * t]]
    bcB = [bcBt[:, 0:t], bcBt[:, t:2 * t]]
    stA = {nm: sb(f"stA_{nm}", [1, t]) for nm in ["mu", "ms", "var"]}
    stB = {nm: sb(f"stB_{nm}", [1, t]) for nm in ["mu", "ms", "var"]}
    stA_rn = sb("stA_rn", [1, 2 * t])
    stB_rn = sb("stB_rn", [1, 2 * t])
    for st_, rn_ in ((stA, stA_rn), (stB, stB_rn)):
        st_["rn"] = rn_
        st_["rstd"] = rn_[:, 0:t]
        st_["nmr"] = rn_[:, t:2 * t]
    st_r = stB["mu"]
    sg128 = sb("sg128", [128, t], BF16)
    stgb = sb("stgb", [128, t], BF16)
    slab = [sb(f"slab{i}", [128, 2048], BF16) for i in range(12)]
    gwr = sb("gwr", [128, 64]); gwb = sb("gwb", [128, 64], BF16)
    onesb = sb("onesb", [128, 1], BF16)
    vecs = sb("vecs_s", [128, 80]); cvec = sb("cvec_s", [128, 8])
    onesrow = sb("onesrow_s", [1, 128]); bsel = sb("bsel_s", [8, 128])
    zer = sb("zer_s", [128, t], BF16)
    bcr = sb("bcr", [128, 8]); bci = sb("bci", [128, 8])
    pb = [es.enter_context(nc.psum_tensor(f"pb{i}", [128, 512], F32)) for i in range(8)]

    sem = lambda name: es.enter_context(nc.semaphore(name))
    P = Prog(loose_sync=loose_sync)
    P.add_eng("pe", nc.tensor, Chan(sem("s_pe"), 1))
    P.add_eng("act", nc.scalar, Chan(sem("s_act"), 1))
    P.add_eng("dve", nc.vector, Chan(sem("s_dve"), 1))
    P.add_eng("gps", nc.gpsimd, Chan(sem("s_gps"), 1))
    P.add_eng("qload", nc.sync, None)
    P.add_eng("qstore", nc.gpsimd, None)
    ch_const = Chan(sem("c_const"), 16)
    ch_x = [Chan(sem(f"c_x{i}"), 16) for i in range(2)]
    ch_slot = [Chan(sem(f"c_sl{i}"), 16) for i in range(12)]
    ch_out = [Chan(sem(f"c_out{i}"), 16) for i in range(2)]

    B = {}
    for nm in ["bx0", "bx1", "bzb0", "bzb1", "bw", "bwb",
               "bst0", "bst1", "bcA0", "bcA1", "bcB0", "bcB1", "st_r", "sp",
               "sg128", "stgb", "gwr", "gwb", "onesb", "vecs", "cvec",
               "onesrow", "bsel", "zer"]:
        B[nm] = Buf(nm)
    Bh1 = [Buf(f"bh1_{q}") for q in range(4)]
    for nm in ["bk", "bv", "bar", "bai", "bq", "bg", "bs",
               "bcos", "bnsin", "bhr", "bhi", "bt1", "bt2", "bmix0", "bmix1"]:
        B[nm + "0"] = Buf(nm + "0")
        B[nm + "1"] = Buf(nm + "1")
    for nm in ["mu", "ms", "var", "rstd", "nmr"]:
        B[f"stA_{nm}"] = Buf(f"stA_{nm}")
        B[f"stB_{nm}"] = Buf(f"stB_{nm}")
    B["st_r"] = B["stB_mu"]
    stpA = {k: B[f"stA_{k}"] for k in ["mu", "ms", "var", "rstd", "nmr"]}
    stpB = {k: B[f"stB_{k}"] for k in ["mu", "ms", "var", "rstd", "nmr"]}
    B["sp"] = B["bst0"]
    Bslab = [Buf(f"slab{i}") for i in range(12)]
    Bb1 = [Buf(f"b1_{m}") for m in range(NF)]
    Bbank = [Buf(f"pb{i}") for i in range(8)]
    Bcr = [Buf(f"bcr{c}") for c in range(NK)]
    Bci = [Buf(f"bci{c}") for c in range(NK)]

    block_cm = nc.Block()
    es.enter_context(block_cm)

    def dma(ename, ch, out_ap, in_ap, reads, writes, note="", cost=800):
        return P.emit(ename, reads, writes,
                      lambda e: e.dma_start(out=out_ap, in_=in_ap), chan=ch, note=note,
                      cost=cost)

    dma("qload", ch_const, gwr[:], gwh, [], [B["gwr"]])
    dma("qload", ch_const, vecs[:], vecsh, [], [B["vecs"]] + Bb1)
    for m_ in range(NK):
        P.emit("dve", [B["vecs"]], [B["vecs"]],
               lambda e, m_=m_: e.tensor_tensor(
                   out=vecs[:, V_B2C + m_:V_B2C + m_ + 1],
                   in0=vecs[:, V_B2C + m_:V_B2C + m_ + 1],
                   in1=vecs[:, V_LN1B + m_:V_LN1B + m_ + 1], op=ALU.add), cost=100)
    dma("qload", ch_const, cvec[:], cvech, [], [B["cvec"]])
    dma("qload", ch_const, onesrow[:], onesrowh, [], [B["onesrow"]])
    dma("qload", ch_const, bsel[:], bselh, [], [B["bsel"]])
    dma("qload", ch_const, zer[:], zerh, [], [B["zer"]])

    ch_sin = [Chan(sem(f"c_si{i}"), 16) for i in range(4)]
    ch_sout = [Chan(sem(f"c_so{i}"), 16) for i in range(4)]
    Bscr = {name: [Buf(f"scr_{name}_{i2}") for i2 in range(ap_.shape[0])]
            for name, ap_ in scr.items()}
    Bcsi = [Buf("cs_in0"), Buf("cs_in1")]
    Bcso = [Buf("cs_out0"), Buf("cs_out1")]
    si = [0]

    def cast_slab(src_ap, dst_ap, dst_buf, w1_m=None):
        ic = si[0] % 4
        i = si[0] % 2
        si[0] += 1
        ft, fb = cs_in[i], Bcsi[i]
        bt_ap, bb = cs_out[i][:], Bcso[i]
        dma("qload", ch_sin[ic], ft[:, 0:1024], src_ap, [], [fb], note="scast in")
        if w1_m is not None:
            for k in range(NK):
                P.emit("dve", [fb, B["vecs"]], [bb],
                       lambda e, ft=ft, k=k: e.tensor_scalar(
                           out=cs_out[i][:, k * 128:(k + 1) * 128],
                           in0=ft[:, k * 128:(k + 1) * 128],
                           scalar1=vecs[:, V_LN1S + k:V_LN1S + k + 1],
                           scalar2=None, op0=ALU.mult), cost=140)
            parts = [(ft[:, k * 128:(k + 1) * 128],
                      vecs[:, V_LN1B + k:V_LN1B + k + 1]) for k in range(NK)]
            outp, obuf = mmg(rotB, (4, 5, 6, 7), parts, [fb, B["vecs"]],
                             note="b1fold", m_rows=128, n_cols=1, cost=2000)
            P.emit("dve", [obuf, B["vecs"]], [Bb1[w1_m]],
                   lambda e, outp=outp, m=w1_m: e.scalar_tensor_tensor(
                       out=vecs[:, V_B1 + m:V_B1 + m + 1], in0=outp[:, 0:1],
                       scalar=1.0, in1=vecs[:, V_B1 + m:V_B1 + m + 1],
                       op0=ALU.mult, op1=ALU.add), cost=130)
        elif i == 0:
            P.emit("dve", [fb], [bb],
                   lambda e, ft=ft, bt_ap=bt_ap: e.tensor_copy(
                       out=bt_ap, in_=ft[:, 0:1024]))
        else:
            P.emit("act", [fb], [bb],
                   lambda e, ft=ft, bt_ap=bt_ap: e.activation(
                       out=bt_ap, in_=ft[:, 0:1024], func=AF.Copy))
        dma("qstore", ch_sout[ic], dst_ap, bt_ap, [bb], [dst_buf], note="scast out")

    P.emit("dve", [B["gwr"]], [B["gwb"]],
           lambda e: e.tensor_copy(out=gwb[:], in_=gwr[:]))
    P.emit("dve", [B["cvec"]], [B["onesb"]],
           lambda e: e.tensor_copy(out=onesb[:], in_=cvec[:, C_ONE:C_ONE + 1]))

    poolA = {"slots": [0, 1, 2, 3, 4], "ctr": [0], "q": "qload"}
    poolB = {"slots": [5, 6, 7, 8, 9, 10, 11], "ctr": [0], "q": "qload"}
    pinned = set()

    def load_pair(pool, name, m0, n2=2, note="", pin=False):
        slots = pool["slots"]
        while True:
            i = slots[pool["ctr"][0] % len(slots)]
            pool["ctr"][0] += 1
            if i not in pinned:
                break
        src_ap = scr[name][m0:m0 + n2, :, :].rearrange("a p m -> p a m")
        dma(pool["q"], ch_slot[i], slab[i][:, 0:n2 * 1024].rearrange(
            "p (a m) -> p a m", a=n2), src_ap,
            Bscr[name][m0:m0 + n2], [Bslab[i]], note=note)
        if pin:
            pinned.add(i)
            return slab[i], Bslab[i], i
        return slab[i], Bslab[i]

    rotA = [0]
    rotB = [0]

    def mmg(rot, banks, parts, reads, m_rows=128, note="", n_cols=None, cost=None):
        bi = banks[rot[0] % len(banks)]
        rot[0] += 1
        bank, bbuf = pb[bi], Bbank[bi]
        out = bank[0:m_rows, 0:(t if n_cols is None else n_cols)]
        n = len(parts)

        def fn(e):
            last = None
            for i, (l, r) in enumerate(parts):
                last = e.matmul(out, l, r, start=(i == 0), stop=(i == n - 1))
            return last
        P.emit("pe", reads, [bbuf], fn, note=note,
               cost=(180 + 116 * len(parts)) if cost is None else cost)
        return out, bbuf

    for name, n in W_NS:
        for i2 in range(n):
            cast_slab(wslab[name][i2, :, :], scr[name][i2, :, :], Bscr[name][i2],
                      w1_m=(i2 if name == "w1p" else None))
    cast_slab(v2h, scr["v2"][0, :, :], Bscr["v2"][0])

    def mmg_at(bi, parts, reads, m_rows=128, note="", cost=None):
        bank, bbuf = pb[bi], Bbank[bi]
        out = bank[0:m_rows, 0:t]
        n = len(parts)

        def fn(e):
            last = None
            for i, (l, r) in enumerate(parts):
                last = e.matmul(out, l, r, start=(i == 0), stop=(i == n - 1))
            return last
        P.emit("pe", reads, [bbuf], fn, note=note,
               cost=(180 + 116 * len(parts)) if cost is None else cost)
        return out, bbuf

    def ck(buf_t, c):
        return buf_t[:, c * t:(c + 1) * t]

    def copy_fn(out_ap, in_ap):
        def fn(e):
            if hasattr(e, "tensor_copy"):
                return e.tensor_copy(out=out_ap, in_=in_ap)
            return e.copy(out=out_ap, in_=in_ap)
        return fn

    E = P.emit

    def ln_stats(src_t, src_b, cast_t, cast_bs, sq_t, sq_bs, banks, st, stp, bc, bcb,
                 eps_col, bc2=None):
        E("act", [src_b], list(sq_bs),
          lambda e: e.activation(out=sq_t[:, 0:CH], in_=src_t[:], func=AF.Square),
          cost=2000)
        mmg_at(banks[0], [(cvec[0:128, C_ONE:C_ONE + 1], ck(src_t, c))
                          for c in range(NK)],
               [src_b, B["cvec"]], m_rows=1, note="sum", cost=1900)
        E("dve", [Bbank[banks[0]]], [stp["mu"]],
          lambda e: e.tensor_scalar(out=st["mu"][:], in0=pb[banks[0]][0:1, 0:t],
                                    scalar1=1.0 / D, scalar2=None, op0=ALU.mult))
        mmg_at(banks[1], [(onesb[:, 0:1], ck(sq_t, c)) for c in range(NK)],
               list(sq_bs) + [B["onesb"]], m_rows=1, note="sumsq")
        E("dve", [Bbank[banks[1]]], [stp["ms"]],
          lambda e: e.tensor_scalar(out=st["ms"][:], in0=pb[banks[1]][0:1, 0:t],
                                    scalar1=1.0 / D, scalar2=None, op0=ALU.mult))
        E("dve", [stp["mu"]], [stp["var"]],
          lambda e: e.tensor_tensor(out=st["var"][:], in0=st["mu"][:], in1=st["mu"][:],
                                    op=ALU.mult))
        E("dve", [stp["ms"], stp["var"]], [stp["var"]],
          lambda e: e.tensor_tensor(out=st["var"][:], in0=st["ms"][:], in1=st["var"][:],
                                    op=ALU.subtract))
        eps_val = EPS_MOD if eps_col == C_EPS_MOD else EPS_IN
        x_ap = st["ms"]
        xi = x_ap.bitcast(I32)
        yi = st["rstd"].bitcast(I32)
        E("dve", [stp["var"]], [stp["ms"]],
          lambda e: e.tensor_scalar(out=x_ap[:], in0=st["var"][:], scalar1=eps_val,
                                    scalar2=None, op0=ALU.add), cost=120)
        E("dve", [stp["ms"]], [stp["rstd"]],
          lambda e: e.tensor_scalar(out=yi[:], in0=xi[:], scalar1=1, scalar2=None,
                                    op0=ALU.logical_shift_right), cost=120)
        E("dve", [stp["rstd"]], [stp["rstd"]],
          lambda e: e.tensor_scalar(out=yi[:], in0=yi[:], scalar1=-1,
                                    scalar2=RSQRT_MAGIC, op0=ALU.mult,
                                    op1=ALU.add), cost=120)
        E("dve", [stp["ms"]], [stp["ms"]],
          lambda e: e.tensor_scalar(out=x_ap[:], in0=x_ap[:], scalar1=0.5,
                                    scalar2=None, op0=ALU.mult), cost=120)
        for _ in range(2):
            E("dve", [stp["rstd"]], [stp["var"]],
              lambda e: e.tensor_tensor(out=st["var"][:], in0=st["rstd"][:],
                                        in1=st["rstd"][:], op=ALU.mult), cost=120)
            E("dve", [stp["ms"], stp["var"]], [stp["var"]],
              lambda e: e.tensor_tensor(out=st["var"][:], in0=x_ap[:],
                                        in1=st["var"][:], op=ALU.mult), cost=120)
            E("dve", [stp["var"]], [stp["var"]],
              lambda e: e.tensor_scalar(out=st["var"][:], in0=st["var"][:],
                                        scalar1=-1.0, scalar2=1.5, op0=ALU.mult,
                                        op1=ALU.add), cost=120)
            E("dve", [stp["rstd"], stp["var"]], [stp["rstd"]],
              lambda e: e.tensor_tensor(out=st["rstd"][:], in0=st["rstd"][:],
                                        in1=st["var"][:], op=ALU.mult), cost=120)
        E("dve", [stp["mu"], stp["rstd"]], [stp["nmr"]],
          lambda e: e.scalar_tensor_tensor(out=st["nmr"][:], in0=st["mu"][:],
                                           scalar=-1.0, in1=st["rstd"][:],
                                           op0=ALU.mult, op1=ALU.mult))
        bank0 = pb[banks[0]]
        P.emit("pe", [B["onesrow"], stp["rstd"], stp["nmr"]], [Bbank[banks[0]]],
               lambda e: e.matmul(bank0[0:128, 0:2 * t], onesrow[0:1, :],
                                  st["rn"][0:1, 0:2 * t], start=True, stop=True),
               note="bc", cost=700)
        E("act", [Bbank[banks[0]]], [bcb[0], bcb[1]],
          copy_fn(bc2[:, 0:2 * t], bank0[:, 0:2 * t]), cost=900,
          alts={"act": 900.0, "dve": 900.0})

    def ln_apply(src_t, src_b, dst_t, dst_b, bc, bcb, affine=None, eng="dve"):
        for c in range(NK):
            ec = "dve"
            op_cost = 450
            E(ec, [src_b, bcb[0]], [dst_b],
              lambda e, c=c: e.tensor_tensor(out=ck(dst_t, c), in0=ck(src_t, c),
                                             in1=bc[0][:, 0:t], op=ALU.mult),
              cost=op_cost, flex=True)
            E(ec, [dst_b, bcb[1]], [dst_b],
              lambda e, c=c: e.tensor_tensor(out=ck(dst_t, c), in0=ck(dst_t, c),
                                             in1=bc[1][:, 0:t], op=ALU.add),
              cost=op_cost, flex=True)
            if affine is not None:
                sc, bcv = affine
                E(ec, [dst_b, B["vecs"]], [dst_b],
                  lambda e, c=c: e.tensor_scalar(
                      out=ck(dst_t, c), in0=ck(dst_t, c),
                      scalar1=vecs[:, sc + c:sc + c + 1],
                      scalar2=vecs[:, bcv + c:bcv + c + 1],
                      op0=ALU.mult, op1=ALU.add), cost=op_cost, flex=True)

    def proj(wname, dst_t, dst_bs, nm, rhs_t, rhs_b, m_off=0, act_fn=AF.Copy):
        for m0 in range(0, nm, 2):
            svt, svb = load_pair(poolA, wname, m0 + m_off, note=f"{wname} m{m0}")
            bi = (0, 1, 2, 3)[rotA[0] % 4]
            rotA[0] += 1
            bank, bbuf = pb[bi], Bbank[bi]

            def fn(e, svt=svt, bank=bank, rhs_t=rhs_t):
                last = None
                for d in range(2):
                    for k in range(NK):
                        last = e.matmul(
                            bank[0:128, d * t:(d + 1) * t],
                            svt[:, d * 1024 + k * 128:d * 1024 + (k + 1) * 128],
                            ck(rhs_t, k), start=(k == 0), stop=(k == NK - 1))
                return last
            P.emit("pe", [svb, rhs_b], [bbuf], fn, note=f"{wname} m{m0}",
                   cost=180 + 116 * 16)
            wide = dst_t[:, m0 * t:(m0 + 2) * t]
            if act_fn == AF.Copy:
                E("act", [bbuf], [dst_bs[m0 // 4]],
                  copy_fn(wide, bank[:, 0:2 * t]), cost=930,
                  alts={"act": 930.0, "dve": 930.0})
            else:
                E("act", [bbuf], [dst_bs[m0 // 4]],
                  lambda e, wide=wide, bank=bank, act_fn=act_fn: e.activation(
                      out=wide, in_=bank[:, 0:2 * t], func=act_fn),
                  table=("silu" if act_fn == AF.Silu else None), cost=930)

    def xload(i, b, j):
        par = i % 2
        t0 = j * t
        bxt, bxb = bx[par], B[f"bx{par}"]
        dma("qload", ch_x[par], bxt[:].rearrange("p (c t) -> p c t", c=NK),
            xh[b, :, t0:t0 + t].rearrange("(c p) t -> p c t", p=128),
            [], [bxb], note=f"x b{b} j{j}", cost=1500)

    def A1a(i, b, j):
        par = i % 2
        bxt, bxb = bx[par], B[f"bx{par}"]
        bzt, bzbf = bzbD[par], B[f"bzb{par}"]
        if j == 0:
            E("dve", [], Bcr, lambda e: e.memset(bcr[:], 0.0), cost=100)
            E("dve", [], Bci, lambda e: e.memset(bci[:], 0.0), cost=100)
        ln_stats(bxt, bxb, None, [],
                 bmix[par], [B[f"bmix{par}0"], B[f"bmix{par}1"]],
                 (3, 3), stA, stpA, bcA,
                 [B["bcA0"], B["bcA1"]], C_EPS_MOD, bc2=bcAt)
        for c in range(NK):
            hB = B[f"bt1{c // 4}"]
            E("dve", [bxb, B["bcA0"]], [hB],
              lambda e, c=c: e.tensor_tensor(out=ck(bt1, c), in0=ck(bxt, c),
                                             in1=bcA[0][:, 0:t], op=ALU.mult),
              cost=450, alts={"dve": 450.0, "gps": 1000.0})
            E("dve", [hB, B["bcA1"]], [bzbf],
              lambda e, c=c: e.tensor_tensor(out=ck(bzt, c), in0=ck(bt1, c),
                                             in1=bcA[1][:, 0:t], op=ALU.add),
              cost=450, alts={"dve": 450.0, "gps": 1000.0})

    def A1b(i):
        par = i % 2
        bzt, bzbf = bzbD[par], B[f"bzb{par}"]
        proj("wa", bar, [B["bar0"], B["bar1"]], NK, bzt, bzbf)
        proj("wa", bai, [B["bai0"], B["bai1"]], NK, bzt, bzbf, m_off=NK)
        proj("wk", bk, [B["bk0"], B["bk1"]], NK, bzt, bzbf)
        proj("wv", bv, [B["bv0"], B["bv1"]], NK, bzt, bzbf)

    Hw = CH // 2
    CD = 1350.0
    CA = 1150.0

    def chain_half(i, h):
        HSL = slice(h * Hw, (h + 1) * Hw)
        HS = lambda buf: buf[:, HSL]
        Bh = lambda nm: B[f"{nm}{h}"]
        Bk, Bv = B[f"bk{h}"], B[f"bv{h}"]
        bki_h = bar[:, HSL].bitcast(I32)
        E("dve", [Bk, Bv], [Bk],
          lambda e: e.tensor_tensor(out=HS(bk), in0=HS(bk), in1=HS(bv),
                                    op=ALU.mult), cost=700)
        E("act", [Bh("bar")], [Bh("bt1")],
          lambda e: e.activation(out=HS(bt1), in_=HS(bar), func=AF.Square), cost=CA)
        E("act", [Bh("bai")], [Bh("bt2")],
          lambda e: e.activation(out=HS(bt2), in_=HS(bai), func=AF.Square), cost=CA)
        E("dve", [Bh("bt1"), Bh("bt2")], [Bh("bt1")],
          lambda e: e.tensor_tensor(out=HS(bt1), in0=HS(bt1), in1=HS(bt2),
                                    op=ALU.add), cost=CD, flex=True)
        E("act", [Bh("bt1")], [Bh("bt2")],
          lambda e: e.activation(out=HS(bt2), in_=HS(bt1), func=AF.Ln),
          table="nlexp", cost=CA)
        E("act", [Bh("bt2"), B["cvec"]], [Bh("bt1")],
          lambda e: e.activation(out=HS(bt1), in_=HS(bt2), func=AF.Exp,
                                 scale=cvec[:, C_HALF:C_HALF + 1]),
          table="nlexp", cost=CA)
        E("act", [Bh("bar")], [Bh("bhr")],
          lambda e: e.activation(out=HS(bhr), in_=HS(bar), func=AF.Abs), cost=CA)
        E("act", [Bh("bar")], [Bh("bcos")],
          lambda e: e.activation(out=HS(bcos), in_=HS(bar), func=AF.Sign), cost=CA)
        E("act", [Bh("bai")], [Bh("bnsin")],
          lambda e: e.activation(out=HS(bnsin), in_=HS(bai), func=AF.Sign), cost=CA)
        E("dve", [Bh("bt1"), Bh("bhr")], [Bh("bt2")],
          lambda e: e.tensor_tensor(out=HS(bt2), in0=HS(bt1), in1=HS(bhr),
                                    op=ALU.add), cost=CD, flex=True)
        E("dve", [Bh("bt2")], [Bh("bt2")],
          lambda e: e.tensor_scalar(out=HS(bt2), in0=HS(bt2), scalar1=1e-30,
                                    scalar2=None, op0=ALU.max), cost=CD * 0.55)
        E("act", [Bh("bt2")], [Bh("bhr")],
          lambda e: e.activation(out=HS(bhr), in_=HS(bt2), func=AF.Ln),
          table="nlexp", cost=CA)
        E("act", [Bh("bhr"), B["cvec"]], [Bh("bt2")],
          lambda e: e.activation(out=HS(bt2), in_=HS(bhr), func=AF.Exp,
                                 scale=cvec[:, C_NEG1:C_NEG1 + 1]),
          table="nlexp", cost=CA)
        E("act", [Bh("bt1")], [Bh("bs")],
          lambda e: e.activation(out=HS(bs_), in_=HS(bt1), func=AF.Sigmoid),
          table="sigm", cost=CA)
        E("dve", [Bh("bai"), Bh("bt2")], [Bh("bt1")],
          lambda e: e.tensor_tensor(out=HS(bt1), in0=HS(bai), in1=HS(bt2),
                                    op=ALU.mult), cost=CD, flex=True)
        E("act", [Bh("bt1")], [Bh("bt2")],
          lambda e: e.activation(out=HS(bt2), in_=HS(bt1), func=AF.Arctan),
          table="sigm", cost=CA)
        E("dve", [Bh("bcos")], [Bh("bcos")],
          lambda e: e.tensor_scalar(out=HS(bcos), in0=HS(bcos), scalar1=-0.5,
                                    scalar2=0.5, op0=ALU.mult, op1=ALU.add),
          cost=CD * 0.55)
        E("dve", [Bh("bcos")], [Bh("bhr")],
          lambda e: e.tensor_scalar(out=HS(bhr), in0=HS(bcos), scalar1=-4.0,
                                    scalar2=2.0, op0=ALU.mult, op1=ALU.add),
          cost=CD * 0.55)
        E("dve", [Bh("bcos"), Bh("bnsin")], [Bh("bt1")],
          lambda e: e.tensor_tensor(out=HS(bt1), in0=HS(bcos), in1=HS(bnsin),
                                    op=ALU.mult), cost=CD, flex=True)
        E("dve", [Bh("bt2"), Bh("bhr")], [Bh("bt2")],
          lambda e: e.tensor_tensor(out=HS(bt2), in0=HS(bt2), in1=HS(bhr),
                                    op=ALU.mult), cost=CD, flex=True)
        E("dve", [Bh("bt1"), Bh("bt2")], [Bh("bai")],
          lambda e: e.scalar_tensor_tensor(out=HS(bai), in0=HS(bt1), scalar=PI,
                                           in1=HS(bt2), op0=ALU.mult,
                                           op1=ALU.add), cost=CD)
        for c in range(4 * h, 4 * h + 4):
            E("dve", [Bh("bai"), B["zer"]], [Bh("bai")],
              lambda e, c=c: e.tensor_tensor_scan(
                  out=ck(bai, c), data0=ck(bai, c), data1=zer[:, 0:t],
                  initial=zer[:, 0:1], op0=ALU.add, op1=ALU.add), cost=450)

        def red_sin(alpha_ap, alpha_bufs, dst_ap, dst_buf, neg):
            E("dve", alpha_bufs, [Bh("bar")],
              lambda e: e.tensor_scalar(out=bki_h, in0=alpha_ap, scalar1=INV_2PI,
                                        scalar2=None, op0=ALU.mult), cost=CD)
            E("dve", [Bh("bar")] + alpha_bufs, [Bh("bt1")],
              lambda e: e.scalar_tensor_tensor(out=HS(bt1), in0=bki_h,
                                               scalar=-TWO_PI, in1=alpha_ap,
                                               op0=ALU.mult, op1=ALU.add), cost=CD)
            if sim_safe_wrap:
                E("dve", [Bh("bt1")], [Bh("bt2")],
                  lambda e: e.tensor_scalar(out=HS(bt2), in0=HS(bt1), scalar1=-PI,
                                            scalar2=None, op0=ALU.is_lt), cost=CD)
                E("dve", [Bh("bt1")], [Bh("bhr")],
                  lambda e: e.tensor_scalar(out=HS(bhr), in0=HS(bt1), scalar1=PI,
                                            scalar2=None, op0=ALU.is_gt), cost=CD)
                E("dve", [Bh("bt2"), Bh("bhr")], [Bh("bt2")],
                  lambda e: e.tensor_tensor(out=HS(bt2), in0=HS(bt2), in1=HS(bhr),
                                            op=ALU.subtract), cost=CD)
                E("dve", [Bh("bt2"), Bh("bt1")], [Bh("bt1")],
                  lambda e: e.scalar_tensor_tensor(out=HS(bt1), in0=HS(bt2),
                                                   scalar=TWO_PI, in1=HS(bt1),
                                                   op0=ALU.mult, op1=ALU.add),
                  cost=CD)
            else:
                E("dve", [Bh("bt1")], [Bh("bt1")],
                  lambda e: e.tensor_scalar(out=HS(bt1), in0=HS(bt1), scalar1=-PI,
                                            scalar2=PI, op0=ALU.max, op1=ALU.min),
                  cost=CD * 0.55)
            if neg:
                E("act", [Bh("bt1"), B["cvec"]], [dst_buf],
                  lambda e: e.activation(out=dst_ap, in_=HS(bt1), func=AF.Sin,
                                         scale=cvec[:, C_NEG1:C_NEG1 + 1]),
                  table="silu", cost=CA)
            else:
                E("act", [Bh("bt1")], [dst_buf],
                  lambda e: e.activation(out=dst_ap, in_=HS(bt1), func=AF.Sin),
                  table="silu", cost=CA)

        red_sin(HS(bai), [Bh("bai")], HS(bnsin), Bh("bnsin"), neg=True)
        E("dve", [Bh("bai")], [Bh("bar")],
          lambda e: e.tensor_scalar(out=bki_h, in0=HS(bai), scalar1=INV_2PI,
                                    scalar2=0.25, op0=ALU.mult, op1=ALU.add),
          cost=CD)
        E("dve", [Bh("bar"), Bh("bai")], [Bh("bt1")],
          lambda e: e.scalar_tensor_tensor(out=HS(bt1), in0=bki_h,
                                           scalar=-TWO_PI, in1=HS(bai),
                                           op0=ALU.mult, op1=ALU.add), cost=CD)
        if sim_safe_wrap:
            E("dve", [Bh("bt1")], [Bh("bt2")],
              lambda e: e.tensor_scalar(out=HS(bt2), in0=HS(bt1),
                                        scalar1=-3 * PI / 2,
                                        scalar2=None, op0=ALU.is_lt), cost=CD)
            E("dve", [Bh("bt1")], [Bh("bhr")],
              lambda e: e.tensor_scalar(out=HS(bhr), in0=HS(bt1),
                                        scalar1=PI / 2,
                                        scalar2=None, op0=ALU.is_gt), cost=CD)
            E("dve", [Bh("bt2"), Bh("bhr")], [Bh("bt2")],
              lambda e: e.tensor_tensor(out=HS(bt2), in0=HS(bt2), in1=HS(bhr),
                                        op=ALU.subtract), cost=CD)
            E("dve", [Bh("bt2"), Bh("bt1")], [Bh("bt1")],
              lambda e: e.scalar_tensor_tensor(out=HS(bt1), in0=HS(bt2),
                                               scalar=TWO_PI, in1=HS(bt1),
                                               op0=ALU.mult, op1=ALU.add), cost=CD)
        else:
            E("dve", [Bh("bt1")], [Bh("bt1")],
              lambda e: e.tensor_scalar(out=HS(bt1), in0=HS(bt1),
                                        scalar1=-3 * PI / 2, scalar2=PI / 2,
                                        op0=ALU.max, op1=ALU.min), cost=CD * 0.55)
        E("act", [Bh("bt1"), B["cvec"]], [Bh("bcos")],
          lambda e: e.activation(out=HS(bcos), in_=HS(bt1), func=AF.Sin,
                                 bias=cvec[:, C_HALFPI:C_HALFPI + 1]),
          table="silu", cost=CA)
        E("dve", [Bk, Bh("bcos")], [Bh("bhr")],
          lambda e: e.tensor_tensor(out=HS(bhr), in0=HS(bk), in1=HS(bcos),
                                    op=ALU.mult), cost=700)
        E("dve", [Bk, Bh("bnsin")], [Bh("bhi")],
          lambda e: e.tensor_tensor(out=HS(bhi), in0=HS(bk), in1=HS(bnsin),
                                    op=ALU.mult), cost=700)
        for c in range(4 * h, 4 * h + 4):
            E("dve", [Bh("bs"), Bh("bhr"), Bcr[c]], [Bh("bhr")],
              lambda e, c=c: e.tensor_tensor_scan(
                  out=ck(bhr, c), data0=ck(bs_, c), data1=ck(bhr, c),
                  initial=bcr[:, c:c + 1], op0=ALU.mult, op1=ALU.add), cost=450)
            E("dve", [Bh("bs"), Bh("bhi"), Bci[c]], [Bh("bhi")],
              lambda e, c=c: e.tensor_tensor_scan(
                  out=ck(bhi, c), data0=ck(bs_, c), data1=ck(bhi, c),
                  initial=bci[:, c:c + 1], op0=ALU.mult, op1=ALU.add), cost=450)
        E("dve", [Bh("bhr"), Bh("bcos"), Bh("bs")], [Bh("bs")],
          lambda e: e.tensor_tensor(out=HS(bs_), in0=HS(bhr), in1=HS(bcos),
                                    op=ALU.mult), cost=700)
        E("dve", [Bh("bhi"), Bh("bnsin")], [Bk],
          lambda e: e.tensor_tensor(out=HS(bk), in0=HS(bhi), in1=HS(bnsin),
                                    op=ALU.mult), cost=700)
        E("dve", [Bh("bs"), Bk], [Bh("bs")],
          lambda e: e.tensor_tensor(out=HS(bs_), in0=HS(bs_), in1=HS(bk),
                                    op=ALU.add), cost=700)
        for c in range(4 * h, 4 * h + 4):
            L = c * t + t - 1
            s0 = h * Hw + 2 * (c - 4 * h)
            E("dve", [Bh("bs")], [Bcr[c]],
              lambda e, c=c, L=L: e.tensor_copy(out=bcr[:, c:c + 1],
                                                in_=bs_[:, L:L + 1]), cost=150)
            E("dve", [Bh("bhr"), Bh("bnsin")], [Bh("bt2")],
              lambda e, s0=s0, L=L: e.tensor_tensor(
                  out=bt2[:, s0:s0 + 1], in0=bhr[:, L:L + 1],
                  in1=bnsin[:, L:L + 1], op=ALU.mult), cost=150)
            E("dve", [Bh("bhi"), Bh("bcos")], [Bh("bt2")],
              lambda e, s0=s0, L=L: e.tensor_tensor(
                  out=bt2[:, s0 + 1:s0 + 2], in0=bhi[:, L:L + 1],
                  in1=bcos[:, L:L + 1], op=ALU.mult), cost=150)
            E("dve", [Bh("bt2")], [Bci[c]],
              lambda e, c=c, s0=s0: e.tensor_tensor(
                  out=bci[:, c:c + 1], in0=bt2[:, s0 + 1:s0 + 2],
                  in1=bt2[:, s0:s0 + 1], op=ALU.subtract), cost=150)

    def CH_(i):
        chain_half(i, 0)
        yield
        chain_half(i, 1)
        yield

    def A3a(i):
        proj("wq", bq, [B["bq0"], B["bq1"]], NK, bzbD[i % 2], B[f"bzb{i % 2}"])

    def A3b(i):
        par = i % 2
        proj("wg", bg, [B["bg0"], B["bg1"]], NK, bzbD[par], B[f"bzb{par}"],
             act_fn=AF.Silu)
        for h in (0, 1):
            HSL = slice(h * Hw, (h + 1) * Hw)
            HS = lambda buf, HSL=HSL: buf[:, HSL]
            Bh = lambda nm, h=h: B[f"{nm}{h}"]
            E("dve", [Bh("bq"), Bh("bs")], [Bh("bs")],
              lambda e, HS=HS: e.tensor_tensor(out=HS(bs_), in0=HS(bs_),
                                               in1=HS(bq), op=ALU.mult), cost=700)
            E("dve", [Bh("bs"), Bh("bg")], [B[f"bmix{par}{h}"]],
              lambda e, HS=HS, par=par: e.tensor_tensor(
                  out=HS(bmix[par]), in0=HS(bs_), in1=HS(bg), op=ALU.mult),
              cost=700)

    def B1(i, take=lambda n=1: None):
        par = i % 2
        bzt, bzbf = bzbD[par], B[f"bzb{par}"]
        for m0 in range(0, NK, 2):
            svt, svb = load_pair(poolB, "wo", m0, note=f"wo m{m0}")
            bi = (4, 5, 6, 7)[rotB[0] % 4]
            rotB[0] += 1
            bank, bbuf = pb[bi], Bbank[bi]

            def fn(e, svt=svt, bank=bank, par=par):
                last = None
                for d in range(2):
                    for k in range(NK):
                        last = e.matmul(
                            bank[0:128, d * t:(d + 1) * t],
                            svt[:, d * 1024 + k * 128:d * 1024 + (k + 1) * 128],
                            ck(bmix[par], k), start=(k == 0), stop=(k == NK - 1))
                return last
            P.emit("pe", [svb, B[f"bmix{par}0"], B[f"bmix{par}1"]], [bbuf], fn,
                   note=f"wo m{m0}", cost=180 + 116 * 16)
            E("dve", [bbuf, bzbf], [B["bw"]],
              lambda e, m0=m0, bank=bank: e.tensor_tensor(
                  out=bw[:, m0 * t:(m0 + 2) * t], in0=bank[:, 0:2 * t],
                  in1=bzt[:, m0 * t:(m0 + 2) * t], op=ALU.add), cost=900)
            take(2)
        ln_stats(bw, B["bw"], bwb, [B["bwb"]], bmix[par],
                 [B[f"bmix{par}0"], B[f"bmix{par}1"]],
                 (7, 7), stB, stpB, bcB,
                 [B["bcB0"], B["bcB1"]], C_EPS_IN, bc2=bcBt)
        for c in range(NK):
            E("dve", [B["bw"], B["bcB0"]], [B["bw"]],
              lambda e, c=c: e.tensor_tensor(out=ck(bw, c), in0=ck(bw, c),
                                             in1=bcB[0][:, 0:t], op=ALU.mult),
              cost=450, alts={"dve": 450.0, "gps": 1000.0})
            E("dve", [B["bw"], B["bcB1"]], [B["bwb"]],
              lambda e, c=c: e.tensor_tensor(out=ck(bwb, c), in0=ck(bw, c),
                                             in1=bcB[1][:, 0:t], op=ALU.add),
              cost=450, alts={"dve": 450.0, "gps": 1000.0})

    def B2(i, take=lambda n=1: None):
        for f0 in range(0, NF, 2):
            svt, svb = load_pair(poolB, "w1p", f0, note=f"w1 f{f0}")
            for d in range(2):
                f = f0 + d
                parts = [(svt[:, d * 1024 + k * 128:d * 1024 + (k + 1) * 128],
                          ck(bwb, k)) for k in range(NK)]
                out, bbuf = mmg(rotB, (4, 5, 6, 7), parts, [svb, B["bwb"]], note=f"w1 f{f}")
                E("act", [bbuf, B["vecs"], Bb1[f]], [Bh1[f // 8]],
                  lambda e, f=f, out=out: e.activation(
                      out=bh1[:, f * t:(f + 1) * t], in_=out, func=AF.Gelu_apprx_tanh,
                      bias=vecs[:, V_B1 + f:V_B1 + f + 1]), table="gelu", cost=550)
                take(2)

    def B3a(i, b, j):
        par = i % 2
        t0 = j * t
        bxt, bxb = bx[par], B[f"bx{par}"]
        for m in range(NK):
            w2s = [load_pair(poolB, "w2p", m * 4, note=f"w2 {m * 4}"),
                   load_pair(poolB, "w2p", m * 4 + 2, note=f"w2 {m * 4 + 2}")]
            parts = []
            for kb in range(NF):
                sl = w2s[kb // 16]
                d = (kb // 8) % 2
                parts.append((sl[0][:, d * 1024 + (kb % 8) * 128:
                                    d * 1024 + (kb % 8 + 1) * 128],
                              bh1[:, kb * t:(kb + 1) * t]))
            out, bbuf = mmg(rotB, (4, 5, 6, 7), parts,
                            [sv[1] for sv in w2s] + list(Bh1), note=f"w2 m{m}")
            E("dve", [bbuf, B["vecs"], B["bwb"]], [B["bw"]],
              lambda e, m=m, out=out: e.scalar_tensor_tensor(
                  out=ck(bw, m), in0=ck(bwb, m),
                  scalar=vecs[:, V_LN1S + m:V_LN1S + m + 1],
                  in1=out, op0=ALU.mult, op1=ALU.add))
            E("dve", [B["bw"], B["vecs"]], [B["bw"]],
              lambda e, m=m: e.tensor_scalar(
                  out=ck(bw, m), in0=ck(bw, m),
                  scalar1=vecs[:, V_B2C + m:V_B2C + m + 1], scalar2=None,
                  op0=ALU.add), cost=450)
        ln_stats(bw, B["bw"], bwb, [B["bwb"]], bmix[par],
                 [B[f"bmix{par}0"], B[f"bmix{par}1"]],
                 (7, 7), stB, stpB, bcB,
                 [B["bcB0"], B["bcB1"]], C_EPS_IN, bc2=bcBt)
        for c in range(NK):
            E("dve", [B["bw"], B["bcB0"]], [B["bw"]],
              lambda e, c=c: e.tensor_tensor(out=ck(bw, c), in0=ck(bw, c),
                                             in1=bcB[0][:, 0:t], op=ALU.mult),
              cost=450, alts={"dve": 450.0, "gps": 1000.0})
            E("dve", [B["bw"], B["bcB1"]], [B["bw"]],
              lambda e, c=c: e.tensor_tensor(out=ck(bw, c), in0=ck(bw, c),
                                             in1=bcB[1][:, 0:t], op=ALU.add),
              cost=450, alts={"dve": 450.0, "gps": 1000.0})
            E("dve", [B["bw"], B["vecs"]], [B["bw"]],
              lambda e, c=c: e.tensor_scalar(
                  out=ck(bw, c), in0=ck(bw, c),
                  scalar1=vecs[:, V_LN2S + c:V_LN2S + c + 1],
                  scalar2=vecs[:, V_LN2B + c:V_LN2B + c + 1],
                  op0=ALU.mult, op1=ALU.add), cost=450)

    def B3b(i, b, j):
        par = i % 2
        t0 = j * t
        bxt, bxb = bx[par], B[f"bx{par}"]
        E("dve", [bxb, B["bw"]], [bxb],
          lambda e: e.tensor_tensor(out=bxt[:, 0:Hw], in0=bxt[:, 0:Hw],
                                    in1=bw[:, 0:Hw], op=ALU.add), cost=1350,
          alts={"dve": 1350.0, "gps": 2900.0})
        E("dve", [bxb, B["bw"]], [bxb],
          lambda e: e.tensor_tensor(out=bxt[:, Hw:CH], in0=bxt[:, Hw:CH],
                                    in1=bw[:, Hw:CH], op=ALU.add), cost=1350,
          alts={"dve": 1350.0, "gps": 2900.0})
        ln_stats(bxt, bxb, bwb, [B["bwb"]], bmix[par],
                 [B[f"bmix{par}0"], B[f"bmix{par}1"]],
                 (7, 7), stB, stpB, bcB,
                 [B["bcB0"], B["bcB1"]], C_EPS_MOD, bc2=bcBt)
        for c in range(NK):
            E("dve", [bxb, B["bcB0"]], [B["bw"]],
              lambda e, c=c: e.tensor_tensor(out=ck(bw, c), in0=ck(bxt, c),
                                             in1=bcB[0][:, 0:t], op=ALU.mult),
              cost=450, alts={"dve": 450.0, "gps": 1000.0})
            E("dve", [B["bw"], B["bcB1"]], [B["bwb"]],
              lambda e, c=c: e.tensor_tensor(out=ck(bwb, c), in0=ck(bw, c),
                                             in1=bcB[1][:, 0:t], op=ALU.add),
              cost=450, alts={"dve": 450.0, "gps": 1000.0})
        mmg_at(7, [(gwb[:, k * 8:(k + 1) * 8], ck(bwb, k)) for k in range(NK)],
               [B["gwb"], B["bwb"]], m_rows=8, note="gate logits")
        E("act", [Bbank[7]], [B["sp"]],
          lambda e: e.activation(out=sp, in_=pb[7][0:8, 0:t], func=AF.Exp),
          table="nlexp", cost=400)
        mmg_at(7, [(cvec[0:8, C_ONE:C_ONE + 1], sp)],
               [B["sp"], B["cvec"]], m_rows=1, note="gate sum")
        E("dve", [Bbank[7]], [B["st_r"]],
          lambda e: e.reciprocal(out=st_r[:], in_=pb[7][0:1, 0:t]))
        E("dve", [B["st_r"]], [B["st_r"]],
          lambda e: e.tensor_scalar(out=st_r[:], in0=st_r[:], scalar1=1.0 / 16.0,
                                    scalar2=None, op0=ALU.mult))
        mmg_at(7, [(bsel[:], sp)], [B["bsel"], B["sp"]], m_rows=128, note="gate bc")
        E("act", [Bbank[7]], [B["sg128"]],
          lambda e: e.activation(out=sg128[:], in_=pb[7][:, 0:t], func=AF.Copy))
        uat, uab = load_pair(poolB, "uall", 0, n2=1, note="uall")
        out_t, bbuf_t = mmg(rotB, (4, 5, 6, 7),
                            [(uat[:, k * 128:(k + 1) * 128], ck(bwb, k))
                             for k in range(NK)], [uab, B["bwb"]], note="t lora")
        out_r, bbuf_r = mmg(rotB, (4, 5, 6, 7),
                            [(onesrow[0:1, :], st_r[0:1, 0:t])],
                            [B["onesrow"], B["st_r"]], note="r bc")
        E("dve", [bbuf_t, B["sg128"]], [B["stgb"]],
          lambda e: e.tensor_tensor(out=stgb[:], in0=out_t, in1=sg128[:],
                                    op=ALU.mult))
        E("dve", [B["stgb"], bbuf_r], [B["stgb"]],
          lambda e: e.tensor_tensor(out=stgb[:], in0=stgb[:], in1=out_r,
                                    op=ALU.mult))
        v2t, v2b, v2slot = load_pair(poolB, "v2", 0, n2=1, note="v2", pin=True)
        for m0 in range(0, NK, 2):
            svt, svb = load_pair(poolB, "fw", m0, note=f"fw m{m0}")
            for d in range(2):
                m = m0 + d
                parts = [(svt[:, d * 1024 + k * 128:d * 1024 + (k + 1) * 128],
                          ck(bwb, k)) for k in range(NK)]
                parts.append((v2t[:, m * 128:(m + 1) * 128], stgb[:]))
                out, bbuf = mmg(rotB, (4, 5, 6, 7), parts, [svb, B["bwb"], v2b, B["stgb"]],
                                note=f"ffn2 m{m}")
                sti = m % 2
                E("dve", [bbuf, bxb], [B[f"bst{sti}"]],
                  lambda e, m=m, out=out, sti=sti: e.tensor_tensor(
                      out=bst[sti][:], in0=out, in1=ck(bxt, m), op=ALU.add))
                dma("qstore", ch_out[sti],
                    oh[b, m * 128:(m + 1) * 128, t0:t0 + t], bst[sti][:],
                    [B[f"bst{sti}"]], [Buf("_out")], note=f"out b{b} j{j} m{m}")
        pinned.discard(v2slot)

    with es:
        for i in range(total + 1):
            if i < total:
                b, j = divmod(i, nt)
                xload(i, b, j)
                A1a(i, b, j)
                A1b(i)
                gen = CH_(i)
            else:
                gen = iter(())

            def take(n=1, gen=gen):
                for _ in range(n):
                    if next(gen, "_done") == "_done":
                        return
            if i > 0:
                B1(i - 1, take)
                B2(i - 1, take)
            take(10 ** 6)
            if i > 0:
                bb, jb = divmod(i - 1, nt)
                B3a(i - 1, bb, jb)
            if i < total:
                A3a(i)
            if i > 0:
                B3b(i - 1, bb, jb)
            if i < total:
                A3b(i)
        est = P.schedule_and_replay()
        import os
        if os.environ.get("K_PRINT_EST"):
            print(f"[sched] estimated makespan: {est / 1e6:.3f} ms")
        for ch in ch_out:
            if ch.cnt:
                P.engs["qstore"].raw.wait_ge(ch.sem, ch.cnt)
    return nc



def kernel(**inputs):
    x = np.asarray(inputs["L0_attn_x"], np.float32)
    w = host_pack_weights(inputs)
    nc = build_core()
    in_maps = []
    for ci in range(N_CORES):
        xc = np.ascontiguousarray(x[ci * NB:(ci + 1) * NB].transpose(0, 2, 1))
        m = dict(w)
        m["x"] = xc
        in_maps.append(m)
    res = run_bass_kernel_spmd(nc, in_maps, core_ids=list(range(N_CORES)))
    outs = [res.results[i]["out"] for i in range(N_CORES)]
    full = np.concatenate(outs, axis=0)
    return np.ascontiguousarray(full.transpose(0, 2, 1)).astype(np.float32)


# version 38
# speedup vs baseline: 1.1167x; 1.1167x over previous
import numpy as np
from contextlib import ExitStack

import concourse.bass as bass
import concourse.mybir as mybir
from concourse.bass_utils import run_bass_kernel_spmd

F32 = mybir.dt.float32
BF16 = mybir.dt.bfloat16
I32 = mybir.dt.int32
AF = mybir.ActivationFunctionType
ALU = mybir.AluOpType

D = 1024
NK = 8
DF = 4096
NF = 32
PI = float(np.pi)
TWO_PI = 2.0 * PI
INV_2PI = 1.0 / TWO_PI
EPS_MOD = 1e-5
EPS_IN = 1e-6

N_CORES = 8
NB = 4
S = 2048
T = 256
NSLOT = 12

C_EPS_MOD, C_EPS_IN, C_HALFPI, C_NEG1, C_ONE, C_NEGH, C_HALF = 0, 1, 2, 3, 4, 5, 6
V_LN1S, V_LN1B, V_LN2S, V_LN2B, V_B2, V_B1 = 0, 8, 16, 24, 32, 40
V_B2C = 72


class Buf:
    __slots__ = ("name", "writer", "readers")

    def __init__(self, name):
        self.name = name
        self.writer = None
        self.readers = []


class Chan:
    __slots__ = ("sem", "step", "cnt", "last_node")

    def __init__(self, sem, step):
        self.sem = sem
        self.step = step
        self.cnt = 0
        self.last_node = None


class Eng:
    __slots__ = ("raw", "name", "chan", "sat")

    def __init__(self, raw, name, chan):
        self.raw = raw
        self.name = name
        self.chan = chan
        self.sat = {}


class Node:
    __slots__ = ("nid", "ename", "fn", "reads", "writes", "chan", "note",
                 "cost", "table", "deps", "prio", "finish", "alts")

    def __init__(self, nid, ename, fn, reads, writes, chan, note, cost, table):
        self.nid = nid
        self.ename = ename
        self.alts = None
        self.fn = fn
        self.reads = reads
        self.writes = writes
        self.chan = chan
        self.note = note
        self.cost = cost
        self.table = table
        self.deps = set()
        self.prio = 0.0
        self.finish = 0.0


ACT_TABLE_NS = 5000.0
RSQRT_MAGIC = 0x5f3759df


class Prog:

    def __init__(self, loose_sync=False):
        self.engs = {}
        self.nodes = []
        self.capture = True
        self.loose_sync = loose_sync

    def add_eng(self, name, raw, chan):
        self.engs[name] = Eng(raw, name, chan)

    def emit(self, ename, reads, writes, fn, chan=None, note="", cost=None,
             table=None, flex=False, alts=None):
        if not self.capture:
            return self._emit_real(ename, reads, writes, fn, chan, note)
        if cost is None:
            cost = {"pe": 1000.0, "dve": 450.0, "act": 450.0, "gps": 5000.0}.get(
                ename, 2500.0)
        n = Node(len(self.nodes), ename, fn, list(reads), list(writes), chan,
                 note, float(cost), table)
        import os as _os
        if not _os.environ.get("K_FLEX"):
            flex = False
        if alts is not None:
            n.alts = dict(alts)
        elif flex == "copy3":
            n.alts = {"dve": float(cost), "gps": float(cost) * 2.0,
                      "act": float(cost) * 1.4}
        elif flex:
            n.alts = {"dve": float(cost), "gps": float(cost) * 2.0}
        if chan is not None and getattr(chan, "last_node", None) is not None:
            n.deps.add(chan.last_node)
        for b in reads:
            if b.writer is not None:
                n.deps.add(b.writer)
        for b in writes:
            if b.writer is not None:
                n.deps.add(b.writer)
            for r in b.readers:
                n.deps.add(r)
        n.deps.discard(n.nid)
        for b in writes:
            b.writer = n.nid
            b.readers = []
        for b in reads:
            b.readers.append(n.nid)
        if chan is not None:
            chan.last_node = n.nid
        self.nodes.append(n)
        return n

    def schedule_and_replay(self):
        nodes = self.nodes
        nn = len(nodes)
        succ = [[] for _ in range(nn)]
        indeg = [0] * nn
        for n in nodes:
            for d in n.deps:
                succ[d].append(n.nid)
                indeg[n.nid] += 1
        import heapq
        import os
        simkey = lambda e: "gps" if e == "qstore" else e
        keys = {simkey(e) for e in self.engs}
        rev = []
        rem = [len(succ[i]) for i in range(nn)]
        stack = [i for i in range(nn) if rem[i] == 0]
        while stack:
            i = stack.pop()
            rev.append(i)
            for p in nodes[i].deps:
                rem[p] -= 1
                if rem[p] == 0:
                    stack.append(p)

        def run_pass(wc, costs):
            prio = [0.0] * nn
            for i in rev:
                best = 0.0
                for s in succ[i]:
                    if prio[s] > best:
                        best = prio[s]
                prio[i] = costs[i] + best
            eng_time = {e: 0.0 for e in keys}
            act_tab = None
            ready = {e: [] for e in keys}
            dready = [0.0] * nn
            ind = indeg[:]
            scheduled = [False] * nn
            assign = [None] * nn

            def push(i):
                n3 = nodes[i]
                if n3.alts:
                    for e3 in n3.alts:
                        heapq.heappush(ready[e3], (-prio[i], i))
                else:
                    heapq.heappush(ready[simkey(n3.ename)], (-prio[i], i))

            for i in range(nn):
                if ind[i] == 0:
                    push(i)
            sched = []
            nsched = 0
            while nsched < nn:
                best = None
                for e, rl in ready.items():
                    while rl and scheduled[rl[0][1]]:
                        heapq.heappop(rl)
                    if not rl:
                        continue
                    p, i = rl[0]
                    start = max(eng_time[e], dready[i])
                    n3 = nodes[i]
                    ecost = n3.alts.get(e, costs[i]) if n3.alts else costs[i]
                    key = (start + wc * ecost, p)
                    if best is None or key < best[0]:
                        best = (key, e)
                key, e = best
                _, i = heapq.heappop(ready[e])
                n = nodes[i]
                scheduled[i] = True
                if n.alts:
                    assign[i] = e
                    cost = n.alts.get(e, costs[i])
                else:
                    assign[i] = n.ename
                    cost = costs[i]
                start = max(eng_time[e], dready[i])
                if e == "act" and n.table is not None and act_tab != n.table:
                    cost += ACT_TABLE_NS
                    act_tab = n.table
                fin = start + cost
                eng_time[e] = fin
                sched.append(i)
                nsched += 1
                for s in succ[i]:
                    if dready[s] < fin:
                        dready[s] = fin
                    ind[s] -= 1
                    if ind[s] == 0:
                        push(s)
            return sched, assign

        def evaluate(sched, assign):
            eng_time = {e: 0.0 for e in keys}
            act_tab = None
            dready = [0.0] * nn
            fin = [0.0] * nn
            for i in sched:
                n = nodes[i]
                e = simkey(assign[i])
                cost = n.alts.get(e, n.cost) if n.alts else n.cost
                start = max(eng_time[e], dready[i])
                if e == "act" and n.table is not None and act_tab != n.table:
                    cost += ACT_TABLE_NS
                    act_tab = n.table
                fin[i] = start + cost
                eng_time[e] = fin[i]
                for s in succ[i]:
                    if dready[s] < fin[i]:
                        dready[s] = fin[i]
            return max(fin), fin

        base_costs = [n.cost for n in nodes]
        trials = [(0.5, base_costs)]
        for wc in (0.0, 0.15, 0.25, 0.35, 0.42, 0.58, 0.65, 0.8, 1.0, 1.3):
            trials.append((wc, base_costs))
        n_extra = int(os.environ.get("K_SCHED_TRIALS", "0"))
        rngs = np.random.default_rng(1234)
        for tix in range(n_extra):
            wc = (0.35, 0.5, 0.65)[tix % 3]
            amp = 0.25 if tix % 2 else 0.12
            jit = rngs.uniform(1.0 - amp, 1.0 + amp, nn)
            trials.append((wc, [base_costs[i] * jit[i] for i in range(nn)]))
        best_pick = None
        for wc, costs in trials:
            sched, assign = run_pass(wc, costs)
            mk, fin = evaluate(sched, assign)
            if os.environ.get("K_PRINT_TRIALS"):
                print("[sched] trial wc=%s mk=%.3f" % (wc, mk / 1e6))
            if best_pick is None or mk < best_pick[0]:
                best_pick = (mk, sched, assign, fin, wc)
        mk_best, sched, assign, fin, wc_best = best_pick
        for i in range(nn):
            nodes[i].finish = fin[i]
            if nodes[i].alts:
                nodes[i].ename = assign[i] if assign[i] != "qstore" else "gps"
        if os.environ.get("K_PRINT_EST"):
            print("[sched] portfolio best wc=%s makespan %.3f ms"
                  % (wc_best, mk_best / 1e6))
        if os.environ.get("K_GAPS"):
            from collections import Counter
            gape = os.environ.get("K_GAPS")
            seq = [i for i in sched if simkey(nodes[i].ename) == gape]
            gap_c = Counter()
            gap_t = Counter()
            prev_end = 0.0
            for i in seq:
                n = nodes[i]
                stt_ = fin[i] - (n.alts.get(simkey(n.ename), n.cost)
                                 if n.alts else n.cost)
                if stt_ > prev_end + 500.0:
                    dmax, dn = -1.0, None
                    for d in n.deps:
                        if fin[d] > dmax:
                            dmax, dn = fin[d], d
                    if dn is not None:
                        k = nodes[dn]
                        nm = (k.note or "L%d" % k.fn.__code__.co_firstlineno)
                        me = (n.note or "L%d" % n.fn.__code__.co_firstlineno)
                        gap_c[(me.split(" ")[0], "<-", simkey(k.ename),
                               nm.split(" ")[0])] += 1
                        gap_t[(me.split(" ")[0], "<-", simkey(k.ename),
                               nm.split(" ")[0])] += stt_ - prev_end
                prev_end = max(prev_end, fin[i])
            tot = sum(gap_t.values())
            if os.environ.get("K_GAP1"):
                want = os.environ.get("K_GAP1")
                prev_end2 = 0.0
                shown = 0
                for i in seq:
                    n = nodes[i]
                    cst = n.alts.get(simkey(n.ename), n.cost) if n.alts else n.cost
                    stt_ = fin[i] - cst
                    nm = (n.note or "L%d" % n.fn.__code__.co_firstlineno)
                    if (stt_ > prev_end2 + 3000.0 and nm.startswith(want)
                            and shown < 2 and fin[i] > 1.5e6):
                        shown += 1
                        print("[gap1] %s start=%.1fus gap=%.2fus deps:"
                              % (nm, stt_ / 1e3, (stt_ - prev_end2) / 1e3))
                        dl = sorted(n.deps, key=lambda d: -fin[d])[:6]
                        for d in dl:
                            k = nodes[d]
                            dn = (k.note or "L%d" % k.fn.__code__.co_firstlineno)
                            dc = k.alts.get(simkey(k.ename), k.cost) if k.alts else k.cost
                            print("    dep fin=%.1fus start=%.1fus %s %s"
                                  % (fin[d] / 1e3, (fin[d] - dc) / 1e3,
                                     simkey(k.ename), dn))
                    prev_end2 = max(prev_end2, fin[i])
            print("[sched] %s gaps total %.3f ms; top releasers:" % (gape, tot / 1e6))
            for k2, v in gap_t.most_common(18):
                print("    %.3f ms n=%d  %s" % (v / 1e6, gap_c[k2], k2))
        if os.environ.get("K_PRINT_EST"):
            busy = {}
            for n2 in nodes:
                k = simkey(n2.ename)
                busy[k] = busy.get(k, 0.0) + n2.cost
            mk = max(nd.finish for nd in nodes)
            print("[sched] busy:", {k: round(v / 1e6, 2) for k, v in busy.items()},
                  "makespan", round(mk / 1e6, 2))
            eng_prev = {}
            prev_on_eng = {}
            start_t = {}
            for idx2 in sched:
                n2 = nodes[idx2]
                k = simkey(n2.ename)
                eng_prev[idx2] = prev_on_eng.get(k)
                prev_on_eng[k] = idx2
            cur = max(range(nn), key=lambda i2: nodes[i2].finish)
            from collections import Counter
            busy_c = Counter()
            stall_c = Counter()
            hops = 0
            while cur is not None and hops < 20000:
                hops += 1
                n2 = nodes[cur]
                st2 = n2.finish - n2.cost
                ep = eng_prev.get(cur)
                dmax, dnode = -1.0, None
                for d in n2.deps:
                    if nodes[d].finish > dmax:
                        dmax, dnode = nodes[d].finish, d
                epf = nodes[ep].finish if ep is not None else -1.0
                key2 = (simkey(n2.ename), (n2.note or "?").split(" ")[0])
                busy_c[key2] += n2.cost
                if epf >= dmax and ep is not None:
                    cur = ep
                elif dnode is not None:
                    if dmax < st2 - 1.0:
                        stall_c[("idle-before", key2[0])] += st2 - dmax
                    cur = dnode
                else:
                    break
            print("[sched] walk busy by (eng, op):")
            for k2, v in busy_c.most_common(14):
                print("   ", k2, round(v / 1e6, 2), "ms")
            print("[sched] stalls:", {k2: round(v / 1e6, 2) for k2, v in stall_c.most_common(8)})
        for n in nodes:
            for b in n.reads + n.writes:
                b.writer = None
                b.readers = []
            if n.chan is not None:
                n.chan.last_node = None
        self.capture = False
        for i in sched:
            n = nodes[i]
            self._emit_real(n.ename, n.reads, n.writes, n.fn, n.chan, n.note)
        return max(nd.finish for nd in nodes)

    def _emit_real(self, ename, reads, writes, fn, chan=None, note=""):
        eng = self.engs[ename]
        own = chan is None
        ch = eng.chan if own else chan
        deps = {}

        def add(mark):
            if mark is None:
                return
            sem, val, owner = mark
            k = id(sem)
            if k not in deps or deps[k][1] < val:
                deps[k] = mark

        if not own and ch.cnt > 0:
            add((ch.sem, ch.cnt, None))
        for b in reads:
            add(b.writer)
        for b in writes:
            add(b.writer)
            for m in b.readers:
                add(m)
        for sem, val, _owner in deps.values():
            if self.loose_sync and _owner is eng:
                continue
            cur = eng.sat.get(id(sem), 0)
            if val > cur:
                eng.raw.wait_ge(sem, val)
                eng.sat[id(sem)] = val
        last = fn(eng.raw)
        ch.cnt += ch.step
        last.then_inc(ch.sem, ch.step)
        mark = (ch.sem, ch.cnt, eng if own else None)
        for b in writes:
            b.writer = mark
            b.readers = []
        for b in reads:
            b.readers.append(mark)
        return mark



def _kslabs(w_eff):
    M = w_eff.shape[1]
    return np.ascontiguousarray(w_eff.reshape(NK, 128, M))


def _pack_resident(w_eff):
    M = w_eff.shape[1]
    return np.ascontiguousarray(
        w_eff.reshape(NK, 128, M).transpose(1, 0, 2).reshape(128, NK * M))


def _col(v):
    return np.ascontiguousarray(v.reshape(8, 128).T)


def _dmaj(w_eff, nm):
    return np.ascontiguousarray(
        w_eff.reshape(NK, 128, nm, 128).transpose(2, 1, 0, 3).reshape(nm, 128, 1024))


def host_pack_weights(I, t_tile=T):
    f32 = lambda a: np.ascontiguousarray(np.asarray(a, np.float32))
    w = {}
    w["wq"] = _dmaj(f32(I["L0_attn_Wq"]), 8)
    w["wk"] = _dmaj(f32(I["L0_attn_Wk"]), 8)
    w["wv"] = _dmaj(f32(I["L0_attn_Wv"]), 8)
    w["wg"] = _dmaj(f32(I["L0_attn_Wg"]), 8)
    w["wo"] = _dmaj(f32(I["L0_attn_Wo"]), 8)
    w["wa"] = _dmaj(f32(I["L0_attn_Wa"]), 16)
    w["w1p"] = _dmaj(f32(I["L0_attn_W1"]), NF)
    w2 = f32(I["L0_attn_W2"])
    w["w2p"] = np.ascontiguousarray(
        w2.reshape(NF, 128, NK, 128).transpose(2, 0, 1, 3)
        .reshape(NK, 4, NK, 128, 128).transpose(0, 1, 3, 2, 4)
        .reshape(NF, 128, 1024))
    w["fw"] = _dmaj(f32(I["L0_ffn_weight"]).T, 8)
    w["uall"] = _dmaj(f32(I["L0_ffn_lora_U"]).reshape(128, 1024).T, 1)
    w["v2"] = np.ascontiguousarray(f32(I["L0_ffn_lora_V"]).transpose(0, 2, 1).reshape(128, 1024))
    w["gw"] = _pack_resident(f32(I["L0_ffn_gate_w"]).T)
    vecs = np.zeros((128, 80), np.float32)
    vecs[:, V_LN1S:V_LN1S + 8] = _col(f32(I["L0_attn_ln1_scale"]))
    vecs[:, V_LN1B:V_LN1B + 8] = _col(f32(I["L0_attn_ln1_bias"]))
    vecs[:, V_LN2S:V_LN2S + 8] = _col(f32(I["L0_attn_ln2_scale"]))
    vecs[:, V_LN2B:V_LN2B + 8] = _col(f32(I["L0_attn_ln2_bias"]))
    vecs[:, V_B2:V_B2 + 8] = _col(f32(I["L0_attn_b2"]))
    vecs[:, V_B2C:V_B2C + 8] = _col(f32(I["L0_attn_b2"]))
    vecs[:, V_B1:V_B1 + 32] = np.ascontiguousarray(f32(I["L0_attn_b1"]).reshape(32, 128).T)
    w["vecs"] = vecs
    cvec = np.zeros((128, 8), np.float32)
    cvec[:, C_EPS_MOD] = EPS_MOD
    cvec[:, C_EPS_IN] = EPS_IN
    cvec[:, C_HALFPI] = PI / 2
    cvec[:, C_NEG1] = -1.0
    cvec[:, C_ONE] = 1.0
    cvec[:, C_NEGH] = -0.5
    cvec[:, C_HALF] = 0.5
    w["cvec"] = cvec
    w["onesrow"] = np.ones((1, 128), np.float32)
    bsel = np.zeros((8, 128), np.float32)
    for e in range(8):
        bsel[e, 16 * e:16 * (e + 1)] = 1.0
    w["bsel"] = bsel
    import ml_dtypes
    w["zer"] = np.zeros((128, t_tile), ml_dtypes.bfloat16)
    return w




def build_core(nb=NB, s=S, t=T, sim_safe_wrap=False, loose_sync=True):
    CH = NK * t
    nt = s // t
    total = nb * nt
    nc = bass.Bass("TRN2", target_bir_lowering=False)

    xh = nc.dram_tensor("x", [nb, D, s], F32, kind="ExternalInput").ap()
    oh = nc.dram_tensor("out", [nb, D, s], F32, kind="ExternalOutput").ap()
    wslab = {}
    for name, n in [("wq", 8), ("wk", 8), ("wv", 8), ("wg", 8), ("wo", 8),
                    ("wa", 16), ("w1p", 32), ("w2p", 32), ("fw", 8)]:
        wslab[name] = nc.dram_tensor(name, [n, 128, 1024], F32, kind="ExternalInput").ap()
    wslab["uall"] = nc.dram_tensor("uall", [1, 128, 1024], F32, kind="ExternalInput").ap()
    v2h = nc.dram_tensor("v2", [128, 1024], F32, kind="ExternalInput").ap()
    W_NS = [("wa", 16), ("wk", 8), ("wv", 8), ("wq", 8), ("wg", 8), ("wo", 8),
            ("w1p", 32), ("w2p", 32), ("fw", 8), ("uall", 1)]
    scr = {name: nc.dram_tensor(f"scr_{name}", [n, 128, 1024], BF16, kind="Internal").ap()
           for name, n in W_NS}
    scr["v2"] = nc.dram_tensor("scr_v2", [1, 128, 1024], BF16, kind="Internal").ap()
    gwh = nc.dram_tensor("gw", [128, 64], F32, kind="ExternalInput").ap()
    vecsh = nc.dram_tensor("vecs", [128, 80], F32, kind="ExternalInput").ap()
    cvech = nc.dram_tensor("cvec", [128, 8], F32, kind="ExternalInput").ap()
    onesrowh = nc.dram_tensor("onesrow", [1, 128], F32, kind="ExternalInput").ap()
    bselh = nc.dram_tensor("bsel", [8, 128], F32, kind="ExternalInput").ap()
    zerh = nc.dram_tensor("zer", [128, t], BF16, kind="ExternalInput").ap()

    es = ExitStack()
    sb = lambda name, shape, dt=F32: es.enter_context(nc.sbuf_tensor(name, shape, dt))

    bx = [sb("bx0", [128, CH]), sb("bx1", [128, CH])]
    bzbD = [sb("bzb0", [128, CH], BF16), sb("bzb1", [128, CH], BF16)]
    bk = sb("bk", [128, CH], BF16); bv = sb("bv", [128, CH], BF16)
    cs_in = [sb("cs_in0", [128, 1024]), sb("cs_in1", [128, 1024])]
    cs_out = [sb("cs_out0", [128, 1024], BF16), sb("cs_out1", [128, 1024], BF16)]
    bar = sb("bar", [128, CH]); bai = sb("bai", [128, CH])
    bq = sb("bq", [128, CH], BF16); bg = sb("bg", [128, CH], BF16)
    bs_ = sb("bs", [128, CH], BF16)
    bcos = sb("bcos", [128, CH], BF16); bnsin = sb("bnsin", [128, CH], BF16)
    bhr = sb("bhr", [128, CH], BF16); bhi = sb("bhi", [128, CH], BF16)
    bt1 = sb("bt1", [128, CH]); bt2 = sb("bt2", [128, CH])
    bmix = [sb("bmix0", [128, CH], BF16), sb("bmix1", [128, CH], BF16)]
    bki = bar[:, 0:CH].bitcast(I32)
    bw = sb("bw", [128, CH])
    bwb = sb("bwb", [128, CH], BF16)
    bh1 = sb("bh1", [128, NF * t], BF16)
    bst = [sb("bst0", [128, t]), sb("bst1", [128, t])]
    sp = bst[0][0:8, :]
    bcAt = sb("bcAt", [128, 2 * t])
    bcBt = sb("bcBt", [128, 2 * t])
    bcA = [bcAt[:, 0:t], bcAt[:, t:2 * t]]
    bcB = [bcBt[:, 0:t], bcBt[:, t:2 * t]]
    stA = {nm: sb(f"stA_{nm}", [1, t]) for nm in ["mu", "ms", "var"]}
    stB = {nm: sb(f"stB_{nm}", [1, t]) for nm in ["mu", "ms", "var"]}
    stA_rn = sb("stA_rn", [1, 2 * t])
    stB_rn = sb("stB_rn", [1, 2 * t])
    for st_, rn_ in ((stA, stA_rn), (stB, stB_rn)):
        st_["rn"] = rn_
        st_["rstd"] = rn_[:, 0:t]
        st_["nmr"] = rn_[:, t:2 * t]
    st_r = stB["mu"]
    sg128 = sb("sg128", [128, t], BF16)
    stgb = sb("stgb", [128, t], BF16)
    slab = [sb(f"slab{i}", [128, 2048], BF16) for i in range(12)]
    gwr = sb("gwr", [128, 64]); gwb = sb("gwb", [128, 64], BF16)
    onesb = sb("onesb", [128, 1], BF16)
    vecs = sb("vecs_s", [128, 80]); cvec = sb("cvec_s", [128, 8])
    onesrow = sb("onesrow_s", [1, 128]); bsel = sb("bsel_s", [8, 128])
    zer = sb("zer_s", [128, t], BF16)
    bcr = sb("bcr", [128, 8]); bci = sb("bci", [128, 8])
    pb = [es.enter_context(nc.psum_tensor(f"pb{i}", [128, 512], F32)) for i in range(8)]

    sem = lambda name: es.enter_context(nc.semaphore(name))
    P = Prog(loose_sync=loose_sync)
    P.add_eng("pe", nc.tensor, Chan(sem("s_pe"), 1))
    P.add_eng("act", nc.scalar, Chan(sem("s_act"), 1))
    P.add_eng("dve", nc.vector, Chan(sem("s_dve"), 1))
    P.add_eng("gps", nc.gpsimd, Chan(sem("s_gps"), 1))
    P.add_eng("qload", nc.sync, None)
    P.add_eng("qstore", nc.gpsimd, None)
    ch_const = Chan(sem("c_const"), 16)
    ch_x = [Chan(sem(f"c_x{i}"), 16) for i in range(2)]
    ch_slot = [Chan(sem(f"c_sl{i}"), 16) for i in range(12)]
    ch_out = [Chan(sem(f"c_out{i}"), 16) for i in range(2)]

    B = {}
    for nm in ["bx0", "bx1", "bzb0", "bzb1", "bw", "bwb",
               "bst0", "bst1", "bcA0", "bcA1", "bcB0", "bcB1", "st_r", "sp",
               "sg128", "stgb", "gwr", "gwb", "onesb", "vecs", "cvec",
               "onesrow", "bsel", "zer"]:
        B[nm] = Buf(nm)
    Bh1 = [Buf(f"bh1_{q}") for q in range(4)]
    for nm in ["bk", "bv", "bar", "bai", "bq", "bg", "bs",
               "bcos", "bnsin", "bhr", "bhi", "bt1", "bt2", "bmix0", "bmix1"]:
        B[nm + "0"] = Buf(nm + "0")
        B[nm + "1"] = Buf(nm + "1")
    for nm in ["mu", "ms", "var", "rstd", "nmr"]:
        B[f"stA_{nm}"] = Buf(f"stA_{nm}")
        B[f"stB_{nm}"] = Buf(f"stB_{nm}")
    B["st_r"] = B["stB_mu"]
    stpA = {k: B[f"stA_{k}"] for k in ["mu", "ms", "var", "rstd", "nmr"]}
    stpB = {k: B[f"stB_{k}"] for k in ["mu", "ms", "var", "rstd", "nmr"]}
    B["sp"] = B["bst0"]
    Bslab = [Buf(f"slab{i}") for i in range(12)]
    Bb1 = [Buf(f"b1_{m}") for m in range(NF)]
    Bbank = [Buf(f"pb{i}") for i in range(8)]
    Bcr = [Buf(f"bcr{c}") for c in range(NK)]
    Bci = [Buf(f"bci{c}") for c in range(NK)]

    block_cm = nc.Block()
    es.enter_context(block_cm)

    def dma(ename, ch, out_ap, in_ap, reads, writes, note="", cost=800):
        return P.emit(ename, reads, writes,
                      lambda e: e.dma_start(out=out_ap, in_=in_ap), chan=ch, note=note,
                      cost=cost)

    dma("qload", ch_const, gwr[:], gwh, [], [B["gwr"]])
    dma("qload", ch_const, vecs[:], vecsh, [], [B["vecs"]] + Bb1)
    for m_ in range(NK):
        P.emit("dve", [B["vecs"]], [B["vecs"]],
               lambda e, m_=m_: e.tensor_tensor(
                   out=vecs[:, V_B2C + m_:V_B2C + m_ + 1],
                   in0=vecs[:, V_B2C + m_:V_B2C + m_ + 1],
                   in1=vecs[:, V_LN1B + m_:V_LN1B + m_ + 1], op=ALU.add), cost=100)
    dma("qload", ch_const, cvec[:], cvech, [], [B["cvec"]])
    dma("qload", ch_const, onesrow[:], onesrowh, [], [B["onesrow"]])
    dma("qload", ch_const, bsel[:], bselh, [], [B["bsel"]])
    dma("qload", ch_const, zer[:], zerh, [], [B["zer"]])

    ch_sin = [Chan(sem(f"c_si{i}"), 16) for i in range(4)]
    ch_sout = [Chan(sem(f"c_so{i}"), 16) for i in range(4)]
    Bscr = {name: [Buf(f"scr_{name}_{i2}") for i2 in range(ap_.shape[0])]
            for name, ap_ in scr.items()}
    Bcsi = [Buf("cs_in0"), Buf("cs_in1")]
    Bcso = [Buf("cs_out0"), Buf("cs_out1")]
    si = [0]

    def cast_slab(src_ap, dst_ap, dst_buf, w1_m=None):
        ic = si[0] % 4
        i = si[0] % 2
        si[0] += 1
        ft, fb = cs_in[i], Bcsi[i]
        bt_ap, bb = cs_out[i][:], Bcso[i]
        dma("qload", ch_sin[ic], ft[:, 0:1024], src_ap, [], [fb], note="scast in")
        if w1_m is not None:
            for k in range(NK):
                P.emit("dve", [fb, B["vecs"]], [bb],
                       lambda e, ft=ft, k=k: e.tensor_scalar(
                           out=cs_out[i][:, k * 128:(k + 1) * 128],
                           in0=ft[:, k * 128:(k + 1) * 128],
                           scalar1=vecs[:, V_LN1S + k:V_LN1S + k + 1],
                           scalar2=None, op0=ALU.mult), cost=140)
            parts = [(ft[:, k * 128:(k + 1) * 128],
                      vecs[:, V_LN1B + k:V_LN1B + k + 1]) for k in range(NK)]
            outp, obuf = mmg(rotB, (4, 5, 6, 7), parts, [fb, B["vecs"]],
                             note="b1fold", m_rows=128, n_cols=1, cost=2000)
            P.emit("dve", [obuf, B["vecs"]], [Bb1[w1_m]],
                   lambda e, outp=outp, m=w1_m: e.scalar_tensor_tensor(
                       out=vecs[:, V_B1 + m:V_B1 + m + 1], in0=outp[:, 0:1],
                       scalar=1.0, in1=vecs[:, V_B1 + m:V_B1 + m + 1],
                       op0=ALU.mult, op1=ALU.add), cost=130)
        elif i == 0:
            P.emit("dve", [fb], [bb],
                   lambda e, ft=ft, bt_ap=bt_ap: e.tensor_copy(
                       out=bt_ap, in_=ft[:, 0:1024]))
        else:
            P.emit("act", [fb], [bb],
                   lambda e, ft=ft, bt_ap=bt_ap: e.activation(
                       out=bt_ap, in_=ft[:, 0:1024], func=AF.Copy))
        dma("qstore", ch_sout[ic], dst_ap, bt_ap, [bb], [dst_buf], note="scast out")

    P.emit("dve", [B["gwr"]], [B["gwb"]],
           lambda e: e.tensor_copy(out=gwb[:], in_=gwr[:]))
    P.emit("dve", [B["cvec"]], [B["onesb"]],
           lambda e: e.tensor_copy(out=onesb[:], in_=cvec[:, C_ONE:C_ONE + 1]))

    poolA = {"slots": [0, 1, 2, 3, 4], "ctr": [0], "q": "qload"}
    poolB = {"slots": [5, 6, 7, 8, 9, 10, 11], "ctr": [0], "q": "qload"}
    pinned = set()

    def load_pair(pool, name, m0, n2=2, note="", pin=False):
        slots = pool["slots"]
        while True:
            i = slots[pool["ctr"][0] % len(slots)]
            pool["ctr"][0] += 1
            if i not in pinned:
                break
        src_ap = scr[name][m0:m0 + n2, :, :].rearrange("a p m -> p a m")
        dma(pool["q"], ch_slot[i], slab[i][:, 0:n2 * 1024].rearrange(
            "p (a m) -> p a m", a=n2), src_ap,
            Bscr[name][m0:m0 + n2], [Bslab[i]], note=note)
        if pin:
            pinned.add(i)
            return slab[i], Bslab[i], i
        return slab[i], Bslab[i]

    rotA = [0]
    rotB = [0]

    def mmg(rot, banks, parts, reads, m_rows=128, note="", n_cols=None, cost=None):
        bi = banks[rot[0] % len(banks)]
        rot[0] += 1
        bank, bbuf = pb[bi], Bbank[bi]
        out = bank[0:m_rows, 0:(t if n_cols is None else n_cols)]
        n = len(parts)

        def fn(e):
            last = None
            for i, (l, r) in enumerate(parts):
                last = e.matmul(out, l, r, start=(i == 0), stop=(i == n - 1))
            return last
        P.emit("pe", reads, [bbuf], fn, note=note,
               cost=(180 + 116 * len(parts)) if cost is None else cost)
        return out, bbuf

    for name, n in W_NS:
        for i2 in range(n):
            cast_slab(wslab[name][i2, :, :], scr[name][i2, :, :], Bscr[name][i2],
                      w1_m=(i2 if name == "w1p" else None))
    cast_slab(v2h, scr["v2"][0, :, :], Bscr["v2"][0])

    def mmg_at(bi, parts, reads, m_rows=128, note="", cost=None):
        bank, bbuf = pb[bi], Bbank[bi]
        out = bank[0:m_rows, 0:t]
        n = len(parts)

        def fn(e):
            last = None
            for i, (l, r) in enumerate(parts):
                last = e.matmul(out, l, r, start=(i == 0), stop=(i == n - 1))
            return last
        P.emit("pe", reads, [bbuf], fn, note=note,
               cost=(180 + 116 * len(parts)) if cost is None else cost)
        return out, bbuf

    def ck(buf_t, c):
        return buf_t[:, c * t:(c + 1) * t]

    def copy_fn(out_ap, in_ap):
        def fn(e):
            if hasattr(e, "tensor_copy"):
                return e.tensor_copy(out=out_ap, in_=in_ap)
            return e.copy(out=out_ap, in_=in_ap)
        return fn

    E = P.emit

    def ln_stats(src_t, src_b, cast_t, cast_bs, sq_t, sq_bs, banks, st, stp, bc, bcb,
                 eps_col, bc2=None):
        E("act", [src_b], list(sq_bs),
          lambda e: e.activation(out=sq_t[:, 0:CH], in_=src_t[:], func=AF.Square),
          cost=2000)
        mmg_at(banks[0], [(cvec[0:128, C_ONE:C_ONE + 1], ck(src_t, c))
                          for c in range(NK)],
               [src_b, B["cvec"]], m_rows=1, note="sum", cost=1900)
        E("dve", [Bbank[banks[0]]], [stp["mu"]],
          lambda e: e.tensor_scalar(out=st["mu"][:], in0=pb[banks[0]][0:1, 0:t],
                                    scalar1=1.0 / D, scalar2=None, op0=ALU.mult))
        mmg_at(banks[1], [(onesb[:, 0:1], ck(sq_t, c)) for c in range(NK)],
               list(sq_bs) + [B["onesb"]], m_rows=1, note="sumsq")
        E("dve", [Bbank[banks[1]]], [stp["ms"]],
          lambda e: e.tensor_scalar(out=st["ms"][:], in0=pb[banks[1]][0:1, 0:t],
                                    scalar1=1.0 / D, scalar2=None, op0=ALU.mult))
        E("dve", [stp["mu"]], [stp["var"]],
          lambda e: e.tensor_tensor(out=st["var"][:], in0=st["mu"][:], in1=st["mu"][:],
                                    op=ALU.mult))
        E("dve", [stp["ms"], stp["var"]], [stp["var"]],
          lambda e: e.tensor_tensor(out=st["var"][:], in0=st["ms"][:], in1=st["var"][:],
                                    op=ALU.subtract))
        eps_val = EPS_MOD if eps_col == C_EPS_MOD else EPS_IN
        x_ap = st["ms"]
        xi = x_ap.bitcast(I32)
        yi = st["rstd"].bitcast(I32)
        E("dve", [stp["var"]], [stp["ms"]],
          lambda e: e.tensor_scalar(out=x_ap[:], in0=st["var"][:], scalar1=eps_val,
                                    scalar2=None, op0=ALU.add), cost=120)
        E("dve", [stp["ms"]], [stp["rstd"]],
          lambda e: e.tensor_scalar(out=yi[:], in0=xi[:], scalar1=1, scalar2=None,
                                    op0=ALU.logical_shift_right), cost=120)
        E("dve", [stp["rstd"]], [stp["rstd"]],
          lambda e: e.tensor_scalar(out=yi[:], in0=yi[:], scalar1=-1,
                                    scalar2=RSQRT_MAGIC, op0=ALU.mult,
                                    op1=ALU.add), cost=120)
        E("dve", [stp["ms"]], [stp["ms"]],
          lambda e: e.tensor_scalar(out=x_ap[:], in0=x_ap[:], scalar1=0.5,
                                    scalar2=None, op0=ALU.mult), cost=120)
        for _ in range(2):
            E("dve", [stp["rstd"]], [stp["var"]],
              lambda e: e.tensor_tensor(out=st["var"][:], in0=st["rstd"][:],
                                        in1=st["rstd"][:], op=ALU.mult), cost=120)
            E("dve", [stp["ms"], stp["var"]], [stp["var"]],
              lambda e: e.tensor_tensor(out=st["var"][:], in0=x_ap[:],
                                        in1=st["var"][:], op=ALU.mult), cost=120)
            E("dve", [stp["var"]], [stp["var"]],
              lambda e: e.tensor_scalar(out=st["var"][:], in0=st["var"][:],
                                        scalar1=-1.0, scalar2=1.5, op0=ALU.mult,
                                        op1=ALU.add), cost=120)
            E("dve", [stp["rstd"], stp["var"]], [stp["rstd"]],
              lambda e: e.tensor_tensor(out=st["rstd"][:], in0=st["rstd"][:],
                                        in1=st["var"][:], op=ALU.mult), cost=120)
        E("dve", [stp["mu"], stp["rstd"]], [stp["nmr"]],
          lambda e: e.scalar_tensor_tensor(out=st["nmr"][:], in0=st["mu"][:],
                                           scalar=-1.0, in1=st["rstd"][:],
                                           op0=ALU.mult, op1=ALU.mult))
        bank0 = pb[banks[0]]
        P.emit("pe", [B["onesrow"], stp["rstd"], stp["nmr"]], [Bbank[banks[0]]],
               lambda e: e.matmul(bank0[0:128, 0:2 * t], onesrow[0:1, :],
                                  st["rn"][0:1, 0:2 * t], start=True, stop=True),
               note="bc", cost=700)
        E("act", [Bbank[banks[0]]], [bcb[0], bcb[1]],
          copy_fn(bc2[:, 0:2 * t], bank0[:, 0:2 * t]), cost=900,
          alts={"act": 900.0, "dve": 900.0})

    def ln_apply(src_t, src_b, dst_t, dst_b, bc, bcb, affine=None, eng="dve"):
        for c in range(NK):
            ec = "dve"
            op_cost = 450
            E(ec, [src_b, bcb[0]], [dst_b],
              lambda e, c=c: e.tensor_tensor(out=ck(dst_t, c), in0=ck(src_t, c),
                                             in1=bc[0][:, 0:t], op=ALU.mult),
              cost=op_cost, flex=True)
            E(ec, [dst_b, bcb[1]], [dst_b],
              lambda e, c=c: e.tensor_tensor(out=ck(dst_t, c), in0=ck(dst_t, c),
                                             in1=bc[1][:, 0:t], op=ALU.add),
              cost=op_cost, flex=True)
            if affine is not None:
                sc, bcv = affine
                E(ec, [dst_b, B["vecs"]], [dst_b],
                  lambda e, c=c: e.tensor_scalar(
                      out=ck(dst_t, c), in0=ck(dst_t, c),
                      scalar1=vecs[:, sc + c:sc + c + 1],
                      scalar2=vecs[:, bcv + c:bcv + c + 1],
                      op0=ALU.mult, op1=ALU.add), cost=op_cost, flex=True)

    def proj(wname, dst_t, dst_bs, nm, rhs_t, rhs_b, m_off=0, act_fn=AF.Copy):
        for m0 in range(0, nm, 2):
            svt, svb = load_pair(poolA, wname, m0 + m_off, note=f"{wname} m{m0}")
            for d in range(2):
                m = m0 + d
                parts = [(svt[:, d * 1024 + k * 128:d * 1024 + (k + 1) * 128],
                          ck(rhs_t, k)) for k in range(NK)]
                out, bbuf = mmg(rotA, (0, 1, 2, 3), parts, [svb, rhs_b],
                                note=f"{wname} m{m}")
                if act_fn == AF.Copy:
                    E("act", [bbuf], [dst_bs[m // 4]],
                      copy_fn(ck(dst_t, m), out), cost=480,
                      alts={"act": 480.0, "dve": 480.0})
                else:
                    E("act", [bbuf], [dst_bs[m // 4]],
                      lambda e, m=m, out=out, dst_t=dst_t, act_fn=act_fn: e.activation(
                          out=ck(dst_t, m), in_=out, func=act_fn),
                      table=("silu" if act_fn == AF.Silu else None), cost=480)

    def xload(i, b, j):
        par = i % 2
        t0 = j * t
        bxt, bxb = bx[par], B[f"bx{par}"]
        dma("qload", ch_x[par], bxt[:].rearrange("p (c t) -> p c t", c=NK),
            xh[b, :, t0:t0 + t].rearrange("(c p) t -> p c t", p=128),
            [], [bxb], note=f"x b{b} j{j}", cost=1500)

    def A1a(i, b, j):
        par = i % 2
        bxt, bxb = bx[par], B[f"bx{par}"]
        bzt, bzbf = bzbD[par], B[f"bzb{par}"]
        if j == 0:
            E("dve", [], Bcr, lambda e: e.memset(bcr[:], 0.0), cost=100)
            E("dve", [], Bci, lambda e: e.memset(bci[:], 0.0), cost=100)
        ln_stats(bxt, bxb, None, [],
                 bmix[par], [B[f"bmix{par}0"], B[f"bmix{par}1"]],
                 (3, 3), stA, stpA, bcA,
                 [B["bcA0"], B["bcA1"]], C_EPS_MOD, bc2=bcAt)
        for c in range(NK):
            hB = B[f"bt1{c // 4}"]
            E("dve", [bxb, B["bcA0"]], [hB],
              lambda e, c=c: e.tensor_tensor(out=ck(bt1, c), in0=ck(bxt, c),
                                             in1=bcA[0][:, 0:t], op=ALU.mult),
              cost=450, alts={"dve": 450.0, "gps": 1000.0})
            E("dve", [hB, B["bcA1"]], [bzbf],
              lambda e, c=c: e.tensor_tensor(out=ck(bzt, c), in0=ck(bt1, c),
                                             in1=bcA[1][:, 0:t], op=ALU.add),
              cost=450, alts={"dve": 450.0, "gps": 1000.0})

    def A1b(i):
        par = i % 2
        bzt, bzbf = bzbD[par], B[f"bzb{par}"]
        proj("wa", bar, [B["bar0"], B["bar1"]], NK, bzt, bzbf)
        proj("wa", bai, [B["bai0"], B["bai1"]], NK, bzt, bzbf, m_off=NK)
        proj("wk", bk, [B["bk0"], B["bk1"]], NK, bzt, bzbf)
        proj("wv", bv, [B["bv0"], B["bv1"]], NK, bzt, bzbf)

    Hw = CH // 2
    CD = 1350.0
    CA = 1150.0

    def chain_half(i, h):
        HSL = slice(h * Hw, (h + 1) * Hw)
        HS = lambda buf: buf[:, HSL]
        Bh = lambda nm: B[f"{nm}{h}"]
        Bk, Bv = B[f"bk{h}"], B[f"bv{h}"]
        bki_h = bar[:, HSL].bitcast(I32)
        E("dve", [Bk, Bv], [Bk],
          lambda e: e.tensor_tensor(out=HS(bk), in0=HS(bk), in1=HS(bv),
                                    op=ALU.mult), cost=700)
        E("act", [Bh("bar")], [Bh("bt1")],
          lambda e: e.activation(out=HS(bt1), in_=HS(bar), func=AF.Square), cost=CA)
        E("act", [Bh("bai")], [Bh("bt2")],
          lambda e: e.activation(out=HS(bt2), in_=HS(bai), func=AF.Square), cost=CA)
        E("dve", [Bh("bt1"), Bh("bt2")], [Bh("bt1")],
          lambda e: e.tensor_tensor(out=HS(bt1), in0=HS(bt1), in1=HS(bt2),
                                    op=ALU.add), cost=CD, flex=True)
        E("act", [Bh("bt1")], [Bh("bt2")],
          lambda e: e.activation(out=HS(bt2), in_=HS(bt1), func=AF.Ln),
          table="nlexp", cost=CA)
        E("act", [Bh("bt2"), B["cvec"]], [Bh("bt1")],
          lambda e: e.activation(out=HS(bt1), in_=HS(bt2), func=AF.Exp,
                                 scale=cvec[:, C_HALF:C_HALF + 1]),
          table="nlexp", cost=CA)
        E("act", [Bh("bar")], [Bh("bhr")],
          lambda e: e.activation(out=HS(bhr), in_=HS(bar), func=AF.Abs), cost=CA)
        E("act", [Bh("bar")], [Bh("bcos")],
          lambda e: e.activation(out=HS(bcos), in_=HS(bar), func=AF.Sign), cost=CA)
        E("act", [Bh("bai")], [Bh("bnsin")],
          lambda e: e.activation(out=HS(bnsin), in_=HS(bai), func=AF.Sign), cost=CA)
        E("dve", [Bh("bt1"), Bh("bhr")], [Bh("bt2")],
          lambda e: e.tensor_tensor(out=HS(bt2), in0=HS(bt1), in1=HS(bhr),
                                    op=ALU.add), cost=CD, flex=True)
        E("dve", [Bh("bt2")], [Bh("bt2")],
          lambda e: e.tensor_scalar(out=HS(bt2), in0=HS(bt2), scalar1=1e-30,
                                    scalar2=None, op0=ALU.max), cost=CD * 0.55)
        E("act", [Bh("bt2")], [Bh("bhr")],
          lambda e: e.activation(out=HS(bhr), in_=HS(bt2), func=AF.Ln),
          table="nlexp", cost=CA)
        E("act", [Bh("bhr"), B["cvec"]], [Bh("bt2")],
          lambda e: e.activation(out=HS(bt2), in_=HS(bhr), func=AF.Exp,
                                 scale=cvec[:, C_NEG1:C_NEG1 + 1]),
          table="nlexp", cost=CA)
        E("act", [Bh("bt1")], [Bh("bs")],
          lambda e: e.activation(out=HS(bs_), in_=HS(bt1), func=AF.Sigmoid),
          table="sigm", cost=CA)
        E("dve", [Bh("bai"), Bh("bt2")], [Bh("bt1")],
          lambda e: e.tensor_tensor(out=HS(bt1), in0=HS(bai), in1=HS(bt2),
                                    op=ALU.mult), cost=CD, flex=True)
        E("act", [Bh("bt1")], [Bh("bt2")],
          lambda e: e.activation(out=HS(bt2), in_=HS(bt1), func=AF.Arctan),
          table="sigm", cost=CA)
        E("dve", [Bh("bcos")], [Bh("bcos")],
          lambda e: e.tensor_scalar(out=HS(bcos), in0=HS(bcos), scalar1=-0.5,
                                    scalar2=0.5, op0=ALU.mult, op1=ALU.add),
          cost=CD * 0.55)
        E("dve", [Bh("bcos")], [Bh("bhr")],
          lambda e: e.tensor_scalar(out=HS(bhr), in0=HS(bcos), scalar1=-4.0,
                                    scalar2=2.0, op0=ALU.mult, op1=ALU.add),
          cost=CD * 0.55)
        E("dve", [Bh("bcos"), Bh("bnsin")], [Bh("bt1")],
          lambda e: e.tensor_tensor(out=HS(bt1), in0=HS(bcos), in1=HS(bnsin),
                                    op=ALU.mult), cost=CD, flex=True)
        E("dve", [Bh("bt2"), Bh("bhr")], [Bh("bt2")],
          lambda e: e.tensor_tensor(out=HS(bt2), in0=HS(bt2), in1=HS(bhr),
                                    op=ALU.mult), cost=CD, flex=True)
        E("dve", [Bh("bt1"), Bh("bt2")], [Bh("bai")],
          lambda e: e.scalar_tensor_tensor(out=HS(bai), in0=HS(bt1), scalar=PI,
                                           in1=HS(bt2), op0=ALU.mult,
                                           op1=ALU.add), cost=CD)
        for c in range(4 * h, 4 * h + 4):
            E("dve", [Bh("bai"), B["zer"]], [Bh("bai")],
              lambda e, c=c: e.tensor_tensor_scan(
                  out=ck(bai, c), data0=ck(bai, c), data1=zer[:, 0:t],
                  initial=zer[:, 0:1], op0=ALU.add, op1=ALU.add), cost=450)

        bki2_h = bt2[:, HSL].bitcast(I32)

        def red_sin(alpha_ap, alpha_bufs, dst_ap, dst_buf, neg,
                    ki_ap=None, ki_buf=None, w_ap=None, w_buf=None):
            if ki_ap is None:
                ki_ap, ki_buf = bki_h, Bh("bar")
            if w_ap is None:
                w_ap, w_buf = HS(bt1), Bh("bt1")
            E("dve", alpha_bufs, [ki_buf],
              lambda e: e.tensor_scalar(out=ki_ap, in0=alpha_ap, scalar1=INV_2PI,
                                        scalar2=None, op0=ALU.mult), cost=CD)
            E("dve", [ki_buf] + alpha_bufs, [w_buf],
              lambda e: e.scalar_tensor_tensor(out=w_ap, in0=ki_ap,
                                               scalar=-TWO_PI, in1=alpha_ap,
                                               op0=ALU.mult, op1=ALU.add), cost=CD)
            if sim_safe_wrap:
                E("dve", [Bh("bt1")], [Bh("bt2")],
                  lambda e: e.tensor_scalar(out=HS(bt2), in0=HS(bt1), scalar1=-PI,
                                            scalar2=None, op0=ALU.is_lt), cost=CD)
                E("dve", [Bh("bt1")], [Bh("bhr")],
                  lambda e: e.tensor_scalar(out=HS(bhr), in0=HS(bt1), scalar1=PI,
                                            scalar2=None, op0=ALU.is_gt), cost=CD)
                E("dve", [Bh("bt2"), Bh("bhr")], [Bh("bt2")],
                  lambda e: e.tensor_tensor(out=HS(bt2), in0=HS(bt2), in1=HS(bhr),
                                            op=ALU.subtract), cost=CD)
                E("dve", [Bh("bt2"), Bh("bt1")], [Bh("bt1")],
                  lambda e: e.scalar_tensor_tensor(out=HS(bt1), in0=HS(bt2),
                                                   scalar=TWO_PI, in1=HS(bt1),
                                                   op0=ALU.mult, op1=ALU.add),
                  cost=CD)
            else:
                E("dve", [w_buf], [w_buf],
                  lambda e: e.tensor_scalar(out=w_ap, in0=w_ap, scalar1=-PI,
                                            scalar2=PI, op0=ALU.max, op1=ALU.min),
                  cost=CD * 0.55)
            if neg:
                E("act", [w_buf, B["cvec"]], [dst_buf],
                  lambda e: e.activation(out=dst_ap, in_=w_ap, func=AF.Sin,
                                         scale=cvec[:, C_NEG1:C_NEG1 + 1]),
                  table="silu", cost=CA)
            else:
                E("act", [w_buf], [dst_buf],
                  lambda e: e.activation(out=dst_ap, in_=w_ap, func=AF.Sin),
                  table="silu", cost=CA)

        if sim_safe_wrap:
            red_sin(HS(bai), [Bh("bai")], HS(bnsin), Bh("bnsin"), neg=True)
        else:
            red_sin(HS(bai), [Bh("bai")], HS(bnsin), Bh("bnsin"), neg=True,
                    ki_ap=bki2_h, ki_buf=Bh("bt2"), w_ap=HS(bar), w_buf=Bh("bar"))
        if sim_safe_wrap:
            ck_ap, ck_buf = bki_h, Bh("bar")
        else:
            ck_ap, ck_buf = bki2_h, Bh("bt2")
        E("dve", [Bh("bai")], [ck_buf],
          lambda e: e.tensor_scalar(out=ck_ap, in0=HS(bai), scalar1=INV_2PI,
                                    scalar2=0.25, op0=ALU.mult, op1=ALU.add),
          cost=CD)
        E("dve", [ck_buf, Bh("bai")], [Bh("bt1")],
          lambda e: e.scalar_tensor_tensor(out=HS(bt1), in0=ck_ap,
                                           scalar=-TWO_PI, in1=HS(bai),
                                           op0=ALU.mult, op1=ALU.add), cost=CD)
        if sim_safe_wrap:
            E("dve", [Bh("bt1")], [Bh("bt2")],
              lambda e: e.tensor_scalar(out=HS(bt2), in0=HS(bt1),
                                        scalar1=-3 * PI / 2,
                                        scalar2=None, op0=ALU.is_lt), cost=CD)
            E("dve", [Bh("bt1")], [Bh("bhr")],
              lambda e: e.tensor_scalar(out=HS(bhr), in0=HS(bt1),
                                        scalar1=PI / 2,
                                        scalar2=None, op0=ALU.is_gt), cost=CD)
            E("dve", [Bh("bt2"), Bh("bhr")], [Bh("bt2")],
              lambda e: e.tensor_tensor(out=HS(bt2), in0=HS(bt2), in1=HS(bhr),
                                        op=ALU.subtract), cost=CD)
            E("dve", [Bh("bt2"), Bh("bt1")], [Bh("bt1")],
              lambda e: e.scalar_tensor_tensor(out=HS(bt1), in0=HS(bt2),
                                               scalar=TWO_PI, in1=HS(bt1),
                                               op0=ALU.mult, op1=ALU.add), cost=CD)
        else:
            E("dve", [Bh("bt1")], [Bh("bt1")],
              lambda e: e.tensor_scalar(out=HS(bt1), in0=HS(bt1),
                                        scalar1=-3 * PI / 2, scalar2=PI / 2,
                                        op0=ALU.max, op1=ALU.min), cost=CD * 0.55)
        E("act", [Bh("bt1"), B["cvec"]], [Bh("bcos")],
          lambda e: e.activation(out=HS(bcos), in_=HS(bt1), func=AF.Sin,
                                 bias=cvec[:, C_HALFPI:C_HALFPI + 1]),
          table="silu", cost=CA)
        E("dve", [Bk, Bh("bcos")], [Bh("bhr")],
          lambda e: e.tensor_tensor(out=HS(bhr), in0=HS(bk), in1=HS(bcos),
                                    op=ALU.mult), cost=700)
        E("dve", [Bk, Bh("bnsin")], [Bh("bhi")],
          lambda e: e.tensor_tensor(out=HS(bhi), in0=HS(bk), in1=HS(bnsin),
                                    op=ALU.mult), cost=700)
        for c in range(4 * h, 4 * h + 4):
            E("dve", [Bh("bs"), Bh("bhr"), Bcr[c]], [Bh("bhr")],
              lambda e, c=c: e.tensor_tensor_scan(
                  out=ck(bhr, c), data0=ck(bs_, c), data1=ck(bhr, c),
                  initial=bcr[:, c:c + 1], op0=ALU.mult, op1=ALU.add), cost=450)
            E("dve", [Bh("bs"), Bh("bhi"), Bci[c]], [Bh("bhi")],
              lambda e, c=c: e.tensor_tensor_scan(
                  out=ck(bhi, c), data0=ck(bs_, c), data1=ck(bhi, c),
                  initial=bci[:, c:c + 1], op0=ALU.mult, op1=ALU.add), cost=450)
        E("dve", [Bh("bhr"), Bh("bcos"), Bh("bs")], [Bh("bs")],
          lambda e: e.tensor_tensor(out=HS(bs_), in0=HS(bhr), in1=HS(bcos),
                                    op=ALU.mult), cost=700)
        E("dve", [Bh("bhi"), Bh("bnsin")], [Bk],
          lambda e: e.tensor_tensor(out=HS(bk), in0=HS(bhi), in1=HS(bnsin),
                                    op=ALU.mult), cost=700)
        E("dve", [Bh("bs"), Bk], [Bh("bs")],
          lambda e: e.tensor_tensor(out=HS(bs_), in0=HS(bs_), in1=HS(bk),
                                    op=ALU.add), cost=700)
        for c in range(4 * h, 4 * h + 4):
            L = c * t + t - 1
            s0 = h * Hw + 2 * (c - 4 * h)
            E("dve", [Bh("bs")], [Bcr[c]],
              lambda e, c=c, L=L: e.tensor_copy(out=bcr[:, c:c + 1],
                                                in_=bs_[:, L:L + 1]), cost=150)
            E("dve", [Bh("bhr"), Bh("bnsin")], [Bh("bt2")],
              lambda e, s0=s0, L=L: e.tensor_tensor(
                  out=bt2[:, s0:s0 + 1], in0=bhr[:, L:L + 1],
                  in1=bnsin[:, L:L + 1], op=ALU.mult), cost=150)
            E("dve", [Bh("bhi"), Bh("bcos")], [Bh("bt2")],
              lambda e, s0=s0, L=L: e.tensor_tensor(
                  out=bt2[:, s0 + 1:s0 + 2], in0=bhi[:, L:L + 1],
                  in1=bcos[:, L:L + 1], op=ALU.mult), cost=150)
            E("dve", [Bh("bt2")], [Bci[c]],
              lambda e, c=c, s0=s0: e.tensor_tensor(
                  out=bci[:, c:c + 1], in0=bt2[:, s0 + 1:s0 + 2],
                  in1=bt2[:, s0:s0 + 1], op=ALU.subtract), cost=150)

    def CH_(i):
        chain_half(i, 0)
        yield
        chain_half(i, 1)
        yield

    def A3a(i):
        proj("wq", bq, [B["bq0"], B["bq1"]], NK, bzbD[i % 2], B[f"bzb{i % 2}"])

    def A3b(i):
        par = i % 2
        proj("wg", bg, [B["bg0"], B["bg1"]], NK, bzbD[par], B[f"bzb{par}"],
             act_fn=AF.Silu)
        for h in (0, 1):
            HSL = slice(h * Hw, (h + 1) * Hw)
            HS = lambda buf, HSL=HSL: buf[:, HSL]
            Bh = lambda nm, h=h: B[f"{nm}{h}"]
            E("dve", [Bh("bq"), Bh("bs")], [Bh("bs")],
              lambda e, HS=HS: e.tensor_tensor(out=HS(bs_), in0=HS(bs_),
                                               in1=HS(bq), op=ALU.mult), cost=700)
            E("dve", [Bh("bs"), Bh("bg")], [B[f"bmix{par}{h}"]],
              lambda e, HS=HS, par=par: e.tensor_tensor(
                  out=HS(bmix[par]), in0=HS(bs_), in1=HS(bg), op=ALU.mult),
              cost=700)

    def B1(i, take=lambda n=1: None):
        par = i % 2
        bzt, bzbf = bzbD[par], B[f"bzb{par}"]
        for m0 in range(0, NK, 2):
            svt, svb = load_pair(poolB, "wo", m0, note=f"wo m{m0}")
            for d in range(2):
                m = m0 + d
                parts = [(svt[:, d * 1024 + k * 128:d * 1024 + (k + 1) * 128],
                          ck(bmix[par], k)) for k in range(NK)]
                out, bbuf = mmg(rotB, (4, 5, 6, 7), parts,
                                [svb, B[f"bmix{par}0"], B[f"bmix{par}1"]],
                                note=f"wo m{m}")
                E("dve", [bbuf, bzbf], [B["bw"]],
                  lambda e, m=m, out=out: e.tensor_tensor(
                      out=ck(bw, m), in0=out, in1=ck(bzt, m), op=ALU.add))
                take(2)
        ln_stats(bw, B["bw"], bwb, [B["bwb"]], bmix[par],
                 [B[f"bmix{par}0"], B[f"bmix{par}1"]],
                 (7, 7), stB, stpB, bcB,
                 [B["bcB0"], B["bcB1"]], C_EPS_IN, bc2=bcBt)
        for c in range(NK):
            E("dve", [B["bw"], B["bcB0"]], [B["bw"]],
              lambda e, c=c: e.tensor_tensor(out=ck(bw, c), in0=ck(bw, c),
                                             in1=bcB[0][:, 0:t], op=ALU.mult),
              cost=450, alts={"dve": 450.0, "gps": 1000.0})
            E("dve", [B["bw"], B["bcB1"]], [B["bwb"]],
              lambda e, c=c: e.tensor_tensor(out=ck(bwb, c), in0=ck(bw, c),
                                             in1=bcB[1][:, 0:t], op=ALU.add),
              cost=450, alts={"dve": 450.0, "gps": 1000.0})

    def B2(i, take=lambda n=1: None):
        for f0 in range(0, NF, 2):
            svt, svb = load_pair(poolB, "w1p", f0, note=f"w1 f{f0}")
            for d in range(2):
                f = f0 + d
                parts = [(svt[:, d * 1024 + k * 128:d * 1024 + (k + 1) * 128],
                          ck(bwb, k)) for k in range(NK)]
                out, bbuf = mmg(rotB, (4, 5, 6, 7), parts, [svb, B["bwb"]], note=f"w1 f{f}")
                E("act", [bbuf, B["vecs"], Bb1[f]], [Bh1[f // 8]],
                  lambda e, f=f, out=out: e.activation(
                      out=bh1[:, f * t:(f + 1) * t], in_=out, func=AF.Gelu_apprx_tanh,
                      bias=vecs[:, V_B1 + f:V_B1 + f + 1]), table="gelu", cost=550)
                take(2)

    def B3a(i, b, j):
        par = i % 2
        t0 = j * t
        bxt, bxb = bx[par], B[f"bx{par}"]
        for m in range(NK):
            w2s = [load_pair(poolB, "w2p", m * 4, note=f"w2 {m * 4}"),
                   load_pair(poolB, "w2p", m * 4 + 2, note=f"w2 {m * 4 + 2}")]
            parts = []
            for kb in range(NF):
                sl = w2s[kb // 16]
                d = (kb // 8) % 2
                parts.append((sl[0][:, d * 1024 + (kb % 8) * 128:
                                    d * 1024 + (kb % 8 + 1) * 128],
                              bh1[:, kb * t:(kb + 1) * t]))
            out, bbuf = mmg(rotB, (4, 5, 6, 7), parts,
                            [sv[1] for sv in w2s] + list(Bh1), note=f"w2 m{m}")
            E("dve", [bbuf, B["vecs"], B["bwb"]], [B["bw"]],
              lambda e, m=m, out=out: e.scalar_tensor_tensor(
                  out=ck(bw, m), in0=ck(bwb, m),
                  scalar=vecs[:, V_LN1S + m:V_LN1S + m + 1],
                  in1=out, op0=ALU.mult, op1=ALU.add))
            E("dve", [B["bw"], B["vecs"]], [B["bw"]],
              lambda e, m=m: e.tensor_scalar(
                  out=ck(bw, m), in0=ck(bw, m),
                  scalar1=vecs[:, V_B2C + m:V_B2C + m + 1], scalar2=None,
                  op0=ALU.add), cost=450)
        ln_stats(bw, B["bw"], bwb, [B["bwb"]], bmix[par],
                 [B[f"bmix{par}0"], B[f"bmix{par}1"]],
                 (7, 7), stB, stpB, bcB,
                 [B["bcB0"], B["bcB1"]], C_EPS_IN, bc2=bcBt)
        for c in range(NK):
            E("dve", [B["bw"], B["bcB0"]], [B["bw"]],
              lambda e, c=c: e.tensor_tensor(out=ck(bw, c), in0=ck(bw, c),
                                             in1=bcB[0][:, 0:t], op=ALU.mult),
              cost=450, alts={"dve": 450.0, "gps": 1000.0})
            E("dve", [B["bw"], B["bcB1"]], [B["bw"]],
              lambda e, c=c: e.tensor_tensor(out=ck(bw, c), in0=ck(bw, c),
                                             in1=bcB[1][:, 0:t], op=ALU.add),
              cost=450, alts={"dve": 450.0, "gps": 1000.0})
            E("dve", [B["bw"], B["vecs"]], [B["bw"]],
              lambda e, c=c: e.tensor_scalar(
                  out=ck(bw, c), in0=ck(bw, c),
                  scalar1=vecs[:, V_LN2S + c:V_LN2S + c + 1],
                  scalar2=vecs[:, V_LN2B + c:V_LN2B + c + 1],
                  op0=ALU.mult, op1=ALU.add), cost=450)

    def B3b(i, b, j):
        par = i % 2
        t0 = j * t
        bxt, bxb = bx[par], B[f"bx{par}"]
        E("dve", [bxb, B["bw"]], [bxb],
          lambda e: e.tensor_tensor(out=bxt[:, 0:Hw], in0=bxt[:, 0:Hw],
                                    in1=bw[:, 0:Hw], op=ALU.add), cost=1350,
          alts={"dve": 1350.0, "gps": 2900.0})
        E("dve", [bxb, B["bw"]], [bxb],
          lambda e: e.tensor_tensor(out=bxt[:, Hw:CH], in0=bxt[:, Hw:CH],
                                    in1=bw[:, Hw:CH], op=ALU.add), cost=1350,
          alts={"dve": 1350.0, "gps": 2900.0})
        ln_stats(bxt, bxb, bwb, [B["bwb"]], bmix[par],
                 [B[f"bmix{par}0"], B[f"bmix{par}1"]],
                 (7, 7), stB, stpB, bcB,
                 [B["bcB0"], B["bcB1"]], C_EPS_MOD, bc2=bcBt)
        for c in range(NK):
            E("dve", [bxb, B["bcB0"]], [B["bw"]],
              lambda e, c=c: e.tensor_tensor(out=ck(bw, c), in0=ck(bxt, c),
                                             in1=bcB[0][:, 0:t], op=ALU.mult),
              cost=450, alts={"dve": 450.0, "gps": 1000.0})
            E("dve", [B["bw"], B["bcB1"]], [B["bwb"]],
              lambda e, c=c: e.tensor_tensor(out=ck(bwb, c), in0=ck(bw, c),
                                             in1=bcB[1][:, 0:t], op=ALU.add),
              cost=450, alts={"dve": 450.0, "gps": 1000.0})
        mmg_at(7, [(gwb[:, k * 8:(k + 1) * 8], ck(bwb, k)) for k in range(NK)],
               [B["gwb"], B["bwb"]], m_rows=8, note="gate logits")
        E("act", [Bbank[7]], [B["sp"]],
          lambda e: e.activation(out=sp, in_=pb[7][0:8, 0:t], func=AF.Exp),
          table="nlexp", cost=400)
        mmg_at(7, [(cvec[0:8, C_ONE:C_ONE + 1], sp)],
               [B["sp"], B["cvec"]], m_rows=1, note="gate sum")
        E("dve", [Bbank[7]], [B["st_r"]],
          lambda e: e.reciprocal(out=st_r[:], in_=pb[7][0:1, 0:t]))
        E("dve", [B["st_r"]], [B["st_r"]],
          lambda e: e.tensor_scalar(out=st_r[:], in0=st_r[:], scalar1=1.0 / 16.0,
                                    scalar2=None, op0=ALU.mult))
        mmg_at(7, [(bsel[:], sp)], [B["bsel"], B["sp"]], m_rows=128, note="gate bc")
        E("act", [Bbank[7]], [B["sg128"]],
          lambda e: e.activation(out=sg128[:], in_=pb[7][:, 0:t], func=AF.Copy))
        uat, uab = load_pair(poolB, "uall", 0, n2=1, note="uall")
        out_t, bbuf_t = mmg(rotB, (4, 5, 6, 7),
                            [(uat[:, k * 128:(k + 1) * 128], ck(bwb, k))
                             for k in range(NK)], [uab, B["bwb"]], note="t lora")
        out_r, bbuf_r = mmg(rotB, (4, 5, 6, 7),
                            [(onesrow[0:1, :], st_r[0:1, 0:t])],
                            [B["onesrow"], B["st_r"]], note="r bc")
        E("dve", [bbuf_t, B["sg128"]], [B["stgb"]],
          lambda e: e.tensor_tensor(out=stgb[:], in0=out_t, in1=sg128[:],
                                    op=ALU.mult))
        E("dve", [B["stgb"], bbuf_r], [B["stgb"]],
          lambda e: e.tensor_tensor(out=stgb[:], in0=stgb[:], in1=out_r,
                                    op=ALU.mult))
        v2t, v2b, v2slot = load_pair(poolB, "v2", 0, n2=1, note="v2", pin=True)
        for m0 in range(0, NK, 2):
            svt, svb = load_pair(poolB, "fw", m0, note=f"fw m{m0}")
            for d in range(2):
                m = m0 + d
                parts = [(svt[:, d * 1024 + k * 128:d * 1024 + (k + 1) * 128],
                          ck(bwb, k)) for k in range(NK)]
                parts.append((v2t[:, m * 128:(m + 1) * 128], stgb[:]))
                out, bbuf = mmg(rotB, (4, 5, 6, 7), parts, [svb, B["bwb"], v2b, B["stgb"]],
                                note=f"ffn2 m{m}")
                sti = m % 2
                E("dve", [bbuf, bxb], [B[f"bst{sti}"]],
                  lambda e, m=m, out=out, sti=sti: e.tensor_tensor(
                      out=bst[sti][:], in0=out, in1=ck(bxt, m), op=ALU.add))
                dma("qstore", ch_out[sti],
                    oh[b, m * 128:(m + 1) * 128, t0:t0 + t], bst[sti][:],
                    [B[f"bst{sti}"]], [Buf("_out")], note=f"out b{b} j{j} m{m}")
        pinned.discard(v2slot)

    with es:
        for i in range(total + 1):
            if i < total:
                b, j = divmod(i, nt)
                xload(i, b, j)
                A1a(i, b, j)
                A1b(i)
                gen = CH_(i)
            else:
                gen = iter(())

            def take(n=1, gen=gen):
                for _ in range(n):
                    if next(gen, "_done") == "_done":
                        return
            if i > 0:
                B1(i - 1, take)
                B2(i - 1, take)
            take(10 ** 6)
            if i > 0:
                bb, jb = divmod(i - 1, nt)
                B3a(i - 1, bb, jb)
            if i < total:
                A3a(i)
            if i > 0:
                B3b(i - 1, bb, jb)
            if i < total:
                A3b(i)
        est = P.schedule_and_replay()
        import os
        if os.environ.get("K_PRINT_EST"):
            print(f"[sched] estimated makespan: {est / 1e6:.3f} ms")
        for ch in ch_out:
            if ch.cnt:
                P.engs["qstore"].raw.wait_ge(ch.sem, ch.cnt)
    return nc



def kernel(**inputs):
    x = np.asarray(inputs["L0_attn_x"], np.float32)
    w = host_pack_weights(inputs)
    nc = build_core()
    in_maps = []
    for ci in range(N_CORES):
        xc = np.ascontiguousarray(x[ci * NB:(ci + 1) * NB].transpose(0, 2, 1))
        m = dict(w)
        m["x"] = xc
        in_maps.append(m)
    res = run_bass_kernel_spmd(nc, in_maps, core_ids=list(range(N_CORES)))
    outs = [res.results[i]["out"] for i in range(N_CORES)]
    full = np.concatenate(outs, axis=0)
    return np.ascontiguousarray(full.transpose(0, 2, 1)).astype(np.float32)


# version 40
# speedup vs baseline: 1.1551x; 1.0344x over previous
import numpy as np
from contextlib import ExitStack

import concourse.bass as bass
import concourse.mybir as mybir
from concourse.bass_utils import run_bass_kernel_spmd

F32 = mybir.dt.float32
BF16 = mybir.dt.bfloat16
I32 = mybir.dt.int32
AF = mybir.ActivationFunctionType
ALU = mybir.AluOpType

D = 1024
NK = 8
DF = 4096
NF = 32
PI = float(np.pi)
TWO_PI = 2.0 * PI
INV_2PI = 1.0 / TWO_PI
EPS_MOD = 1e-5
EPS_IN = 1e-6

N_CORES = 8
NB = 4
S = 2048
T = 256
NSLOT = 12

C_EPS_MOD, C_EPS_IN, C_HALFPI, C_NEG1, C_ONE, C_NEGH, C_HALF = 0, 1, 2, 3, 4, 5, 6
V_LN1S, V_LN1B, V_LN2S, V_LN2B, V_B2, V_B1 = 0, 8, 16, 24, 32, 40
V_B2C = 72


class Buf:
    __slots__ = ("name", "writer", "readers")

    def __init__(self, name):
        self.name = name
        self.writer = None
        self.readers = []


class Chan:
    __slots__ = ("sem", "step", "cnt", "last_node")

    def __init__(self, sem, step):
        self.sem = sem
        self.step = step
        self.cnt = 0
        self.last_node = None


class Eng:
    __slots__ = ("raw", "name", "chan", "sat")

    def __init__(self, raw, name, chan):
        self.raw = raw
        self.name = name
        self.chan = chan
        self.sat = {}


class Node:
    __slots__ = ("nid", "ename", "fn", "reads", "writes", "chan", "note",
                 "cost", "table", "deps", "prio", "finish", "alts")

    def __init__(self, nid, ename, fn, reads, writes, chan, note, cost, table):
        self.nid = nid
        self.ename = ename
        self.alts = None
        self.fn = fn
        self.reads = reads
        self.writes = writes
        self.chan = chan
        self.note = note
        self.cost = cost
        self.table = table
        self.deps = set()
        self.prio = 0.0
        self.finish = 0.0


ACT_TABLE_NS = 5000.0
RSQRT_MAGIC = 0x5f3759df


class Prog:

    def __init__(self, loose_sync=False):
        self.engs = {}
        self.nodes = []
        self.capture = True
        self.loose_sync = loose_sync

    def add_eng(self, name, raw, chan):
        self.engs[name] = Eng(raw, name, chan)

    def emit(self, ename, reads, writes, fn, chan=None, note="", cost=None,
             table=None, flex=False, alts=None):
        if not self.capture:
            return self._emit_real(ename, reads, writes, fn, chan, note)
        if cost is None:
            cost = {"pe": 1000.0, "dve": 450.0, "act": 450.0, "gps": 5000.0}.get(
                ename, 2500.0)
        n = Node(len(self.nodes), ename, fn, list(reads), list(writes), chan,
                 note, float(cost), table)
        import os as _os
        if not _os.environ.get("K_FLEX"):
            flex = False
        if alts is not None:
            n.alts = dict(alts)
        elif flex == "copy3":
            n.alts = {"dve": float(cost), "gps": float(cost) * 2.0,
                      "act": float(cost) * 1.4}
        elif flex:
            n.alts = {"dve": float(cost), "gps": float(cost) * 2.0}
        if chan is not None and getattr(chan, "last_node", None) is not None:
            n.deps.add(chan.last_node)
        for b in reads:
            if b.writer is not None:
                n.deps.add(b.writer)
        for b in writes:
            if b.writer is not None:
                n.deps.add(b.writer)
            for r in b.readers:
                n.deps.add(r)
        n.deps.discard(n.nid)
        for b in writes:
            b.writer = n.nid
            b.readers = []
        for b in reads:
            b.readers.append(n.nid)
        if chan is not None:
            chan.last_node = n.nid
        self.nodes.append(n)
        return n

    def schedule_and_replay(self):
        nodes = self.nodes
        nn = len(nodes)
        succ = [[] for _ in range(nn)]
        indeg = [0] * nn
        for n in nodes:
            for d in n.deps:
                succ[d].append(n.nid)
                indeg[n.nid] += 1
        import heapq
        import os
        simkey = lambda e: "gps" if e == "qstore" else e
        keys = {simkey(e) for e in self.engs}
        rev = []
        rem = [len(succ[i]) for i in range(nn)]
        stack = [i for i in range(nn) if rem[i] == 0]
        while stack:
            i = stack.pop()
            rev.append(i)
            for p in nodes[i].deps:
                rem[p] -= 1
                if rem[p] == 0:
                    stack.append(p)

        def run_pass(wc, costs):
            prio = [0.0] * nn
            for i in rev:
                best = 0.0
                for s in succ[i]:
                    if prio[s] > best:
                        best = prio[s]
                prio[i] = costs[i] + best
            eng_time = {e: 0.0 for e in keys}
            act_tab = None
            ready = {e: [] for e in keys}
            dready = [0.0] * nn
            ind = indeg[:]
            scheduled = [False] * nn
            assign = [None] * nn

            def push(i):
                n3 = nodes[i]
                if n3.alts:
                    for e3 in n3.alts:
                        heapq.heappush(ready[e3], (-prio[i], i))
                else:
                    heapq.heappush(ready[simkey(n3.ename)], (-prio[i], i))

            for i in range(nn):
                if ind[i] == 0:
                    push(i)
            sched = []
            nsched = 0
            while nsched < nn:
                best = None
                for e, rl in ready.items():
                    while rl and scheduled[rl[0][1]]:
                        heapq.heappop(rl)
                    if not rl:
                        continue
                    p, i = rl[0]
                    start = max(eng_time[e], dready[i])
                    n3 = nodes[i]
                    ecost = n3.alts.get(e, costs[i]) if n3.alts else costs[i]
                    key = (start + wc * ecost, p)
                    if best is None or key < best[0]:
                        best = (key, e)
                key, e = best
                _, i = heapq.heappop(ready[e])
                n = nodes[i]
                scheduled[i] = True
                if n.alts:
                    assign[i] = e
                    cost = n.alts.get(e, costs[i])
                else:
                    assign[i] = n.ename
                    cost = costs[i]
                start = max(eng_time[e], dready[i])
                if e == "act" and n.table is not None and act_tab != n.table:
                    cost += ACT_TABLE_NS
                    act_tab = n.table
                fin = start + cost
                eng_time[e] = fin
                sched.append(i)
                nsched += 1
                for s in succ[i]:
                    if dready[s] < fin:
                        dready[s] = fin
                    ind[s] -= 1
                    if ind[s] == 0:
                        push(s)
            return sched, assign

        def evaluate(sched, assign):
            eng_time = {e: 0.0 for e in keys}
            act_tab = None
            dready = [0.0] * nn
            fin = [0.0] * nn
            for i in sched:
                n = nodes[i]
                e = simkey(assign[i])
                cost = n.alts.get(e, n.cost) if n.alts else n.cost
                start = max(eng_time[e], dready[i])
                if e == "act" and n.table is not None and act_tab != n.table:
                    cost += ACT_TABLE_NS
                    act_tab = n.table
                fin[i] = start + cost
                eng_time[e] = fin[i]
                for s in succ[i]:
                    if dready[s] < fin[i]:
                        dready[s] = fin[i]
            return max(fin), fin

        base_costs = [n.cost for n in nodes]
        trials = [(0.5, base_costs)]
        for wc in (0.0, 0.15, 0.25, 0.35, 0.42, 0.58, 0.65, 0.8, 1.0, 1.3):
            trials.append((wc, base_costs))
        n_extra = int(os.environ.get("K_SCHED_TRIALS", "0"))
        rngs = np.random.default_rng(1234)
        for tix in range(n_extra):
            wc = (0.35, 0.5, 0.65)[tix % 3]
            amp = 0.25 if tix % 2 else 0.12
            jit = rngs.uniform(1.0 - amp, 1.0 + amp, nn)
            trials.append((wc, [base_costs[i] * jit[i] for i in range(nn)]))
        best_pick = None
        for wc, costs in trials:
            sched, assign = run_pass(wc, costs)
            mk, fin = evaluate(sched, assign)
            if os.environ.get("K_PRINT_TRIALS"):
                print("[sched] trial wc=%s mk=%.3f" % (wc, mk / 1e6))
            if best_pick is None or mk < best_pick[0]:
                best_pick = (mk, sched, assign, fin, wc)
        mk_best, sched, assign, fin, wc_best = best_pick
        for i in range(nn):
            nodes[i].finish = fin[i]
            if nodes[i].alts:
                nodes[i].ename = assign[i] if assign[i] != "qstore" else "gps"
        if os.environ.get("K_PRINT_EST"):
            print("[sched] portfolio best wc=%s makespan %.3f ms"
                  % (wc_best, mk_best / 1e6))
        if os.environ.get("K_GAPS"):
            from collections import Counter
            gape = os.environ.get("K_GAPS")
            seq = [i for i in sched if simkey(nodes[i].ename) == gape]
            gap_c = Counter()
            gap_t = Counter()
            prev_end = 0.0
            for i in seq:
                n = nodes[i]
                stt_ = fin[i] - (n.alts.get(simkey(n.ename), n.cost)
                                 if n.alts else n.cost)
                if stt_ > prev_end + 500.0:
                    dmax, dn = -1.0, None
                    for d in n.deps:
                        if fin[d] > dmax:
                            dmax, dn = fin[d], d
                    if dn is not None:
                        k = nodes[dn]
                        nm = (k.note or "L%d" % k.fn.__code__.co_firstlineno)
                        me = (n.note or "L%d" % n.fn.__code__.co_firstlineno)
                        gap_c[(me.split(" ")[0], "<-", simkey(k.ename),
                               nm.split(" ")[0])] += 1
                        gap_t[(me.split(" ")[0], "<-", simkey(k.ename),
                               nm.split(" ")[0])] += stt_ - prev_end
                prev_end = max(prev_end, fin[i])
            tot = sum(gap_t.values())
            if os.environ.get("K_GAP1"):
                want = os.environ.get("K_GAP1")
                prev_end2 = 0.0
                shown = 0
                for i in seq:
                    n = nodes[i]
                    cst = n.alts.get(simkey(n.ename), n.cost) if n.alts else n.cost
                    stt_ = fin[i] - cst
                    nm = (n.note or "L%d" % n.fn.__code__.co_firstlineno)
                    if (stt_ > prev_end2 + 3000.0 and nm.startswith(want)
                            and shown < 2 and fin[i] > 1.5e6):
                        shown += 1
                        print("[gap1] %s start=%.1fus gap=%.2fus deps:"
                              % (nm, stt_ / 1e3, (stt_ - prev_end2) / 1e3))
                        dl = sorted(n.deps, key=lambda d: -fin[d])[:6]
                        for d in dl:
                            k = nodes[d]
                            dn = (k.note or "L%d" % k.fn.__code__.co_firstlineno)
                            dc = k.alts.get(simkey(k.ename), k.cost) if k.alts else k.cost
                            print("    dep fin=%.1fus start=%.1fus %s %s"
                                  % (fin[d] / 1e3, (fin[d] - dc) / 1e3,
                                     simkey(k.ename), dn))
                    prev_end2 = max(prev_end2, fin[i])
            print("[sched] %s gaps total %.3f ms; top releasers:" % (gape, tot / 1e6))
            for k2, v in gap_t.most_common(18):
                print("    %.3f ms n=%d  %s" % (v / 1e6, gap_c[k2], k2))
        if os.environ.get("K_PRINT_EST"):
            busy = {}
            for n2 in nodes:
                k = simkey(n2.ename)
                busy[k] = busy.get(k, 0.0) + n2.cost
            mk = max(nd.finish for nd in nodes)
            print("[sched] busy:", {k: round(v / 1e6, 2) for k, v in busy.items()},
                  "makespan", round(mk / 1e6, 2))
            eng_prev = {}
            prev_on_eng = {}
            start_t = {}
            for idx2 in sched:
                n2 = nodes[idx2]
                k = simkey(n2.ename)
                eng_prev[idx2] = prev_on_eng.get(k)
                prev_on_eng[k] = idx2
            cur = max(range(nn), key=lambda i2: nodes[i2].finish)
            from collections import Counter
            busy_c = Counter()
            stall_c = Counter()
            hops = 0
            while cur is not None and hops < 20000:
                hops += 1
                n2 = nodes[cur]
                st2 = n2.finish - n2.cost
                ep = eng_prev.get(cur)
                dmax, dnode = -1.0, None
                for d in n2.deps:
                    if nodes[d].finish > dmax:
                        dmax, dnode = nodes[d].finish, d
                epf = nodes[ep].finish if ep is not None else -1.0
                key2 = (simkey(n2.ename), (n2.note or "?").split(" ")[0])
                busy_c[key2] += n2.cost
                if epf >= dmax and ep is not None:
                    cur = ep
                elif dnode is not None:
                    if dmax < st2 - 1.0:
                        stall_c[("idle-before", key2[0])] += st2 - dmax
                    cur = dnode
                else:
                    break
            print("[sched] walk busy by (eng, op):")
            for k2, v in busy_c.most_common(14):
                print("   ", k2, round(v / 1e6, 2), "ms")
            print("[sched] stalls:", {k2: round(v / 1e6, 2) for k2, v in stall_c.most_common(8)})
        for n in nodes:
            for b in n.reads + n.writes:
                b.writer = None
                b.readers = []
            if n.chan is not None:
                n.chan.last_node = None
        self.capture = False
        for i in sched:
            n = nodes[i]
            self._emit_real(n.ename, n.reads, n.writes, n.fn, n.chan, n.note)
        return max(nd.finish for nd in nodes)

    def _emit_real(self, ename, reads, writes, fn, chan=None, note=""):
        eng = self.engs[ename]
        own = chan is None
        ch = eng.chan if own else chan
        deps = {}

        def add(mark):
            if mark is None:
                return
            sem, val, owner = mark
            k = id(sem)
            if k not in deps or deps[k][1] < val:
                deps[k] = mark

        if not own and ch.cnt > 0:
            add((ch.sem, ch.cnt, None))
        for b in reads:
            add(b.writer)
        for b in writes:
            add(b.writer)
            for m in b.readers:
                add(m)
        for sem, val, _owner in deps.values():
            if self.loose_sync and _owner is eng:
                continue
            cur = eng.sat.get(id(sem), 0)
            if val > cur:
                eng.raw.wait_ge(sem, val)
                eng.sat[id(sem)] = val
        last = fn(eng.raw)
        ch.cnt += ch.step
        last.then_inc(ch.sem, ch.step)
        mark = (ch.sem, ch.cnt, eng if own else None)
        for b in writes:
            b.writer = mark
            b.readers = []
        for b in reads:
            b.readers.append(mark)
        return mark



def _kslabs(w_eff):
    M = w_eff.shape[1]
    return np.ascontiguousarray(w_eff.reshape(NK, 128, M))


def _pack_resident(w_eff):
    M = w_eff.shape[1]
    return np.ascontiguousarray(
        w_eff.reshape(NK, 128, M).transpose(1, 0, 2).reshape(128, NK * M))


def _col(v):
    return np.ascontiguousarray(v.reshape(8, 128).T)


def _dmaj(w_eff, nm):
    return np.ascontiguousarray(
        w_eff.reshape(NK, 128, nm, 128).transpose(2, 1, 0, 3).reshape(nm, 128, 1024))


def host_pack_weights(I, t_tile=T):
    f32 = lambda a: np.ascontiguousarray(np.asarray(a, np.float32))
    w = {}
    w["wq"] = _dmaj(f32(I["L0_attn_Wq"]), 8)
    w["wk"] = _dmaj(f32(I["L0_attn_Wk"]), 8)
    w["wv"] = _dmaj(f32(I["L0_attn_Wv"]), 8)
    w["wg"] = _dmaj(f32(I["L0_attn_Wg"]), 8)
    w["wo"] = _dmaj(f32(I["L0_attn_Wo"]), 8)
    w["wa"] = _dmaj(f32(I["L0_attn_Wa"]), 16)
    w["w1p"] = _dmaj(f32(I["L0_attn_W1"]), NF)
    w2 = f32(I["L0_attn_W2"])
    w["w2p"] = np.ascontiguousarray(
        w2.reshape(NF, 128, NK, 128).transpose(2, 0, 1, 3)
        .reshape(NK, 4, NK, 128, 128).transpose(0, 1, 3, 2, 4)
        .reshape(NF, 128, 1024))
    w["fw"] = _dmaj(f32(I["L0_ffn_weight"]).T, 8)
    w["uall"] = _dmaj(f32(I["L0_ffn_lora_U"]).reshape(128, 1024).T, 1)
    w["v2"] = np.ascontiguousarray(f32(I["L0_ffn_lora_V"]).transpose(0, 2, 1).reshape(128, 1024))
    w["gw"] = _pack_resident(f32(I["L0_ffn_gate_w"]).T)
    vecs = np.zeros((128, 80), np.float32)
    vecs[:, V_LN1S:V_LN1S + 8] = _col(f32(I["L0_attn_ln1_scale"]))
    vecs[:, V_LN1B:V_LN1B + 8] = _col(f32(I["L0_attn_ln1_bias"]))
    vecs[:, V_LN2S:V_LN2S + 8] = _col(f32(I["L0_attn_ln2_scale"]))
    vecs[:, V_LN2B:V_LN2B + 8] = _col(f32(I["L0_attn_ln2_bias"]))
    vecs[:, V_B2:V_B2 + 8] = _col(f32(I["L0_attn_b2"]))
    vecs[:, V_B2C:V_B2C + 8] = _col(f32(I["L0_attn_b2"]))
    vecs[:, V_B1:V_B1 + 32] = np.ascontiguousarray(f32(I["L0_attn_b1"]).reshape(32, 128).T)
    w["vecs"] = vecs
    cvec = np.zeros((128, 8), np.float32)
    cvec[:, C_EPS_MOD] = EPS_MOD
    cvec[:, C_EPS_IN] = EPS_IN
    cvec[:, C_HALFPI] = PI / 2
    cvec[:, C_NEG1] = -1.0
    cvec[:, C_ONE] = 1.0
    cvec[:, C_NEGH] = -0.5
    cvec[:, C_HALF] = 0.5
    w["cvec"] = cvec
    w["onesrow"] = np.ones((1, 128), np.float32)
    bsel = np.zeros((8, 128), np.float32)
    for e in range(8):
        bsel[e, 16 * e:16 * (e + 1)] = 1.0
    w["bsel"] = bsel
    import ml_dtypes
    w["zer"] = np.zeros((128, t_tile), ml_dtypes.bfloat16)
    return w




def build_core(nb=NB, s=S, t=T, sim_safe_wrap=False, loose_sync=True):
    CH = NK * t
    nt = s // t
    total = nb * nt
    nc = bass.Bass("TRN2", target_bir_lowering=False)

    xh = nc.dram_tensor("x", [nb, D, s], F32, kind="ExternalInput").ap()
    oh = nc.dram_tensor("out", [nb, D, s], F32, kind="ExternalOutput").ap()
    wslab = {}
    for name, n in [("wq", 8), ("wk", 8), ("wv", 8), ("wg", 8), ("wo", 8),
                    ("wa", 16), ("w1p", 32), ("w2p", 32), ("fw", 8)]:
        wslab[name] = nc.dram_tensor(name, [n, 128, 1024], F32, kind="ExternalInput").ap()
    wslab["uall"] = nc.dram_tensor("uall", [1, 128, 1024], F32, kind="ExternalInput").ap()
    v2h = nc.dram_tensor("v2", [128, 1024], F32, kind="ExternalInput").ap()
    W_NS = [("wa", 16), ("wk", 8), ("wv", 8), ("wq", 8), ("wg", 8), ("wo", 8),
            ("w1p", 32), ("w2p", 32), ("fw", 8), ("uall", 1)]
    scr = {name: nc.dram_tensor(f"scr_{name}", [n, 128, 1024], BF16, kind="Internal").ap()
           for name, n in W_NS}
    scr["v2"] = nc.dram_tensor("scr_v2", [1, 128, 1024], BF16, kind="Internal").ap()
    gwh = nc.dram_tensor("gw", [128, 64], F32, kind="ExternalInput").ap()
    vecsh = nc.dram_tensor("vecs", [128, 80], F32, kind="ExternalInput").ap()
    cvech = nc.dram_tensor("cvec", [128, 8], F32, kind="ExternalInput").ap()
    onesrowh = nc.dram_tensor("onesrow", [1, 128], F32, kind="ExternalInput").ap()
    bselh = nc.dram_tensor("bsel", [8, 128], F32, kind="ExternalInput").ap()
    zerh = nc.dram_tensor("zer", [128, t], BF16, kind="ExternalInput").ap()

    es = ExitStack()
    sb = lambda name, shape, dt=F32: es.enter_context(nc.sbuf_tensor(name, shape, dt))

    bx = [sb("bx0", [128, CH]), sb("bx1", [128, CH])]
    bzbD = [sb("bzb0", [128, CH], BF16), sb("bzb1", [128, CH], BF16)]
    bk = sb("bk", [128, CH], BF16); bv = sb("bv", [128, CH], BF16)
    cs_in = [sb("cs_in0", [128, 1024]), sb("cs_in1", [128, 1024])]
    cs_out = [sb("cs_out0", [128, 1024], BF16), sb("cs_out1", [128, 1024], BF16)]
    bar = sb("bar", [128, CH]); bai = sb("bai", [128, CH])
    bq = sb("bq", [128, CH], BF16); bg = sb("bg", [128, CH], BF16)
    bs_ = sb("bs", [128, CH], BF16)
    bcos = sb("bcos", [128, CH], BF16); bnsin = sb("bnsin", [128, CH], BF16)
    bhr = sb("bhr", [128, CH], BF16); bhi = sb("bhi", [128, CH], BF16)
    bt1 = sb("bt1", [128, CH]); bt2 = sb("bt2", [128, CH])
    bmix = [sb("bmix0", [128, CH], BF16), sb("bmix1", [128, CH], BF16)]
    bki = bar[:, 0:CH].bitcast(I32)
    bw = sb("bw", [128, CH])
    bwb = sb("bwb", [128, CH], BF16)
    bh1 = sb("bh1", [128, NF * t], BF16)
    bst = [sb("bst0", [128, t]), sb("bst1", [128, t])]
    sp = bst[0][0:8, :]
    bcAt = sb("bcAt", [128, 2 * t])
    bcBt = sb("bcBt", [128, 2 * t])
    bcA = [bcAt[:, 0:t], bcAt[:, t:2 * t]]
    bcB = [bcBt[:, 0:t], bcBt[:, t:2 * t]]
    stA = {nm: sb(f"stA_{nm}", [1, t]) for nm in ["mu", "ms", "var"]}
    stB = {nm: sb(f"stB_{nm}", [1, t]) for nm in ["mu", "ms", "var"]}
    stA_rn = sb("stA_rn", [1, 2 * t])
    stB_rn = sb("stB_rn", [1, 2 * t])
    for st_, rn_ in ((stA, stA_rn), (stB, stB_rn)):
        st_["rn"] = rn_
        st_["rstd"] = rn_[:, 0:t]
        st_["nmr"] = rn_[:, t:2 * t]
    st_r = stB["mu"]
    sg128 = sb("sg128", [128, t], BF16)
    stgb = sb("stgb", [128, t], BF16)
    slab = [sb(f"slab{i}", [128, 2048], BF16) for i in range(12)]
    gwr = sb("gwr", [128, 64]); gwb = sb("gwb", [128, 64], BF16)
    onesb = sb("onesb", [128, 1], BF16)
    vecs = sb("vecs_s", [128, 80]); cvec = sb("cvec_s", [128, 8])
    onesrow = sb("onesrow_s", [1, 128]); bsel = sb("bsel_s", [8, 128])
    zer = sb("zer_s", [128, t], BF16)
    bcr = sb("bcr", [128, 8]); bci = sb("bci", [128, 8])
    pb = [es.enter_context(nc.psum_tensor(f"pb{i}", [128, 512], F32)) for i in range(8)]

    sem = lambda name: es.enter_context(nc.semaphore(name))
    P = Prog(loose_sync=loose_sync)
    P.add_eng("pe", nc.tensor, Chan(sem("s_pe"), 1))
    P.add_eng("act", nc.scalar, Chan(sem("s_act"), 1))
    P.add_eng("dve", nc.vector, Chan(sem("s_dve"), 1))
    P.add_eng("gps", nc.gpsimd, Chan(sem("s_gps"), 1))
    P.add_eng("qload", nc.sync, None)
    P.add_eng("qstore", nc.gpsimd, None)
    ch_const = Chan(sem("c_const"), 16)
    ch_x = [Chan(sem(f"c_x{i}"), 16) for i in range(2)]
    ch_slot = [Chan(sem(f"c_sl{i}"), 16) for i in range(12)]
    ch_out = [Chan(sem(f"c_out{i}"), 16) for i in range(2)]

    B = {}
    for nm in ["bx0", "bx1", "bzb0", "bzb1", "bw", "bwb",
               "bst0", "bst1", "bcA0", "bcA1", "bcB0", "bcB1", "st_r", "sp",
               "sg128", "stgb", "gwr", "gwb", "onesb", "vecs", "cvec",
               "onesrow", "bsel", "zer"]:
        B[nm] = Buf(nm)
    Bh1 = [Buf(f"bh1_{q}") for q in range(4)]
    for nm in ["bk", "bv", "bar", "bai", "bq", "bg", "bs",
               "bcos", "bnsin", "bhr", "bhi", "bt1", "bt2", "bmix0", "bmix1"]:
        B[nm + "0"] = Buf(nm + "0")
        B[nm + "1"] = Buf(nm + "1")
    for nm in ["mu", "ms", "var", "rstd", "nmr"]:
        B[f"stA_{nm}"] = Buf(f"stA_{nm}")
        B[f"stB_{nm}"] = Buf(f"stB_{nm}")
    B["st_r"] = B["stB_mu"]
    stpA = {k: B[f"stA_{k}"] for k in ["mu", "ms", "var", "rstd", "nmr"]}
    stpB = {k: B[f"stB_{k}"] for k in ["mu", "ms", "var", "rstd", "nmr"]}
    B["sp"] = B["bst0"]
    Bslab = [Buf(f"slab{i}") for i in range(12)]
    Bb1 = [Buf(f"b1_{m}") for m in range(NF)]
    Bbank = [Buf(f"pb{i}") for i in range(8)]
    Bcr = [Buf(f"bcr{c}") for c in range(NK)]
    Bci = [Buf(f"bci{c}") for c in range(NK)]

    block_cm = nc.Block()
    es.enter_context(block_cm)

    def dma(ename, ch, out_ap, in_ap, reads, writes, note="", cost=800):
        return P.emit(ename, reads, writes,
                      lambda e: e.dma_start(out=out_ap, in_=in_ap), chan=ch, note=note,
                      cost=cost)

    dma("qload", ch_const, gwr[:], gwh, [], [B["gwr"]])
    dma("qload", ch_const, vecs[:], vecsh, [], [B["vecs"]] + Bb1)
    for m_ in range(NK):
        P.emit("dve", [B["vecs"]], [B["vecs"]],
               lambda e, m_=m_: e.tensor_tensor(
                   out=vecs[:, V_B2C + m_:V_B2C + m_ + 1],
                   in0=vecs[:, V_B2C + m_:V_B2C + m_ + 1],
                   in1=vecs[:, V_LN1B + m_:V_LN1B + m_ + 1], op=ALU.add), cost=100)
    dma("qload", ch_const, cvec[:], cvech, [], [B["cvec"]])
    dma("qload", ch_const, onesrow[:], onesrowh, [], [B["onesrow"]])
    dma("qload", ch_const, bsel[:], bselh, [], [B["bsel"]])
    dma("qload", ch_const, zer[:], zerh, [], [B["zer"]])

    ch_sin = [Chan(sem(f"c_si{i}"), 16) for i in range(4)]
    ch_sout = [Chan(sem(f"c_so{i}"), 16) for i in range(4)]
    Bscr = {name: [Buf(f"scr_{name}_{i2}") for i2 in range(ap_.shape[0])]
            for name, ap_ in scr.items()}
    Bcsi = [Buf("cs_in0"), Buf("cs_in1")]
    Bcso = [Buf("cs_out0"), Buf("cs_out1")]
    si = [0]

    def cast_slab(src_ap, dst_ap, dst_buf, w1_m=None):
        ic = si[0] % 4
        i = si[0] % 2
        si[0] += 1
        ft, fb = cs_in[i], Bcsi[i]
        bt_ap, bb = cs_out[i][:], Bcso[i]
        dma("qload", ch_sin[ic], ft[:, 0:1024], src_ap, [], [fb], note="scast in")
        if w1_m is not None:
            for k in range(NK):
                P.emit("dve", [fb, B["vecs"]], [bb],
                       lambda e, ft=ft, k=k: e.tensor_scalar(
                           out=cs_out[i][:, k * 128:(k + 1) * 128],
                           in0=ft[:, k * 128:(k + 1) * 128],
                           scalar1=vecs[:, V_LN1S + k:V_LN1S + k + 1],
                           scalar2=None, op0=ALU.mult), cost=140)
            parts = [(ft[:, k * 128:(k + 1) * 128],
                      vecs[:, V_LN1B + k:V_LN1B + k + 1]) for k in range(NK)]
            outp, obuf = mmg(rotB, (4, 5, 6, 7), parts, [fb, B["vecs"]],
                             note="b1fold", m_rows=128, n_cols=1, cost=2000)
            P.emit("dve", [obuf, B["vecs"]], [Bb1[w1_m]],
                   lambda e, outp=outp, m=w1_m: e.scalar_tensor_tensor(
                       out=vecs[:, V_B1 + m:V_B1 + m + 1], in0=outp[:, 0:1],
                       scalar=1.0, in1=vecs[:, V_B1 + m:V_B1 + m + 1],
                       op0=ALU.mult, op1=ALU.add), cost=130)
        elif i == 0:
            P.emit("dve", [fb], [bb],
                   lambda e, ft=ft, bt_ap=bt_ap: e.tensor_copy(
                       out=bt_ap, in_=ft[:, 0:1024]))
        else:
            P.emit("act", [fb], [bb],
                   lambda e, ft=ft, bt_ap=bt_ap: e.activation(
                       out=bt_ap, in_=ft[:, 0:1024], func=AF.Copy))
        dma("qstore", ch_sout[ic], dst_ap, bt_ap, [bb], [dst_buf], note="scast out")

    P.emit("dve", [B["gwr"]], [B["gwb"]],
           lambda e: e.tensor_copy(out=gwb[:], in_=gwr[:]))
    P.emit("dve", [B["cvec"]], [B["onesb"]],
           lambda e: e.tensor_copy(out=onesb[:], in_=cvec[:, C_ONE:C_ONE + 1]))

    poolA = {"slots": [0, 1, 2, 3, 4], "ctr": [0], "q": "qload"}
    poolB = {"slots": [5, 6, 7, 8, 9, 10, 11], "ctr": [0], "q": "qload"}
    pinned = set()

    def load_pair(pool, name, m0, n2=2, note="", pin=False):
        slots = pool["slots"]
        while True:
            i = slots[pool["ctr"][0] % len(slots)]
            pool["ctr"][0] += 1
            if i not in pinned:
                break
        src_ap = scr[name][m0:m0 + n2, :, :].rearrange("a p m -> p a m")
        dma(pool["q"], ch_slot[i], slab[i][:, 0:n2 * 1024].rearrange(
            "p (a m) -> p a m", a=n2), src_ap,
            Bscr[name][m0:m0 + n2], [Bslab[i]], note=note)
        if pin:
            pinned.add(i)
            return slab[i], Bslab[i], i
        return slab[i], Bslab[i]

    rotA = [0]
    rotB = [0]

    def mmg(rot, banks, parts, reads, m_rows=128, note="", n_cols=None, cost=None):
        bi = banks[rot[0] % len(banks)]
        rot[0] += 1
        bank, bbuf = pb[bi], Bbank[bi]
        out = bank[0:m_rows, 0:(t if n_cols is None else n_cols)]
        n = len(parts)

        def fn(e):
            last = None
            for i, (l, r) in enumerate(parts):
                last = e.matmul(out, l, r, start=(i == 0), stop=(i == n - 1))
            return last
        P.emit("pe", reads, [bbuf], fn, note=note,
               cost=(180 + 116 * len(parts)) if cost is None else cost)
        return out, bbuf

    for name, n in W_NS:
        for i2 in range(n):
            cast_slab(wslab[name][i2, :, :], scr[name][i2, :, :], Bscr[name][i2],
                      w1_m=(i2 if name == "w1p" else None))
    cast_slab(v2h, scr["v2"][0, :, :], Bscr["v2"][0])

    def mmg_at(bi, parts, reads, m_rows=128, note="", cost=None):
        bank, bbuf = pb[bi], Bbank[bi]
        out = bank[0:m_rows, 0:t]
        n = len(parts)

        def fn(e):
            last = None
            for i, (l, r) in enumerate(parts):
                last = e.matmul(out, l, r, start=(i == 0), stop=(i == n - 1))
            return last
        P.emit("pe", reads, [bbuf], fn, note=note,
               cost=(180 + 116 * len(parts)) if cost is None else cost)
        return out, bbuf

    def ck(buf_t, c):
        return buf_t[:, c * t:(c + 1) * t]

    def copy_fn(out_ap, in_ap):
        def fn(e):
            if hasattr(e, "tensor_copy"):
                return e.tensor_copy(out=out_ap, in_=in_ap)
            return e.copy(out=out_ap, in_=in_ap)
        return fn

    E = P.emit

    def ln_stats(src_t, src_b, cast_t, cast_bs, sq_t, sq_bs, banks, st, stp, bc, bcb,
                 eps_col, bc2=None):
        E("act", [src_b], list(sq_bs),
          lambda e: e.activation(out=sq_t[:, 0:CH], in_=src_t[:], func=AF.Square),
          cost=2000)
        mmg_at(banks[0], [(cvec[0:128, C_ONE:C_ONE + 1], ck(src_t, c))
                          for c in range(NK)],
               [src_b, B["cvec"]], m_rows=1, note="sum", cost=1900)
        E("dve", [Bbank[banks[0]]], [stp["mu"]],
          lambda e: e.tensor_scalar(out=st["mu"][:], in0=pb[banks[0]][0:1, 0:t],
                                    scalar1=1.0 / D, scalar2=None, op0=ALU.mult))
        mmg_at(banks[1], [(onesb[:, 0:1], ck(sq_t, c)) for c in range(NK)],
               list(sq_bs) + [B["onesb"]], m_rows=1, note="sumsq")
        E("dve", [Bbank[banks[1]]], [stp["ms"]],
          lambda e: e.tensor_scalar(out=st["ms"][:], in0=pb[banks[1]][0:1, 0:t],
                                    scalar1=1.0 / D, scalar2=None, op0=ALU.mult))
        E("dve", [stp["mu"]], [stp["var"]],
          lambda e: e.tensor_tensor(out=st["var"][:], in0=st["mu"][:], in1=st["mu"][:],
                                    op=ALU.mult))
        E("dve", [stp["ms"], stp["var"]], [stp["var"]],
          lambda e: e.tensor_tensor(out=st["var"][:], in0=st["ms"][:], in1=st["var"][:],
                                    op=ALU.subtract))
        eps_val = EPS_MOD if eps_col == C_EPS_MOD else EPS_IN
        x_ap = st["ms"]
        xi = x_ap.bitcast(I32)
        yi = st["rstd"].bitcast(I32)
        E("dve", [stp["var"]], [stp["ms"]],
          lambda e: e.tensor_scalar(out=x_ap[:], in0=st["var"][:], scalar1=eps_val,
                                    scalar2=None, op0=ALU.add), cost=120)
        E("dve", [stp["ms"]], [stp["rstd"]],
          lambda e: e.tensor_scalar(out=yi[:], in0=xi[:], scalar1=1, scalar2=None,
                                    op0=ALU.logical_shift_right), cost=120)
        E("dve", [stp["rstd"]], [stp["rstd"]],
          lambda e: e.tensor_scalar(out=yi[:], in0=yi[:], scalar1=-1,
                                    scalar2=RSQRT_MAGIC, op0=ALU.mult,
                                    op1=ALU.add), cost=120)
        E("dve", [stp["ms"]], [stp["ms"]],
          lambda e: e.tensor_scalar(out=x_ap[:], in0=x_ap[:], scalar1=0.5,
                                    scalar2=None, op0=ALU.mult), cost=120)
        for _ in range(2):
            E("dve", [stp["rstd"]], [stp["var"]],
              lambda e: e.tensor_tensor(out=st["var"][:], in0=st["rstd"][:],
                                        in1=st["rstd"][:], op=ALU.mult), cost=120)
            E("dve", [stp["ms"], stp["var"]], [stp["var"]],
              lambda e: e.tensor_tensor(out=st["var"][:], in0=x_ap[:],
                                        in1=st["var"][:], op=ALU.mult), cost=120)
            E("dve", [stp["var"]], [stp["var"]],
              lambda e: e.tensor_scalar(out=st["var"][:], in0=st["var"][:],
                                        scalar1=-1.0, scalar2=1.5, op0=ALU.mult,
                                        op1=ALU.add), cost=120)
            E("dve", [stp["rstd"], stp["var"]], [stp["rstd"]],
              lambda e: e.tensor_tensor(out=st["rstd"][:], in0=st["rstd"][:],
                                        in1=st["var"][:], op=ALU.mult), cost=120)
        E("dve", [stp["mu"], stp["rstd"]], [stp["nmr"]],
          lambda e: e.scalar_tensor_tensor(out=st["nmr"][:], in0=st["mu"][:],
                                           scalar=-1.0, in1=st["rstd"][:],
                                           op0=ALU.mult, op1=ALU.mult))
        bank0 = pb[banks[0]]
        P.emit("pe", [B["onesrow"], stp["rstd"], stp["nmr"]], [Bbank[banks[0]]],
               lambda e: e.matmul(bank0[0:128, 0:2 * t], onesrow[0:1, :],
                                  st["rn"][0:1, 0:2 * t], start=True, stop=True),
               note="bc", cost=700)
        E("act", [Bbank[banks[0]]], [bcb[0], bcb[1]],
          copy_fn(bc2[:, 0:2 * t], bank0[:, 0:2 * t]), cost=900,
          alts={"act": 900.0, "dve": 900.0})

    def ln_apply(src_t, src_b, dst_t, dst_b, bc, bcb, affine=None, eng="dve"):
        for c in range(NK):
            ec = "dve"
            op_cost = 450
            E(ec, [src_b, bcb[0]], [dst_b],
              lambda e, c=c: e.tensor_tensor(out=ck(dst_t, c), in0=ck(src_t, c),
                                             in1=bc[0][:, 0:t], op=ALU.mult),
              cost=op_cost, flex=True)
            E(ec, [dst_b, bcb[1]], [dst_b],
              lambda e, c=c: e.tensor_tensor(out=ck(dst_t, c), in0=ck(dst_t, c),
                                             in1=bc[1][:, 0:t], op=ALU.add),
              cost=op_cost, flex=True)
            if affine is not None:
                sc, bcv = affine
                E(ec, [dst_b, B["vecs"]], [dst_b],
                  lambda e, c=c: e.tensor_scalar(
                      out=ck(dst_t, c), in0=ck(dst_t, c),
                      scalar1=vecs[:, sc + c:sc + c + 1],
                      scalar2=vecs[:, bcv + c:bcv + c + 1],
                      op0=ALU.mult, op1=ALU.add), cost=op_cost, flex=True)

    def proj(wname, dst_t, dst_bs, nm, rhs_t, rhs_b, m_off=0, act_fn=AF.Copy):
        for m0 in range(0, nm, 2):
            svt, svb = load_pair(poolA, wname, m0 + m_off, note=f"{wname} m{m0}")
            for d in range(2):
                m = m0 + d
                parts = [(svt[:, d * 1024 + k * 128:d * 1024 + (k + 1) * 128],
                          ck(rhs_t, k)) for k in range(NK)]
                out, bbuf = mmg(rotA, (0, 1, 2, 3), parts, [svb, rhs_b],
                                note=f"{wname} m{m}")
                if act_fn == AF.Copy:
                    E("act", [bbuf], [dst_bs[m // 4]],
                      copy_fn(ck(dst_t, m), out), cost=480,
                      alts={"act": 480.0, "dve": 480.0})
                else:
                    E("act", [bbuf], [dst_bs[m // 4]],
                      lambda e, m=m, out=out, dst_t=dst_t, act_fn=act_fn: e.activation(
                          out=ck(dst_t, m), in_=out, func=act_fn),
                      table=("silu" if act_fn == AF.Silu else None), cost=480)

    def xload(i, b, j):
        par = i % 2
        t0 = j * t
        bxt, bxb = bx[par], B[f"bx{par}"]
        dma("qload", ch_x[par], bxt[:].rearrange("p (c t) -> p c t", c=NK),
            xh[b, :, t0:t0 + t].rearrange("(c p) t -> p c t", p=128),
            [], [bxb], note=f"x b{b} j{j}", cost=1500)

    def A1a(i, b, j):
        par = i % 2
        bxt, bxb = bx[par], B[f"bx{par}"]
        bzt, bzbf = bzbD[par], B[f"bzb{par}"]
        if j == 0:
            E("dve", [], Bcr, lambda e: e.memset(bcr[:], 0.0), cost=100)
            E("dve", [], Bci, lambda e: e.memset(bci[:], 0.0), cost=100)
        ln_stats(bxt, bxb, None, [],
                 bmix[par], [B[f"bmix{par}0"], B[f"bmix{par}1"]],
                 (3, 3), stA, stpA, bcA,
                 [B["bcA0"], B["bcA1"]], C_EPS_MOD, bc2=bcAt)
        for c in range(NK):
            hB = B[f"bt1{c // 4}"]
            E("dve", [bxb, B["bcA0"]], [hB],
              lambda e, c=c: e.tensor_tensor(out=ck(bt1, c), in0=ck(bxt, c),
                                             in1=bcA[0][:, 0:t], op=ALU.mult),
              cost=450, alts={"dve": 450.0, "gps": 1000.0})
            E("dve", [hB, B["bcA1"]], [bzbf],
              lambda e, c=c: e.tensor_tensor(out=ck(bzt, c), in0=ck(bt1, c),
                                             in1=bcA[1][:, 0:t], op=ALU.add),
              cost=450, alts={"dve": 450.0, "gps": 1000.0})

    def A1b(i):
        par = i % 2
        bzt, bzbf = bzbD[par], B[f"bzb{par}"]
        proj("wa", bar, [B["bar0"], B["bar1"]], NK, bzt, bzbf)
        proj("wa", bai, [B["bai0"], B["bai1"]], NK, bzt, bzbf, m_off=NK)
        proj("wk", bk, [B["bk0"], B["bk1"]], NK, bzt, bzbf)
        proj("wv", bv, [B["bv0"], B["bv1"]], NK, bzt, bzbf)

    Hw = CH // 2
    CD = 1350.0
    CA = 1150.0

    def chain_half(i, h):
        HSL = slice(h * Hw, (h + 1) * Hw)
        HS = lambda buf: buf[:, HSL]
        Bh = lambda nm: B[f"{nm}{h}"]
        Bk, Bv = B[f"bk{h}"], B[f"bv{h}"]
        bki_h = bar[:, HSL].bitcast(I32)
        E("dve", [Bk, Bv], [Bk],
          lambda e: e.tensor_tensor(out=HS(bk), in0=HS(bk), in1=HS(bv),
                                    op=ALU.mult), cost=700)
        E("act", [Bh("bar")], [Bh("bt1")],
          lambda e: e.activation(out=HS(bt1), in_=HS(bar), func=AF.Square), cost=CA)
        E("act", [Bh("bai")], [Bh("bt2")],
          lambda e: e.activation(out=HS(bt2), in_=HS(bai), func=AF.Square), cost=CA)
        E("dve", [Bh("bt1"), Bh("bt2")], [Bh("bt1")],
          lambda e: e.tensor_tensor(out=HS(bt1), in0=HS(bt1), in1=HS(bt2),
                                    op=ALU.add), cost=CD, flex=True)
        E("act", [Bh("bt1")], [Bh("bt2")],
          lambda e: e.activation(out=HS(bt2), in_=HS(bt1), func=AF.Ln),
          table="nlexp", cost=CA)
        E("act", [Bh("bt2"), B["cvec"]], [Bh("bt1")],
          lambda e: e.activation(out=HS(bt1), in_=HS(bt2), func=AF.Exp,
                                 scale=cvec[:, C_HALF:C_HALF + 1]),
          table="nlexp", cost=CA)
        E("act", [Bh("bar")], [Bh("bhr")],
          lambda e: e.activation(out=HS(bhr), in_=HS(bar), func=AF.Abs), cost=CA)
        E("dve", [Bh("bar")], [Bh("bcos")],
          lambda e: e.tensor_scalar(out=HS(bcos), in0=HS(bar), scalar1=0.0,
                                    scalar2=None, op0=ALU.is_ge),
          cost=CD * 0.55)
        E("dve", [Bh("bai")], [Bh("bnsin")],
          lambda e: e.tensor_scalar(out=HS(bnsin), in0=HS(bai), scalar1=0.0,
                                    scalar2=None, op0=ALU.is_ge),
          cost=CD * 0.55)
        E("dve", [Bh("bnsin")], [Bh("bnsin")],
          lambda e: e.tensor_scalar(out=HS(bnsin), in0=HS(bnsin), scalar1=2.0,
                                    scalar2=-1.0, op0=ALU.mult, op1=ALU.add),
          cost=CD * 0.55)
        E("dve", [Bh("bt1"), Bh("bhr")], [Bh("bt2")],
          lambda e: e.tensor_tensor(out=HS(bt2), in0=HS(bt1), in1=HS(bhr),
                                    op=ALU.add), cost=CD, flex=True)
        E("dve", [Bh("bt2")], [Bh("bt2")],
          lambda e: e.tensor_scalar(out=HS(bt2), in0=HS(bt2), scalar1=1e-30,
                                    scalar2=None, op0=ALU.max), cost=CD * 0.55)
        E("act", [Bh("bt2")], [Bh("bhr")],
          lambda e: e.activation(out=HS(bhr), in_=HS(bt2), func=AF.Ln),
          table="nlexp", cost=CA)
        E("act", [Bh("bhr"), B["cvec"]], [Bh("bt2")],
          lambda e: e.activation(out=HS(bt2), in_=HS(bhr), func=AF.Exp,
                                 scale=cvec[:, C_NEG1:C_NEG1 + 1]),
          table="nlexp", cost=CA)
        E("act", [Bh("bt1")], [Bh("bs")],
          lambda e: e.activation(out=HS(bs_), in_=HS(bt1), func=AF.Sigmoid),
          table="sigm", cost=CA)
        E("dve", [Bh("bai"), Bh("bt2")], [Bh("bt1")],
          lambda e: e.tensor_tensor(out=HS(bt1), in0=HS(bai), in1=HS(bt2),
                                    op=ALU.mult), cost=CD, flex=True)
        E("act", [Bh("bt1")], [Bh("bt2")],
          lambda e: e.activation(out=HS(bt2), in_=HS(bt1), func=AF.Arctan),
          table="sigm", cost=CA)
        E("dve", [Bh("bcos")], [Bh("bcos")],
          lambda e: e.tensor_scalar(out=HS(bcos), in0=HS(bcos), scalar1=-1.0,
                                    scalar2=1.0, op0=ALU.mult, op1=ALU.add),
          cost=CD * 0.55)
        E("dve", [Bh("bcos")], [Bh("bhr")],
          lambda e: e.tensor_scalar(out=HS(bhr), in0=HS(bcos), scalar1=-4.0,
                                    scalar2=2.0, op0=ALU.mult, op1=ALU.add),
          cost=CD * 0.55)
        E("dve", [Bh("bcos"), Bh("bnsin")], [Bh("bt1")],
          lambda e: e.tensor_tensor(out=HS(bt1), in0=HS(bcos), in1=HS(bnsin),
                                    op=ALU.mult), cost=CD, flex=True)
        E("dve", [Bh("bt2"), Bh("bhr")], [Bh("bt2")],
          lambda e: e.tensor_tensor(out=HS(bt2), in0=HS(bt2), in1=HS(bhr),
                                    op=ALU.mult), cost=CD, flex=True)
        E("dve", [Bh("bt1"), Bh("bt2")], [Bh("bai")],
          lambda e: e.scalar_tensor_tensor(out=HS(bai), in0=HS(bt1), scalar=PI,
                                           in1=HS(bt2), op0=ALU.mult,
                                           op1=ALU.add), cost=CD)
        for c in range(4 * h, 4 * h + 4):
            E("dve", [Bh("bai"), B["zer"]], [Bh("bai")],
              lambda e, c=c: e.tensor_tensor_scan(
                  out=ck(bai, c), data0=ck(bai, c), data1=zer[:, 0:t],
                  initial=zer[:, 0:1], op0=ALU.add, op1=ALU.add), cost=450)

        def red_sin(alpha_ap, alpha_bufs, dst_ap, dst_buf, neg):
            E("dve", alpha_bufs, [Bh("bar")],
              lambda e: e.tensor_scalar(out=bki_h, in0=alpha_ap, scalar1=INV_2PI,
                                        scalar2=None, op0=ALU.mult), cost=CD)
            E("dve", [Bh("bar")] + alpha_bufs, [Bh("bt1")],
              lambda e: e.scalar_tensor_tensor(out=HS(bt1), in0=bki_h,
                                               scalar=-TWO_PI, in1=alpha_ap,
                                               op0=ALU.mult, op1=ALU.add), cost=CD)
            if sim_safe_wrap:
                E("dve", [Bh("bt1")], [Bh("bt2")],
                  lambda e: e.tensor_scalar(out=HS(bt2), in0=HS(bt1), scalar1=-PI,
                                            scalar2=None, op0=ALU.is_lt), cost=CD)
                E("dve", [Bh("bt1")], [Bh("bhr")],
                  lambda e: e.tensor_scalar(out=HS(bhr), in0=HS(bt1), scalar1=PI,
                                            scalar2=None, op0=ALU.is_gt), cost=CD)
                E("dve", [Bh("bt2"), Bh("bhr")], [Bh("bt2")],
                  lambda e: e.tensor_tensor(out=HS(bt2), in0=HS(bt2), in1=HS(bhr),
                                            op=ALU.subtract), cost=CD)
                E("dve", [Bh("bt2"), Bh("bt1")], [Bh("bt1")],
                  lambda e: e.scalar_tensor_tensor(out=HS(bt1), in0=HS(bt2),
                                                   scalar=TWO_PI, in1=HS(bt1),
                                                   op0=ALU.mult, op1=ALU.add),
                  cost=CD)
            else:
                E("dve", [Bh("bt1")], [Bh("bt1")],
                  lambda e: e.tensor_scalar(out=HS(bt1), in0=HS(bt1), scalar1=-PI,
                                            scalar2=PI, op0=ALU.max, op1=ALU.min),
                  cost=CD * 0.55)
            if neg:
                E("act", [Bh("bt1"), B["cvec"]], [dst_buf],
                  lambda e: e.activation(out=dst_ap, in_=HS(bt1), func=AF.Sin,
                                         scale=cvec[:, C_NEG1:C_NEG1 + 1]),
                  table="silu", cost=CA)
            else:
                E("act", [Bh("bt1")], [dst_buf],
                  lambda e: e.activation(out=dst_ap, in_=HS(bt1), func=AF.Sin),
                  table="silu", cost=CA)

        red_sin(HS(bai), [Bh("bai")], HS(bnsin), Bh("bnsin"), neg=True)
        E("dve", [Bh("bai")], [Bh("bar")],
          lambda e: e.tensor_scalar(out=bki_h, in0=HS(bai), scalar1=INV_2PI,
                                    scalar2=0.25, op0=ALU.mult, op1=ALU.add),
          cost=CD)
        E("dve", [Bh("bar"), Bh("bai")], [Bh("bt1")],
          lambda e: e.scalar_tensor_tensor(out=HS(bt1), in0=bki_h,
                                           scalar=-TWO_PI, in1=HS(bai),
                                           op0=ALU.mult, op1=ALU.add), cost=CD)
        if sim_safe_wrap:
            E("dve", [Bh("bt1")], [Bh("bt2")],
              lambda e: e.tensor_scalar(out=HS(bt2), in0=HS(bt1),
                                        scalar1=-3 * PI / 2,
                                        scalar2=None, op0=ALU.is_lt), cost=CD)
            E("dve", [Bh("bt1")], [Bh("bhr")],
              lambda e: e.tensor_scalar(out=HS(bhr), in0=HS(bt1),
                                        scalar1=PI / 2,
                                        scalar2=None, op0=ALU.is_gt), cost=CD)
            E("dve", [Bh("bt2"), Bh("bhr")], [Bh("bt2")],
              lambda e: e.tensor_tensor(out=HS(bt2), in0=HS(bt2), in1=HS(bhr),
                                        op=ALU.subtract), cost=CD)
            E("dve", [Bh("bt2"), Bh("bt1")], [Bh("bt1")],
              lambda e: e.scalar_tensor_tensor(out=HS(bt1), in0=HS(bt2),
                                               scalar=TWO_PI, in1=HS(bt1),
                                               op0=ALU.mult, op1=ALU.add), cost=CD)
        else:
            E("dve", [Bh("bt1")], [Bh("bt1")],
              lambda e: e.tensor_scalar(out=HS(bt1), in0=HS(bt1),
                                        scalar1=-3 * PI / 2, scalar2=PI / 2,
                                        op0=ALU.max, op1=ALU.min), cost=CD * 0.55)
        E("act", [Bh("bt1"), B["cvec"]], [Bh("bcos")],
          lambda e: e.activation(out=HS(bcos), in_=HS(bt1), func=AF.Sin,
                                 bias=cvec[:, C_HALFPI:C_HALFPI + 1]),
          table="silu", cost=CA)
        E("dve", [Bk, Bh("bcos")], [Bh("bhr")],
          lambda e: e.tensor_tensor(out=HS(bhr), in0=HS(bk), in1=HS(bcos),
                                    op=ALU.mult), cost=700)
        E("dve", [Bk, Bh("bnsin")], [Bh("bhi")],
          lambda e: e.tensor_tensor(out=HS(bhi), in0=HS(bk), in1=HS(bnsin),
                                    op=ALU.mult), cost=700)
        for c in range(4 * h, 4 * h + 4):
            E("dve", [Bh("bs"), Bh("bhr"), Bcr[c]], [Bh("bhr")],
              lambda e, c=c: e.tensor_tensor_scan(
                  out=ck(bhr, c), data0=ck(bs_, c), data1=ck(bhr, c),
                  initial=bcr[:, c:c + 1], op0=ALU.mult, op1=ALU.add), cost=450)
            E("dve", [Bh("bs"), Bh("bhi"), Bci[c]], [Bh("bhi")],
              lambda e, c=c: e.tensor_tensor_scan(
                  out=ck(bhi, c), data0=ck(bs_, c), data1=ck(bhi, c),
                  initial=bci[:, c:c + 1], op0=ALU.mult, op1=ALU.add), cost=450)
        E("dve", [Bh("bhr"), Bh("bcos"), Bh("bs")], [Bh("bs")],
          lambda e: e.tensor_tensor(out=HS(bs_), in0=HS(bhr), in1=HS(bcos),
                                    op=ALU.mult), cost=700)
        E("dve", [Bh("bhi"), Bh("bnsin")], [Bk],
          lambda e: e.tensor_tensor(out=HS(bk), in0=HS(bhi), in1=HS(bnsin),
                                    op=ALU.mult), cost=700)
        E("dve", [Bh("bs"), Bk], [Bh("bs")],
          lambda e: e.tensor_tensor(out=HS(bs_), in0=HS(bs_), in1=HS(bk),
                                    op=ALU.add), cost=700)
        for c in range(4 * h, 4 * h + 4):
            L = c * t + t - 1
            s0 = h * Hw + 2 * (c - 4 * h)
            E("dve", [Bh("bs")], [Bcr[c]],
              lambda e, c=c, L=L: e.tensor_copy(out=bcr[:, c:c + 1],
                                                in_=bs_[:, L:L + 1]), cost=150)
            E("dve", [Bh("bhr"), Bh("bnsin")], [Bh("bt2")],
              lambda e, s0=s0, L=L: e.tensor_tensor(
                  out=bt2[:, s0:s0 + 1], in0=bhr[:, L:L + 1],
                  in1=bnsin[:, L:L + 1], op=ALU.mult), cost=150)
            E("dve", [Bh("bhi"), Bh("bcos")], [Bh("bt2")],
              lambda e, s0=s0, L=L: e.tensor_tensor(
                  out=bt2[:, s0 + 1:s0 + 2], in0=bhi[:, L:L + 1],
                  in1=bcos[:, L:L + 1], op=ALU.mult), cost=150)
            E("dve", [Bh("bt2")], [Bci[c]],
              lambda e, c=c, s0=s0: e.tensor_tensor(
                  out=bci[:, c:c + 1], in0=bt2[:, s0 + 1:s0 + 2],
                  in1=bt2[:, s0:s0 + 1], op=ALU.subtract), cost=150)

    def CH_(i):
        chain_half(i, 0)
        yield
        chain_half(i, 1)
        yield

    def A3a(i):
        proj("wq", bq, [B["bq0"], B["bq1"]], NK, bzbD[i % 2], B[f"bzb{i % 2}"])

    def A3b(i):
        par = i % 2
        proj("wg", bg, [B["bg0"], B["bg1"]], NK, bzbD[par], B[f"bzb{par}"],
             act_fn=AF.Silu)
        for h in (0, 1):
            HSL = slice(h * Hw, (h + 1) * Hw)
            HS = lambda buf, HSL=HSL: buf[:, HSL]
            Bh = lambda nm, h=h: B[f"{nm}{h}"]
            E("dve", [Bh("bq"), Bh("bs")], [Bh("bs")],
              lambda e, HS=HS: e.tensor_tensor(out=HS(bs_), in0=HS(bs_),
                                               in1=HS(bq), op=ALU.mult), cost=700)
            E("dve", [Bh("bs"), Bh("bg")], [B[f"bmix{par}{h}"]],
              lambda e, HS=HS, par=par: e.tensor_tensor(
                  out=HS(bmix[par]), in0=HS(bs_), in1=HS(bg), op=ALU.mult),
              cost=700)

    def B1(i, take=lambda n=1: None):
        par = i % 2
        bzt, bzbf = bzbD[par], B[f"bzb{par}"]
        for m0 in range(0, NK, 2):
            svt, svb = load_pair(poolB, "wo", m0, note=f"wo m{m0}")
            for d in range(2):
                m = m0 + d
                parts = [(svt[:, d * 1024 + k * 128:d * 1024 + (k + 1) * 128],
                          ck(bmix[par], k)) for k in range(NK)]
                out, bbuf = mmg(rotB, (4, 5, 6, 7), parts,
                                [svb, B[f"bmix{par}0"], B[f"bmix{par}1"]],
                                note=f"wo m{m}")
                E("dve", [bbuf, bzbf], [B["bw"]],
                  lambda e, m=m, out=out: e.tensor_tensor(
                      out=ck(bw, m), in0=out, in1=ck(bzt, m), op=ALU.add))
                take(2)
        ln_stats(bw, B["bw"], bwb, [B["bwb"]], bmix[par],
                 [B[f"bmix{par}0"], B[f"bmix{par}1"]],
                 (7, 7), stB, stpB, bcB,
                 [B["bcB0"], B["bcB1"]], C_EPS_IN, bc2=bcBt)
        for c in range(NK):
            E("dve", [B["bw"], B["bcB0"]], [B["bw"]],
              lambda e, c=c: e.tensor_tensor(out=ck(bw, c), in0=ck(bw, c),
                                             in1=bcB[0][:, 0:t], op=ALU.mult),
              cost=450, alts={"dve": 450.0, "gps": 1000.0})
            E("dve", [B["bw"], B["bcB1"]], [B["bwb"]],
              lambda e, c=c: e.tensor_tensor(out=ck(bwb, c), in0=ck(bw, c),
                                             in1=bcB[1][:, 0:t], op=ALU.add),
              cost=450, alts={"dve": 450.0, "gps": 1000.0})

    def B2(i, take=lambda n=1: None):
        for f0 in range(0, NF, 2):
            svt, svb = load_pair(poolB, "w1p", f0, note=f"w1 f{f0}")
            for d in range(2):
                f = f0 + d
                parts = [(svt[:, d * 1024 + k * 128:d * 1024 + (k + 1) * 128],
                          ck(bwb, k)) for k in range(NK)]
                out, bbuf = mmg(rotB, (4, 5, 6, 7), parts, [svb, B["bwb"]], note=f"w1 f{f}")
                E("act", [bbuf, B["vecs"], Bb1[f]], [Bh1[f // 8]],
                  lambda e, f=f, out=out: e.activation(
                      out=bh1[:, f * t:(f + 1) * t], in_=out, func=AF.Gelu_apprx_tanh,
                      bias=vecs[:, V_B1 + f:V_B1 + f + 1]), table="gelu", cost=550)
                take(2)

    def B3a(i, b, j):
        par = i % 2
        t0 = j * t
        bxt, bxb = bx[par], B[f"bx{par}"]
        for m in range(NK):
            w2s = [load_pair(poolB, "w2p", m * 4, note=f"w2 {m * 4}"),
                   load_pair(poolB, "w2p", m * 4 + 2, note=f"w2 {m * 4 + 2}")]
            parts = []
            for kb in range(NF):
                sl = w2s[kb // 16]
                d = (kb // 8) % 2
                parts.append((sl[0][:, d * 1024 + (kb % 8) * 128:
                                    d * 1024 + (kb % 8 + 1) * 128],
                              bh1[:, kb * t:(kb + 1) * t]))
            out, bbuf = mmg(rotB, (4, 5, 6, 7), parts,
                            [sv[1] for sv in w2s] + list(Bh1), note=f"w2 m{m}")
            E("dve", [bbuf, B["vecs"], B["bwb"]], [B["bw"]],
              lambda e, m=m, out=out: e.scalar_tensor_tensor(
                  out=ck(bw, m), in0=ck(bwb, m),
                  scalar=vecs[:, V_LN1S + m:V_LN1S + m + 1],
                  in1=out, op0=ALU.mult, op1=ALU.add))
            E("dve", [B["bw"], B["vecs"]], [B["bw"]],
              lambda e, m=m: e.tensor_scalar(
                  out=ck(bw, m), in0=ck(bw, m),
                  scalar1=vecs[:, V_B2C + m:V_B2C + m + 1], scalar2=None,
                  op0=ALU.add), cost=450)
        ln_stats(bw, B["bw"], bwb, [B["bwb"]], bmix[par],
                 [B[f"bmix{par}0"], B[f"bmix{par}1"]],
                 (7, 7), stB, stpB, bcB,
                 [B["bcB0"], B["bcB1"]], C_EPS_IN, bc2=bcBt)
        for c in range(NK):
            E("dve", [B["bw"], B["bcB0"]], [B["bw"]],
              lambda e, c=c: e.tensor_tensor(out=ck(bw, c), in0=ck(bw, c),
                                             in1=bcB[0][:, 0:t], op=ALU.mult),
              cost=450, alts={"dve": 450.0, "gps": 1000.0})
            E("dve", [B["bw"], B["bcB1"]], [B["bw"]],
              lambda e, c=c: e.tensor_tensor(out=ck(bw, c), in0=ck(bw, c),
                                             in1=bcB[1][:, 0:t], op=ALU.add),
              cost=450, alts={"dve": 450.0, "gps": 1000.0})
            E("dve", [B["bw"], B["vecs"]], [B["bw"]],
              lambda e, c=c: e.tensor_scalar(
                  out=ck(bw, c), in0=ck(bw, c),
                  scalar1=vecs[:, V_LN2S + c:V_LN2S + c + 1],
                  scalar2=vecs[:, V_LN2B + c:V_LN2B + c + 1],
                  op0=ALU.mult, op1=ALU.add), cost=450)

    def B3b(i, b, j):
        par = i % 2
        t0 = j * t
        bxt, bxb = bx[par], B[f"bx{par}"]
        E("dve", [bxb, B["bw"]], [bxb],
          lambda e: e.tensor_tensor(out=bxt[:, 0:Hw], in0=bxt[:, 0:Hw],
                                    in1=bw[:, 0:Hw], op=ALU.add), cost=1350,
          alts={"dve": 1350.0, "gps": 2900.0})
        E("dve", [bxb, B["bw"]], [bxb],
          lambda e: e.tensor_tensor(out=bxt[:, Hw:CH], in0=bxt[:, Hw:CH],
                                    in1=bw[:, Hw:CH], op=ALU.add), cost=1350,
          alts={"dve": 1350.0, "gps": 2900.0})
        ln_stats(bxt, bxb, bwb, [B["bwb"]], bmix[par],
                 [B[f"bmix{par}0"], B[f"bmix{par}1"]],
                 (7, 7), stB, stpB, bcB,
                 [B["bcB0"], B["bcB1"]], C_EPS_MOD, bc2=bcBt)
        for c in range(NK):
            E("dve", [bxb, B["bcB0"]], [B["bw"]],
              lambda e, c=c: e.tensor_tensor(out=ck(bw, c), in0=ck(bxt, c),
                                             in1=bcB[0][:, 0:t], op=ALU.mult),
              cost=450, alts={"dve": 450.0, "gps": 1000.0})
            E("dve", [B["bw"], B["bcB1"]], [B["bwb"]],
              lambda e, c=c: e.tensor_tensor(out=ck(bwb, c), in0=ck(bw, c),
                                             in1=bcB[1][:, 0:t], op=ALU.add),
              cost=450, alts={"dve": 450.0, "gps": 1000.0})
        mmg_at(7, [(gwb[:, k * 8:(k + 1) * 8], ck(bwb, k)) for k in range(NK)],
               [B["gwb"], B["bwb"]], m_rows=8, note="gate logits")
        E("act", [Bbank[7]], [B["sp"]],
          lambda e: e.activation(out=sp, in_=pb[7][0:8, 0:t], func=AF.Exp),
          table="nlexp", cost=400)
        mmg_at(7, [(cvec[0:8, C_ONE:C_ONE + 1], sp)],
               [B["sp"], B["cvec"]], m_rows=1, note="gate sum")
        E("dve", [Bbank[7]], [B["st_r"]],
          lambda e: e.reciprocal(out=st_r[:], in_=pb[7][0:1, 0:t]))
        E("dve", [B["st_r"]], [B["st_r"]],
          lambda e: e.tensor_scalar(out=st_r[:], in0=st_r[:], scalar1=1.0 / 16.0,
                                    scalar2=None, op0=ALU.mult))
        mmg_at(7, [(bsel[:], sp)], [B["bsel"], B["sp"]], m_rows=128, note="gate bc")
        E("act", [Bbank[7]], [B["sg128"]],
          lambda e: e.activation(out=sg128[:], in_=pb[7][:, 0:t], func=AF.Copy))
        uat, uab = load_pair(poolB, "uall", 0, n2=1, note="uall")
        out_t, bbuf_t = mmg(rotB, (4, 5, 6, 7),
                            [(uat[:, k * 128:(k + 1) * 128], ck(bwb, k))
                             for k in range(NK)], [uab, B["bwb"]], note="t lora")
        out_r, bbuf_r = mmg(rotB, (4, 5, 6, 7),
                            [(onesrow[0:1, :], st_r[0:1, 0:t])],
                            [B["onesrow"], B["st_r"]], note="r bc")
        E("dve", [bbuf_t, B["sg128"]], [B["stgb"]],
          lambda e: e.tensor_tensor(out=stgb[:], in0=out_t, in1=sg128[:],
                                    op=ALU.mult))
        E("dve", [B["stgb"], bbuf_r], [B["stgb"]],
          lambda e: e.tensor_tensor(out=stgb[:], in0=stgb[:], in1=out_r,
                                    op=ALU.mult))
        v2t, v2b, v2slot = load_pair(poolB, "v2", 0, n2=1, note="v2", pin=True)
        for m0 in range(0, NK, 2):
            svt, svb = load_pair(poolB, "fw", m0, note=f"fw m{m0}")
            for d in range(2):
                m = m0 + d
                parts = [(svt[:, d * 1024 + k * 128:d * 1024 + (k + 1) * 128],
                          ck(bwb, k)) for k in range(NK)]
                parts.append((v2t[:, m * 128:(m + 1) * 128], stgb[:]))
                out, bbuf = mmg(rotB, (4, 5, 6, 7), parts, [svb, B["bwb"], v2b, B["stgb"]],
                                note=f"ffn2 m{m}")
                sti = m % 2
                E("dve", [bbuf, bxb], [B[f"bst{sti}"]],
                  lambda e, m=m, out=out, sti=sti: e.tensor_tensor(
                      out=bst[sti][:], in0=out, in1=ck(bxt, m), op=ALU.add))
                dma("qstore", ch_out[sti],
                    oh[b, m * 128:(m + 1) * 128, t0:t0 + t], bst[sti][:],
                    [B[f"bst{sti}"]], [Buf("_out")], note=f"out b{b} j{j} m{m}")
        pinned.discard(v2slot)

    with es:
        for i in range(total + 1):
            if i < total:
                b, j = divmod(i, nt)
                xload(i, b, j)
                A1a(i, b, j)
                A1b(i)
                gen = CH_(i)
            else:
                gen = iter(())

            def take(n=1, gen=gen):
                for _ in range(n):
                    if next(gen, "_done") == "_done":
                        return
            if i > 0:
                B1(i - 1, take)
                B2(i - 1, take)
            take(10 ** 6)
            if i > 0:
                bb, jb = divmod(i - 1, nt)
                B3a(i - 1, bb, jb)
            if i < total:
                A3a(i)
            if i > 0:
                B3b(i - 1, bb, jb)
            if i < total:
                A3b(i)
        est = P.schedule_and_replay()
        import os
        if os.environ.get("K_PRINT_EST"):
            print(f"[sched] estimated makespan: {est / 1e6:.3f} ms")
        for ch in ch_out:
            if ch.cnt:
                P.engs["qstore"].raw.wait_ge(ch.sem, ch.cnt)
    return nc



def kernel(**inputs):
    x = np.asarray(inputs["L0_attn_x"], np.float32)
    w = host_pack_weights(inputs)
    nc = build_core()
    in_maps = []
    for ci in range(N_CORES):
        xc = np.ascontiguousarray(x[ci * NB:(ci + 1) * NB].transpose(0, 2, 1))
        m = dict(w)
        m["x"] = xc
        in_maps.append(m)
    res = run_bass_kernel_spmd(nc, in_maps, core_ids=list(range(N_CORES)))
    outs = [res.results[i]["out"] for i in range(N_CORES)]
    full = np.concatenate(outs, axis=0)
    return np.ascontiguousarray(full.transpose(0, 2, 1)).astype(np.float32)


# version 41
# speedup vs baseline: 1.1637x; 1.0075x over previous
import numpy as np
from contextlib import ExitStack

import concourse.bass as bass
import concourse.mybir as mybir
from concourse.bass_utils import run_bass_kernel_spmd

F32 = mybir.dt.float32
BF16 = mybir.dt.bfloat16
I32 = mybir.dt.int32
AF = mybir.ActivationFunctionType
ALU = mybir.AluOpType

D = 1024
NK = 8
DF = 4096
NF = 32
PI = float(np.pi)
TWO_PI = 2.0 * PI
INV_2PI = 1.0 / TWO_PI
EPS_MOD = 1e-5
EPS_IN = 1e-6

N_CORES = 8
NB = 4
S = 2048
T = 256
NSLOT = 12

C_EPS_MOD, C_EPS_IN, C_HALFPI, C_NEG1, C_ONE, C_NEGH, C_HALF = 0, 1, 2, 3, 4, 5, 6
V_LN1S, V_LN1B, V_LN2S, V_LN2B, V_B2, V_B1 = 0, 8, 16, 24, 32, 40
V_B2C = 72


class Buf:
    __slots__ = ("name", "writer", "readers")

    def __init__(self, name):
        self.name = name
        self.writer = None
        self.readers = []


class Chan:
    __slots__ = ("sem", "step", "cnt", "last_node")

    def __init__(self, sem, step):
        self.sem = sem
        self.step = step
        self.cnt = 0
        self.last_node = None


class Eng:
    __slots__ = ("raw", "name", "chan", "sat")

    def __init__(self, raw, name, chan):
        self.raw = raw
        self.name = name
        self.chan = chan
        self.sat = {}


class Node:
    __slots__ = ("nid", "ename", "fn", "reads", "writes", "chan", "note",
                 "cost", "table", "deps", "prio", "finish", "alts")

    def __init__(self, nid, ename, fn, reads, writes, chan, note, cost, table):
        self.nid = nid
        self.ename = ename
        self.alts = None
        self.fn = fn
        self.reads = reads
        self.writes = writes
        self.chan = chan
        self.note = note
        self.cost = cost
        self.table = table
        self.deps = set()
        self.prio = 0.0
        self.finish = 0.0


ACT_TABLE_NS = 5000.0
RSQRT_MAGIC = 0x5f3759df


class Prog:

    def __init__(self, loose_sync=False):
        self.engs = {}
        self.nodes = []
        self.capture = True
        self.loose_sync = loose_sync

    def add_eng(self, name, raw, chan):
        self.engs[name] = Eng(raw, name, chan)

    def emit(self, ename, reads, writes, fn, chan=None, note="", cost=None,
             table=None, flex=False, alts=None):
        if not self.capture:
            return self._emit_real(ename, reads, writes, fn, chan, note)
        if cost is None:
            cost = {"pe": 1000.0, "dve": 450.0, "act": 450.0, "gps": 5000.0}.get(
                ename, 2500.0)
        n = Node(len(self.nodes), ename, fn, list(reads), list(writes), chan,
                 note, float(cost), table)
        import os as _os
        if not _os.environ.get("K_FLEX"):
            flex = False
        if alts is not None:
            n.alts = dict(alts)
        elif flex == "copy3":
            n.alts = {"dve": float(cost), "gps": float(cost) * 2.0,
                      "act": float(cost) * 1.4}
        elif flex:
            n.alts = {"dve": float(cost), "gps": float(cost) * 2.0}
        if chan is not None and getattr(chan, "last_node", None) is not None:
            n.deps.add(chan.last_node)
        for b in reads:
            if b.writer is not None:
                n.deps.add(b.writer)
        for b in writes:
            if b.writer is not None:
                n.deps.add(b.writer)
            for r in b.readers:
                n.deps.add(r)
        n.deps.discard(n.nid)
        for b in writes:
            b.writer = n.nid
            b.readers = []
        for b in reads:
            b.readers.append(n.nid)
        if chan is not None:
            chan.last_node = n.nid
        self.nodes.append(n)
        return n

    def schedule_and_replay(self):
        nodes = self.nodes
        nn = len(nodes)
        succ = [[] for _ in range(nn)]
        indeg = [0] * nn
        for n in nodes:
            for d in n.deps:
                succ[d].append(n.nid)
                indeg[n.nid] += 1
        import heapq
        import os
        simkey = lambda e: "gps" if e == "qstore" else e
        keys = {simkey(e) for e in self.engs}
        rev = []
        rem = [len(succ[i]) for i in range(nn)]
        stack = [i for i in range(nn) if rem[i] == 0]
        while stack:
            i = stack.pop()
            rev.append(i)
            for p in nodes[i].deps:
                rem[p] -= 1
                if rem[p] == 0:
                    stack.append(p)

        def run_pass(wc, costs):
            prio = [0.0] * nn
            for i in rev:
                best = 0.0
                for s in succ[i]:
                    if prio[s] > best:
                        best = prio[s]
                prio[i] = costs[i] + best
            eng_time = {e: 0.0 for e in keys}
            act_tab = None
            ready = {e: [] for e in keys}
            dready = [0.0] * nn
            ind = indeg[:]
            scheduled = [False] * nn
            assign = [None] * nn

            def push(i):
                n3 = nodes[i]
                if n3.alts:
                    for e3 in n3.alts:
                        heapq.heappush(ready[e3], (-prio[i], i))
                else:
                    heapq.heappush(ready[simkey(n3.ename)], (-prio[i], i))

            for i in range(nn):
                if ind[i] == 0:
                    push(i)
            sched = []
            nsched = 0
            while nsched < nn:
                best = None
                for e, rl in ready.items():
                    while rl and scheduled[rl[0][1]]:
                        heapq.heappop(rl)
                    if not rl:
                        continue
                    p, i = rl[0]
                    start = max(eng_time[e], dready[i])
                    n3 = nodes[i]
                    ecost = n3.alts.get(e, costs[i]) if n3.alts else costs[i]
                    key = (start + wc * ecost, p)
                    if best is None or key < best[0]:
                        best = (key, e)
                key, e = best
                _, i = heapq.heappop(ready[e])
                n = nodes[i]
                scheduled[i] = True
                if n.alts:
                    assign[i] = e
                    cost = n.alts.get(e, costs[i])
                else:
                    assign[i] = n.ename
                    cost = costs[i]
                start = max(eng_time[e], dready[i])
                if e == "act" and n.table is not None and act_tab != n.table:
                    cost += ACT_TABLE_NS
                    act_tab = n.table
                fin = start + cost
                eng_time[e] = fin
                sched.append(i)
                nsched += 1
                for s in succ[i]:
                    if dready[s] < fin:
                        dready[s] = fin
                    ind[s] -= 1
                    if ind[s] == 0:
                        push(s)
            return sched, assign

        def evaluate(sched, assign):
            eng_time = {e: 0.0 for e in keys}
            act_tab = None
            dready = [0.0] * nn
            fin = [0.0] * nn
            for i in sched:
                n = nodes[i]
                e = simkey(assign[i])
                cost = n.alts.get(e, n.cost) if n.alts else n.cost
                start = max(eng_time[e], dready[i])
                if e == "act" and n.table is not None and act_tab != n.table:
                    cost += ACT_TABLE_NS
                    act_tab = n.table
                fin[i] = start + cost
                eng_time[e] = fin[i]
                for s in succ[i]:
                    if dready[s] < fin[i]:
                        dready[s] = fin[i]
            return max(fin), fin

        base_costs = [n.cost for n in nodes]
        trials = [(0.5, base_costs)]
        for wc in (0.0, 0.15, 0.25, 0.35, 0.42, 0.58, 0.65, 0.8, 1.0, 1.3):
            trials.append((wc, base_costs))
        n_extra = int(os.environ.get("K_SCHED_TRIALS", "0"))
        rngs = np.random.default_rng(1234)
        for tix in range(n_extra):
            wc = (0.35, 0.5, 0.65)[tix % 3]
            amp = 0.25 if tix % 2 else 0.12
            jit = rngs.uniform(1.0 - amp, 1.0 + amp, nn)
            trials.append((wc, [base_costs[i] * jit[i] for i in range(nn)]))
        best_pick = None
        for wc, costs in trials:
            sched, assign = run_pass(wc, costs)
            mk, fin = evaluate(sched, assign)
            if os.environ.get("K_PRINT_TRIALS"):
                print("[sched] trial wc=%s mk=%.3f" % (wc, mk / 1e6))
            if best_pick is None or mk < best_pick[0]:
                best_pick = (mk, sched, assign, fin, wc)
        mk_best, sched, assign, fin, wc_best = best_pick
        for i in range(nn):
            nodes[i].finish = fin[i]
            if nodes[i].alts:
                nodes[i].ename = assign[i] if assign[i] != "qstore" else "gps"
        if os.environ.get("K_PRINT_EST"):
            print("[sched] portfolio best wc=%s makespan %.3f ms"
                  % (wc_best, mk_best / 1e6))
        if os.environ.get("K_GAPS"):
            from collections import Counter
            gape = os.environ.get("K_GAPS")
            seq = [i for i in sched if simkey(nodes[i].ename) == gape]
            gap_c = Counter()
            gap_t = Counter()
            prev_end = 0.0
            for i in seq:
                n = nodes[i]
                stt_ = fin[i] - (n.alts.get(simkey(n.ename), n.cost)
                                 if n.alts else n.cost)
                if stt_ > prev_end + 500.0:
                    dmax, dn = -1.0, None
                    for d in n.deps:
                        if fin[d] > dmax:
                            dmax, dn = fin[d], d
                    if dn is not None:
                        k = nodes[dn]
                        nm = (k.note or "L%d" % k.fn.__code__.co_firstlineno)
                        me = (n.note or "L%d" % n.fn.__code__.co_firstlineno)
                        gap_c[(me.split(" ")[0], "<-", simkey(k.ename),
                               nm.split(" ")[0])] += 1
                        gap_t[(me.split(" ")[0], "<-", simkey(k.ename),
                               nm.split(" ")[0])] += stt_ - prev_end
                prev_end = max(prev_end, fin[i])
            tot = sum(gap_t.values())
            if os.environ.get("K_GAP1"):
                want = os.environ.get("K_GAP1")
                prev_end2 = 0.0
                shown = 0
                for i in seq:
                    n = nodes[i]
                    cst = n.alts.get(simkey(n.ename), n.cost) if n.alts else n.cost
                    stt_ = fin[i] - cst
                    nm = (n.note or "L%d" % n.fn.__code__.co_firstlineno)
                    if (stt_ > prev_end2 + 3000.0 and nm.startswith(want)
                            and shown < 2 and fin[i] > 1.5e6):
                        shown += 1
                        print("[gap1] %s start=%.1fus gap=%.2fus deps:"
                              % (nm, stt_ / 1e3, (stt_ - prev_end2) / 1e3))
                        dl = sorted(n.deps, key=lambda d: -fin[d])[:6]
                        for d in dl:
                            k = nodes[d]
                            dn = (k.note or "L%d" % k.fn.__code__.co_firstlineno)
                            dc = k.alts.get(simkey(k.ename), k.cost) if k.alts else k.cost
                            print("    dep fin=%.1fus start=%.1fus %s %s"
                                  % (fin[d] / 1e3, (fin[d] - dc) / 1e3,
                                     simkey(k.ename), dn))
                    prev_end2 = max(prev_end2, fin[i])
            print("[sched] %s gaps total %.3f ms; top releasers:" % (gape, tot / 1e6))
            for k2, v in gap_t.most_common(18):
                print("    %.3f ms n=%d  %s" % (v / 1e6, gap_c[k2], k2))
        if os.environ.get("K_PRINT_EST"):
            busy = {}
            for n2 in nodes:
                k = simkey(n2.ename)
                busy[k] = busy.get(k, 0.0) + n2.cost
            mk = max(nd.finish for nd in nodes)
            print("[sched] busy:", {k: round(v / 1e6, 2) for k, v in busy.items()},
                  "makespan", round(mk / 1e6, 2))
            eng_prev = {}
            prev_on_eng = {}
            start_t = {}
            for idx2 in sched:
                n2 = nodes[idx2]
                k = simkey(n2.ename)
                eng_prev[idx2] = prev_on_eng.get(k)
                prev_on_eng[k] = idx2
            cur = max(range(nn), key=lambda i2: nodes[i2].finish)
            from collections import Counter
            busy_c = Counter()
            stall_c = Counter()
            hops = 0
            while cur is not None and hops < 20000:
                hops += 1
                n2 = nodes[cur]
                st2 = n2.finish - n2.cost
                ep = eng_prev.get(cur)
                dmax, dnode = -1.0, None
                for d in n2.deps:
                    if nodes[d].finish > dmax:
                        dmax, dnode = nodes[d].finish, d
                epf = nodes[ep].finish if ep is not None else -1.0
                key2 = (simkey(n2.ename), (n2.note or "?").split(" ")[0])
                busy_c[key2] += n2.cost
                if epf >= dmax and ep is not None:
                    cur = ep
                elif dnode is not None:
                    if dmax < st2 - 1.0:
                        stall_c[("idle-before", key2[0])] += st2 - dmax
                    cur = dnode
                else:
                    break
            print("[sched] walk busy by (eng, op):")
            for k2, v in busy_c.most_common(14):
                print("   ", k2, round(v / 1e6, 2), "ms")
            print("[sched] stalls:", {k2: round(v / 1e6, 2) for k2, v in stall_c.most_common(8)})
        for n in nodes:
            for b in n.reads + n.writes:
                b.writer = None
                b.readers = []
            if n.chan is not None:
                n.chan.last_node = None
        self.capture = False
        for i in sched:
            n = nodes[i]
            self._emit_real(n.ename, n.reads, n.writes, n.fn, n.chan, n.note)
        return max(nd.finish for nd in nodes)

    def _emit_real(self, ename, reads, writes, fn, chan=None, note=""):
        eng = self.engs[ename]
        own = chan is None
        ch = eng.chan if own else chan
        deps = {}

        def add(mark):
            if mark is None:
                return
            sem, val, owner = mark
            k = id(sem)
            if k not in deps or deps[k][1] < val:
                deps[k] = mark

        if not own and ch.cnt > 0:
            add((ch.sem, ch.cnt, None))
        for b in reads:
            add(b.writer)
        for b in writes:
            add(b.writer)
            for m in b.readers:
                add(m)
        for sem, val, _owner in deps.values():
            if self.loose_sync and _owner is eng:
                continue
            cur = eng.sat.get(id(sem), 0)
            if val > cur:
                eng.raw.wait_ge(sem, val)
                eng.sat[id(sem)] = val
        last = fn(eng.raw)
        ch.cnt += ch.step
        last.then_inc(ch.sem, ch.step)
        mark = (ch.sem, ch.cnt, eng if own else None)
        for b in writes:
            b.writer = mark
            b.readers = []
        for b in reads:
            b.readers.append(mark)
        return mark



def _kslabs(w_eff):
    M = w_eff.shape[1]
    return np.ascontiguousarray(w_eff.reshape(NK, 128, M))


def _pack_resident(w_eff):
    M = w_eff.shape[1]
    return np.ascontiguousarray(
        w_eff.reshape(NK, 128, M).transpose(1, 0, 2).reshape(128, NK * M))


def _col(v):
    return np.ascontiguousarray(v.reshape(8, 128).T)


def _dmaj(w_eff, nm):
    return np.ascontiguousarray(
        w_eff.reshape(NK, 128, nm, 128).transpose(2, 1, 0, 3).reshape(nm, 128, 1024))


def host_pack_weights(I, t_tile=T):
    f32 = lambda a: np.ascontiguousarray(np.asarray(a, np.float32))
    w = {}
    w["wq"] = _dmaj(f32(I["L0_attn_Wq"]), 8)
    w["wk"] = _dmaj(f32(I["L0_attn_Wk"]), 8)
    w["wv"] = _dmaj(f32(I["L0_attn_Wv"]), 8)
    w["wg"] = _dmaj(f32(I["L0_attn_Wg"]), 8)
    w["wo"] = _dmaj(f32(I["L0_attn_Wo"]), 8)
    w["wa"] = _dmaj(f32(I["L0_attn_Wa"]), 16)
    w["w1p"] = _dmaj(f32(I["L0_attn_W1"]), NF)
    w2 = f32(I["L0_attn_W2"])
    w["w2p"] = np.ascontiguousarray(
        w2.reshape(NF, 128, NK, 128).transpose(2, 0, 1, 3)
        .reshape(NK, 4, NK, 128, 128).transpose(0, 1, 3, 2, 4)
        .reshape(NF, 128, 1024))
    w["fw"] = _dmaj(f32(I["L0_ffn_weight"]).T, 8)
    w["uall"] = _dmaj(f32(I["L0_ffn_lora_U"]).reshape(128, 1024).T, 1)
    w["v2"] = np.ascontiguousarray(f32(I["L0_ffn_lora_V"]).transpose(0, 2, 1).reshape(128, 1024))
    w["gw"] = _pack_resident(f32(I["L0_ffn_gate_w"]).T)
    vecs = np.zeros((128, 80), np.float32)
    vecs[:, V_LN1S:V_LN1S + 8] = _col(f32(I["L0_attn_ln1_scale"]))
    vecs[:, V_LN1B:V_LN1B + 8] = _col(f32(I["L0_attn_ln1_bias"]))
    vecs[:, V_LN2S:V_LN2S + 8] = _col(f32(I["L0_attn_ln2_scale"]))
    vecs[:, V_LN2B:V_LN2B + 8] = _col(f32(I["L0_attn_ln2_bias"]))
    vecs[:, V_B2:V_B2 + 8] = _col(f32(I["L0_attn_b2"]))
    vecs[:, V_B2C:V_B2C + 8] = _col(f32(I["L0_attn_b2"]))
    vecs[:, V_B1:V_B1 + 32] = np.ascontiguousarray(f32(I["L0_attn_b1"]).reshape(32, 128).T)
    w["vecs"] = vecs
    cvec = np.zeros((128, 8), np.float32)
    cvec[:, C_EPS_MOD] = EPS_MOD
    cvec[:, C_EPS_IN] = EPS_IN
    cvec[:, C_HALFPI] = PI / 2
    cvec[:, C_NEG1] = -1.0
    cvec[:, C_ONE] = 1.0
    cvec[:, C_NEGH] = -0.5
    cvec[:, C_HALF] = 0.5
    w["cvec"] = cvec
    w["onesrow"] = np.ones((1, 128), np.float32)
    bsel = np.zeros((8, 128), np.float32)
    for e in range(8):
        bsel[e, 16 * e:16 * (e + 1)] = 1.0
    w["bsel"] = bsel
    import ml_dtypes
    w["zer"] = np.zeros((128, t_tile), ml_dtypes.bfloat16)
    return w




def build_core(nb=NB, s=S, t=T, sim_safe_wrap=False, loose_sync=True):
    CH = NK * t
    nt = s // t
    total = nb * nt
    nc = bass.Bass("TRN2", target_bir_lowering=False)

    xh = nc.dram_tensor("x", [nb, D, s], F32, kind="ExternalInput").ap()
    oh = nc.dram_tensor("out", [nb, D, s], F32, kind="ExternalOutput").ap()
    wslab = {}
    for name, n in [("wq", 8), ("wk", 8), ("wv", 8), ("wg", 8), ("wo", 8),
                    ("wa", 16), ("w1p", 32), ("w2p", 32), ("fw", 8)]:
        wslab[name] = nc.dram_tensor(name, [n, 128, 1024], F32, kind="ExternalInput").ap()
    wslab["uall"] = nc.dram_tensor("uall", [1, 128, 1024], F32, kind="ExternalInput").ap()
    v2h = nc.dram_tensor("v2", [128, 1024], F32, kind="ExternalInput").ap()
    W_NS = [("wa", 16), ("wk", 8), ("wv", 8), ("wq", 8), ("wg", 8), ("wo", 8),
            ("w1p", 32), ("w2p", 32), ("fw", 8), ("uall", 1)]
    scr = {name: nc.dram_tensor(f"scr_{name}", [n, 128, 1024], BF16, kind="Internal").ap()
           for name, n in W_NS}
    scr["v2"] = nc.dram_tensor("scr_v2", [1, 128, 1024], BF16, kind="Internal").ap()
    gwh = nc.dram_tensor("gw", [128, 64], F32, kind="ExternalInput").ap()
    vecsh = nc.dram_tensor("vecs", [128, 80], F32, kind="ExternalInput").ap()
    cvech = nc.dram_tensor("cvec", [128, 8], F32, kind="ExternalInput").ap()
    onesrowh = nc.dram_tensor("onesrow", [1, 128], F32, kind="ExternalInput").ap()
    bselh = nc.dram_tensor("bsel", [8, 128], F32, kind="ExternalInput").ap()
    zerh = nc.dram_tensor("zer", [128, t], BF16, kind="ExternalInput").ap()

    es = ExitStack()
    sb = lambda name, shape, dt=F32: es.enter_context(nc.sbuf_tensor(name, shape, dt))

    bx = [sb("bx0", [128, CH]), sb("bx1", [128, CH])]
    bzbD = [sb("bzb0", [128, CH], BF16), sb("bzb1", [128, CH], BF16)]
    bk = sb("bk", [128, CH], BF16); bv = sb("bv", [128, CH], BF16)
    cs_in = [sb(f"cs_in{i}", [128, 512]) for i in range(4)]
    cs_out = [sb(f"cs_out{i}", [128, 512], BF16) for i in range(4)]
    bar = sb("bar", [128, CH]); bai = sb("bai", [128, CH])
    bq = sb("bq", [128, CH], BF16); bg = sb("bg", [128, CH], BF16)
    bs_ = sb("bs", [128, CH], BF16)
    bcos = sb("bcos", [128, CH], BF16); bnsin = sb("bnsin", [128, CH], BF16)
    bhr = sb("bhr", [128, CH], BF16); bhi = sb("bhi", [128, CH], BF16)
    bt1 = sb("bt1", [128, CH]); bt2 = sb("bt2", [128, CH])
    bmix = [sb("bmix0", [128, CH], BF16), sb("bmix1", [128, CH], BF16)]
    bki = bar[:, 0:CH].bitcast(I32)
    bw = sb("bw", [128, CH])
    bwb = sb("bwb", [128, CH], BF16)
    bh1 = sb("bh1", [128, NF * t], BF16)
    bst = [sb("bst0", [128, t]), sb("bst1", [128, t])]
    sp = bst[0][0:8, :]
    bcAt = sb("bcAt", [128, 2 * t])
    bcBt = sb("bcBt", [128, 2 * t])
    bcA = [bcAt[:, 0:t], bcAt[:, t:2 * t]]
    bcB = [bcBt[:, 0:t], bcBt[:, t:2 * t]]
    stA = {nm: sb(f"stA_{nm}", [1, t]) for nm in ["mu", "ms", "var"]}
    stB = {nm: sb(f"stB_{nm}", [1, t]) for nm in ["mu", "ms", "var"]}
    stA_rn = sb("stA_rn", [1, 2 * t])
    stB_rn = sb("stB_rn", [1, 2 * t])
    for st_, rn_ in ((stA, stA_rn), (stB, stB_rn)):
        st_["rn"] = rn_
        st_["rstd"] = rn_[:, 0:t]
        st_["nmr"] = rn_[:, t:2 * t]
    st_r = stB["mu"]
    sg128 = sb("sg128", [128, t], BF16)
    stgb = sb("stgb", [128, t], BF16)
    slab = [sb(f"slab{i}", [128, 2048], BF16) for i in range(12)]
    gwr = sb("gwr", [128, 64]); gwb = sb("gwb", [128, 64], BF16)
    onesb = sb("onesb", [128, 1], BF16)
    vecs = sb("vecs_s", [128, 80]); cvec = sb("cvec_s", [128, 8])
    onesrow = sb("onesrow_s", [1, 128]); bsel = sb("bsel_s", [8, 128])
    zer = sb("zer_s", [128, t], BF16)
    bcr = sb("bcr", [128, 8]); bci = sb("bci", [128, 8])
    pb = [es.enter_context(nc.psum_tensor(f"pb{i}", [128, 512], F32)) for i in range(8)]

    sem = lambda name: es.enter_context(nc.semaphore(name))
    P = Prog(loose_sync=loose_sync)
    P.add_eng("pe", nc.tensor, Chan(sem("s_pe"), 1))
    P.add_eng("act", nc.scalar, Chan(sem("s_act"), 1))
    P.add_eng("dve", nc.vector, Chan(sem("s_dve"), 1))
    P.add_eng("gps", nc.gpsimd, Chan(sem("s_gps"), 1))
    P.add_eng("qload", nc.sync, None)
    P.add_eng("qstore", nc.gpsimd, None)
    ch_const = Chan(sem("c_const"), 16)
    ch_x = [Chan(sem(f"c_x{i}"), 16) for i in range(2)]
    ch_slot = [Chan(sem(f"c_sl{i}"), 16) for i in range(12)]
    ch_out = [Chan(sem(f"c_out{i}"), 16) for i in range(2)]

    B = {}
    for nm in ["bx0", "bx1", "bzb0", "bzb1", "bw", "bwb",
               "bst0", "bst1", "bcA0", "bcA1", "bcB0", "bcB1", "st_r", "sp",
               "sg128", "stgb", "gwr", "gwb", "onesb", "vecs", "cvec",
               "onesrow", "bsel", "zer"]:
        B[nm] = Buf(nm)
    Bh1 = [Buf(f"bh1_{q}") for q in range(4)]
    for nm in ["bk", "bv", "bar", "bai", "bq", "bg", "bs",
               "bcos", "bnsin", "bhr", "bhi", "bt1", "bt2", "bmix0", "bmix1"]:
        B[nm + "0"] = Buf(nm + "0")
        B[nm + "1"] = Buf(nm + "1")
    for nm in ["mu", "ms", "var", "rstd", "nmr"]:
        B[f"stA_{nm}"] = Buf(f"stA_{nm}")
        B[f"stB_{nm}"] = Buf(f"stB_{nm}")
    B["st_r"] = B["stB_mu"]
    stpA = {k: B[f"stA_{k}"] for k in ["mu", "ms", "var", "rstd", "nmr"]}
    stpB = {k: B[f"stB_{k}"] for k in ["mu", "ms", "var", "rstd", "nmr"]}
    B["sp"] = B["bst0"]
    Bslab = [Buf(f"slab{i}") for i in range(12)]
    Bb1 = [Buf(f"b1_{m}") for m in range(NF)]
    Bbank = [Buf(f"pb{i}") for i in range(8)]
    Bcr = [Buf(f"bcr{c}") for c in range(NK)]
    Bci = [Buf(f"bci{c}") for c in range(NK)]

    block_cm = nc.Block()
    es.enter_context(block_cm)

    def dma(ename, ch, out_ap, in_ap, reads, writes, note="", cost=800):
        return P.emit(ename, reads, writes,
                      lambda e: e.dma_start(out=out_ap, in_=in_ap), chan=ch, note=note,
                      cost=cost)

    dma("qload", ch_const, gwr[:], gwh, [], [B["gwr"]])
    dma("qload", ch_const, vecs[:], vecsh, [], [B["vecs"]] + Bb1)
    for m_ in range(NK):
        P.emit("dve", [B["vecs"]], [B["vecs"]],
               lambda e, m_=m_: e.tensor_tensor(
                   out=vecs[:, V_B2C + m_:V_B2C + m_ + 1],
                   in0=vecs[:, V_B2C + m_:V_B2C + m_ + 1],
                   in1=vecs[:, V_LN1B + m_:V_LN1B + m_ + 1], op=ALU.add), cost=100)
    dma("qload", ch_const, cvec[:], cvech, [], [B["cvec"]])
    dma("qload", ch_const, onesrow[:], onesrowh, [], [B["onesrow"]])
    dma("qload", ch_const, bsel[:], bselh, [], [B["bsel"]])
    dma("qload", ch_const, zer[:], zerh, [], [B["zer"]])

    ch_sin = [Chan(sem(f"c_si{i}"), 16) for i in range(4)]
    ch_sout = [Chan(sem(f"c_so{i}"), 16) for i in range(4)]
    Bscr = {name: [Buf(f"scr_{name}_{i2}") for i2 in range(ap_.shape[0])]
            for name, ap_ in scr.items()}
    Bcsi = [Buf(f"cs_in{i}") for i in range(4)]
    Bcso = [Buf(f"cs_out{i}") for i in range(4)]
    si = [0]

    def cast_slab(src_ap, dst_ap, dst_buf, w1_m=None):
        hs = []
        for h in range(2):
            i = si[0] % 4
            si[0] += 1
            ft, fb = cs_in[i], Bcsi[i]
            bt_ap, bb = cs_out[i][:], Bcso[i]
            dma("qload", ch_sin[i], ft[:, 0:512], src_ap[:, h * 512:(h + 1) * 512],
                [], [fb], note="scast in", cost=900)
            hs.append((i, ft, fb, bt_ap, bb))
            if w1_m is not None:
                for k2 in range(4):
                    k = h * 4 + k2
                    P.emit("dve", [fb, B["vecs"]], [bb],
                           lambda e, ft=ft, k2=k2, k=k, i=i: e.tensor_scalar(
                               out=cs_out[i][:, k2 * 128:(k2 + 1) * 128],
                               in0=ft[:, k2 * 128:(k2 + 1) * 128],
                               scalar1=vecs[:, V_LN1S + k:V_LN1S + k + 1],
                               scalar2=None, op0=ALU.mult), cost=100)
            elif i % 2 == 0:
                P.emit("dve", [fb], [bb],
                       lambda e, ft=ft, bt_ap=bt_ap: e.tensor_copy(
                           out=bt_ap, in_=ft[:, 0:512]))
            else:
                P.emit("act", [fb], [bb],
                       lambda e, ft=ft, bt_ap=bt_ap: e.activation(
                           out=bt_ap, in_=ft[:, 0:512], func=AF.Copy))
            dma("qstore", ch_sout[i], dst_ap[:, h * 512:(h + 1) * 512], bt_ap,
                [bb], [dst_buf], note="scast out", cost=700)
        if w1_m is not None:
            parts = [(hs[k // 4][1][:, (k % 4) * 128:(k % 4 + 1) * 128],
                      vecs[:, V_LN1B + k:V_LN1B + k + 1]) for k in range(NK)]
            outp, obuf = mmg(rotB, (4, 5, 6, 7), parts,
                             [hs[0][2], hs[1][2], B["vecs"]],
                             note="b1fold", m_rows=128, n_cols=1, cost=2000)
            P.emit("dve", [obuf, B["vecs"]], [Bb1[w1_m]],
                   lambda e, outp=outp, m=w1_m: e.scalar_tensor_tensor(
                       out=vecs[:, V_B1 + m:V_B1 + m + 1], in0=outp[:, 0:1],
                       scalar=1.0, in1=vecs[:, V_B1 + m:V_B1 + m + 1],
                       op0=ALU.mult, op1=ALU.add), cost=130)

    P.emit("dve", [B["gwr"]], [B["gwb"]],
           lambda e: e.tensor_copy(out=gwb[:], in_=gwr[:]))
    P.emit("dve", [B["cvec"]], [B["onesb"]],
           lambda e: e.tensor_copy(out=onesb[:], in_=cvec[:, C_ONE:C_ONE + 1]))

    poolA = {"slots": [0, 1, 2, 3, 4], "ctr": [0], "q": "qload"}
    poolB = {"slots": [5, 6, 7, 8, 9, 10, 11], "ctr": [0], "q": "qload"}
    pinned = set()

    def load_pair(pool, name, m0, n2=2, note="", pin=False):
        slots = pool["slots"]
        while True:
            i = slots[pool["ctr"][0] % len(slots)]
            pool["ctr"][0] += 1
            if i not in pinned:
                break
        src_ap = scr[name][m0:m0 + n2, :, :].rearrange("a p m -> p a m")
        dma(pool["q"], ch_slot[i], slab[i][:, 0:n2 * 1024].rearrange(
            "p (a m) -> p a m", a=n2), src_ap,
            Bscr[name][m0:m0 + n2], [Bslab[i]], note=note)
        if pin:
            pinned.add(i)
            return slab[i], Bslab[i], i
        return slab[i], Bslab[i]

    rotA = [0]
    rotB = [0]

    def mmg(rot, banks, parts, reads, m_rows=128, note="", n_cols=None, cost=None):
        bi = banks[rot[0] % len(banks)]
        rot[0] += 1
        bank, bbuf = pb[bi], Bbank[bi]
        out = bank[0:m_rows, 0:(t if n_cols is None else n_cols)]
        n = len(parts)

        def fn(e):
            last = None
            for i, (l, r) in enumerate(parts):
                last = e.matmul(out, l, r, start=(i == 0), stop=(i == n - 1))
            return last
        P.emit("pe", reads, [bbuf], fn, note=note,
               cost=(180 + 116 * len(parts)) if cost is None else cost)
        return out, bbuf

    for name, n in W_NS:
        for i2 in range(n):
            cast_slab(wslab[name][i2, :, :], scr[name][i2, :, :], Bscr[name][i2],
                      w1_m=(i2 if name == "w1p" else None))
    cast_slab(v2h, scr["v2"][0, :, :], Bscr["v2"][0])

    def mmg_at(bi, parts, reads, m_rows=128, note="", cost=None):
        bank, bbuf = pb[bi], Bbank[bi]
        out = bank[0:m_rows, 0:t]
        n = len(parts)

        def fn(e):
            last = None
            for i, (l, r) in enumerate(parts):
                last = e.matmul(out, l, r, start=(i == 0), stop=(i == n - 1))
            return last
        P.emit("pe", reads, [bbuf], fn, note=note,
               cost=(180 + 116 * len(parts)) if cost is None else cost)
        return out, bbuf

    def ck(buf_t, c):
        return buf_t[:, c * t:(c + 1) * t]

    def copy_fn(out_ap, in_ap):
        def fn(e):
            if hasattr(e, "tensor_copy"):
                return e.tensor_copy(out=out_ap, in_=in_ap)
            return e.copy(out=out_ap, in_=in_ap)
        return fn

    E = P.emit

    def ln_stats(src_t, src_b, cast_t, cast_bs, sq_t, sq_bs, banks, st, stp, bc, bcb,
                 eps_col, bc2=None):
        E("act", [src_b], list(sq_bs),
          lambda e: e.activation(out=sq_t[:, 0:CH], in_=src_t[:], func=AF.Square),
          cost=2000)
        mmg_at(banks[0], [(cvec[0:128, C_ONE:C_ONE + 1], ck(src_t, c))
                          for c in range(NK)],
               [src_b, B["cvec"]], m_rows=1, note="sum", cost=1900)
        E("dve", [Bbank[banks[0]]], [stp["mu"]],
          lambda e: e.tensor_scalar(out=st["mu"][:], in0=pb[banks[0]][0:1, 0:t],
                                    scalar1=1.0 / D, scalar2=None, op0=ALU.mult))
        mmg_at(banks[1], [(onesb[:, 0:1], ck(sq_t, c)) for c in range(NK)],
               list(sq_bs) + [B["onesb"]], m_rows=1, note="sumsq")
        E("dve", [Bbank[banks[1]]], [stp["ms"]],
          lambda e: e.tensor_scalar(out=st["ms"][:], in0=pb[banks[1]][0:1, 0:t],
                                    scalar1=1.0 / D, scalar2=None, op0=ALU.mult))
        E("dve", [stp["mu"]], [stp["var"]],
          lambda e: e.tensor_tensor(out=st["var"][:], in0=st["mu"][:], in1=st["mu"][:],
                                    op=ALU.mult))
        E("dve", [stp["ms"], stp["var"]], [stp["var"]],
          lambda e: e.tensor_tensor(out=st["var"][:], in0=st["ms"][:], in1=st["var"][:],
                                    op=ALU.subtract))
        eps_val = EPS_MOD if eps_col == C_EPS_MOD else EPS_IN
        x_ap = st["ms"]
        xi = x_ap.bitcast(I32)
        yi = st["rstd"].bitcast(I32)
        E("dve", [stp["var"]], [stp["ms"]],
          lambda e: e.tensor_scalar(out=x_ap[:], in0=st["var"][:], scalar1=eps_val,
                                    scalar2=None, op0=ALU.add), cost=120)
        E("dve", [stp["ms"]], [stp["rstd"]],
          lambda e: e.tensor_scalar(out=yi[:], in0=xi[:], scalar1=1, scalar2=None,
                                    op0=ALU.logical_shift_right), cost=120)
        E("dve", [stp["rstd"]], [stp["rstd"]],
          lambda e: e.tensor_scalar(out=yi[:], in0=yi[:], scalar1=-1,
                                    scalar2=RSQRT_MAGIC, op0=ALU.mult,
                                    op1=ALU.add), cost=120)
        E("dve", [stp["ms"]], [stp["ms"]],
          lambda e: e.tensor_scalar(out=x_ap[:], in0=x_ap[:], scalar1=0.5,
                                    scalar2=None, op0=ALU.mult), cost=120)
        for _ in range(2):
            E("dve", [stp["rstd"]], [stp["var"]],
              lambda e: e.tensor_tensor(out=st["var"][:], in0=st["rstd"][:],
                                        in1=st["rstd"][:], op=ALU.mult), cost=120)
            E("dve", [stp["ms"], stp["var"]], [stp["var"]],
              lambda e: e.tensor_tensor(out=st["var"][:], in0=x_ap[:],
                                        in1=st["var"][:], op=ALU.mult), cost=120)
            E("dve", [stp["var"]], [stp["var"]],
              lambda e: e.tensor_scalar(out=st["var"][:], in0=st["var"][:],
                                        scalar1=-1.0, scalar2=1.5, op0=ALU.mult,
                                        op1=ALU.add), cost=120)
            E("dve", [stp["rstd"], stp["var"]], [stp["rstd"]],
              lambda e: e.tensor_tensor(out=st["rstd"][:], in0=st["rstd"][:],
                                        in1=st["var"][:], op=ALU.mult), cost=120)
        E("dve", [stp["mu"], stp["rstd"]], [stp["nmr"]],
          lambda e: e.scalar_tensor_tensor(out=st["nmr"][:], in0=st["mu"][:],
                                           scalar=-1.0, in1=st["rstd"][:],
                                           op0=ALU.mult, op1=ALU.mult))
        bank0 = pb[banks[0]]
        P.emit("pe", [B["onesrow"], stp["rstd"], stp["nmr"]], [Bbank[banks[0]]],
               lambda e: e.matmul(bank0[0:128, 0:2 * t], onesrow[0:1, :],
                                  st["rn"][0:1, 0:2 * t], start=True, stop=True),
               note="bc", cost=700)
        E("act", [Bbank[banks[0]]], [bcb[0], bcb[1]],
          copy_fn(bc2[:, 0:2 * t], bank0[:, 0:2 * t]), cost=900,
          alts={"act": 900.0, "dve": 900.0})

    def ln_apply(src_t, src_b, dst_t, dst_b, bc, bcb, affine=None, eng="dve"):
        for c in range(NK):
            ec = "dve"
            op_cost = 450
            E(ec, [src_b, bcb[0]], [dst_b],
              lambda e, c=c: e.tensor_tensor(out=ck(dst_t, c), in0=ck(src_t, c),
                                             in1=bc[0][:, 0:t], op=ALU.mult),
              cost=op_cost, flex=True)
            E(ec, [dst_b, bcb[1]], [dst_b],
              lambda e, c=c: e.tensor_tensor(out=ck(dst_t, c), in0=ck(dst_t, c),
                                             in1=bc[1][:, 0:t], op=ALU.add),
              cost=op_cost, flex=True)
            if affine is not None:
                sc, bcv = affine
                E(ec, [dst_b, B["vecs"]], [dst_b],
                  lambda e, c=c: e.tensor_scalar(
                      out=ck(dst_t, c), in0=ck(dst_t, c),
                      scalar1=vecs[:, sc + c:sc + c + 1],
                      scalar2=vecs[:, bcv + c:bcv + c + 1],
                      op0=ALU.mult, op1=ALU.add), cost=op_cost, flex=True)

    def proj(wname, dst_t, dst_bs, nm, rhs_t, rhs_b, m_off=0, act_fn=AF.Copy):
        for m0 in range(0, nm, 2):
            svt, svb = load_pair(poolA, wname, m0 + m_off, note=f"{wname} m{m0}")
            for d in range(2):
                m = m0 + d
                parts = [(svt[:, d * 1024 + k * 128:d * 1024 + (k + 1) * 128],
                          ck(rhs_t, k)) for k in range(NK)]
                out, bbuf = mmg(rotA, (0, 1, 2, 3), parts, [svb, rhs_b],
                                note=f"{wname} m{m}")
                if act_fn == AF.Copy:
                    E("act", [bbuf], [dst_bs[m // 4]],
                      copy_fn(ck(dst_t, m), out), cost=480,
                      alts={"act": 480.0, "dve": 480.0})
                else:
                    E("act", [bbuf], [dst_bs[m // 4]],
                      lambda e, m=m, out=out, dst_t=dst_t, act_fn=act_fn: e.activation(
                          out=ck(dst_t, m), in_=out, func=act_fn),
                      table=("silu" if act_fn == AF.Silu else None), cost=480)

    def xload(i, b, j):
        par = i % 2
        t0 = j * t
        bxt, bxb = bx[par], B[f"bx{par}"]
        dma("qload", ch_x[par], bxt[:].rearrange("p (c t) -> p c t", c=NK),
            xh[b, :, t0:t0 + t].rearrange("(c p) t -> p c t", p=128),
            [], [bxb], note=f"x b{b} j{j}", cost=1500)

    def A1a(i, b, j):
        par = i % 2
        bxt, bxb = bx[par], B[f"bx{par}"]
        bzt, bzbf = bzbD[par], B[f"bzb{par}"]
        if j == 0:
            E("dve", [], Bcr, lambda e: e.memset(bcr[:], 0.0), cost=100)
            E("dve", [], Bci, lambda e: e.memset(bci[:], 0.0), cost=100)
        ln_stats(bxt, bxb, None, [],
                 bmix[par], [B[f"bmix{par}0"], B[f"bmix{par}1"]],
                 (3, 3), stA, stpA, bcA,
                 [B["bcA0"], B["bcA1"]], C_EPS_MOD, bc2=bcAt)
        for c in range(NK):
            hB = B[f"bt1{c // 4}"]
            E("dve", [bxb, B["bcA0"]], [hB],
              lambda e, c=c: e.tensor_tensor(out=ck(bt1, c), in0=ck(bxt, c),
                                             in1=bcA[0][:, 0:t], op=ALU.mult),
              cost=450, alts={"dve": 450.0, "gps": 1000.0})
            E("dve", [hB, B["bcA1"]], [bzbf],
              lambda e, c=c: e.tensor_tensor(out=ck(bzt, c), in0=ck(bt1, c),
                                             in1=bcA[1][:, 0:t], op=ALU.add),
              cost=450, alts={"dve": 450.0, "gps": 1000.0})

    def A1b(i):
        par = i % 2
        bzt, bzbf = bzbD[par], B[f"bzb{par}"]
        proj("wa", bar, [B["bar0"], B["bar1"]], NK, bzt, bzbf)
        proj("wa", bai, [B["bai0"], B["bai1"]], NK, bzt, bzbf, m_off=NK)
        proj("wk", bk, [B["bk0"], B["bk1"]], NK, bzt, bzbf)
        proj("wv", bv, [B["bv0"], B["bv1"]], NK, bzt, bzbf)

    Hw = CH // 2
    CD = 1350.0
    CA = 1150.0

    def chain_half(i, h):
        HSL = slice(h * Hw, (h + 1) * Hw)
        HS = lambda buf: buf[:, HSL]
        Bh = lambda nm: B[f"{nm}{h}"]
        Bk, Bv = B[f"bk{h}"], B[f"bv{h}"]
        bki_h = bar[:, HSL].bitcast(I32)
        E("dve", [Bk, Bv], [Bk],
          lambda e: e.tensor_tensor(out=HS(bk), in0=HS(bk), in1=HS(bv),
                                    op=ALU.mult), cost=700)
        E("act", [Bh("bar")], [Bh("bt1")],
          lambda e: e.activation(out=HS(bt1), in_=HS(bar), func=AF.Square), cost=CA)
        E("act", [Bh("bai")], [Bh("bt2")],
          lambda e: e.activation(out=HS(bt2), in_=HS(bai), func=AF.Square), cost=CA)
        E("dve", [Bh("bt1"), Bh("bt2")], [Bh("bt1")],
          lambda e: e.tensor_tensor(out=HS(bt1), in0=HS(bt1), in1=HS(bt2),
                                    op=ALU.add), cost=CD, flex=True)
        E("act", [Bh("bt1")], [Bh("bt2")],
          lambda e: e.activation(out=HS(bt2), in_=HS(bt1), func=AF.Ln),
          table="nlexp", cost=CA)
        E("act", [Bh("bt2"), B["cvec"]], [Bh("bt1")],
          lambda e: e.activation(out=HS(bt1), in_=HS(bt2), func=AF.Exp,
                                 scale=cvec[:, C_HALF:C_HALF + 1]),
          table="nlexp", cost=CA)
        E("act", [Bh("bar")], [Bh("bhr")],
          lambda e: e.activation(out=HS(bhr), in_=HS(bar), func=AF.Abs), cost=CA)
        E("act", [Bh("bar")], [Bh("bcos")],
          lambda e: e.activation(out=HS(bcos), in_=HS(bar), func=AF.Sign), cost=CA)
        E("act", [Bh("bai")], [Bh("bnsin")],
          lambda e: e.activation(out=HS(bnsin), in_=HS(bai), func=AF.Sign), cost=CA)
        E("dve", [Bh("bt1"), Bh("bhr")], [Bh("bt2")],
          lambda e: e.tensor_tensor(out=HS(bt2), in0=HS(bt1), in1=HS(bhr),
                                    op=ALU.add), cost=CD, flex=True)
        E("dve", [Bh("bt2")], [Bh("bt2")],
          lambda e: e.tensor_scalar(out=HS(bt2), in0=HS(bt2), scalar1=1e-30,
                                    scalar2=None, op0=ALU.max), cost=CD * 0.55)
        E("act", [Bh("bt2")], [Bh("bhr")],
          lambda e: e.activation(out=HS(bhr), in_=HS(bt2), func=AF.Ln),
          table="nlexp", cost=CA)
        E("act", [Bh("bhr"), B["cvec"]], [Bh("bt2")],
          lambda e: e.activation(out=HS(bt2), in_=HS(bhr), func=AF.Exp,
                                 scale=cvec[:, C_NEG1:C_NEG1 + 1]),
          table="nlexp", cost=CA)
        E("act", [Bh("bt1")], [Bh("bs")],
          lambda e: e.activation(out=HS(bs_), in_=HS(bt1), func=AF.Sigmoid),
          table="sigm", cost=CA)
        E("dve", [Bh("bai"), Bh("bt2")], [Bh("bt1")],
          lambda e: e.tensor_tensor(out=HS(bt1), in0=HS(bai), in1=HS(bt2),
                                    op=ALU.mult), cost=CD, flex=True)
        E("act", [Bh("bt1")], [Bh("bt2")],
          lambda e: e.activation(out=HS(bt2), in_=HS(bt1), func=AF.Arctan),
          table="sigm", cost=CA)
        E("dve", [Bh("bcos")], [Bh("bcos")],
          lambda e: e.tensor_scalar(out=HS(bcos), in0=HS(bcos), scalar1=-0.5,
                                    scalar2=0.5, op0=ALU.mult, op1=ALU.add),
          cost=CD * 0.55)
        E("dve", [Bh("bcos")], [Bh("bhr")],
          lambda e: e.tensor_scalar(out=HS(bhr), in0=HS(bcos), scalar1=-4.0,
                                    scalar2=2.0, op0=ALU.mult, op1=ALU.add),
          cost=CD * 0.55)
        E("dve", [Bh("bcos"), Bh("bnsin")], [Bh("bt1")],
          lambda e: e.tensor_tensor(out=HS(bt1), in0=HS(bcos), in1=HS(bnsin),
                                    op=ALU.mult), cost=CD, flex=True)
        E("dve", [Bh("bt2"), Bh("bhr")], [Bh("bt2")],
          lambda e: e.tensor_tensor(out=HS(bt2), in0=HS(bt2), in1=HS(bhr),
                                    op=ALU.mult), cost=CD, flex=True)
        E("dve", [Bh("bt1"), Bh("bt2")], [Bh("bai")],
          lambda e: e.scalar_tensor_tensor(out=HS(bai), in0=HS(bt1), scalar=PI,
                                           in1=HS(bt2), op0=ALU.mult,
                                           op1=ALU.add), cost=CD)
        for c in range(4 * h, 4 * h + 4):
            E("dve", [Bh("bai"), B["zer"]], [Bh("bai")],
              lambda e, c=c: e.tensor_tensor_scan(
                  out=ck(bai, c), data0=ck(bai, c), data1=zer[:, 0:t],
                  initial=zer[:, 0:1], op0=ALU.add, op1=ALU.add), cost=450)

        def red_sin(alpha_ap, alpha_bufs, dst_ap, dst_buf, neg):
            E("dve", alpha_bufs, [Bh("bar")],
              lambda e: e.tensor_scalar(out=bki_h, in0=alpha_ap, scalar1=INV_2PI,
                                        scalar2=None, op0=ALU.mult), cost=CD)
            E("dve", [Bh("bar")] + alpha_bufs, [Bh("bt1")],
              lambda e: e.scalar_tensor_tensor(out=HS(bt1), in0=bki_h,
                                               scalar=-TWO_PI, in1=alpha_ap,
                                               op0=ALU.mult, op1=ALU.add), cost=CD)
            if sim_safe_wrap:
                E("dve", [Bh("bt1")], [Bh("bt2")],
                  lambda e: e.tensor_scalar(out=HS(bt2), in0=HS(bt1), scalar1=-PI,
                                            scalar2=None, op0=ALU.is_lt), cost=CD)
                E("dve", [Bh("bt1")], [Bh("bhr")],
                  lambda e: e.tensor_scalar(out=HS(bhr), in0=HS(bt1), scalar1=PI,
                                            scalar2=None, op0=ALU.is_gt), cost=CD)
                E("dve", [Bh("bt2"), Bh("bhr")], [Bh("bt2")],
                  lambda e: e.tensor_tensor(out=HS(bt2), in0=HS(bt2), in1=HS(bhr),
                                            op=ALU.subtract), cost=CD)
                E("dve", [Bh("bt2"), Bh("bt1")], [Bh("bt1")],
                  lambda e: e.scalar_tensor_tensor(out=HS(bt1), in0=HS(bt2),
                                                   scalar=TWO_PI, in1=HS(bt1),
                                                   op0=ALU.mult, op1=ALU.add),
                  cost=CD)
            else:
                E("dve", [Bh("bt1")], [Bh("bt1")],
                  lambda e: e.tensor_scalar(out=HS(bt1), in0=HS(bt1), scalar1=-PI,
                                            scalar2=PI, op0=ALU.max, op1=ALU.min),
                  cost=CD * 0.55)
            if neg:
                E("act", [Bh("bt1"), B["cvec"]], [dst_buf],
                  lambda e: e.activation(out=dst_ap, in_=HS(bt1), func=AF.Sin,
                                         scale=cvec[:, C_NEG1:C_NEG1 + 1]),
                  table="silu", cost=CA)
            else:
                E("act", [Bh("bt1")], [dst_buf],
                  lambda e: e.activation(out=dst_ap, in_=HS(bt1), func=AF.Sin),
                  table="silu", cost=CA)

        red_sin(HS(bai), [Bh("bai")], HS(bnsin), Bh("bnsin"), neg=True)
        E("dve", [Bh("bai")], [Bh("bar")],
          lambda e: e.tensor_scalar(out=bki_h, in0=HS(bai), scalar1=INV_2PI,
                                    scalar2=0.25, op0=ALU.mult, op1=ALU.add),
          cost=CD)
        E("dve", [Bh("bar"), Bh("bai")], [Bh("bt1")],
          lambda e: e.scalar_tensor_tensor(out=HS(bt1), in0=bki_h,
                                           scalar=-TWO_PI, in1=HS(bai),
                                           op0=ALU.mult, op1=ALU.add), cost=CD)
        if sim_safe_wrap:
            E("dve", [Bh("bt1")], [Bh("bt2")],
              lambda e: e.tensor_scalar(out=HS(bt2), in0=HS(bt1),
                                        scalar1=-3 * PI / 2,
                                        scalar2=None, op0=ALU.is_lt), cost=CD)
            E("dve", [Bh("bt1")], [Bh("bhr")],
              lambda e: e.tensor_scalar(out=HS(bhr), in0=HS(bt1),
                                        scalar1=PI / 2,
                                        scalar2=None, op0=ALU.is_gt), cost=CD)
            E("dve", [Bh("bt2"), Bh("bhr")], [Bh("bt2")],
              lambda e: e.tensor_tensor(out=HS(bt2), in0=HS(bt2), in1=HS(bhr),
                                        op=ALU.subtract), cost=CD)
            E("dve", [Bh("bt2"), Bh("bt1")], [Bh("bt1")],
              lambda e: e.scalar_tensor_tensor(out=HS(bt1), in0=HS(bt2),
                                               scalar=TWO_PI, in1=HS(bt1),
                                               op0=ALU.mult, op1=ALU.add), cost=CD)
        else:
            E("dve", [Bh("bt1")], [Bh("bt1")],
              lambda e: e.tensor_scalar(out=HS(bt1), in0=HS(bt1),
                                        scalar1=-3 * PI / 2, scalar2=PI / 2,
                                        op0=ALU.max, op1=ALU.min), cost=CD * 0.55)
        E("act", [Bh("bt1"), B["cvec"]], [Bh("bcos")],
          lambda e: e.activation(out=HS(bcos), in_=HS(bt1), func=AF.Sin,
                                 bias=cvec[:, C_HALFPI:C_HALFPI + 1]),
          table="silu", cost=CA)
        E("dve", [Bk, Bh("bcos")], [Bh("bhr")],
          lambda e: e.tensor_tensor(out=HS(bhr), in0=HS(bk), in1=HS(bcos),
                                    op=ALU.mult), cost=700)
        E("dve", [Bk, Bh("bnsin")], [Bh("bhi")],
          lambda e: e.tensor_tensor(out=HS(bhi), in0=HS(bk), in1=HS(bnsin),
                                    op=ALU.mult), cost=700)
        for c in range(4 * h, 4 * h + 4):
            E("dve", [Bh("bs"), Bh("bhr"), Bcr[c]], [Bh("bhr")],
              lambda e, c=c: e.tensor_tensor_scan(
                  out=ck(bhr, c), data0=ck(bs_, c), data1=ck(bhr, c),
                  initial=bcr[:, c:c + 1], op0=ALU.mult, op1=ALU.add), cost=450)
            E("dve", [Bh("bs"), Bh("bhi"), Bci[c]], [Bh("bhi")],
              lambda e, c=c: e.tensor_tensor_scan(
                  out=ck(bhi, c), data0=ck(bs_, c), data1=ck(bhi, c),
                  initial=bci[:, c:c + 1], op0=ALU.mult, op1=ALU.add), cost=450)
        E("dve", [Bh("bhr"), Bh("bcos"), Bh("bs")], [Bh("bs")],
          lambda e: e.tensor_tensor(out=HS(bs_), in0=HS(bhr), in1=HS(bcos),
                                    op=ALU.mult), cost=700)
        E("dve", [Bh("bhi"), Bh("bnsin")], [Bk],
          lambda e: e.tensor_tensor(out=HS(bk), in0=HS(bhi), in1=HS(bnsin),
                                    op=ALU.mult), cost=700)
        E("dve", [Bh("bs"), Bk], [Bh("bs")],
          lambda e: e.tensor_tensor(out=HS(bs_), in0=HS(bs_), in1=HS(bk),
                                    op=ALU.add), cost=700)
        for c in range(4 * h, 4 * h + 4):
            L = c * t + t - 1
            s0 = h * Hw + 2 * (c - 4 * h)
            E("dve", [Bh("bs")], [Bcr[c]],
              lambda e, c=c, L=L: e.tensor_copy(out=bcr[:, c:c + 1],
                                                in_=bs_[:, L:L + 1]), cost=150)
            E("dve", [Bh("bhr"), Bh("bnsin")], [Bh("bt2")],
              lambda e, s0=s0, L=L: e.tensor_tensor(
                  out=bt2[:, s0:s0 + 1], in0=bhr[:, L:L + 1],
                  in1=bnsin[:, L:L + 1], op=ALU.mult), cost=150)
            E("dve", [Bh("bhi"), Bh("bcos")], [Bh("bt2")],
              lambda e, s0=s0, L=L: e.tensor_tensor(
                  out=bt2[:, s0 + 1:s0 + 2], in0=bhi[:, L:L + 1],
                  in1=bcos[:, L:L + 1], op=ALU.mult), cost=150)
            E("dve", [Bh("bt2")], [Bci[c]],
              lambda e, c=c, s0=s0: e.tensor_tensor(
                  out=bci[:, c:c + 1], in0=bt2[:, s0 + 1:s0 + 2],
                  in1=bt2[:, s0:s0 + 1], op=ALU.subtract), cost=150)

    def CH_(i):
        chain_half(i, 0)
        yield
        chain_half(i, 1)
        yield

    def A3a(i):
        proj("wq", bq, [B["bq0"], B["bq1"]], NK, bzbD[i % 2], B[f"bzb{i % 2}"])

    def A3b(i):
        par = i % 2
        proj("wg", bg, [B["bg0"], B["bg1"]], NK, bzbD[par], B[f"bzb{par}"],
             act_fn=AF.Silu)
        for h in (0, 1):
            HSL = slice(h * Hw, (h + 1) * Hw)
            HS = lambda buf, HSL=HSL: buf[:, HSL]
            Bh = lambda nm, h=h: B[f"{nm}{h}"]
            E("dve", [Bh("bq"), Bh("bs")], [Bh("bs")],
              lambda e, HS=HS: e.tensor_tensor(out=HS(bs_), in0=HS(bs_),
                                               in1=HS(bq), op=ALU.mult), cost=700)
            E("dve", [Bh("bs"), Bh("bg")], [B[f"bmix{par}{h}"]],
              lambda e, HS=HS, par=par: e.tensor_tensor(
                  out=HS(bmix[par]), in0=HS(bs_), in1=HS(bg), op=ALU.mult),
              cost=700)

    def B1(i, take=lambda n=1: None):
        par = i % 2
        bzt, bzbf = bzbD[par], B[f"bzb{par}"]
        for m0 in range(0, NK, 2):
            svt, svb = load_pair(poolB, "wo", m0, note=f"wo m{m0}")
            for d in range(2):
                m = m0 + d
                parts = [(svt[:, d * 1024 + k * 128:d * 1024 + (k + 1) * 128],
                          ck(bmix[par], k)) for k in range(NK)]
                out, bbuf = mmg(rotB, (4, 5, 6, 7), parts,
                                [svb, B[f"bmix{par}0"], B[f"bmix{par}1"]],
                                note=f"wo m{m}")
                E("dve", [bbuf, bzbf], [B["bw"]],
                  lambda e, m=m, out=out: e.tensor_tensor(
                      out=ck(bw, m), in0=out, in1=ck(bzt, m), op=ALU.add))
                take(2)
        ln_stats(bw, B["bw"], bwb, [B["bwb"]], bmix[par],
                 [B[f"bmix{par}0"], B[f"bmix{par}1"]],
                 (7, 7), stB, stpB, bcB,
                 [B["bcB0"], B["bcB1"]], C_EPS_IN, bc2=bcBt)
        for c in range(NK):
            E("dve", [B["bw"], B["bcB0"]], [B["bw"]],
              lambda e, c=c: e.tensor_tensor(out=ck(bw, c), in0=ck(bw, c),
                                             in1=bcB[0][:, 0:t], op=ALU.mult),
              cost=450, alts={"dve": 450.0, "gps": 1000.0})
            E("dve", [B["bw"], B["bcB1"]], [B["bwb"]],
              lambda e, c=c: e.tensor_tensor(out=ck(bwb, c), in0=ck(bw, c),
                                             in1=bcB[1][:, 0:t], op=ALU.add),
              cost=450, alts={"dve": 450.0, "gps": 1000.0})

    def B2(i, take=lambda n=1: None):
        for f0 in range(0, NF, 2):
            svt, svb = load_pair(poolB, "w1p", f0, note=f"w1 f{f0}")
            for d in range(2):
                f = f0 + d
                parts = [(svt[:, d * 1024 + k * 128:d * 1024 + (k + 1) * 128],
                          ck(bwb, k)) for k in range(NK)]
                out, bbuf = mmg(rotB, (4, 5, 6, 7), parts, [svb, B["bwb"]], note=f"w1 f{f}")
                E("act", [bbuf, B["vecs"], Bb1[f]], [Bh1[f // 8]],
                  lambda e, f=f, out=out: e.activation(
                      out=bh1[:, f * t:(f + 1) * t], in_=out, func=AF.Gelu_apprx_tanh,
                      bias=vecs[:, V_B1 + f:V_B1 + f + 1]), table="gelu", cost=550)
                take(2)

    def B3a(i, b, j):
        par = i % 2
        t0 = j * t
        bxt, bxb = bx[par], B[f"bx{par}"]
        for m in range(NK):
            w2s = [load_pair(poolB, "w2p", m * 4, note=f"w2 {m * 4}"),
                   load_pair(poolB, "w2p", m * 4 + 2, note=f"w2 {m * 4 + 2}")]
            parts = []
            for kb in range(NF):
                sl = w2s[kb // 16]
                d = (kb // 8) % 2
                parts.append((sl[0][:, d * 1024 + (kb % 8) * 128:
                                    d * 1024 + (kb % 8 + 1) * 128],
                              bh1[:, kb * t:(kb + 1) * t]))
            out, bbuf = mmg(rotB, (4, 5, 6, 7), parts,
                            [sv[1] for sv in w2s] + list(Bh1), note=f"w2 m{m}")
            E("dve", [bbuf, B["vecs"], B["bwb"]], [B["bw"]],
              lambda e, m=m, out=out: e.scalar_tensor_tensor(
                  out=ck(bw, m), in0=ck(bwb, m),
                  scalar=vecs[:, V_LN1S + m:V_LN1S + m + 1],
                  in1=out, op0=ALU.mult, op1=ALU.add))
            E("dve", [B["bw"], B["vecs"]], [B["bw"]],
              lambda e, m=m: e.tensor_scalar(
                  out=ck(bw, m), in0=ck(bw, m),
                  scalar1=vecs[:, V_B2C + m:V_B2C + m + 1], scalar2=None,
                  op0=ALU.add), cost=450)
        ln_stats(bw, B["bw"], bwb, [B["bwb"]], bmix[par],
                 [B[f"bmix{par}0"], B[f"bmix{par}1"]],
                 (7, 7), stB, stpB, bcB,
                 [B["bcB0"], B["bcB1"]], C_EPS_IN, bc2=bcBt)
        for c in range(NK):
            E("dve", [B["bw"], B["bcB0"]], [B["bw"]],
              lambda e, c=c: e.tensor_tensor(out=ck(bw, c), in0=ck(bw, c),
                                             in1=bcB[0][:, 0:t], op=ALU.mult),
              cost=450, alts={"dve": 450.0, "gps": 1000.0})
            E("dve", [B["bw"], B["bcB1"]], [B["bw"]],
              lambda e, c=c: e.tensor_tensor(out=ck(bw, c), in0=ck(bw, c),
                                             in1=bcB[1][:, 0:t], op=ALU.add),
              cost=450, alts={"dve": 450.0, "gps": 1000.0})
            E("dve", [B["bw"], B["vecs"]], [B["bw"]],
              lambda e, c=c: e.tensor_scalar(
                  out=ck(bw, c), in0=ck(bw, c),
                  scalar1=vecs[:, V_LN2S + c:V_LN2S + c + 1],
                  scalar2=vecs[:, V_LN2B + c:V_LN2B + c + 1],
                  op0=ALU.mult, op1=ALU.add), cost=450)

    def B3b(i, b, j):
        par = i % 2
        t0 = j * t
        bxt, bxb = bx[par], B[f"bx{par}"]
        E("dve", [bxb, B["bw"]], [bxb],
          lambda e: e.tensor_tensor(out=bxt[:, 0:Hw], in0=bxt[:, 0:Hw],
                                    in1=bw[:, 0:Hw], op=ALU.add), cost=1350,
          alts={"dve": 1350.0, "gps": 2900.0})
        E("dve", [bxb, B["bw"]], [bxb],
          lambda e: e.tensor_tensor(out=bxt[:, Hw:CH], in0=bxt[:, Hw:CH],
                                    in1=bw[:, Hw:CH], op=ALU.add), cost=1350,
          alts={"dve": 1350.0, "gps": 2900.0})
        ln_stats(bxt, bxb, bwb, [B["bwb"]], bmix[par],
                 [B[f"bmix{par}0"], B[f"bmix{par}1"]],
                 (7, 7), stB, stpB, bcB,
                 [B["bcB0"], B["bcB1"]], C_EPS_MOD, bc2=bcBt)
        for c in range(NK):
            E("dve", [bxb, B["bcB0"]], [B["bw"]],
              lambda e, c=c: e.tensor_tensor(out=ck(bw, c), in0=ck(bxt, c),
                                             in1=bcB[0][:, 0:t], op=ALU.mult),
              cost=450, alts={"dve": 450.0, "gps": 1000.0})
            E("dve", [B["bw"], B["bcB1"]], [B["bwb"]],
              lambda e, c=c: e.tensor_tensor(out=ck(bwb, c), in0=ck(bw, c),
                                             in1=bcB[1][:, 0:t], op=ALU.add),
              cost=450, alts={"dve": 450.0, "gps": 1000.0})
        mmg_at(7, [(gwb[:, k * 8:(k + 1) * 8], ck(bwb, k)) for k in range(NK)],
               [B["gwb"], B["bwb"]], m_rows=8, note="gate logits")
        E("act", [Bbank[7]], [B["sp"]],
          lambda e: e.activation(out=sp, in_=pb[7][0:8, 0:t], func=AF.Exp),
          table="nlexp", cost=400)
        mmg_at(7, [(cvec[0:8, C_ONE:C_ONE + 1], sp)],
               [B["sp"], B["cvec"]], m_rows=1, note="gate sum")
        E("dve", [Bbank[7]], [B["st_r"]],
          lambda e: e.reciprocal(out=st_r[:], in_=pb[7][0:1, 0:t]))
        E("dve", [B["st_r"]], [B["st_r"]],
          lambda e: e.tensor_scalar(out=st_r[:], in0=st_r[:], scalar1=1.0 / 16.0,
                                    scalar2=None, op0=ALU.mult))
        mmg_at(7, [(bsel[:], sp)], [B["bsel"], B["sp"]], m_rows=128, note="gate bc")
        E("act", [Bbank[7]], [B["sg128"]],
          lambda e: e.activation(out=sg128[:], in_=pb[7][:, 0:t], func=AF.Copy))
        uat, uab = load_pair(poolB, "uall", 0, n2=1, note="uall")
        out_t, bbuf_t = mmg(rotB, (4, 5, 6, 7),
                            [(uat[:, k * 128:(k + 1) * 128], ck(bwb, k))
                             for k in range(NK)], [uab, B["bwb"]], note="t lora")
        out_r, bbuf_r = mmg(rotB, (4, 5, 6, 7),
                            [(onesrow[0:1, :], st_r[0:1, 0:t])],
                            [B["onesrow"], B["st_r"]], note="r bc")
        E("dve", [bbuf_t, B["sg128"]], [B["stgb"]],
          lambda e: e.tensor_tensor(out=stgb[:], in0=out_t, in1=sg128[:],
                                    op=ALU.mult))
        E("dve", [B["stgb"], bbuf_r], [B["stgb"]],
          lambda e: e.tensor_tensor(out=stgb[:], in0=stgb[:], in1=out_r,
                                    op=ALU.mult))
        v2t, v2b, v2slot = load_pair(poolB, "v2", 0, n2=1, note="v2", pin=True)
        for m0 in range(0, NK, 2):
            svt, svb = load_pair(poolB, "fw", m0, note=f"fw m{m0}")
            for d in range(2):
                m = m0 + d
                parts = [(svt[:, d * 1024 + k * 128:d * 1024 + (k + 1) * 128],
                          ck(bwb, k)) for k in range(NK)]
                parts.append((v2t[:, m * 128:(m + 1) * 128], stgb[:]))
                out, bbuf = mmg(rotB, (4, 5, 6, 7), parts, [svb, B["bwb"], v2b, B["stgb"]],
                                note=f"ffn2 m{m}")
                sti = m % 2
                E("dve", [bbuf, bxb], [B[f"bst{sti}"]],
                  lambda e, m=m, out=out, sti=sti: e.tensor_tensor(
                      out=bst[sti][:], in0=out, in1=ck(bxt, m), op=ALU.add))
                dma("qstore", ch_out[sti],
                    oh[b, m * 128:(m + 1) * 128, t0:t0 + t], bst[sti][:],
                    [B[f"bst{sti}"]], [Buf("_out")], note=f"out b{b} j{j} m{m}")
        pinned.discard(v2slot)

    with es:
        for i in range(total + 1):
            if i < total:
                b, j = divmod(i, nt)
                xload(i, b, j)
                A1a(i, b, j)
                A1b(i)
                gen = CH_(i)
            else:
                gen = iter(())

            def take(n=1, gen=gen):
                for _ in range(n):
                    if next(gen, "_done") == "_done":
                        return
            if i > 0:
                B1(i - 1, take)
                B2(i - 1, take)
            take(10 ** 6)
            if i > 0:
                bb, jb = divmod(i - 1, nt)
                B3a(i - 1, bb, jb)
            if i < total:
                A3a(i)
            if i > 0:
                B3b(i - 1, bb, jb)
            if i < total:
                A3b(i)
        est = P.schedule_and_replay()
        import os
        if os.environ.get("K_PRINT_EST"):
            print(f"[sched] estimated makespan: {est / 1e6:.3f} ms")
        for ch in ch_out:
            if ch.cnt:
                P.engs["qstore"].raw.wait_ge(ch.sem, ch.cnt)
    return nc



def kernel(**inputs):
    x = np.asarray(inputs["L0_attn_x"], np.float32)
    w = host_pack_weights(inputs)
    nc = build_core()
    in_maps = []
    for ci in range(N_CORES):
        xc = np.ascontiguousarray(x[ci * NB:(ci + 1) * NB].transpose(0, 2, 1))
        m = dict(w)
        m["x"] = xc
        in_maps.append(m)
    res = run_bass_kernel_spmd(nc, in_maps, core_ids=list(range(N_CORES)))
    outs = [res.results[i]["out"] for i in range(N_CORES)]
    full = np.concatenate(outs, axis=0)
    return np.ascontiguousarray(full.transpose(0, 2, 1)).astype(np.float32)


# version 42
# speedup vs baseline: 1.1795x; 1.0136x over previous
import numpy as np
from contextlib import ExitStack

import concourse.bass as bass
import concourse.mybir as mybir
from concourse.bass_utils import run_bass_kernel_spmd

F32 = mybir.dt.float32
BF16 = mybir.dt.bfloat16
I32 = mybir.dt.int32
AF = mybir.ActivationFunctionType
ALU = mybir.AluOpType

D = 1024
NK = 8
DF = 4096
NF = 32
PI = float(np.pi)
TWO_PI = 2.0 * PI
INV_2PI = 1.0 / TWO_PI
EPS_MOD = 1e-5
EPS_IN = 1e-6

N_CORES = 8
NB = 4
S = 2048
T = 256
NSLOT = 12

C_EPS_MOD, C_EPS_IN, C_HALFPI, C_NEG1, C_ONE, C_NEGH, C_HALF = 0, 1, 2, 3, 4, 5, 6
V_LN1S, V_LN1B, V_LN2S, V_LN2B, V_B2, V_B1 = 0, 8, 16, 24, 32, 40
V_B2C = 72


class Buf:
    __slots__ = ("name", "writer", "readers")

    def __init__(self, name):
        self.name = name
        self.writer = None
        self.readers = []


class Chan:
    __slots__ = ("sem", "step", "cnt", "last_node")

    def __init__(self, sem, step):
        self.sem = sem
        self.step = step
        self.cnt = 0
        self.last_node = None


class Eng:
    __slots__ = ("raw", "name", "chan", "sat")

    def __init__(self, raw, name, chan):
        self.raw = raw
        self.name = name
        self.chan = chan
        self.sat = {}


class Node:
    __slots__ = ("nid", "ename", "fn", "reads", "writes", "chan", "note",
                 "cost", "table", "deps", "prio", "finish", "alts")

    def __init__(self, nid, ename, fn, reads, writes, chan, note, cost, table):
        self.nid = nid
        self.ename = ename
        self.alts = None
        self.fn = fn
        self.reads = reads
        self.writes = writes
        self.chan = chan
        self.note = note
        self.cost = cost
        self.table = table
        self.deps = set()
        self.prio = 0.0
        self.finish = 0.0


ACT_TABLE_NS = 5000.0
RSQRT_MAGIC = 0x5f3759df


class Prog:

    def __init__(self, loose_sync=False):
        self.engs = {}
        self.nodes = []
        self.capture = True
        self.loose_sync = loose_sync

    def add_eng(self, name, raw, chan):
        self.engs[name] = Eng(raw, name, chan)

    def emit(self, ename, reads, writes, fn, chan=None, note="", cost=None,
             table=None, flex=False, alts=None):
        if not self.capture:
            return self._emit_real(ename, reads, writes, fn, chan, note)
        if cost is None:
            cost = {"pe": 1000.0, "dve": 450.0, "act": 450.0, "gps": 5000.0}.get(
                ename, 2500.0)
        n = Node(len(self.nodes), ename, fn, list(reads), list(writes), chan,
                 note, float(cost), table)
        import os as _os
        if not _os.environ.get("K_FLEX"):
            flex = False
        if alts is not None:
            n.alts = dict(alts)
        elif flex == "copy3":
            n.alts = {"dve": float(cost), "gps": float(cost) * 2.0,
                      "act": float(cost) * 1.4}
        elif flex:
            n.alts = {"dve": float(cost), "gps": float(cost) * 2.0}
        if chan is not None and getattr(chan, "last_node", None) is not None:
            n.deps.add(chan.last_node)
        for b in reads:
            if b.writer is not None:
                n.deps.add(b.writer)
        for b in writes:
            if b.writer is not None:
                n.deps.add(b.writer)
            for r in b.readers:
                n.deps.add(r)
        n.deps.discard(n.nid)
        for b in writes:
            b.writer = n.nid
            b.readers = []
        for b in reads:
            b.readers.append(n.nid)
        if chan is not None:
            chan.last_node = n.nid
        self.nodes.append(n)
        return n

    def schedule_and_replay(self):
        nodes = self.nodes
        nn = len(nodes)
        succ = [[] for _ in range(nn)]
        indeg = [0] * nn
        for n in nodes:
            for d in n.deps:
                succ[d].append(n.nid)
                indeg[n.nid] += 1
        import heapq
        import os
        simkey = lambda e: "gps" if e == "qstore" else e
        keys = {simkey(e) for e in self.engs}
        rev = []
        rem = [len(succ[i]) for i in range(nn)]
        stack = [i for i in range(nn) if rem[i] == 0]
        while stack:
            i = stack.pop()
            rev.append(i)
            for p in nodes[i].deps:
                rem[p] -= 1
                if rem[p] == 0:
                    stack.append(p)

        def run_pass(wc, costs):
            prio = [0.0] * nn
            for i in rev:
                best = 0.0
                for s in succ[i]:
                    if prio[s] > best:
                        best = prio[s]
                prio[i] = costs[i] + best
            eng_time = {e: 0.0 for e in keys}
            act_tab = None
            ready = {e: [] for e in keys}
            dready = [0.0] * nn
            ind = indeg[:]
            scheduled = [False] * nn
            assign = [None] * nn

            def push(i):
                n3 = nodes[i]
                if n3.alts:
                    for e3 in n3.alts:
                        heapq.heappush(ready[e3], (-prio[i], i))
                else:
                    heapq.heappush(ready[simkey(n3.ename)], (-prio[i], i))

            for i in range(nn):
                if ind[i] == 0:
                    push(i)
            sched = []
            nsched = 0
            while nsched < nn:
                best = None
                for e, rl in ready.items():
                    while rl and scheduled[rl[0][1]]:
                        heapq.heappop(rl)
                    if not rl:
                        continue
                    p, i = rl[0]
                    start = max(eng_time[e], dready[i])
                    n3 = nodes[i]
                    ecost = n3.alts.get(e, costs[i]) if n3.alts else costs[i]
                    key = (start + wc * ecost, p)
                    if best is None or key < best[0]:
                        best = (key, e)
                key, e = best
                _, i = heapq.heappop(ready[e])
                n = nodes[i]
                scheduled[i] = True
                if n.alts:
                    assign[i] = e
                    cost = n.alts.get(e, costs[i])
                else:
                    assign[i] = n.ename
                    cost = costs[i]
                start = max(eng_time[e], dready[i])
                if e == "act" and n.table is not None and act_tab != n.table:
                    cost += ACT_TABLE_NS
                    act_tab = n.table
                fin = start + cost
                eng_time[e] = fin
                sched.append(i)
                nsched += 1
                for s in succ[i]:
                    if dready[s] < fin:
                        dready[s] = fin
                    ind[s] -= 1
                    if ind[s] == 0:
                        push(s)
            return sched, assign

        def evaluate(sched, assign):
            eng_time = {e: 0.0 for e in keys}
            act_tab = None
            dready = [0.0] * nn
            fin = [0.0] * nn
            for i in sched:
                n = nodes[i]
                e = simkey(assign[i])
                cost = n.alts.get(e, n.cost) if n.alts else n.cost
                start = max(eng_time[e], dready[i])
                if e == "act" and n.table is not None and act_tab != n.table:
                    cost += ACT_TABLE_NS
                    act_tab = n.table
                fin[i] = start + cost
                eng_time[e] = fin[i]
                for s in succ[i]:
                    if dready[s] < fin[i]:
                        dready[s] = fin[i]
            return max(fin), fin

        base_costs = [n.cost for n in nodes]
        trials = [(0.58, base_costs)]
        n_extra = int(os.environ.get("K_SCHED_TRIALS", "0"))
        rngs = np.random.default_rng(1234)
        for tix in range(n_extra):
            wc = (0.35, 0.5, 0.65)[tix % 3]
            amp = 0.25 if tix % 2 else 0.12
            jit = rngs.uniform(1.0 - amp, 1.0 + amp, nn)
            trials.append((wc, [base_costs[i] * jit[i] for i in range(nn)]))
        best_pick = None
        for wc, costs in trials:
            sched, assign = run_pass(wc, costs)
            mk, fin = evaluate(sched, assign)
            if os.environ.get("K_PRINT_TRIALS"):
                print("[sched] trial wc=%s mk=%.3f" % (wc, mk / 1e6))
            if best_pick is None or mk < best_pick[0]:
                best_pick = (mk, sched, assign, fin, wc)
        mk_best, sched, assign, fin, wc_best = best_pick
        for i in range(nn):
            nodes[i].finish = fin[i]
            if nodes[i].alts:
                nodes[i].ename = assign[i] if assign[i] != "qstore" else "gps"
        if os.environ.get("K_PRINT_EST"):
            print("[sched] portfolio best wc=%s makespan %.3f ms"
                  % (wc_best, mk_best / 1e6))
        if os.environ.get("K_GAPS"):
            from collections import Counter
            gape = os.environ.get("K_GAPS")
            seq = [i for i in sched if simkey(nodes[i].ename) == gape]
            gap_c = Counter()
            gap_t = Counter()
            prev_end = 0.0
            for i in seq:
                n = nodes[i]
                stt_ = fin[i] - (n.alts.get(simkey(n.ename), n.cost)
                                 if n.alts else n.cost)
                if stt_ > prev_end + 500.0:
                    dmax, dn = -1.0, None
                    for d in n.deps:
                        if fin[d] > dmax:
                            dmax, dn = fin[d], d
                    if dn is not None:
                        k = nodes[dn]
                        nm = (k.note or "L%d" % k.fn.__code__.co_firstlineno)
                        me = (n.note or "L%d" % n.fn.__code__.co_firstlineno)
                        gap_c[(me.split(" ")[0], "<-", simkey(k.ename),
                               nm.split(" ")[0])] += 1
                        gap_t[(me.split(" ")[0], "<-", simkey(k.ename),
                               nm.split(" ")[0])] += stt_ - prev_end
                prev_end = max(prev_end, fin[i])
            tot = sum(gap_t.values())
            if os.environ.get("K_GAP1"):
                want = os.environ.get("K_GAP1")
                prev_end2 = 0.0
                shown = 0
                for i in seq:
                    n = nodes[i]
                    cst = n.alts.get(simkey(n.ename), n.cost) if n.alts else n.cost
                    stt_ = fin[i] - cst
                    nm = (n.note or "L%d" % n.fn.__code__.co_firstlineno)
                    if (stt_ > prev_end2 + 3000.0 and nm.startswith(want)
                            and shown < 2 and fin[i] > 1.5e6):
                        shown += 1
                        print("[gap1] %s start=%.1fus gap=%.2fus deps:"
                              % (nm, stt_ / 1e3, (stt_ - prev_end2) / 1e3))
                        dl = sorted(n.deps, key=lambda d: -fin[d])[:6]
                        for d in dl:
                            k = nodes[d]
                            dn = (k.note or "L%d" % k.fn.__code__.co_firstlineno)
                            dc = k.alts.get(simkey(k.ename), k.cost) if k.alts else k.cost
                            print("    dep fin=%.1fus start=%.1fus %s %s"
                                  % (fin[d] / 1e3, (fin[d] - dc) / 1e3,
                                     simkey(k.ename), dn))
                    prev_end2 = max(prev_end2, fin[i])
            print("[sched] %s gaps total %.3f ms; top releasers:" % (gape, tot / 1e6))
            for k2, v in gap_t.most_common(18):
                print("    %.3f ms n=%d  %s" % (v / 1e6, gap_c[k2], k2))
        if os.environ.get("K_PRINT_EST"):
            busy = {}
            for n2 in nodes:
                k = simkey(n2.ename)
                busy[k] = busy.get(k, 0.0) + n2.cost
            mk = max(nd.finish for nd in nodes)
            print("[sched] busy:", {k: round(v / 1e6, 2) for k, v in busy.items()},
                  "makespan", round(mk / 1e6, 2))
            eng_prev = {}
            prev_on_eng = {}
            start_t = {}
            for idx2 in sched:
                n2 = nodes[idx2]
                k = simkey(n2.ename)
                eng_prev[idx2] = prev_on_eng.get(k)
                prev_on_eng[k] = idx2
            cur = max(range(nn), key=lambda i2: nodes[i2].finish)
            from collections import Counter
            busy_c = Counter()
            stall_c = Counter()
            hops = 0
            while cur is not None and hops < 20000:
                hops += 1
                n2 = nodes[cur]
                st2 = n2.finish - n2.cost
                ep = eng_prev.get(cur)
                dmax, dnode = -1.0, None
                for d in n2.deps:
                    if nodes[d].finish > dmax:
                        dmax, dnode = nodes[d].finish, d
                epf = nodes[ep].finish if ep is not None else -1.0
                key2 = (simkey(n2.ename), (n2.note or "?").split(" ")[0])
                busy_c[key2] += n2.cost
                if epf >= dmax and ep is not None:
                    cur = ep
                elif dnode is not None:
                    if dmax < st2 - 1.0:
                        stall_c[("idle-before", key2[0])] += st2 - dmax
                    cur = dnode
                else:
                    break
            print("[sched] walk busy by (eng, op):")
            for k2, v in busy_c.most_common(14):
                print("   ", k2, round(v / 1e6, 2), "ms")
            print("[sched] stalls:", {k2: round(v / 1e6, 2) for k2, v in stall_c.most_common(8)})
        for n in nodes:
            for b in n.reads + n.writes:
                b.writer = None
                b.readers = []
            if n.chan is not None:
                n.chan.last_node = None
        self.capture = False
        for i in sched:
            n = nodes[i]
            self._emit_real(n.ename, n.reads, n.writes, n.fn, n.chan, n.note)
        return max(nd.finish for nd in nodes)

    def _emit_real(self, ename, reads, writes, fn, chan=None, note=""):
        eng = self.engs[ename]
        own = chan is None
        ch = eng.chan if own else chan
        deps = {}

        def add(mark):
            if mark is None:
                return
            sem, val, owner = mark
            k = id(sem)
            if k not in deps or deps[k][1] < val:
                deps[k] = mark

        if not own and ch.cnt > 0:
            add((ch.sem, ch.cnt, None))
        for b in reads:
            add(b.writer)
        for b in writes:
            add(b.writer)
            for m in b.readers:
                add(m)
        for sem, val, _owner in deps.values():
            if self.loose_sync and _owner is eng:
                continue
            cur = eng.sat.get(id(sem), 0)
            if val > cur:
                eng.raw.wait_ge(sem, val)
                eng.sat[id(sem)] = val
        last = fn(eng.raw)
        ch.cnt += ch.step
        last.then_inc(ch.sem, ch.step)
        mark = (ch.sem, ch.cnt, eng if own else None)
        for b in writes:
            b.writer = mark
            b.readers = []
        for b in reads:
            b.readers.append(mark)
        return mark



def _kslabs(w_eff):
    M = w_eff.shape[1]
    return np.ascontiguousarray(w_eff.reshape(NK, 128, M))


def _pack_resident(w_eff):
    M = w_eff.shape[1]
    return np.ascontiguousarray(
        w_eff.reshape(NK, 128, M).transpose(1, 0, 2).reshape(128, NK * M))


def _col(v):
    return np.ascontiguousarray(v.reshape(8, 128).T)


def _dmaj(w_eff, nm):
    return np.ascontiguousarray(
        w_eff.reshape(NK, 128, nm, 128).transpose(2, 1, 0, 3).reshape(nm, 128, 1024))


def host_pack_weights(I, t_tile=T):
    f32 = lambda a: np.ascontiguousarray(np.asarray(a, np.float32))
    w = {}
    w["wq"] = _dmaj(f32(I["L0_attn_Wq"]), 8)
    w["wk"] = _dmaj(f32(I["L0_attn_Wk"]), 8)
    w["wv"] = _dmaj(f32(I["L0_attn_Wv"]), 8)
    w["wg"] = _dmaj(f32(I["L0_attn_Wg"]), 8)
    w["wo"] = _dmaj(f32(I["L0_attn_Wo"]), 8)
    w["wa"] = _dmaj(f32(I["L0_attn_Wa"]), 16)
    w["w1p"] = _dmaj(f32(I["L0_attn_W1"]), NF)
    w2 = f32(I["L0_attn_W2"])
    w["w2p"] = np.ascontiguousarray(
        w2.reshape(NF, 128, NK, 128).transpose(2, 0, 1, 3)
        .reshape(NK, 4, NK, 128, 128).transpose(0, 1, 3, 2, 4)
        .reshape(NF, 128, 1024))
    w["fw"] = _dmaj(f32(I["L0_ffn_weight"]).T, 8)
    w["uall"] = _dmaj(f32(I["L0_ffn_lora_U"]).reshape(128, 1024).T, 1)
    w["v2"] = np.ascontiguousarray(f32(I["L0_ffn_lora_V"]).transpose(0, 2, 1).reshape(128, 1024))
    w["gw"] = _pack_resident(f32(I["L0_ffn_gate_w"]).T)
    vecs = np.zeros((128, 80), np.float32)
    vecs[:, V_LN1S:V_LN1S + 8] = _col(f32(I["L0_attn_ln1_scale"]))
    vecs[:, V_LN1B:V_LN1B + 8] = _col(f32(I["L0_attn_ln1_bias"]))
    vecs[:, V_LN2S:V_LN2S + 8] = _col(f32(I["L0_attn_ln2_scale"]))
    vecs[:, V_LN2B:V_LN2B + 8] = _col(f32(I["L0_attn_ln2_bias"]))
    vecs[:, V_B2:V_B2 + 8] = _col(f32(I["L0_attn_b2"]))
    vecs[:, V_B2C:V_B2C + 8] = _col(f32(I["L0_attn_b2"]))
    vecs[:, V_B1:V_B1 + 32] = np.ascontiguousarray(f32(I["L0_attn_b1"]).reshape(32, 128).T)
    w["vecs"] = vecs
    cvec = np.zeros((128, 8), np.float32)
    cvec[:, C_EPS_MOD] = EPS_MOD
    cvec[:, C_EPS_IN] = EPS_IN
    cvec[:, C_HALFPI] = PI / 2
    cvec[:, C_NEG1] = -1.0
    cvec[:, C_ONE] = 1.0
    cvec[:, C_NEGH] = -0.5
    cvec[:, C_HALF] = 0.5
    w["cvec"] = cvec
    w["onesrow"] = np.ones((1, 128), np.float32)
    bsel = np.zeros((8, 128), np.float32)
    for e in range(8):
        bsel[e, 16 * e:16 * (e + 1)] = 1.0
    w["bsel"] = bsel
    import ml_dtypes
    w["zer"] = np.zeros((128, t_tile), ml_dtypes.bfloat16)
    return w




def build_core(nb=NB, s=S, t=T, sim_safe_wrap=False, loose_sync=True):
    CH = NK * t
    nt = s // t
    total = nb * nt
    nc = bass.Bass("TRN2", target_bir_lowering=False)

    xh = nc.dram_tensor("x", [nb, D, s], F32, kind="ExternalInput").ap()
    oh = nc.dram_tensor("out", [nb, D, s], F32, kind="ExternalOutput").ap()
    wslab = {}
    for name, n in [("wq", 8), ("wk", 8), ("wv", 8), ("wg", 8), ("wo", 8),
                    ("wa", 16), ("w1p", 32), ("w2p", 32), ("fw", 8)]:
        wslab[name] = nc.dram_tensor(name, [n, 128, 1024], F32, kind="ExternalInput").ap()
    wslab["uall"] = nc.dram_tensor("uall", [1, 128, 1024], F32, kind="ExternalInput").ap()
    v2h = nc.dram_tensor("v2", [128, 1024], F32, kind="ExternalInput").ap()
    W_NS = [("wa", 16), ("wk", 8), ("wv", 8), ("wq", 8), ("wg", 8), ("wo", 8),
            ("w1p", 32), ("w2p", 32), ("fw", 8), ("uall", 1)]
    scr = {name: nc.dram_tensor(f"scr_{name}", [n, 128, 1024], BF16, kind="Internal").ap()
           for name, n in W_NS}
    scr["v2"] = nc.dram_tensor("scr_v2", [1, 128, 1024], BF16, kind="Internal").ap()
    gwh = nc.dram_tensor("gw", [128, 64], F32, kind="ExternalInput").ap()
    vecsh = nc.dram_tensor("vecs", [128, 80], F32, kind="ExternalInput").ap()
    cvech = nc.dram_tensor("cvec", [128, 8], F32, kind="ExternalInput").ap()
    onesrowh = nc.dram_tensor("onesrow", [1, 128], F32, kind="ExternalInput").ap()
    bselh = nc.dram_tensor("bsel", [8, 128], F32, kind="ExternalInput").ap()
    zerh = nc.dram_tensor("zer", [128, t], BF16, kind="ExternalInput").ap()

    es = ExitStack()
    sb = lambda name, shape, dt=F32: es.enter_context(nc.sbuf_tensor(name, shape, dt))

    bx = [sb("bx0", [128, CH]), sb("bx1", [128, CH])]
    bzbD = [sb("bzb0", [128, CH], BF16), sb("bzb1", [128, CH], BF16)]
    bk = sb("bk", [128, CH], BF16); bv = sb("bv", [128, CH], BF16)
    cs_in = [sb("cs_in0", [128, 1024]), sb("cs_in1", [128, 1024])]
    cs_out = [sb("cs_out0", [128, 1024], BF16), sb("cs_out1", [128, 1024], BF16)]
    bar = sb("bar", [128, CH]); bai = sb("bai", [128, CH])
    bq = sb("bq", [128, CH], BF16); bg = sb("bg", [128, CH], BF16)
    bs_ = sb("bs", [128, CH], BF16)
    bcos = sb("bcos", [128, CH], BF16); bnsin = sb("bnsin", [128, CH], BF16)
    bhr = sb("bhr", [128, CH], BF16); bhi = sb("bhi", [128, CH], BF16)
    bt1 = sb("bt1", [128, CH]); bt2 = sb("bt2", [128, CH])
    bmix = [sb("bmix0", [128, CH], BF16), sb("bmix1", [128, CH], BF16)]
    bki = bar[:, 0:CH].bitcast(I32)
    bw = sb("bw", [128, CH])
    bwb = sb("bwb", [128, CH], BF16)
    bh1 = sb("bh1", [128, NF * t], BF16)
    bst = [sb("bst0", [128, t]), sb("bst1", [128, t])]
    sp = bst[0][0:8, :]
    bcAt = sb("bcAt", [128, 2 * t])
    bcBt = sb("bcBt", [128, 2 * t])
    bcA = [bcAt[:, 0:t], bcAt[:, t:2 * t]]
    bcB = [bcBt[:, 0:t], bcBt[:, t:2 * t]]
    stA = {nm: sb(f"stA_{nm}", [1, t]) for nm in ["mu", "ms", "var"]}
    stB = {nm: sb(f"stB_{nm}", [1, t]) for nm in ["mu", "ms", "var"]}
    stA_rn = sb("stA_rn", [1, 2 * t])
    stB_rn = sb("stB_rn", [1, 2 * t])
    for st_, rn_ in ((stA, stA_rn), (stB, stB_rn)):
        st_["rn"] = rn_
        st_["rstd"] = rn_[:, 0:t]
        st_["nmr"] = rn_[:, t:2 * t]
    st_r = stB["mu"]
    sg128 = sb("sg128", [128, t], BF16)
    stgb = sb("stgb", [128, t], BF16)
    slab = [sb(f"slab{i}", [128, 2048], BF16) for i in range(12)]
    gwr = sb("gwr", [128, 64]); gwb = sb("gwb", [128, 64], BF16)
    onesb = sb("onesb", [128, 1], BF16)
    vecs = sb("vecs_s", [128, 80]); cvec = sb("cvec_s", [128, 8])
    onesrow = sb("onesrow_s", [1, 128]); bsel = sb("bsel_s", [8, 128])
    zer = sb("zer_s", [128, t], BF16)
    bcr = sb("bcr", [128, 8]); bci = sb("bci", [128, 8])
    pb = [es.enter_context(nc.psum_tensor(f"pb{i}", [128, 512], F32)) for i in range(8)]

    sem = lambda name: es.enter_context(nc.semaphore(name))
    P = Prog(loose_sync=loose_sync)
    P.add_eng("pe", nc.tensor, Chan(sem("s_pe"), 1))
    P.add_eng("act", nc.scalar, Chan(sem("s_act"), 1))
    P.add_eng("dve", nc.vector, Chan(sem("s_dve"), 1))
    P.add_eng("gps", nc.gpsimd, Chan(sem("s_gps"), 1))
    P.add_eng("qload", nc.sync, None)
    P.add_eng("qstore", nc.gpsimd, None)
    ch_const = Chan(sem("c_const"), 16)
    ch_x = [Chan(sem(f"c_x{i}"), 16) for i in range(2)]
    ch_slot = [Chan(sem(f"c_sl{i}"), 16) for i in range(12)]
    ch_out = [Chan(sem(f"c_out{i}"), 16) for i in range(2)]

    B = {}
    for nm in ["bx0", "bx1", "bzb0", "bzb1", "bw", "bwb",
               "bst0", "bst1", "bcA0", "bcA1", "bcB0", "bcB1", "st_r", "sp",
               "sg128", "stgb", "gwr", "gwb", "onesb", "vecs", "cvec",
               "onesrow", "bsel", "zer"]:
        B[nm] = Buf(nm)
    Bh1 = [Buf(f"bh1_{q}") for q in range(4)]
    for nm in ["bk", "bv", "bar", "bai", "bq", "bg", "bs",
               "bcos", "bnsin", "bhr", "bhi", "bt1", "bt2", "bmix0", "bmix1"]:
        B[nm + "0"] = Buf(nm + "0")
        B[nm + "1"] = Buf(nm + "1")
    for nm in ["mu", "ms", "var", "rstd", "nmr"]:
        B[f"stA_{nm}"] = Buf(f"stA_{nm}")
        B[f"stB_{nm}"] = Buf(f"stB_{nm}")
    B["st_r"] = B["stB_mu"]
    stpA = {k: B[f"stA_{k}"] for k in ["mu", "ms", "var", "rstd", "nmr"]}
    stpB = {k: B[f"stB_{k}"] for k in ["mu", "ms", "var", "rstd", "nmr"]}
    B["sp"] = B["bst0"]
    Bslab = [Buf(f"slab{i}") for i in range(12)]
    Bb1 = [Buf(f"b1_{m}") for m in range(NF)]
    Bbank = [Buf(f"pb{i}") for i in range(8)]
    Bcr = [Buf(f"bcr{c}") for c in range(NK)]
    Bci = [Buf(f"bci{c}") for c in range(NK)]

    block_cm = nc.Block()
    es.enter_context(block_cm)

    def dma(ename, ch, out_ap, in_ap, reads, writes, note="", cost=800):
        return P.emit(ename, reads, writes,
                      lambda e: e.dma_start(out=out_ap, in_=in_ap), chan=ch, note=note,
                      cost=cost)

    dma("qload", ch_const, gwr[:], gwh, [], [B["gwr"]])
    dma("qload", ch_const, vecs[:], vecsh, [], [B["vecs"]] + Bb1)
    for m_ in range(NK):
        P.emit("dve", [B["vecs"]], [B["vecs"]],
               lambda e, m_=m_: e.tensor_tensor(
                   out=vecs[:, V_B2C + m_:V_B2C + m_ + 1],
                   in0=vecs[:, V_B2C + m_:V_B2C + m_ + 1],
                   in1=vecs[:, V_LN1B + m_:V_LN1B + m_ + 1], op=ALU.add), cost=100)
    dma("qload", ch_const, cvec[:], cvech, [], [B["cvec"]])
    dma("qload", ch_const, onesrow[:], onesrowh, [], [B["onesrow"]])
    dma("qload", ch_const, bsel[:], bselh, [], [B["bsel"]])
    dma("qload", ch_const, zer[:], zerh, [], [B["zer"]])

    ch_sin = [Chan(sem(f"c_si{i}"), 16) for i in range(4)]
    ch_sout = [Chan(sem(f"c_so{i}"), 16) for i in range(4)]
    Bscr = {name: [Buf(f"scr_{name}_{i2}") for i2 in range(ap_.shape[0])]
            for name, ap_ in scr.items()}
    Bcsi = [Buf("cs_in0"), Buf("cs_in1")]
    Bcso = [Buf("cs_out0"), Buf("cs_out1")]
    si = [0]

    def cast_slab(src_ap, dst_ap, dst_buf, w1_m=None):
        ic = si[0] % 4
        i = si[0] % 2
        si[0] += 1
        ft, fb = cs_in[i], Bcsi[i]
        bt_ap, bb = cs_out[i][:], Bcso[i]
        dma("qload", ch_sin[ic], ft[:, 0:1024], src_ap, [], [fb], note="scast in")
        if w1_m is not None:
            for k in range(NK):
                P.emit("dve", [fb, B["vecs"]], [bb],
                       lambda e, ft=ft, k=k: e.tensor_scalar(
                           out=cs_out[i][:, k * 128:(k + 1) * 128],
                           in0=ft[:, k * 128:(k + 1) * 128],
                           scalar1=vecs[:, V_LN1S + k:V_LN1S + k + 1],
                           scalar2=None, op0=ALU.mult), cost=140)
            parts = [(ft[:, k * 128:(k + 1) * 128],
                      vecs[:, V_LN1B + k:V_LN1B + k + 1]) for k in range(NK)]
            outp, obuf = mmg(rotB, (4, 5, 6, 7), parts, [fb, B["vecs"]],
                             note="b1fold", m_rows=128, n_cols=1, cost=2000)
            P.emit("dve", [obuf, B["vecs"]], [Bb1[w1_m]],
                   lambda e, outp=outp, m=w1_m: e.scalar_tensor_tensor(
                       out=vecs[:, V_B1 + m:V_B1 + m + 1], in0=outp[:, 0:1],
                       scalar=1.0, in1=vecs[:, V_B1 + m:V_B1 + m + 1],
                       op0=ALU.mult, op1=ALU.add), cost=130)
        elif i == 0:
            P.emit("dve", [fb], [bb],
                   lambda e, ft=ft, bt_ap=bt_ap: e.tensor_copy(
                       out=bt_ap, in_=ft[:, 0:1024]))
        else:
            P.emit("act", [fb], [bb],
                   lambda e, ft=ft, bt_ap=bt_ap: e.activation(
                       out=bt_ap, in_=ft[:, 0:1024], func=AF.Copy))
        dma("qstore", ch_sout[ic], dst_ap, bt_ap, [bb], [dst_buf], note="scast out")

    P.emit("dve", [B["gwr"]], [B["gwb"]],
           lambda e: e.tensor_copy(out=gwb[:], in_=gwr[:]))
    P.emit("dve", [B["cvec"]], [B["onesb"]],
           lambda e: e.tensor_copy(out=onesb[:], in_=cvec[:, C_ONE:C_ONE + 1]))

    poolA = {"slots": [0, 1, 2, 3, 4], "ctr": [0], "q": "qload"}
    poolB = {"slots": [5, 6, 7, 8, 9, 10, 11], "ctr": [0], "q": "qload"}
    pinned = set()

    def load_pair(pool, name, m0, n2=2, note="", pin=False):
        slots = pool["slots"]
        while True:
            i = slots[pool["ctr"][0] % len(slots)]
            pool["ctr"][0] += 1
            if i not in pinned:
                break
        src_ap = scr[name][m0:m0 + n2, :, :].rearrange("a p m -> p a m")
        dma(pool["q"], ch_slot[i], slab[i][:, 0:n2 * 1024].rearrange(
            "p (a m) -> p a m", a=n2), src_ap,
            Bscr[name][m0:m0 + n2], [Bslab[i]], note=note)
        if pin:
            pinned.add(i)
            return slab[i], Bslab[i], i
        return slab[i], Bslab[i]

    rotA = [0]
    rotB = [0]

    def mmg(rot, banks, parts, reads, m_rows=128, note="", n_cols=None, cost=None):
        bi = banks[rot[0] % len(banks)]
        rot[0] += 1
        bank, bbuf = pb[bi], Bbank[bi]
        out = bank[0:m_rows, 0:(t if n_cols is None else n_cols)]
        n = len(parts)

        def fn(e):
            last = None
            for i, (l, r) in enumerate(parts):
                last = e.matmul(out, l, r, start=(i == 0), stop=(i == n - 1))
            return last
        P.emit("pe", reads, [bbuf], fn, note=note,
               cost=(180 + 116 * len(parts)) if cost is None else cost)
        return out, bbuf

    for name, n in W_NS:
        for i2 in range(n):
            cast_slab(wslab[name][i2, :, :], scr[name][i2, :, :], Bscr[name][i2],
                      w1_m=(i2 if name == "w1p" else None))
    cast_slab(v2h, scr["v2"][0, :, :], Bscr["v2"][0])

    def mmg_at(bi, parts, reads, m_rows=128, note="", cost=None):
        bank, bbuf = pb[bi], Bbank[bi]
        out = bank[0:m_rows, 0:t]
        n = len(parts)

        def fn(e):
            last = None
            for i, (l, r) in enumerate(parts):
                last = e.matmul(out, l, r, start=(i == 0), stop=(i == n - 1))
            return last
        P.emit("pe", reads, [bbuf], fn, note=note,
               cost=(180 + 116 * len(parts)) if cost is None else cost)
        return out, bbuf

    def ck(buf_t, c):
        return buf_t[:, c * t:(c + 1) * t]

    def copy_fn(out_ap, in_ap):
        def fn(e):
            if hasattr(e, "tensor_copy"):
                return e.tensor_copy(out=out_ap, in_=in_ap)
            return e.copy(out=out_ap, in_=in_ap)
        return fn

    E = P.emit

    def ln_stats(src_t, src_b, cast_t, cast_bs, sq_t, sq_bs, banks, st, stp, bc, bcb,
                 eps_col, bc2=None):
        E("act", [src_b], list(sq_bs),
          lambda e: e.activation(out=sq_t[:, 0:CH], in_=src_t[:], func=AF.Square),
          cost=2000)
        mmg_at(banks[0], [(cvec[0:128, C_ONE:C_ONE + 1], ck(src_t, c))
                          for c in range(NK)],
               [src_b, B["cvec"]], m_rows=1, note="sum", cost=1900)
        E("dve", [Bbank[banks[0]]], [stp["mu"]],
          lambda e: e.tensor_scalar(out=st["mu"][:], in0=pb[banks[0]][0:1, 0:t],
                                    scalar1=1.0 / D, scalar2=None, op0=ALU.mult))
        mmg_at(banks[1], [(onesb[:, 0:1], ck(sq_t, c)) for c in range(NK)],
               list(sq_bs) + [B["onesb"]], m_rows=1, note="sumsq")
        E("dve", [Bbank[banks[1]]], [stp["ms"]],
          lambda e: e.tensor_scalar(out=st["ms"][:], in0=pb[banks[1]][0:1, 0:t],
                                    scalar1=1.0 / D, scalar2=None, op0=ALU.mult))
        E("dve", [stp["mu"]], [stp["var"]],
          lambda e: e.tensor_tensor(out=st["var"][:], in0=st["mu"][:], in1=st["mu"][:],
                                    op=ALU.mult))
        E("dve", [stp["ms"], stp["var"]], [stp["var"]],
          lambda e: e.tensor_tensor(out=st["var"][:], in0=st["ms"][:], in1=st["var"][:],
                                    op=ALU.subtract))
        eps_val = EPS_MOD if eps_col == C_EPS_MOD else EPS_IN
        x_ap = st["ms"]
        xi = x_ap.bitcast(I32)
        yi = st["rstd"].bitcast(I32)
        E("dve", [stp["var"]], [stp["ms"]],
          lambda e: e.tensor_scalar(out=x_ap[:], in0=st["var"][:], scalar1=eps_val,
                                    scalar2=None, op0=ALU.add), cost=120)
        E("dve", [stp["ms"]], [stp["rstd"]],
          lambda e: e.tensor_scalar(out=yi[:], in0=xi[:], scalar1=1, scalar2=None,
                                    op0=ALU.logical_shift_right), cost=120)
        E("dve", [stp["rstd"]], [stp["rstd"]],
          lambda e: e.tensor_scalar(out=yi[:], in0=yi[:], scalar1=-1,
                                    scalar2=RSQRT_MAGIC, op0=ALU.mult,
                                    op1=ALU.add), cost=120)
        E("dve", [stp["ms"]], [stp["ms"]],
          lambda e: e.tensor_scalar(out=x_ap[:], in0=x_ap[:], scalar1=0.5,
                                    scalar2=None, op0=ALU.mult), cost=120)
        for _ in range(2):
            E("dve", [stp["rstd"]], [stp["var"]],
              lambda e: e.tensor_tensor(out=st["var"][:], in0=st["rstd"][:],
                                        in1=st["rstd"][:], op=ALU.mult), cost=120)
            E("dve", [stp["ms"], stp["var"]], [stp["var"]],
              lambda e: e.tensor_tensor(out=st["var"][:], in0=x_ap[:],
                                        in1=st["var"][:], op=ALU.mult), cost=120)
            E("dve", [stp["var"]], [stp["var"]],
              lambda e: e.tensor_scalar(out=st["var"][:], in0=st["var"][:],
                                        scalar1=-1.0, scalar2=1.5, op0=ALU.mult,
                                        op1=ALU.add), cost=120)
            E("dve", [stp["rstd"], stp["var"]], [stp["rstd"]],
              lambda e: e.tensor_tensor(out=st["rstd"][:], in0=st["rstd"][:],
                                        in1=st["var"][:], op=ALU.mult), cost=120)
        E("dve", [stp["mu"], stp["rstd"]], [stp["nmr"]],
          lambda e: e.scalar_tensor_tensor(out=st["nmr"][:], in0=st["mu"][:],
                                           scalar=-1.0, in1=st["rstd"][:],
                                           op0=ALU.mult, op1=ALU.mult))
        bank0 = pb[banks[0]]
        P.emit("pe", [B["onesrow"], stp["rstd"], stp["nmr"]], [Bbank[banks[0]]],
               lambda e: e.matmul(bank0[0:128, 0:2 * t], onesrow[0:1, :],
                                  st["rn"][0:1, 0:2 * t], start=True, stop=True),
               note="bc", cost=700)
        E("act", [Bbank[banks[0]]], [bcb[0], bcb[1]],
          copy_fn(bc2[:, 0:2 * t], bank0[:, 0:2 * t]), cost=900,
          alts={"act": 900.0, "dve": 900.0})

    def ln_apply(src_t, src_b, dst_t, dst_b, bc, bcb, affine=None, eng="dve"):
        for c in range(NK):
            ec = "dve"
            op_cost = 450
            E(ec, [src_b, bcb[0]], [dst_b],
              lambda e, c=c: e.tensor_tensor(out=ck(dst_t, c), in0=ck(src_t, c),
                                             in1=bc[0][:, 0:t], op=ALU.mult),
              cost=op_cost, flex=True)
            E(ec, [dst_b, bcb[1]], [dst_b],
              lambda e, c=c: e.tensor_tensor(out=ck(dst_t, c), in0=ck(dst_t, c),
                                             in1=bc[1][:, 0:t], op=ALU.add),
              cost=op_cost, flex=True)
            if affine is not None:
                sc, bcv = affine
                E(ec, [dst_b, B["vecs"]], [dst_b],
                  lambda e, c=c: e.tensor_scalar(
                      out=ck(dst_t, c), in0=ck(dst_t, c),
                      scalar1=vecs[:, sc + c:sc + c + 1],
                      scalar2=vecs[:, bcv + c:bcv + c + 1],
                      op0=ALU.mult, op1=ALU.add), cost=op_cost, flex=True)

    def proj(wname, dst_t, dst_bs, nm, rhs_t, rhs_b, m_off=0, act_fn=AF.Copy):
        for m0 in range(0, nm, 2):
            svt, svb = load_pair(poolA, wname, m0 + m_off, note=f"{wname} m{m0}")
            bi = (0, 1, 2, 3)[rotA[0] % 4]
            rotA[0] += 1
            bank, bbuf = pb[bi], Bbank[bi]

            def fn(e, svt=svt, bank=bank, rhs_t=rhs_t):
                last = None
                for d in range(2):
                    for k in range(NK):
                        last = e.matmul(
                            bank[0:128, d * t:(d + 1) * t],
                            svt[:, d * 1024 + k * 128:d * 1024 + (k + 1) * 128],
                            ck(rhs_t, k), start=(k == 0), stop=(k == NK - 1))
                return last
            P.emit("pe", [svb, rhs_b], [bbuf], fn, note=f"{wname} m{m0}",
                   cost=180 + 116 * 16)
            wide = dst_t[:, m0 * t:(m0 + 2) * t]
            if act_fn == AF.Copy:
                E("act", [bbuf], [dst_bs[m0 // 4]],
                  copy_fn(wide, bank[:, 0:2 * t]), cost=930,
                  alts={"act": 930.0, "dve": 930.0})
            else:
                E("act", [bbuf], [dst_bs[m0 // 4]],
                  lambda e, wide=wide, bank=bank, act_fn=act_fn: e.activation(
                      out=wide, in_=bank[:, 0:2 * t], func=act_fn),
                  table=("silu" if act_fn == AF.Silu else None), cost=930)

    def xload(i, b, j):
        par = i % 2
        t0 = j * t
        bxt, bxb = bx[par], B[f"bx{par}"]
        dma("qload", ch_x[par], bxt[:].rearrange("p (c t) -> p c t", c=NK),
            xh[b, :, t0:t0 + t].rearrange("(c p) t -> p c t", p=128),
            [], [bxb], note=f"x b{b} j{j}", cost=1500)

    def A1a(i, b, j):
        par = i % 2
        bxt, bxb = bx[par], B[f"bx{par}"]
        bzt, bzbf = bzbD[par], B[f"bzb{par}"]
        if j == 0:
            E("dve", [], Bcr, lambda e: e.memset(bcr[:], 0.0), cost=100)
            E("dve", [], Bci, lambda e: e.memset(bci[:], 0.0), cost=100)
        ln_stats(bxt, bxb, None, [],
                 bmix[par], [B[f"bmix{par}0"], B[f"bmix{par}1"]],
                 (3, 3), stA, stpA, bcA,
                 [B["bcA0"], B["bcA1"]], C_EPS_MOD, bc2=bcAt)
        for c in range(NK):
            hB = B[f"bt1{c // 4}"]
            E("dve", [bxb, B["bcA0"]], [hB],
              lambda e, c=c: e.tensor_tensor(out=ck(bt1, c), in0=ck(bxt, c),
                                             in1=bcA[0][:, 0:t], op=ALU.mult),
              cost=450, alts={"dve": 450.0, "gps": 1000.0})
            E("dve", [hB, B["bcA1"]], [bzbf],
              lambda e, c=c: e.tensor_tensor(out=ck(bzt, c), in0=ck(bt1, c),
                                             in1=bcA[1][:, 0:t], op=ALU.add),
              cost=450, alts={"dve": 450.0, "gps": 1000.0})

    def A1b(i):
        par = i % 2
        bzt, bzbf = bzbD[par], B[f"bzb{par}"]
        proj("wa", bar, [B["bar0"], B["bar1"]], NK, bzt, bzbf)
        proj("wa", bai, [B["bai0"], B["bai1"]], NK, bzt, bzbf, m_off=NK)
        proj("wk", bk, [B["bk0"], B["bk1"]], NK, bzt, bzbf)
        proj("wv", bv, [B["bv0"], B["bv1"]], NK, bzt, bzbf)

    Hw = CH // 2
    CD = 1350.0
    CA = 1150.0

    def chain_half(i, h):
        HSL = slice(h * Hw, (h + 1) * Hw)
        HS = lambda buf: buf[:, HSL]
        Bh = lambda nm: B[f"{nm}{h}"]
        Bk, Bv = B[f"bk{h}"], B[f"bv{h}"]
        bki_h = bar[:, HSL].bitcast(I32)
        E("dve", [Bk, Bv], [Bk],
          lambda e: e.tensor_tensor(out=HS(bk), in0=HS(bk), in1=HS(bv),
                                    op=ALU.mult), cost=700)
        E("act", [Bh("bar")], [Bh("bt1")],
          lambda e: e.activation(out=HS(bt1), in_=HS(bar), func=AF.Square), cost=CA)
        E("act", [Bh("bai")], [Bh("bt2")],
          lambda e: e.activation(out=HS(bt2), in_=HS(bai), func=AF.Square), cost=CA)
        E("dve", [Bh("bt1"), Bh("bt2")], [Bh("bt1")],
          lambda e: e.tensor_tensor(out=HS(bt1), in0=HS(bt1), in1=HS(bt2),
                                    op=ALU.add), cost=CD, flex=True)
        E("act", [Bh("bt1")], [Bh("bt2")],
          lambda e: e.activation(out=HS(bt2), in_=HS(bt1), func=AF.Ln),
          table="nlexp", cost=CA)
        E("act", [Bh("bt2"), B["cvec"]], [Bh("bt1")],
          lambda e: e.activation(out=HS(bt1), in_=HS(bt2), func=AF.Exp,
                                 scale=cvec[:, C_HALF:C_HALF + 1]),
          table="nlexp", cost=CA)
        E("act", [Bh("bar")], [Bh("bhr")],
          lambda e: e.activation(out=HS(bhr), in_=HS(bar), func=AF.Abs), cost=CA)
        E("act", [Bh("bar")], [Bh("bcos")],
          lambda e: e.activation(out=HS(bcos), in_=HS(bar), func=AF.Sign), cost=CA)
        E("act", [Bh("bai")], [Bh("bnsin")],
          lambda e: e.activation(out=HS(bnsin), in_=HS(bai), func=AF.Sign), cost=CA)
        E("dve", [Bh("bt1"), Bh("bhr")], [Bh("bt2")],
          lambda e: e.tensor_tensor(out=HS(bt2), in0=HS(bt1), in1=HS(bhr),
                                    op=ALU.add), cost=CD, flex=True)
        E("dve", [Bh("bt2")], [Bh("bt2")],
          lambda e: e.tensor_scalar(out=HS(bt2), in0=HS(bt2), scalar1=1e-30,
                                    scalar2=None, op0=ALU.max), cost=CD * 0.55)
        E("act", [Bh("bt2")], [Bh("bhr")],
          lambda e: e.activation(out=HS(bhr), in_=HS(bt2), func=AF.Ln),
          table="nlexp", cost=CA)
        E("act", [Bh("bhr"), B["cvec"]], [Bh("bt2")],
          lambda e: e.activation(out=HS(bt2), in_=HS(bhr), func=AF.Exp,
                                 scale=cvec[:, C_NEG1:C_NEG1 + 1]),
          table="nlexp", cost=CA)
        E("act", [Bh("bt1")], [Bh("bs")],
          lambda e: e.activation(out=HS(bs_), in_=HS(bt1), func=AF.Sigmoid),
          table="sigm", cost=CA)
        E("dve", [Bh("bai"), Bh("bt2")], [Bh("bt1")],
          lambda e: e.tensor_tensor(out=HS(bt1), in0=HS(bai), in1=HS(bt2),
                                    op=ALU.mult), cost=CD, flex=True)
        E("act", [Bh("bt1")], [Bh("bt2")],
          lambda e: e.activation(out=HS(bt2), in_=HS(bt1), func=AF.Arctan),
          table="sigm", cost=CA)
        E("dve", [Bh("bcos")], [Bh("bcos")],
          lambda e: e.tensor_scalar(out=HS(bcos), in0=HS(bcos), scalar1=-0.5,
                                    scalar2=0.5, op0=ALU.mult, op1=ALU.add),
          cost=CD * 0.55)
        E("dve", [Bh("bcos")], [Bh("bhr")],
          lambda e: e.tensor_scalar(out=HS(bhr), in0=HS(bcos), scalar1=-4.0,
                                    scalar2=2.0, op0=ALU.mult, op1=ALU.add),
          cost=CD * 0.55)
        E("dve", [Bh("bcos"), Bh("bnsin")], [Bh("bt1")],
          lambda e: e.tensor_tensor(out=HS(bt1), in0=HS(bcos), in1=HS(bnsin),
                                    op=ALU.mult), cost=CD, flex=True)
        E("dve", [Bh("bt2"), Bh("bhr")], [Bh("bt2")],
          lambda e: e.tensor_tensor(out=HS(bt2), in0=HS(bt2), in1=HS(bhr),
                                    op=ALU.mult), cost=CD, flex=True)
        E("dve", [Bh("bt1"), Bh("bt2")], [Bh("bai")],
          lambda e: e.scalar_tensor_tensor(out=HS(bai), in0=HS(bt1), scalar=PI,
                                           in1=HS(bt2), op0=ALU.mult,
                                           op1=ALU.add), cost=CD)
        for c in range(4 * h, 4 * h + 4):
            E("dve", [Bh("bai"), B["zer"]], [Bh("bai")],
              lambda e, c=c: e.tensor_tensor_scan(
                  out=ck(bai, c), data0=ck(bai, c), data1=zer[:, 0:t],
                  initial=zer[:, 0:1], op0=ALU.add, op1=ALU.add), cost=450)

        def red_sin(alpha_ap, alpha_bufs, dst_ap, dst_buf, neg):
            E("dve", alpha_bufs, [Bh("bar")],
              lambda e: e.tensor_scalar(out=bki_h, in0=alpha_ap, scalar1=INV_2PI,
                                        scalar2=None, op0=ALU.mult), cost=CD)
            E("dve", [Bh("bar")] + alpha_bufs, [Bh("bt1")],
              lambda e: e.scalar_tensor_tensor(out=HS(bt1), in0=bki_h,
                                               scalar=-TWO_PI, in1=alpha_ap,
                                               op0=ALU.mult, op1=ALU.add), cost=CD)
            if sim_safe_wrap:
                E("dve", [Bh("bt1")], [Bh("bt2")],
                  lambda e: e.tensor_scalar(out=HS(bt2), in0=HS(bt1), scalar1=-PI,
                                            scalar2=None, op0=ALU.is_lt), cost=CD)
                E("dve", [Bh("bt1")], [Bh("bhr")],
                  lambda e: e.tensor_scalar(out=HS(bhr), in0=HS(bt1), scalar1=PI,
                                            scalar2=None, op0=ALU.is_gt), cost=CD)
                E("dve", [Bh("bt2"), Bh("bhr")], [Bh("bt2")],
                  lambda e: e.tensor_tensor(out=HS(bt2), in0=HS(bt2), in1=HS(bhr),
                                            op=ALU.subtract), cost=CD)
                E("dve", [Bh("bt2"), Bh("bt1")], [Bh("bt1")],
                  lambda e: e.scalar_tensor_tensor(out=HS(bt1), in0=HS(bt2),
                                                   scalar=TWO_PI, in1=HS(bt1),
                                                   op0=ALU.mult, op1=ALU.add),
                  cost=CD)
            else:
                E("dve", [Bh("bt1")], [Bh("bt1")],
                  lambda e: e.tensor_scalar(out=HS(bt1), in0=HS(bt1), scalar1=-PI,
                                            scalar2=PI, op0=ALU.max, op1=ALU.min),
                  cost=CD * 0.55)
            if neg:
                E("act", [Bh("bt1"), B["cvec"]], [dst_buf],
                  lambda e: e.activation(out=dst_ap, in_=HS(bt1), func=AF.Sin,
                                         scale=cvec[:, C_NEG1:C_NEG1 + 1]),
                  table="silu", cost=CA)
            else:
                E("act", [Bh("bt1")], [dst_buf],
                  lambda e: e.activation(out=dst_ap, in_=HS(bt1), func=AF.Sin),
                  table="silu", cost=CA)

        red_sin(HS(bai), [Bh("bai")], HS(bnsin), Bh("bnsin"), neg=True)
        E("dve", [Bh("bai")], [Bh("bar")],
          lambda e: e.tensor_scalar(out=bki_h, in0=HS(bai), scalar1=INV_2PI,
                                    scalar2=0.25, op0=ALU.mult, op1=ALU.add),
          cost=CD)
        E("dve", [Bh("bar"), Bh("bai")], [Bh("bt1")],
          lambda e: e.scalar_tensor_tensor(out=HS(bt1), in0=bki_h,
                                           scalar=-TWO_PI, in1=HS(bai),
                                           op0=ALU.mult, op1=ALU.add), cost=CD)
        if sim_safe_wrap:
            E("dve", [Bh("bt1")], [Bh("bt2")],
              lambda e: e.tensor_scalar(out=HS(bt2), in0=HS(bt1),
                                        scalar1=-3 * PI / 2,
                                        scalar2=None, op0=ALU.is_lt), cost=CD)
            E("dve", [Bh("bt1")], [Bh("bhr")],
              lambda e: e.tensor_scalar(out=HS(bhr), in0=HS(bt1),
                                        scalar1=PI / 2,
                                        scalar2=None, op0=ALU.is_gt), cost=CD)
            E("dve", [Bh("bt2"), Bh("bhr")], [Bh("bt2")],
              lambda e: e.tensor_tensor(out=HS(bt2), in0=HS(bt2), in1=HS(bhr),
                                        op=ALU.subtract), cost=CD)
            E("dve", [Bh("bt2"), Bh("bt1")], [Bh("bt1")],
              lambda e: e.scalar_tensor_tensor(out=HS(bt1), in0=HS(bt2),
                                               scalar=TWO_PI, in1=HS(bt1),
                                               op0=ALU.mult, op1=ALU.add), cost=CD)
        else:
            E("dve", [Bh("bt1")], [Bh("bt1")],
              lambda e: e.tensor_scalar(out=HS(bt1), in0=HS(bt1),
                                        scalar1=-3 * PI / 2, scalar2=PI / 2,
                                        op0=ALU.max, op1=ALU.min), cost=CD * 0.55)
        E("act", [Bh("bt1"), B["cvec"]], [Bh("bcos")],
          lambda e: e.activation(out=HS(bcos), in_=HS(bt1), func=AF.Sin,
                                 bias=cvec[:, C_HALFPI:C_HALFPI + 1]),
          table="silu", cost=CA)
        E("dve", [Bk, Bh("bcos")], [Bh("bhr")],
          lambda e: e.tensor_tensor(out=HS(bhr), in0=HS(bk), in1=HS(bcos),
                                    op=ALU.mult), cost=700)
        E("dve", [Bk, Bh("bnsin")], [Bh("bhi")],
          lambda e: e.tensor_tensor(out=HS(bhi), in0=HS(bk), in1=HS(bnsin),
                                    op=ALU.mult), cost=700)
        for c in range(4 * h, 4 * h + 4):
            E("dve", [Bh("bs"), Bh("bhr"), Bcr[c]], [Bh("bhr")],
              lambda e, c=c: e.tensor_tensor_scan(
                  out=ck(bhr, c), data0=ck(bs_, c), data1=ck(bhr, c),
                  initial=bcr[:, c:c + 1], op0=ALU.mult, op1=ALU.add), cost=450)
            E("dve", [Bh("bs"), Bh("bhi"), Bci[c]], [Bh("bhi")],
              lambda e, c=c: e.tensor_tensor_scan(
                  out=ck(bhi, c), data0=ck(bs_, c), data1=ck(bhi, c),
                  initial=bci[:, c:c + 1], op0=ALU.mult, op1=ALU.add), cost=450)
        E("dve", [Bh("bhr"), Bh("bcos"), Bh("bs")], [Bh("bs")],
          lambda e: e.tensor_tensor(out=HS(bs_), in0=HS(bhr), in1=HS(bcos),
                                    op=ALU.mult), cost=700)
        E("dve", [Bh("bhi"), Bh("bnsin")], [Bk],
          lambda e: e.tensor_tensor(out=HS(bk), in0=HS(bhi), in1=HS(bnsin),
                                    op=ALU.mult), cost=700)
        E("dve", [Bh("bs"), Bk], [Bh("bs")],
          lambda e: e.tensor_tensor(out=HS(bs_), in0=HS(bs_), in1=HS(bk),
                                    op=ALU.add), cost=700)
        for c in range(4 * h, 4 * h + 4):
            L = c * t + t - 1
            s0 = h * Hw + 2 * (c - 4 * h)
            E("dve", [Bh("bs")], [Bcr[c]],
              lambda e, c=c, L=L: e.tensor_copy(out=bcr[:, c:c + 1],
                                                in_=bs_[:, L:L + 1]), cost=150)
            E("dve", [Bh("bhr"), Bh("bnsin")], [Bh("bt2")],
              lambda e, s0=s0, L=L: e.tensor_tensor(
                  out=bt2[:, s0:s0 + 1], in0=bhr[:, L:L + 1],
                  in1=bnsin[:, L:L + 1], op=ALU.mult), cost=150)
            E("dve", [Bh("bhi"), Bh("bcos")], [Bh("bt2")],
              lambda e, s0=s0, L=L: e.tensor_tensor(
                  out=bt2[:, s0 + 1:s0 + 2], in0=bhi[:, L:L + 1],
                  in1=bcos[:, L:L + 1], op=ALU.mult), cost=150)
            E("dve", [Bh("bt2")], [Bci[c]],
              lambda e, c=c, s0=s0: e.tensor_tensor(
                  out=bci[:, c:c + 1], in0=bt2[:, s0 + 1:s0 + 2],
                  in1=bt2[:, s0:s0 + 1], op=ALU.subtract), cost=150)

    def CH_(i):
        chain_half(i, 0)
        yield
        chain_half(i, 1)
        yield

    def A3a(i):
        proj("wq", bq, [B["bq0"], B["bq1"]], NK, bzbD[i % 2], B[f"bzb{i % 2}"])

    def A3b(i):
        par = i % 2
        proj("wg", bg, [B["bg0"], B["bg1"]], NK, bzbD[par], B[f"bzb{par}"],
             act_fn=AF.Silu)
        for h in (0, 1):
            HSL = slice(h * Hw, (h + 1) * Hw)
            HS = lambda buf, HSL=HSL: buf[:, HSL]
            Bh = lambda nm, h=h: B[f"{nm}{h}"]
            E("dve", [Bh("bq"), Bh("bs")], [Bh("bs")],
              lambda e, HS=HS: e.tensor_tensor(out=HS(bs_), in0=HS(bs_),
                                               in1=HS(bq), op=ALU.mult), cost=700)
            E("dve", [Bh("bs"), Bh("bg")], [B[f"bmix{par}{h}"]],
              lambda e, HS=HS, par=par: e.tensor_tensor(
                  out=HS(bmix[par]), in0=HS(bs_), in1=HS(bg), op=ALU.mult),
              cost=700)

    def B1(i, take=lambda n=1: None):
        par = i % 2
        bzt, bzbf = bzbD[par], B[f"bzb{par}"]
        for m0 in range(0, NK, 2):
            svt, svb = load_pair(poolB, "wo", m0, note=f"wo m{m0}")
            bi = (4, 5, 6, 7)[rotB[0] % 4]
            rotB[0] += 1
            bank, bbuf = pb[bi], Bbank[bi]

            def fn(e, svt=svt, bank=bank, par=par):
                last = None
                for d in range(2):
                    for k in range(NK):
                        last = e.matmul(
                            bank[0:128, d * t:(d + 1) * t],
                            svt[:, d * 1024 + k * 128:d * 1024 + (k + 1) * 128],
                            ck(bmix[par], k), start=(k == 0), stop=(k == NK - 1))
                return last
            P.emit("pe", [svb, B[f"bmix{par}0"], B[f"bmix{par}1"]], [bbuf], fn,
                   note=f"wo m{m0}", cost=180 + 116 * 16)
            E("dve", [bbuf, bzbf], [B["bw"]],
              lambda e, m0=m0, bank=bank: e.tensor_tensor(
                  out=bw[:, m0 * t:(m0 + 2) * t], in0=bank[:, 0:2 * t],
                  in1=bzt[:, m0 * t:(m0 + 2) * t], op=ALU.add), cost=900)
            take(2)
        ln_stats(bw, B["bw"], bwb, [B["bwb"]], bmix[par],
                 [B[f"bmix{par}0"], B[f"bmix{par}1"]],
                 (7, 7), stB, stpB, bcB,
                 [B["bcB0"], B["bcB1"]], C_EPS_IN, bc2=bcBt)
        for c in range(NK):
            E("dve", [B["bw"], B["bcB0"]], [B["bw"]],
              lambda e, c=c: e.tensor_tensor(out=ck(bw, c), in0=ck(bw, c),
                                             in1=bcB[0][:, 0:t], op=ALU.mult),
              cost=450, alts={"dve": 450.0, "gps": 1000.0})
            E("dve", [B["bw"], B["bcB1"]], [B["bwb"]],
              lambda e, c=c: e.tensor_tensor(out=ck(bwb, c), in0=ck(bw, c),
                                             in1=bcB[1][:, 0:t], op=ALU.add),
              cost=450, alts={"dve": 450.0, "gps": 1000.0})

    def B2(i, take=lambda n=1: None):
        for f0 in range(0, NF, 2):
            svt, svb = load_pair(poolB, "w1p", f0, note=f"w1 f{f0}")
            for d in range(2):
                f = f0 + d
                parts = [(svt[:, d * 1024 + k * 128:d * 1024 + (k + 1) * 128],
                          ck(bwb, k)) for k in range(NK)]
                out, bbuf = mmg(rotB, (4, 5, 6, 7), parts, [svb, B["bwb"]], note=f"w1 f{f}")
                E("act", [bbuf, B["vecs"], Bb1[f]], [Bh1[f // 8]],
                  lambda e, f=f, out=out: e.activation(
                      out=bh1[:, f * t:(f + 1) * t], in_=out, func=AF.Gelu_apprx_tanh,
                      bias=vecs[:, V_B1 + f:V_B1 + f + 1]), table="gelu", cost=550)
                take(2)

    def B3a(i, b, j):
        par = i % 2
        t0 = j * t
        bxt, bxb = bx[par], B[f"bx{par}"]
        for m in range(NK):
            w2s = [load_pair(poolB, "w2p", m * 4, note=f"w2 {m * 4}"),
                   load_pair(poolB, "w2p", m * 4 + 2, note=f"w2 {m * 4 + 2}")]
            parts = []
            for kb in range(NF):
                sl = w2s[kb // 16]
                d = (kb // 8) % 2
                parts.append((sl[0][:, d * 1024 + (kb % 8) * 128:
                                    d * 1024 + (kb % 8 + 1) * 128],
                              bh1[:, kb * t:(kb + 1) * t]))
            out, bbuf = mmg(rotB, (4, 5, 6, 7), parts,
                            [sv[1] for sv in w2s] + list(Bh1), note=f"w2 m{m}")
            E("dve", [bbuf, B["vecs"], B["bwb"]], [B["bw"]],
              lambda e, m=m, out=out: e.scalar_tensor_tensor(
                  out=ck(bw, m), in0=ck(bwb, m),
                  scalar=vecs[:, V_LN1S + m:V_LN1S + m + 1],
                  in1=out, op0=ALU.mult, op1=ALU.add))
            E("dve", [B["bw"], B["vecs"]], [B["bw"]],
              lambda e, m=m: e.tensor_scalar(
                  out=ck(bw, m), in0=ck(bw, m),
                  scalar1=vecs[:, V_B2C + m:V_B2C + m + 1], scalar2=None,
                  op0=ALU.add), cost=450)
        ln_stats(bw, B["bw"], bwb, [B["bwb"]], bmix[par],
                 [B[f"bmix{par}0"], B[f"bmix{par}1"]],
                 (7, 7), stB, stpB, bcB,
                 [B["bcB0"], B["bcB1"]], C_EPS_IN, bc2=bcBt)
        for c in range(NK):
            E("dve", [B["bw"], B["bcB0"]], [B["bw"]],
              lambda e, c=c: e.tensor_tensor(out=ck(bw, c), in0=ck(bw, c),
                                             in1=bcB[0][:, 0:t], op=ALU.mult),
              cost=450, alts={"dve": 450.0, "gps": 1000.0})
            E("dve", [B["bw"], B["bcB1"]], [B["bw"]],
              lambda e, c=c: e.tensor_tensor(out=ck(bw, c), in0=ck(bw, c),
                                             in1=bcB[1][:, 0:t], op=ALU.add),
              cost=450, alts={"dve": 450.0, "gps": 1000.0})
            E("dve", [B["bw"], B["vecs"]], [B["bw"]],
              lambda e, c=c: e.tensor_scalar(
                  out=ck(bw, c), in0=ck(bw, c),
                  scalar1=vecs[:, V_LN2S + c:V_LN2S + c + 1],
                  scalar2=vecs[:, V_LN2B + c:V_LN2B + c + 1],
                  op0=ALU.mult, op1=ALU.add), cost=450)

    def B3b(i, b, j):
        par = i % 2
        t0 = j * t
        bxt, bxb = bx[par], B[f"bx{par}"]
        E("dve", [bxb, B["bw"]], [bxb],
          lambda e: e.tensor_tensor(out=bxt[:, 0:Hw], in0=bxt[:, 0:Hw],
                                    in1=bw[:, 0:Hw], op=ALU.add), cost=1350,
          alts={"dve": 1350.0, "gps": 2900.0})
        E("dve", [bxb, B["bw"]], [bxb],
          lambda e: e.tensor_tensor(out=bxt[:, Hw:CH], in0=bxt[:, Hw:CH],
                                    in1=bw[:, Hw:CH], op=ALU.add), cost=1350,
          alts={"dve": 1350.0, "gps": 2900.0})
        ln_stats(bxt, bxb, bwb, [B["bwb"]], bmix[par],
                 [B[f"bmix{par}0"], B[f"bmix{par}1"]],
                 (7, 7), stB, stpB, bcB,
                 [B["bcB0"], B["bcB1"]], C_EPS_MOD, bc2=bcBt)
        for c in range(NK):
            E("dve", [bxb, B["bcB0"]], [B["bw"]],
              lambda e, c=c: e.tensor_tensor(out=ck(bw, c), in0=ck(bxt, c),
                                             in1=bcB[0][:, 0:t], op=ALU.mult),
              cost=450, alts={"dve": 450.0, "gps": 1000.0})
            E("dve", [B["bw"], B["bcB1"]], [B["bwb"]],
              lambda e, c=c: e.tensor_tensor(out=ck(bwb, c), in0=ck(bw, c),
                                             in1=bcB[1][:, 0:t], op=ALU.add),
              cost=450, alts={"dve": 450.0, "gps": 1000.0})
        mmg_at(7, [(gwb[:, k * 8:(k + 1) * 8], ck(bwb, k)) for k in range(NK)],
               [B["gwb"], B["bwb"]], m_rows=8, note="gate logits")
        E("act", [Bbank[7]], [B["sp"]],
          lambda e: e.activation(out=sp, in_=pb[7][0:8, 0:t], func=AF.Exp),
          table="nlexp", cost=400)
        mmg_at(7, [(cvec[0:8, C_ONE:C_ONE + 1], sp)],
               [B["sp"], B["cvec"]], m_rows=1, note="gate sum")
        E("dve", [Bbank[7]], [B["st_r"]],
          lambda e: e.reciprocal(out=st_r[:], in_=pb[7][0:1, 0:t]))
        E("dve", [B["st_r"]], [B["st_r"]],
          lambda e: e.tensor_scalar(out=st_r[:], in0=st_r[:], scalar1=1.0 / 16.0,
                                    scalar2=None, op0=ALU.mult))
        mmg_at(7, [(bsel[:], sp)], [B["bsel"], B["sp"]], m_rows=128, note="gate bc")
        E("act", [Bbank[7]], [B["sg128"]],
          lambda e: e.activation(out=sg128[:], in_=pb[7][:, 0:t], func=AF.Copy))
        uat, uab = load_pair(poolB, "uall", 0, n2=1, note="uall")
        out_t, bbuf_t = mmg(rotB, (4, 5, 6, 7),
                            [(uat[:, k * 128:(k + 1) * 128], ck(bwb, k))
                             for k in range(NK)], [uab, B["bwb"]], note="t lora")
        out_r, bbuf_r = mmg(rotB, (4, 5, 6, 7),
                            [(onesrow[0:1, :], st_r[0:1, 0:t])],
                            [B["onesrow"], B["st_r"]], note="r bc")
        E("dve", [bbuf_t, B["sg128"]], [B["stgb"]],
          lambda e: e.tensor_tensor(out=stgb[:], in0=out_t, in1=sg128[:],
                                    op=ALU.mult))
        E("dve", [B["stgb"], bbuf_r], [B["stgb"]],
          lambda e: e.tensor_tensor(out=stgb[:], in0=stgb[:], in1=out_r,
                                    op=ALU.mult))
        v2t, v2b, v2slot = load_pair(poolB, "v2", 0, n2=1, note="v2", pin=True)
        for m0 in range(0, NK, 2):
            svt, svb = load_pair(poolB, "fw", m0, note=f"fw m{m0}")
            for d in range(2):
                m = m0 + d
                parts = [(svt[:, d * 1024 + k * 128:d * 1024 + (k + 1) * 128],
                          ck(bwb, k)) for k in range(NK)]
                parts.append((v2t[:, m * 128:(m + 1) * 128], stgb[:]))
                out, bbuf = mmg(rotB, (4, 5, 6, 7), parts, [svb, B["bwb"], v2b, B["stgb"]],
                                note=f"ffn2 m{m}")
                sti = m % 2
                E("dve", [bbuf, bxb], [B[f"bst{sti}"]],
                  lambda e, m=m, out=out, sti=sti: e.tensor_tensor(
                      out=bst[sti][:], in0=out, in1=ck(bxt, m), op=ALU.add))
                dma("qstore", ch_out[sti],
                    oh[b, m * 128:(m + 1) * 128, t0:t0 + t], bst[sti][:],
                    [B[f"bst{sti}"]], [Buf("_out")], note=f"out b{b} j{j} m{m}")
        pinned.discard(v2slot)

    with es:
        for i in range(total + 1):
            if i < total:
                b, j = divmod(i, nt)
                xload(i, b, j)
                A1a(i, b, j)
                A1b(i)
                gen = CH_(i)
            else:
                gen = iter(())

            def take(n=1, gen=gen):
                for _ in range(n):
                    if next(gen, "_done") == "_done":
                        return
            if i > 0:
                B1(i - 1, take)
                B2(i - 1, take)
            take(10 ** 6)
            if i > 0:
                bb, jb = divmod(i - 1, nt)
                B3a(i - 1, bb, jb)
            if i < total:
                A3a(i)
            if i > 0:
                B3b(i - 1, bb, jb)
            if i < total:
                A3b(i)
        est = P.schedule_and_replay()
        import os
        if os.environ.get("K_PRINT_EST"):
            print(f"[sched] estimated makespan: {est / 1e6:.3f} ms")
        for ch in ch_out:
            if ch.cnt:
                P.engs["qstore"].raw.wait_ge(ch.sem, ch.cnt)
    return nc



def kernel(**inputs):
    x = np.asarray(inputs["L0_attn_x"], np.float32)
    w = host_pack_weights(inputs)
    nc = build_core()
    in_maps = []
    for ci in range(N_CORES):
        xc = np.ascontiguousarray(x[ci * NB:(ci + 1) * NB].transpose(0, 2, 1))
        m = dict(w)
        m["x"] = xc
        in_maps.append(m)
    res = run_bass_kernel_spmd(nc, in_maps, core_ids=list(range(N_CORES)))
    outs = [res.results[i]["out"] for i in range(N_CORES)]
    full = np.concatenate(outs, axis=0)
    return np.ascontiguousarray(full.transpose(0, 2, 1)).astype(np.float32)


# version 43
# speedup vs baseline: 1.2145x; 1.0297x over previous
import numpy as np
from contextlib import ExitStack

import concourse.bass as bass
import concourse.mybir as mybir
from concourse.bass_utils import run_bass_kernel_spmd

F32 = mybir.dt.float32
BF16 = mybir.dt.bfloat16
I32 = mybir.dt.int32
AF = mybir.ActivationFunctionType
ALU = mybir.AluOpType

D = 1024
NK = 8
DF = 4096
NF = 32
PI = float(np.pi)
TWO_PI = 2.0 * PI
INV_2PI = 1.0 / TWO_PI
EPS_MOD = 1e-5
EPS_IN = 1e-6

N_CORES = 8
NB = 4
S = 2048
T = 256
NSLOT = 12

C_EPS_MOD, C_EPS_IN, C_HALFPI, C_NEG1, C_ONE, C_NEGH, C_HALF = 0, 1, 2, 3, 4, 5, 6
V_LN1S, V_LN1B, V_LN2S, V_LN2B, V_B2, V_B1 = 0, 8, 16, 24, 32, 40
V_B2C = 72


class Buf:
    __slots__ = ("name", "writer", "readers")

    def __init__(self, name):
        self.name = name
        self.writer = None
        self.readers = []


class Chan:
    __slots__ = ("sem", "step", "cnt", "last_node")

    def __init__(self, sem, step):
        self.sem = sem
        self.step = step
        self.cnt = 0
        self.last_node = None


class Eng:
    __slots__ = ("raw", "name", "chan", "sat")

    def __init__(self, raw, name, chan):
        self.raw = raw
        self.name = name
        self.chan = chan
        self.sat = {}


class Node:
    __slots__ = ("nid", "ename", "fn", "reads", "writes", "chan", "note",
                 "cost", "table", "deps", "prio", "finish", "alts")

    def __init__(self, nid, ename, fn, reads, writes, chan, note, cost, table):
        self.nid = nid
        self.ename = ename
        self.alts = None
        self.fn = fn
        self.reads = reads
        self.writes = writes
        self.chan = chan
        self.note = note
        self.cost = cost
        self.table = table
        self.deps = set()
        self.prio = 0.0
        self.finish = 0.0


ACT_TABLE_NS = 5000.0
RSQRT_MAGIC = 0x5f3759df


class Prog:

    def __init__(self, loose_sync=False):
        self.engs = {}
        self.nodes = []
        self.capture = True
        self.loose_sync = loose_sync

    def add_eng(self, name, raw, chan):
        self.engs[name] = Eng(raw, name, chan)

    def emit(self, ename, reads, writes, fn, chan=None, note="", cost=None,
             table=None, flex=False, alts=None):
        if not self.capture:
            return self._emit_real(ename, reads, writes, fn, chan, note)
        if cost is None:
            cost = {"pe": 1000.0, "dve": 450.0, "act": 450.0, "gps": 5000.0}.get(
                ename, 2500.0)
        n = Node(len(self.nodes), ename, fn, list(reads), list(writes), chan,
                 note, float(cost), table)
        import os as _os
        if not _os.environ.get("K_FLEX"):
            flex = False
        if alts is not None:
            n.alts = dict(alts)
        elif flex == "copy3":
            n.alts = {"dve": float(cost), "gps": float(cost) * 2.0,
                      "act": float(cost) * 1.4}
        elif flex:
            n.alts = {"dve": float(cost), "gps": float(cost) * 2.0}
        if chan is not None and getattr(chan, "last_node", None) is not None:
            n.deps.add(chan.last_node)
        for b in reads:
            if b.writer is not None:
                n.deps.add(b.writer)
        for b in writes:
            if b.writer is not None:
                n.deps.add(b.writer)
            for r in b.readers:
                n.deps.add(r)
        n.deps.discard(n.nid)
        for b in writes:
            b.writer = n.nid
            b.readers = []
        for b in reads:
            b.readers.append(n.nid)
        if chan is not None:
            chan.last_node = n.nid
        self.nodes.append(n)
        return n

    def schedule_and_replay(self):
        nodes = self.nodes
        nn = len(nodes)
        succ = [[] for _ in range(nn)]
        indeg = [0] * nn
        for n in nodes:
            for d in n.deps:
                succ[d].append(n.nid)
                indeg[n.nid] += 1
        import heapq
        import os
        simkey = lambda e: "gps" if e == "qstore" else e
        keys = sorted({simkey(e) for e in self.engs})
        rev = []
        rem = [len(succ[i]) for i in range(nn)]
        stack = [i for i in range(nn) if rem[i] == 0]
        while stack:
            i = stack.pop()
            rev.append(i)
            for p in nodes[i].deps:
                rem[p] -= 1
                if rem[p] == 0:
                    stack.append(p)

        def run_pass(wc, costs):
            prio = [0.0] * nn
            for i in rev:
                best = 0.0
                for s in succ[i]:
                    if prio[s] > best:
                        best = prio[s]
                prio[i] = costs[i] + best
            eng_time = {e: 0.0 for e in keys}
            act_tab = None
            ready = {e: [] for e in keys}
            dready = [0.0] * nn
            ind = indeg[:]
            scheduled = [False] * nn
            assign = [None] * nn

            def push(i):
                n3 = nodes[i]
                if n3.alts:
                    for e3 in n3.alts:
                        heapq.heappush(ready[e3], (-prio[i], i))
                else:
                    heapq.heappush(ready[simkey(n3.ename)], (-prio[i], i))

            for i in range(nn):
                if ind[i] == 0:
                    push(i)
            sched = []
            nsched = 0
            while nsched < nn:
                best = None
                for e, rl in ready.items():
                    while rl and scheduled[rl[0][1]]:
                        heapq.heappop(rl)
                    if not rl:
                        continue
                    p, i = rl[0]
                    start = max(eng_time[e], dready[i])
                    n3 = nodes[i]
                    ecost = n3.alts.get(e, costs[i]) if n3.alts else costs[i]
                    key = (start + wc * ecost, p)
                    if best is None or key < best[0]:
                        best = (key, e)
                key, e = best
                _, i = heapq.heappop(ready[e])
                n = nodes[i]
                scheduled[i] = True
                if n.alts:
                    assign[i] = e
                    cost = n.alts.get(e, costs[i])
                else:
                    assign[i] = n.ename
                    cost = costs[i]
                start = max(eng_time[e], dready[i])
                if e == "act" and n.table is not None and act_tab != n.table:
                    cost += ACT_TABLE_NS
                    act_tab = n.table
                fin = start + cost
                eng_time[e] = fin
                sched.append(i)
                nsched += 1
                for s in succ[i]:
                    if dready[s] < fin:
                        dready[s] = fin
                    ind[s] -= 1
                    if ind[s] == 0:
                        push(s)
            return sched, assign

        def evaluate(sched, assign):
            eng_time = {e: 0.0 for e in keys}
            act_tab = None
            dready = [0.0] * nn
            fin = [0.0] * nn
            for i in sched:
                n = nodes[i]
                e = simkey(assign[i])
                cost = n.alts.get(e, n.cost) if n.alts else n.cost
                start = max(eng_time[e], dready[i])
                if e == "act" and n.table is not None and act_tab != n.table:
                    cost += ACT_TABLE_NS
                    act_tab = n.table
                fin[i] = start + cost
                eng_time[e] = fin[i]
                for s in succ[i]:
                    if dready[s] < fin[i]:
                        dready[s] = fin[i]
            return max(fin), fin

        base_costs = [n.cost for n in nodes]
        trials = [(0.5, base_costs)]
        for wc in (0.0, 0.15, 0.25, 0.35, 0.42, 0.58, 0.65, 0.8, 1.0, 1.3):
            trials.append((wc, base_costs))
        n_extra = int(os.environ.get("K_SCHED_TRIALS", "0"))
        rngs = np.random.default_rng(1234)
        for tix in range(n_extra):
            wc = (0.35, 0.5, 0.65)[tix % 3]
            amp = 0.25 if tix % 2 else 0.12
            jit = rngs.uniform(1.0 - amp, 1.0 + amp, nn)
            trials.append((wc, [base_costs[i] * jit[i] for i in range(nn)]))
        best_pick = None
        for wc, costs in trials:
            sched, assign = run_pass(wc, costs)
            mk, fin = evaluate(sched, assign)
            if os.environ.get("K_PRINT_TRIALS"):
                print("[sched] trial wc=%s mk=%.3f" % (wc, mk / 1e6))
            if best_pick is None or mk < best_pick[0]:
                best_pick = (mk, sched, assign, fin, wc)
        mk_best, sched, assign, fin, wc_best = best_pick
        for i in range(nn):
            nodes[i].finish = fin[i]
            if nodes[i].alts:
                nodes[i].ename = assign[i] if assign[i] != "qstore" else "gps"
        if os.environ.get("K_PRINT_EST"):
            print("[sched] portfolio best wc=%s makespan %.3f ms"
                  % (wc_best, mk_best / 1e6))
        if os.environ.get("K_GAPS"):
            from collections import Counter
            gape = os.environ.get("K_GAPS")
            seq = [i for i in sched if simkey(nodes[i].ename) == gape]
            gap_c = Counter()
            gap_t = Counter()
            prev_end = 0.0
            for i in seq:
                n = nodes[i]
                stt_ = fin[i] - (n.alts.get(simkey(n.ename), n.cost)
                                 if n.alts else n.cost)
                if stt_ > prev_end + 500.0:
                    dmax, dn = -1.0, None
                    for d in n.deps:
                        if fin[d] > dmax:
                            dmax, dn = fin[d], d
                    if dn is not None:
                        k = nodes[dn]
                        nm = (k.note or "L%d" % k.fn.__code__.co_firstlineno)
                        me = (n.note or "L%d" % n.fn.__code__.co_firstlineno)
                        gap_c[(me.split(" ")[0], "<-", simkey(k.ename),
                               nm.split(" ")[0])] += 1
                        gap_t[(me.split(" ")[0], "<-", simkey(k.ename),
                               nm.split(" ")[0])] += stt_ - prev_end
                prev_end = max(prev_end, fin[i])
            tot = sum(gap_t.values())
            if os.environ.get("K_GAP1"):
                want = os.environ.get("K_GAP1")
                prev_end2 = 0.0
                shown = 0
                for i in seq:
                    n = nodes[i]
                    cst = n.alts.get(simkey(n.ename), n.cost) if n.alts else n.cost
                    stt_ = fin[i] - cst
                    nm = (n.note or "L%d" % n.fn.__code__.co_firstlineno)
                    if (stt_ > prev_end2 + 3000.0 and nm.startswith(want)
                            and shown < 2 and fin[i] > 1.5e6):
                        shown += 1
                        print("[gap1] %s start=%.1fus gap=%.2fus deps:"
                              % (nm, stt_ / 1e3, (stt_ - prev_end2) / 1e3))
                        dl = sorted(n.deps, key=lambda d: -fin[d])[:6]
                        for d in dl:
                            k = nodes[d]
                            dn = (k.note or "L%d" % k.fn.__code__.co_firstlineno)
                            dc = k.alts.get(simkey(k.ename), k.cost) if k.alts else k.cost
                            print("    dep fin=%.1fus start=%.1fus %s %s"
                                  % (fin[d] / 1e3, (fin[d] - dc) / 1e3,
                                     simkey(k.ename), dn))
                    prev_end2 = max(prev_end2, fin[i])
            print("[sched] %s gaps total %.3f ms; top releasers:" % (gape, tot / 1e6))
            for k2, v in gap_t.most_common(18):
                print("    %.3f ms n=%d  %s" % (v / 1e6, gap_c[k2], k2))
        if os.environ.get("K_PRINT_EST"):
            busy = {}
            for n2 in nodes:
                k = simkey(n2.ename)
                busy[k] = busy.get(k, 0.0) + n2.cost
            mk = max(nd.finish for nd in nodes)
            print("[sched] busy:", {k: round(v / 1e6, 2) for k, v in busy.items()},
                  "makespan", round(mk / 1e6, 2))
            eng_prev = {}
            prev_on_eng = {}
            start_t = {}
            for idx2 in sched:
                n2 = nodes[idx2]
                k = simkey(n2.ename)
                eng_prev[idx2] = prev_on_eng.get(k)
                prev_on_eng[k] = idx2
            cur = max(range(nn), key=lambda i2: nodes[i2].finish)
            from collections import Counter
            busy_c = Counter()
            stall_c = Counter()
            hops = 0
            while cur is not None and hops < 20000:
                hops += 1
                n2 = nodes[cur]
                st2 = n2.finish - n2.cost
                ep = eng_prev.get(cur)
                dmax, dnode = -1.0, None
                for d in n2.deps:
                    if nodes[d].finish > dmax:
                        dmax, dnode = nodes[d].finish, d
                epf = nodes[ep].finish if ep is not None else -1.0
                key2 = (simkey(n2.ename), (n2.note or "?").split(" ")[0])
                busy_c[key2] += n2.cost
                if epf >= dmax and ep is not None:
                    cur = ep
                elif dnode is not None:
                    if dmax < st2 - 1.0:
                        stall_c[("idle-before", key2[0])] += st2 - dmax
                    cur = dnode
                else:
                    break
            print("[sched] walk busy by (eng, op):")
            for k2, v in busy_c.most_common(14):
                print("   ", k2, round(v / 1e6, 2), "ms")
            print("[sched] stalls:", {k2: round(v / 1e6, 2) for k2, v in stall_c.most_common(8)})
        for n in nodes:
            for b in n.reads + n.writes:
                b.writer = None
                b.readers = []
            if n.chan is not None:
                n.chan.last_node = None
        self.capture = False
        for i in sched:
            n = nodes[i]
            self._emit_real(n.ename, n.reads, n.writes, n.fn, n.chan, n.note)
        return max(nd.finish for nd in nodes)

    def _emit_real(self, ename, reads, writes, fn, chan=None, note=""):
        eng = self.engs[ename]
        own = chan is None
        ch = eng.chan if own else chan
        deps = {}

        def add(mark):
            if mark is None:
                return
            sem, val, owner = mark
            k = id(sem)
            if k not in deps or deps[k][1] < val:
                deps[k] = mark

        if not own and ch.cnt > 0:
            add((ch.sem, ch.cnt, None))
        for b in reads:
            add(b.writer)
        for b in writes:
            add(b.writer)
            for m in b.readers:
                add(m)
        for sem, val, _owner in deps.values():
            if self.loose_sync and _owner is eng:
                continue
            cur = eng.sat.get(id(sem), 0)
            if val > cur:
                eng.raw.wait_ge(sem, val)
                eng.sat[id(sem)] = val
        last = fn(eng.raw)
        ch.cnt += ch.step
        last.then_inc(ch.sem, ch.step)
        mark = (ch.sem, ch.cnt, eng if own else None)
        for b in writes:
            b.writer = mark
            b.readers = []
        for b in reads:
            b.readers.append(mark)
        return mark



def _kslabs(w_eff):
    M = w_eff.shape[1]
    return np.ascontiguousarray(w_eff.reshape(NK, 128, M))


def _pack_resident(w_eff):
    M = w_eff.shape[1]
    return np.ascontiguousarray(
        w_eff.reshape(NK, 128, M).transpose(1, 0, 2).reshape(128, NK * M))


def _col(v):
    return np.ascontiguousarray(v.reshape(8, 128).T)


def _dmaj(w_eff, nm):
    return np.ascontiguousarray(
        w_eff.reshape(NK, 128, nm, 128).transpose(2, 1, 0, 3).reshape(nm, 128, 1024))


def host_pack_weights(I, t_tile=T):
    f32 = lambda a: np.ascontiguousarray(np.asarray(a, np.float32))
    w = {}
    w["wq"] = _dmaj(f32(I["L0_attn_Wq"]), 8)
    w["wk"] = _dmaj(f32(I["L0_attn_Wk"]), 8)
    w["wv"] = _dmaj(f32(I["L0_attn_Wv"]), 8)
    w["wg"] = _dmaj(f32(I["L0_attn_Wg"]), 8)
    w["wo"] = _dmaj(f32(I["L0_attn_Wo"]), 8)
    w["wa"] = _dmaj(f32(I["L0_attn_Wa"]), 16)
    w["w1p"] = _dmaj(f32(I["L0_attn_W1"]), NF)
    w2 = f32(I["L0_attn_W2"])
    w["w2p"] = np.ascontiguousarray(
        w2.reshape(NF, 128, NK, 128).transpose(2, 0, 1, 3)
        .reshape(NK, 4, NK, 128, 128).transpose(0, 1, 3, 2, 4)
        .reshape(NF, 128, 1024))
    w["fw"] = _dmaj(f32(I["L0_ffn_weight"]).T, 8)
    w["uall"] = _dmaj(f32(I["L0_ffn_lora_U"]).reshape(128, 1024).T, 1)
    w["v2"] = np.ascontiguousarray(f32(I["L0_ffn_lora_V"]).transpose(0, 2, 1).reshape(128, 1024))
    w["gw"] = _pack_resident(f32(I["L0_ffn_gate_w"]).T)
    vecs = np.zeros((128, 80), np.float32)
    vecs[:, V_LN1S:V_LN1S + 8] = _col(f32(I["L0_attn_ln1_scale"]))
    vecs[:, V_LN1B:V_LN1B + 8] = _col(f32(I["L0_attn_ln1_bias"]))
    vecs[:, V_LN2S:V_LN2S + 8] = _col(f32(I["L0_attn_ln2_scale"]))
    vecs[:, V_LN2B:V_LN2B + 8] = _col(f32(I["L0_attn_ln2_bias"]))
    vecs[:, V_B2:V_B2 + 8] = _col(f32(I["L0_attn_b2"]))
    vecs[:, V_B2C:V_B2C + 8] = _col(f32(I["L0_attn_b2"]))
    vecs[:, V_B1:V_B1 + 32] = np.ascontiguousarray(f32(I["L0_attn_b1"]).reshape(32, 128).T)
    w["vecs"] = vecs
    cvec = np.zeros((128, 8), np.float32)
    cvec[:, C_EPS_MOD] = EPS_MOD
    cvec[:, C_EPS_IN] = EPS_IN
    cvec[:, C_HALFPI] = PI / 2
    cvec[:, C_NEG1] = -1.0
    cvec[:, C_ONE] = 1.0
    cvec[:, C_NEGH] = -0.5
    cvec[:, C_HALF] = 0.5
    w["cvec"] = cvec
    w["onesrow"] = np.ones((1, 128), np.float32)
    bsel = np.zeros((8, 128), np.float32)
    for e in range(8):
        bsel[e, 16 * e:16 * (e + 1)] = 1.0
    w["bsel"] = bsel
    import ml_dtypes
    w["zer"] = np.zeros((128, t_tile), ml_dtypes.bfloat16)
    return w




def build_core(nb=NB, s=S, t=T, sim_safe_wrap=False, loose_sync=True):
    CH = NK * t
    nt = s // t
    total = nb * nt
    nc = bass.Bass("TRN2", target_bir_lowering=False)

    xh = nc.dram_tensor("x", [nb, D, s], F32, kind="ExternalInput").ap()
    oh = nc.dram_tensor("out", [nb, D, s], F32, kind="ExternalOutput").ap()
    wslab = {}
    for name, n in [("wq", 8), ("wk", 8), ("wv", 8), ("wg", 8), ("wo", 8),
                    ("wa", 16), ("w1p", 32), ("w2p", 32), ("fw", 8)]:
        wslab[name] = nc.dram_tensor(name, [n, 128, 1024], F32, kind="ExternalInput").ap()
    wslab["uall"] = nc.dram_tensor("uall", [1, 128, 1024], F32, kind="ExternalInput").ap()
    v2h = nc.dram_tensor("v2", [128, 1024], F32, kind="ExternalInput").ap()
    W_NS = [("wa", 16), ("wk", 8), ("wv", 8), ("wq", 8), ("wg", 8), ("wo", 8),
            ("w1p", 32), ("w2p", 32), ("fw", 8), ("uall", 1)]
    scr = {name: nc.dram_tensor(f"scr_{name}", [n, 128, 1024], BF16, kind="Internal").ap()
           for name, n in W_NS}
    scr["v2"] = nc.dram_tensor("scr_v2", [1, 128, 1024], BF16, kind="Internal").ap()
    gwh = nc.dram_tensor("gw", [128, 64], F32, kind="ExternalInput").ap()
    vecsh = nc.dram_tensor("vecs", [128, 80], F32, kind="ExternalInput").ap()
    cvech = nc.dram_tensor("cvec", [128, 8], F32, kind="ExternalInput").ap()
    onesrowh = nc.dram_tensor("onesrow", [1, 128], F32, kind="ExternalInput").ap()
    bselh = nc.dram_tensor("bsel", [8, 128], F32, kind="ExternalInput").ap()
    zerh = nc.dram_tensor("zer", [128, t], BF16, kind="ExternalInput").ap()

    es = ExitStack()
    sb = lambda name, shape, dt=F32: es.enter_context(nc.sbuf_tensor(name, shape, dt))

    bx = [sb("bx0", [128, CH]), sb("bx1", [128, CH])]
    bzbD = [sb("bzb0", [128, CH], BF16), sb("bzb1", [128, CH], BF16)]
    bk = sb("bk", [128, CH], BF16); bv = sb("bv", [128, CH], BF16)
    cs_in = [sb("cs_in0", [128, 1024]), sb("cs_in1", [128, 1024])]
    cs_out = [sb("cs_out0", [128, 1024], BF16), sb("cs_out1", [128, 1024], BF16)]
    bar = sb("bar", [128, CH]); bai = sb("bai", [128, CH])
    bq = sb("bq", [128, CH], BF16); bg = sb("bg", [128, CH], BF16)
    bs_ = sb("bs", [128, CH], BF16)
    bcos = sb("bcos", [128, CH], BF16); bnsin = sb("bnsin", [128, CH], BF16)
    bhr = sb("bhr", [128, CH], BF16); bhi = sb("bhi", [128, CH], BF16)
    bt1 = sb("bt1", [128, CH]); bt2 = sb("bt2", [128, CH])
    bmix = [sb("bmix0", [128, CH], BF16), sb("bmix1", [128, CH], BF16)]
    bki = bar[:, 0:CH].bitcast(I32)
    bw = sb("bw", [128, CH])
    bwb = sb("bwb", [128, CH], BF16)
    bh1 = sb("bh1", [128, NF * t], BF16)
    bst = [sb("bst0", [128, t]), sb("bst1", [128, t])]
    sp = bst[0][0:8, :]
    bcAt = sb("bcAt", [128, 2 * t])
    bcBt = sb("bcBt", [128, 2 * t])
    bcA = [bcAt[:, 0:t], bcAt[:, t:2 * t]]
    bcB = [bcBt[:, 0:t], bcBt[:, t:2 * t]]
    stA = {nm: sb(f"stA_{nm}", [1, t]) for nm in ["mu", "ms", "var"]}
    stB = {nm: sb(f"stB_{nm}", [1, t]) for nm in ["mu", "ms", "var"]}
    stA_rn = sb("stA_rn", [1, 2 * t])
    stB_rn = sb("stB_rn", [1, 2 * t])
    for st_, rn_ in ((stA, stA_rn), (stB, stB_rn)):
        st_["rn"] = rn_
        st_["rstd"] = rn_[:, 0:t]
        st_["nmr"] = rn_[:, t:2 * t]
    st_r = stB["mu"]
    sg128 = sb("sg128", [128, t], BF16)
    stgb = sb("stgb", [128, t], BF16)
    slab = [sb(f"slab{i}", [128, 2048], BF16) for i in range(12)]
    gwr = sb("gwr", [128, 64]); gwb = sb("gwb", [128, 64], BF16)
    onesb = sb("onesb", [128, 1], BF16)
    vecs = sb("vecs_s", [128, 80]); cvec = sb("cvec_s", [128, 8])
    onesrow = sb("onesrow_s", [1, 128]); bsel = sb("bsel_s", [8, 128])
    zer = sb("zer_s", [128, t], BF16)
    bcr = sb("bcr", [128, 8]); bci = sb("bci", [128, 8])
    pb = [es.enter_context(nc.psum_tensor(f"pb{i}", [128, 512], F32)) for i in range(8)]

    sem = lambda name: es.enter_context(nc.semaphore(name))
    P = Prog(loose_sync=loose_sync)
    P.add_eng("pe", nc.tensor, Chan(sem("s_pe"), 1))
    P.add_eng("act", nc.scalar, Chan(sem("s_act"), 1))
    P.add_eng("dve", nc.vector, Chan(sem("s_dve"), 1))
    P.add_eng("gps", nc.gpsimd, Chan(sem("s_gps"), 1))
    P.add_eng("qload", nc.sync, None)
    P.add_eng("qstore", nc.gpsimd, None)
    ch_const = Chan(sem("c_const"), 16)
    ch_x = [Chan(sem(f"c_x{i}"), 16) for i in range(2)]
    ch_slot = [Chan(sem(f"c_sl{i}"), 16) for i in range(12)]
    ch_out = [Chan(sem(f"c_out{i}"), 16) for i in range(2)]

    B = {}
    for nm in ["bx0", "bx1", "bzb0", "bzb1", "bw", "bwb",
               "bst0", "bst1", "bcA0", "bcA1", "bcB0", "bcB1", "st_r", "sp",
               "sg128", "stgb", "gwr", "gwb", "onesb", "vecs", "cvec",
               "onesrow", "bsel", "zer"]:
        B[nm] = Buf(nm)
    Bh1 = [Buf(f"bh1_{q}") for q in range(4)]
    for nm in ["bk", "bv", "bar", "bai", "bq", "bg", "bs",
               "bcos", "bnsin", "bhr", "bhi", "bt1", "bt2", "bmix0", "bmix1"]:
        B[nm + "0"] = Buf(nm + "0")
        B[nm + "1"] = Buf(nm + "1")
    for nm in ["mu", "ms", "var", "rstd", "nmr"]:
        B[f"stA_{nm}"] = Buf(f"stA_{nm}")
        B[f"stB_{nm}"] = Buf(f"stB_{nm}")
    B["st_r"] = B["stB_mu"]
    stpA = {k: B[f"stA_{k}"] for k in ["mu", "ms", "var", "rstd", "nmr"]}
    stpB = {k: B[f"stB_{k}"] for k in ["mu", "ms", "var", "rstd", "nmr"]}
    B["sp"] = B["bst0"]
    Bslab = [Buf(f"slab{i}") for i in range(12)]
    Bb1 = [Buf(f"b1_{m}") for m in range(NF)]
    Bbank = [Buf(f"pb{i}") for i in range(8)]
    Bcr = [Buf(f"bcr{c}") for c in range(NK)]
    Bci = [Buf(f"bci{c}") for c in range(NK)]

    block_cm = nc.Block()
    es.enter_context(block_cm)

    def dma(ename, ch, out_ap, in_ap, reads, writes, note="", cost=800):
        return P.emit(ename, reads, writes,
                      lambda e: e.dma_start(out=out_ap, in_=in_ap), chan=ch, note=note,
                      cost=cost)

    dma("qload", ch_const, gwr[:], gwh, [], [B["gwr"]])
    dma("qload", ch_const, vecs[:], vecsh, [], [B["vecs"]] + Bb1)
    for m_ in range(NK):
        P.emit("dve", [B["vecs"]], [B["vecs"]],
               lambda e, m_=m_: e.tensor_tensor(
                   out=vecs[:, V_B2C + m_:V_B2C + m_ + 1],
                   in0=vecs[:, V_B2C + m_:V_B2C + m_ + 1],
                   in1=vecs[:, V_LN1B + m_:V_LN1B + m_ + 1], op=ALU.add), cost=100)
    dma("qload", ch_const, cvec[:], cvech, [], [B["cvec"]])
    dma("qload", ch_const, onesrow[:], onesrowh, [], [B["onesrow"]])
    dma("qload", ch_const, bsel[:], bselh, [], [B["bsel"]])
    dma("qload", ch_const, zer[:], zerh, [], [B["zer"]])

    ch_sin = [Chan(sem(f"c_si{i}"), 16) for i in range(4)]
    ch_sout = [Chan(sem(f"c_so{i}"), 16) for i in range(4)]
    Bscr = {name: [Buf(f"scr_{name}_{i2}") for i2 in range(ap_.shape[0])]
            for name, ap_ in scr.items()}
    Bcsi = [Buf("cs_in0"), Buf("cs_in1")]
    Bcso = [Buf("cs_out0"), Buf("cs_out1")]
    si = [0]

    def cast_slab(src_ap, dst_ap, dst_buf, w1_m=None):
        ic = si[0] % 4
        i = si[0] % 2
        si[0] += 1
        ft, fb = cs_in[i], Bcsi[i]
        bt_ap, bb = cs_out[i][:], Bcso[i]
        dma("qload", ch_sin[ic], ft[:, 0:1024], src_ap, [], [fb], note="scast in")
        if w1_m is not None:
            for k in range(NK):
                P.emit("dve", [fb, B["vecs"]], [bb],
                       lambda e, ft=ft, k=k: e.tensor_scalar(
                           out=cs_out[i][:, k * 128:(k + 1) * 128],
                           in0=ft[:, k * 128:(k + 1) * 128],
                           scalar1=vecs[:, V_LN1S + k:V_LN1S + k + 1],
                           scalar2=None, op0=ALU.mult), cost=140)
            parts = [(ft[:, k * 128:(k + 1) * 128],
                      vecs[:, V_LN1B + k:V_LN1B + k + 1]) for k in range(NK)]
            outp, obuf = mmg(rotB, (4, 5, 6, 7), parts, [fb, B["vecs"]],
                             note="b1fold", m_rows=128, n_cols=1, cost=2000)
            P.emit("dve", [obuf, B["vecs"]], [Bb1[w1_m]],
                   lambda e, outp=outp, m=w1_m: e.scalar_tensor_tensor(
                       out=vecs[:, V_B1 + m:V_B1 + m + 1], in0=outp[:, 0:1],
                       scalar=1.0, in1=vecs[:, V_B1 + m:V_B1 + m + 1],
                       op0=ALU.mult, op1=ALU.add), cost=130)
        elif i == 0:
            P.emit("dve", [fb], [bb],
                   lambda e, ft=ft, bt_ap=bt_ap: e.tensor_copy(
                       out=bt_ap, in_=ft[:, 0:1024]))
        else:
            P.emit("act", [fb], [bb],
                   lambda e, ft=ft, bt_ap=bt_ap: e.activation(
                       out=bt_ap, in_=ft[:, 0:1024], func=AF.Copy))
        dma("qstore", ch_sout[ic], dst_ap, bt_ap, [bb], [dst_buf], note="scast out")

    P.emit("dve", [B["gwr"]], [B["gwb"]],
           lambda e: e.tensor_copy(out=gwb[:], in_=gwr[:]))
    P.emit("dve", [B["cvec"]], [B["onesb"]],
           lambda e: e.tensor_copy(out=onesb[:], in_=cvec[:, C_ONE:C_ONE + 1]))

    poolA = {"slots": [0, 1, 2, 3, 4], "ctr": [0], "q": "qload"}
    poolB = {"slots": [5, 6, 7, 8, 9, 10, 11], "ctr": [0], "q": "qload"}
    pinned = set()

    def load_pair(pool, name, m0, n2=2, note="", pin=False):
        slots = pool["slots"]
        while True:
            i = slots[pool["ctr"][0] % len(slots)]
            pool["ctr"][0] += 1
            if i not in pinned:
                break
        src_ap = scr[name][m0:m0 + n2, :, :].rearrange("a p m -> p a m")
        dma(pool["q"], ch_slot[i], slab[i][:, 0:n2 * 1024].rearrange(
            "p (a m) -> p a m", a=n2), src_ap,
            Bscr[name][m0:m0 + n2], [Bslab[i]], note=note)
        if pin:
            pinned.add(i)
            return slab[i], Bslab[i], i
        return slab[i], Bslab[i]

    rotA = [0]
    rotB = [0]

    def mmg(rot, banks, parts, reads, m_rows=128, note="", n_cols=None, cost=None):
        bi = banks[rot[0] % len(banks)]
        rot[0] += 1
        bank, bbuf = pb[bi], Bbank[bi]
        out = bank[0:m_rows, 0:(t if n_cols is None else n_cols)]
        n = len(parts)

        def fn(e):
            last = None
            for i, (l, r) in enumerate(parts):
                last = e.matmul(out, l, r, start=(i == 0), stop=(i == n - 1))
            return last
        P.emit("pe", reads, [bbuf], fn, note=note,
               cost=(180 + 116 * len(parts)) if cost is None else cost)
        return out, bbuf

    for name, n in W_NS:
        for i2 in range(n):
            cast_slab(wslab[name][i2, :, :], scr[name][i2, :, :], Bscr[name][i2],
                      w1_m=(i2 if name == "w1p" else None))
    cast_slab(v2h, scr["v2"][0, :, :], Bscr["v2"][0])

    def mmg_at(bi, parts, reads, m_rows=128, note="", cost=None):
        bank, bbuf = pb[bi], Bbank[bi]
        out = bank[0:m_rows, 0:t]
        n = len(parts)

        def fn(e):
            last = None
            for i, (l, r) in enumerate(parts):
                last = e.matmul(out, l, r, start=(i == 0), stop=(i == n - 1))
            return last
        P.emit("pe", reads, [bbuf], fn, note=note,
               cost=(180 + 116 * len(parts)) if cost is None else cost)
        return out, bbuf

    def ck(buf_t, c):
        return buf_t[:, c * t:(c + 1) * t]

    def copy_fn(out_ap, in_ap):
        def fn(e):
            if hasattr(e, "tensor_copy"):
                return e.tensor_copy(out=out_ap, in_=in_ap)
            return e.copy(out=out_ap, in_=in_ap)
        return fn

    E = P.emit

    def ln_stats(src_t, src_b, cast_t, cast_bs, sq_t, sq_bs, banks, st, stp, bc, bcb,
                 eps_col, bc2=None):
        E("act", [src_b], list(sq_bs),
          lambda e: e.activation(out=sq_t[:, 0:CH], in_=src_t[:], func=AF.Square),
          cost=2000)
        mmg_at(banks[0], [(cvec[0:128, C_ONE:C_ONE + 1], ck(src_t, c))
                          for c in range(NK)],
               [src_b, B["cvec"]], m_rows=1, note="sum", cost=1900)
        E("dve", [Bbank[banks[0]]], [stp["mu"]],
          lambda e: e.tensor_scalar(out=st["mu"][:], in0=pb[banks[0]][0:1, 0:t],
                                    scalar1=1.0 / D, scalar2=None, op0=ALU.mult))
        mmg_at(banks[1], [(onesb[:, 0:1], ck(sq_t, c)) for c in range(NK)],
               list(sq_bs) + [B["onesb"]], m_rows=1, note="sumsq")
        E("dve", [Bbank[banks[1]]], [stp["ms"]],
          lambda e: e.tensor_scalar(out=st["ms"][:], in0=pb[banks[1]][0:1, 0:t],
                                    scalar1=1.0 / D, scalar2=None, op0=ALU.mult))
        E("dve", [stp["mu"]], [stp["var"]],
          lambda e: e.tensor_tensor(out=st["var"][:], in0=st["mu"][:], in1=st["mu"][:],
                                    op=ALU.mult))
        E("dve", [stp["ms"], stp["var"]], [stp["var"]],
          lambda e: e.tensor_tensor(out=st["var"][:], in0=st["ms"][:], in1=st["var"][:],
                                    op=ALU.subtract))
        eps_val = EPS_MOD if eps_col == C_EPS_MOD else EPS_IN
        x_ap = st["ms"]
        xi = x_ap.bitcast(I32)
        yi = st["rstd"].bitcast(I32)
        E("dve", [stp["var"]], [stp["ms"]],
          lambda e: e.tensor_scalar(out=x_ap[:], in0=st["var"][:], scalar1=eps_val,
                                    scalar2=None, op0=ALU.add), cost=120)
        E("dve", [stp["ms"]], [stp["rstd"]],
          lambda e: e.tensor_scalar(out=yi[:], in0=xi[:], scalar1=1, scalar2=None,
                                    op0=ALU.logical_shift_right), cost=120)
        E("dve", [stp["rstd"]], [stp["rstd"]],
          lambda e: e.tensor_scalar(out=yi[:], in0=yi[:], scalar1=-1,
                                    scalar2=RSQRT_MAGIC, op0=ALU.mult,
                                    op1=ALU.add), cost=120)
        E("dve", [stp["ms"]], [stp["ms"]],
          lambda e: e.tensor_scalar(out=x_ap[:], in0=x_ap[:], scalar1=0.5,
                                    scalar2=None, op0=ALU.mult), cost=120)
        for _ in range(2):
            E("dve", [stp["rstd"]], [stp["var"]],
              lambda e: e.tensor_tensor(out=st["var"][:], in0=st["rstd"][:],
                                        in1=st["rstd"][:], op=ALU.mult), cost=120)
            E("dve", [stp["ms"], stp["var"]], [stp["var"]],
              lambda e: e.tensor_tensor(out=st["var"][:], in0=x_ap[:],
                                        in1=st["var"][:], op=ALU.mult), cost=120)
            E("dve", [stp["var"]], [stp["var"]],
              lambda e: e.tensor_scalar(out=st["var"][:], in0=st["var"][:],
                                        scalar1=-1.0, scalar2=1.5, op0=ALU.mult,
                                        op1=ALU.add), cost=120)
            E("dve", [stp["rstd"], stp["var"]], [stp["rstd"]],
              lambda e: e.tensor_tensor(out=st["rstd"][:], in0=st["rstd"][:],
                                        in1=st["var"][:], op=ALU.mult), cost=120)
        E("dve", [stp["mu"], stp["rstd"]], [stp["nmr"]],
          lambda e: e.scalar_tensor_tensor(out=st["nmr"][:], in0=st["mu"][:],
                                           scalar=-1.0, in1=st["rstd"][:],
                                           op0=ALU.mult, op1=ALU.mult))
        bank0 = pb[banks[0]]
        P.emit("pe", [B["onesrow"], stp["rstd"], stp["nmr"]], [Bbank[banks[0]]],
               lambda e: e.matmul(bank0[0:128, 0:2 * t], onesrow[0:1, :],
                                  st["rn"][0:1, 0:2 * t], start=True, stop=True),
               note="bc", cost=700)
        E("act", [Bbank[banks[0]]], [bcb[0], bcb[1]],
          copy_fn(bc2[:, 0:2 * t], bank0[:, 0:2 * t]), cost=900,
          alts={"act": 900.0, "dve": 900.0})

    def ln_apply(src_t, src_b, dst_t, dst_b, bc, bcb, affine=None, eng="dve"):
        for c in range(NK):
            ec = "dve"
            op_cost = 450
            E(ec, [src_b, bcb[0]], [dst_b],
              lambda e, c=c: e.tensor_tensor(out=ck(dst_t, c), in0=ck(src_t, c),
                                             in1=bc[0][:, 0:t], op=ALU.mult),
              cost=op_cost, flex=True)
            E(ec, [dst_b, bcb[1]], [dst_b],
              lambda e, c=c: e.tensor_tensor(out=ck(dst_t, c), in0=ck(dst_t, c),
                                             in1=bc[1][:, 0:t], op=ALU.add),
              cost=op_cost, flex=True)
            if affine is not None:
                sc, bcv = affine
                E(ec, [dst_b, B["vecs"]], [dst_b],
                  lambda e, c=c: e.tensor_scalar(
                      out=ck(dst_t, c), in0=ck(dst_t, c),
                      scalar1=vecs[:, sc + c:sc + c + 1],
                      scalar2=vecs[:, bcv + c:bcv + c + 1],
                      op0=ALU.mult, op1=ALU.add), cost=op_cost, flex=True)

    def proj(wname, dst_t, dst_bs, nm, rhs_t, rhs_b, m_off=0, act_fn=AF.Copy):
        for m0 in range(0, nm, 2):
            svt, svb = load_pair(poolA, wname, m0 + m_off, note=f"{wname} m{m0}")
            for d in range(2):
                m = m0 + d
                parts = [(svt[:, d * 1024 + k * 128:d * 1024 + (k + 1) * 128],
                          ck(rhs_t, k)) for k in range(NK)]
                out, bbuf = mmg(rotA, (0, 1, 2, 3), parts, [svb, rhs_b],
                                note=f"{wname} m{m}")
                if act_fn == AF.Copy:
                    E("act", [bbuf], [dst_bs[m // 4]],
                      copy_fn(ck(dst_t, m), out), cost=480,
                      alts={"act": 480.0, "dve": 480.0})
                else:
                    E("act", [bbuf], [dst_bs[m // 4]],
                      lambda e, m=m, out=out, dst_t=dst_t, act_fn=act_fn: e.activation(
                          out=ck(dst_t, m), in_=out, func=act_fn),
                      table=("silu" if act_fn == AF.Silu else None), cost=480)

    def xload(i, b, j):
        par = i % 2
        t0 = j * t
        bxt, bxb = bx[par], B[f"bx{par}"]
        dma("qload", ch_x[par], bxt[:].rearrange("p (c t) -> p c t", c=NK),
            xh[b, :, t0:t0 + t].rearrange("(c p) t -> p c t", p=128),
            [], [bxb], note=f"x b{b} j{j}", cost=1500)

    def A1a(i, b, j):
        par = i % 2
        bxt, bxb = bx[par], B[f"bx{par}"]
        bzt, bzbf = bzbD[par], B[f"bzb{par}"]
        if j == 0:
            E("dve", [], Bcr, lambda e: e.memset(bcr[:], 0.0), cost=100)
            E("dve", [], Bci, lambda e: e.memset(bci[:], 0.0), cost=100)
        ln_stats(bxt, bxb, None, [],
                 bmix[par], [B[f"bmix{par}0"], B[f"bmix{par}1"]],
                 (3, 3), stA, stpA, bcA,
                 [B["bcA0"], B["bcA1"]], C_EPS_MOD, bc2=bcAt)
        for c in range(NK):
            hB = B[f"bt1{c // 4}"]
            E("dve", [bxb, B["bcA0"]], [hB],
              lambda e, c=c: e.tensor_tensor(out=ck(bt1, c), in0=ck(bxt, c),
                                             in1=bcA[0][:, 0:t], op=ALU.mult),
              cost=450, alts={"dve": 450.0, "gps": 1000.0})
            E("dve", [hB, B["bcA1"]], [bzbf],
              lambda e, c=c: e.tensor_tensor(out=ck(bzt, c), in0=ck(bt1, c),
                                             in1=bcA[1][:, 0:t], op=ALU.add),
              cost=450, alts={"dve": 450.0, "gps": 1000.0})

    def A1b(i):
        par = i % 2
        bzt, bzbf = bzbD[par], B[f"bzb{par}"]
        proj("wa", bar, [B["bar0"], B["bar1"]], NK, bzt, bzbf)
        proj("wa", bai, [B["bai0"], B["bai1"]], NK, bzt, bzbf, m_off=NK)
        proj("wk", bk, [B["bk0"], B["bk1"]], NK, bzt, bzbf)
        proj("wv", bv, [B["bv0"], B["bv1"]], NK, bzt, bzbf)

    Hw = CH // 2
    CD = 1350.0
    CA = 1150.0

    def chain_half(i, h):
        HSL = slice(h * Hw, (h + 1) * Hw)
        HS = lambda buf: buf[:, HSL]
        Bh = lambda nm: B[f"{nm}{h}"]
        Bk, Bv = B[f"bk{h}"], B[f"bv{h}"]
        bki_h = bar[:, HSL].bitcast(I32)
        E("dve", [Bk, Bv], [Bk],
          lambda e: e.tensor_tensor(out=HS(bk), in0=HS(bk), in1=HS(bv),
                                    op=ALU.mult), cost=700)
        E("act", [Bh("bar")], [Bh("bt1")],
          lambda e: e.activation(out=HS(bt1), in_=HS(bar), func=AF.Square), cost=CA)
        E("act", [Bh("bai")], [Bh("bt2")],
          lambda e: e.activation(out=HS(bt2), in_=HS(bai), func=AF.Square), cost=CA)
        E("dve", [Bh("bt1"), Bh("bt2")], [Bh("bt1")],
          lambda e: e.tensor_tensor(out=HS(bt1), in0=HS(bt1), in1=HS(bt2),
                                    op=ALU.add), cost=CD, flex=True)
        E("act", [Bh("bt1")], [Bh("bt2")],
          lambda e: e.activation(out=HS(bt2), in_=HS(bt1), func=AF.Ln),
          table="nlexp", cost=CA)
        E("act", [Bh("bt2"), B["cvec"]], [Bh("bt1")],
          lambda e: e.activation(out=HS(bt1), in_=HS(bt2), func=AF.Exp,
                                 scale=cvec[:, C_HALF:C_HALF + 1]),
          table="nlexp", cost=CA)
        E("act", [Bh("bar")], [Bh("bhr")],
          lambda e: e.activation(out=HS(bhr), in_=HS(bar), func=AF.Abs), cost=CA)
        E("act", [Bh("bar")], [Bh("bcos")],
          lambda e: e.activation(out=HS(bcos), in_=HS(bar), func=AF.Sign), cost=CA)
        E("act", [Bh("bai")], [Bh("bnsin")],
          lambda e: e.activation(out=HS(bnsin), in_=HS(bai), func=AF.Sign), cost=CA)
        E("dve", [Bh("bt1"), Bh("bhr")], [Bh("bt2")],
          lambda e: e.tensor_tensor(out=HS(bt2), in0=HS(bt1), in1=HS(bhr),
                                    op=ALU.add), cost=CD, flex=True)
        E("dve", [Bh("bt2")], [Bh("bt2")],
          lambda e: e.tensor_scalar(out=HS(bt2), in0=HS(bt2), scalar1=1e-30,
                                    scalar2=None, op0=ALU.max), cost=CD * 0.55)
        E("act", [Bh("bt2")], [Bh("bhr")],
          lambda e: e.activation(out=HS(bhr), in_=HS(bt2), func=AF.Ln),
          table="nlexp", cost=CA)
        E("act", [Bh("bhr"), B["cvec"]], [Bh("bt2")],
          lambda e: e.activation(out=HS(bt2), in_=HS(bhr), func=AF.Exp,
                                 scale=cvec[:, C_NEG1:C_NEG1 + 1]),
          table="nlexp", cost=CA)
        E("act", [Bh("bt1")], [Bh("bs")],
          lambda e: e.activation(out=HS(bs_), in_=HS(bt1), func=AF.Sigmoid),
          table="sigm", cost=CA)
        E("dve", [Bh("bai"), Bh("bt2")], [Bh("bt1")],
          lambda e: e.tensor_tensor(out=HS(bt1), in0=HS(bai), in1=HS(bt2),
                                    op=ALU.mult), cost=CD, flex=True)
        E("act", [Bh("bt1")], [Bh("bt2")],
          lambda e: e.activation(out=HS(bt2), in_=HS(bt1), func=AF.Arctan),
          table="sigm", cost=CA)
        E("dve", [Bh("bcos")], [Bh("bcos")],
          lambda e: e.tensor_scalar(out=HS(bcos), in0=HS(bcos), scalar1=-0.5,
                                    scalar2=0.5, op0=ALU.mult, op1=ALU.add),
          cost=CD * 0.55)
        E("dve", [Bh("bcos")], [Bh("bhr")],
          lambda e: e.tensor_scalar(out=HS(bhr), in0=HS(bcos), scalar1=-4.0,
                                    scalar2=2.0, op0=ALU.mult, op1=ALU.add),
          cost=CD * 0.55)
        E("dve", [Bh("bcos"), Bh("bnsin")], [Bh("bt1")],
          lambda e: e.tensor_tensor(out=HS(bt1), in0=HS(bcos), in1=HS(bnsin),
                                    op=ALU.mult), cost=CD, flex=True)
        E("dve", [Bh("bt2"), Bh("bhr")], [Bh("bt2")],
          lambda e: e.tensor_tensor(out=HS(bt2), in0=HS(bt2), in1=HS(bhr),
                                    op=ALU.mult), cost=CD, flex=True)
        E("dve", [Bh("bt1"), Bh("bt2")], [Bh("bai")],
          lambda e: e.scalar_tensor_tensor(out=HS(bai), in0=HS(bt1), scalar=PI,
                                           in1=HS(bt2), op0=ALU.mult,
                                           op1=ALU.add), cost=CD)
        for c in range(4 * h, 4 * h + 4):
            E("dve", [Bh("bai"), B["zer"]], [Bh("bai")],
              lambda e, c=c: e.tensor_tensor_scan(
                  out=ck(bai, c), data0=ck(bai, c), data1=zer[:, 0:t],
                  initial=zer[:, 0:1], op0=ALU.add, op1=ALU.add), cost=450)

        def red_sin(alpha_ap, alpha_bufs, dst_ap, dst_buf, neg):
            E("dve", alpha_bufs, [Bh("bar")],
              lambda e: e.tensor_scalar(out=bki_h, in0=alpha_ap, scalar1=INV_2PI,
                                        scalar2=None, op0=ALU.mult), cost=CD)
            E("dve", [Bh("bar")] + alpha_bufs, [Bh("bt1")],
              lambda e: e.scalar_tensor_tensor(out=HS(bt1), in0=bki_h,
                                               scalar=-TWO_PI, in1=alpha_ap,
                                               op0=ALU.mult, op1=ALU.add), cost=CD)
            if sim_safe_wrap:
                E("dve", [Bh("bt1")], [Bh("bt2")],
                  lambda e: e.tensor_scalar(out=HS(bt2), in0=HS(bt1), scalar1=-PI,
                                            scalar2=None, op0=ALU.is_lt), cost=CD)
                E("dve", [Bh("bt1")], [Bh("bhr")],
                  lambda e: e.tensor_scalar(out=HS(bhr), in0=HS(bt1), scalar1=PI,
                                            scalar2=None, op0=ALU.is_gt), cost=CD)
                E("dve", [Bh("bt2"), Bh("bhr")], [Bh("bt2")],
                  lambda e: e.tensor_tensor(out=HS(bt2), in0=HS(bt2), in1=HS(bhr),
                                            op=ALU.subtract), cost=CD)
                E("dve", [Bh("bt2"), Bh("bt1")], [Bh("bt1")],
                  lambda e: e.scalar_tensor_tensor(out=HS(bt1), in0=HS(bt2),
                                                   scalar=TWO_PI, in1=HS(bt1),
                                                   op0=ALU.mult, op1=ALU.add),
                  cost=CD)
            else:
                E("dve", [Bh("bt1")], [Bh("bt1")],
                  lambda e: e.tensor_scalar(out=HS(bt1), in0=HS(bt1), scalar1=-PI,
                                            scalar2=PI, op0=ALU.max, op1=ALU.min),
                  cost=CD * 0.55)
            if neg:
                E("act", [Bh("bt1"), B["cvec"]], [dst_buf],
                  lambda e: e.activation(out=dst_ap, in_=HS(bt1), func=AF.Sin,
                                         scale=cvec[:, C_NEG1:C_NEG1 + 1]),
                  table="silu", cost=CA)
            else:
                E("act", [Bh("bt1")], [dst_buf],
                  lambda e: e.activation(out=dst_ap, in_=HS(bt1), func=AF.Sin),
                  table="silu", cost=CA)

        red_sin(HS(bai), [Bh("bai")], HS(bnsin), Bh("bnsin"), neg=True)
        E("dve", [Bh("bai")], [Bh("bar")],
          lambda e: e.tensor_scalar(out=bki_h, in0=HS(bai), scalar1=INV_2PI,
                                    scalar2=0.25, op0=ALU.mult, op1=ALU.add),
          cost=CD)
        E("dve", [Bh("bar"), Bh("bai")], [Bh("bt1")],
          lambda e: e.scalar_tensor_tensor(out=HS(bt1), in0=bki_h,
                                           scalar=-TWO_PI, in1=HS(bai),
                                           op0=ALU.mult, op1=ALU.add), cost=CD)
        if sim_safe_wrap:
            E("dve", [Bh("bt1")], [Bh("bt2")],
              lambda e: e.tensor_scalar(out=HS(bt2), in0=HS(bt1),
                                        scalar1=-3 * PI / 2,
                                        scalar2=None, op0=ALU.is_lt), cost=CD)
            E("dve", [Bh("bt1")], [Bh("bhr")],
              lambda e: e.tensor_scalar(out=HS(bhr), in0=HS(bt1),
                                        scalar1=PI / 2,
                                        scalar2=None, op0=ALU.is_gt), cost=CD)
            E("dve", [Bh("bt2"), Bh("bhr")], [Bh("bt2")],
              lambda e: e.tensor_tensor(out=HS(bt2), in0=HS(bt2), in1=HS(bhr),
                                        op=ALU.subtract), cost=CD)
            E("dve", [Bh("bt2"), Bh("bt1")], [Bh("bt1")],
              lambda e: e.scalar_tensor_tensor(out=HS(bt1), in0=HS(bt2),
                                               scalar=TWO_PI, in1=HS(bt1),
                                               op0=ALU.mult, op1=ALU.add), cost=CD)
        else:
            E("dve", [Bh("bt1")], [Bh("bt1")],
              lambda e: e.tensor_scalar(out=HS(bt1), in0=HS(bt1),
                                        scalar1=-3 * PI / 2, scalar2=PI / 2,
                                        op0=ALU.max, op1=ALU.min), cost=CD * 0.55)
        E("act", [Bh("bt1"), B["cvec"]], [Bh("bcos")],
          lambda e: e.activation(out=HS(bcos), in_=HS(bt1), func=AF.Sin,
                                 bias=cvec[:, C_HALFPI:C_HALFPI + 1]),
          table="silu", cost=CA)
        E("dve", [Bk, Bh("bcos")], [Bh("bhr")],
          lambda e: e.tensor_tensor(out=HS(bhr), in0=HS(bk), in1=HS(bcos),
                                    op=ALU.mult), cost=700)
        E("dve", [Bk, Bh("bnsin")], [Bh("bhi")],
          lambda e: e.tensor_tensor(out=HS(bhi), in0=HS(bk), in1=HS(bnsin),
                                    op=ALU.mult), cost=700)
        for c in range(4 * h, 4 * h + 4):
            E("dve", [Bh("bs"), Bh("bhr"), Bcr[c]], [Bh("bhr")],
              lambda e, c=c: e.tensor_tensor_scan(
                  out=ck(bhr, c), data0=ck(bs_, c), data1=ck(bhr, c),
                  initial=bcr[:, c:c + 1], op0=ALU.mult, op1=ALU.add), cost=450)
            E("dve", [Bh("bs"), Bh("bhi"), Bci[c]], [Bh("bhi")],
              lambda e, c=c: e.tensor_tensor_scan(
                  out=ck(bhi, c), data0=ck(bs_, c), data1=ck(bhi, c),
                  initial=bci[:, c:c + 1], op0=ALU.mult, op1=ALU.add), cost=450)
        E("dve", [Bh("bhr"), Bh("bcos"), Bh("bs")], [Bh("bs")],
          lambda e: e.tensor_tensor(out=HS(bs_), in0=HS(bhr), in1=HS(bcos),
                                    op=ALU.mult), cost=700)
        E("dve", [Bh("bhi"), Bh("bnsin")], [Bk],
          lambda e: e.tensor_tensor(out=HS(bk), in0=HS(bhi), in1=HS(bnsin),
                                    op=ALU.mult), cost=700)
        E("dve", [Bh("bs"), Bk], [Bh("bs")],
          lambda e: e.tensor_tensor(out=HS(bs_), in0=HS(bs_), in1=HS(bk),
                                    op=ALU.add), cost=700)
        for c in range(4 * h, 4 * h + 4):
            L = c * t + t - 1
            s0 = h * Hw + 2 * (c - 4 * h)
            E("dve", [Bh("bs")], [Bcr[c]],
              lambda e, c=c, L=L: e.tensor_copy(out=bcr[:, c:c + 1],
                                                in_=bs_[:, L:L + 1]), cost=150)
            E("dve", [Bh("bhr"), Bh("bnsin")], [Bh("bt2")],
              lambda e, s0=s0, L=L: e.tensor_tensor(
                  out=bt2[:, s0:s0 + 1], in0=bhr[:, L:L + 1],
                  in1=bnsin[:, L:L + 1], op=ALU.mult), cost=150)
            E("dve", [Bh("bhi"), Bh("bcos")], [Bh("bt2")],
              lambda e, s0=s0, L=L: e.tensor_tensor(
                  out=bt2[:, s0 + 1:s0 + 2], in0=bhi[:, L:L + 1],
                  in1=bcos[:, L:L + 1], op=ALU.mult), cost=150)
            E("dve", [Bh("bt2")], [Bci[c]],
              lambda e, c=c, s0=s0: e.tensor_tensor(
                  out=bci[:, c:c + 1], in0=bt2[:, s0 + 1:s0 + 2],
                  in1=bt2[:, s0:s0 + 1], op=ALU.subtract), cost=150)

    def CH_(i):
        chain_half(i, 0)
        yield
        chain_half(i, 1)
        yield

    def A3a(i):
        proj("wq", bq, [B["bq0"], B["bq1"]], NK, bzbD[i % 2], B[f"bzb{i % 2}"])

    def A3b(i):
        par = i % 2
        proj("wg", bg, [B["bg0"], B["bg1"]], NK, bzbD[par], B[f"bzb{par}"],
             act_fn=AF.Silu)
        for h in (0, 1):
            HSL = slice(h * Hw, (h + 1) * Hw)
            HS = lambda buf, HSL=HSL: buf[:, HSL]
            Bh = lambda nm, h=h: B[f"{nm}{h}"]
            E("dve", [Bh("bq"), Bh("bs")], [Bh("bs")],
              lambda e, HS=HS: e.tensor_tensor(out=HS(bs_), in0=HS(bs_),
                                               in1=HS(bq), op=ALU.mult), cost=700)
            E("dve", [Bh("bs"), Bh("bg")], [B[f"bmix{par}{h}"]],
              lambda e, HS=HS, par=par: e.tensor_tensor(
                  out=HS(bmix[par]), in0=HS(bs_), in1=HS(bg), op=ALU.mult),
              cost=700)

    def B1(i, take=lambda n=1: None):
        par = i % 2
        bzt, bzbf = bzbD[par], B[f"bzb{par}"]
        for m0 in range(0, NK, 2):
            svt, svb = load_pair(poolB, "wo", m0, note=f"wo m{m0}")
            for d in range(2):
                m = m0 + d
                parts = [(svt[:, d * 1024 + k * 128:d * 1024 + (k + 1) * 128],
                          ck(bmix[par], k)) for k in range(NK)]
                out, bbuf = mmg(rotB, (4, 5, 6, 7), parts,
                                [svb, B[f"bmix{par}0"], B[f"bmix{par}1"]],
                                note=f"wo m{m}")
                E("dve", [bbuf, bzbf], [B["bw"]],
                  lambda e, m=m, out=out: e.tensor_tensor(
                      out=ck(bw, m), in0=out, in1=ck(bzt, m), op=ALU.add))
                take(2)
        ln_stats(bw, B["bw"], bwb, [B["bwb"]], bmix[par],
                 [B[f"bmix{par}0"], B[f"bmix{par}1"]],
                 (7, 7), stB, stpB, bcB,
                 [B["bcB0"], B["bcB1"]], C_EPS_IN, bc2=bcBt)
        for c in range(NK):
            E("dve", [B["bw"], B["bcB0"]], [B["bw"]],
              lambda e, c=c: e.tensor_tensor(out=ck(bw, c), in0=ck(bw, c),
                                             in1=bcB[0][:, 0:t], op=ALU.mult),
              cost=450, alts={"dve": 450.0, "gps": 1000.0})
            E("dve", [B["bw"], B["bcB1"]], [B["bwb"]],
              lambda e, c=c: e.tensor_tensor(out=ck(bwb, c), in0=ck(bw, c),
                                             in1=bcB[1][:, 0:t], op=ALU.add),
              cost=450, alts={"dve": 450.0, "gps": 1000.0})

    def B2(i, take=lambda n=1: None):
        for f0 in range(0, NF, 2):
            svt, svb = load_pair(poolB, "w1p", f0, note=f"w1 f{f0}")
            for d in range(2):
                f = f0 + d
                parts = [(svt[:, d * 1024 + k * 128:d * 1024 + (k + 1) * 128],
                          ck(bwb, k)) for k in range(NK)]
                out, bbuf = mmg(rotB, (4, 5, 6, 7), parts, [svb, B["bwb"]], note=f"w1 f{f}")
                E("act", [bbuf, B["vecs"], Bb1[f]], [Bh1[f // 8]],
                  lambda e, f=f, out=out: e.activation(
                      out=bh1[:, f * t:(f + 1) * t], in_=out, func=AF.Gelu_apprx_tanh,
                      bias=vecs[:, V_B1 + f:V_B1 + f + 1]), table="gelu", cost=550)
                take(2)

    def B3a(i, b, j):
        par = i % 2
        t0 = j * t
        bxt, bxb = bx[par], B[f"bx{par}"]
        for m in range(NK):
            w2s = [load_pair(poolB, "w2p", m * 4, note=f"w2 {m * 4}"),
                   load_pair(poolB, "w2p", m * 4 + 2, note=f"w2 {m * 4 + 2}")]
            parts = []
            for kb in range(NF):
                sl = w2s[kb // 16]
                d = (kb // 8) % 2
                parts.append((sl[0][:, d * 1024 + (kb % 8) * 128:
                                    d * 1024 + (kb % 8 + 1) * 128],
                              bh1[:, kb * t:(kb + 1) * t]))
            out, bbuf = mmg(rotB, (4, 5, 6, 7), parts,
                            [sv[1] for sv in w2s] + list(Bh1), note=f"w2 m{m}")
            E("dve", [bbuf, B["vecs"], B["bwb"]], [B["bw"]],
              lambda e, m=m, out=out: e.scalar_tensor_tensor(
                  out=ck(bw, m), in0=ck(bwb, m),
                  scalar=vecs[:, V_LN1S + m:V_LN1S + m + 1],
                  in1=out, op0=ALU.mult, op1=ALU.add))
            E("dve", [B["bw"], B["vecs"]], [B["bw"]],
              lambda e, m=m: e.tensor_scalar(
                  out=ck(bw, m), in0=ck(bw, m),
                  scalar1=vecs[:, V_B2C + m:V_B2C + m + 1], scalar2=None,
                  op0=ALU.add), cost=450)
        ln_stats(bw, B["bw"], bwb, [B["bwb"]], bmix[par],
                 [B[f"bmix{par}0"], B[f"bmix{par}1"]],
                 (7, 7), stB, stpB, bcB,
                 [B["bcB0"], B["bcB1"]], C_EPS_IN, bc2=bcBt)
        for c in range(NK):
            E("dve", [B["bw"], B["bcB0"]], [B["bw"]],
              lambda e, c=c: e.tensor_tensor(out=ck(bw, c), in0=ck(bw, c),
                                             in1=bcB[0][:, 0:t], op=ALU.mult),
              cost=450, alts={"dve": 450.0, "gps": 1000.0})
            E("dve", [B["bw"], B["bcB1"]], [B["bw"]],
              lambda e, c=c: e.tensor_tensor(out=ck(bw, c), in0=ck(bw, c),
                                             in1=bcB[1][:, 0:t], op=ALU.add),
              cost=450, alts={"dve": 450.0, "gps": 1000.0})
            E("dve", [B["bw"], B["vecs"]], [B["bw"]],
              lambda e, c=c: e.tensor_scalar(
                  out=ck(bw, c), in0=ck(bw, c),
                  scalar1=vecs[:, V_LN2S + c:V_LN2S + c + 1],
                  scalar2=vecs[:, V_LN2B + c:V_LN2B + c + 1],
                  op0=ALU.mult, op1=ALU.add), cost=450)

    def B3b(i, b, j):
        par = i % 2
        t0 = j * t
        bxt, bxb = bx[par], B[f"bx{par}"]
        E("dve", [bxb, B["bw"]], [bxb],
          lambda e: e.tensor_tensor(out=bxt[:, 0:Hw], in0=bxt[:, 0:Hw],
                                    in1=bw[:, 0:Hw], op=ALU.add), cost=1350,
          alts={"dve": 1350.0, "gps": 2900.0})
        E("dve", [bxb, B["bw"]], [bxb],
          lambda e: e.tensor_tensor(out=bxt[:, Hw:CH], in0=bxt[:, Hw:CH],
                                    in1=bw[:, Hw:CH], op=ALU.add), cost=1350,
          alts={"dve": 1350.0, "gps": 2900.0})
        ln_stats(bxt, bxb, bwb, [B["bwb"]], bmix[par],
                 [B[f"bmix{par}0"], B[f"bmix{par}1"]],
                 (7, 7), stB, stpB, bcB,
                 [B["bcB0"], B["bcB1"]], C_EPS_MOD, bc2=bcBt)
        for c in range(NK):
            E("dve", [bxb, B["bcB0"]], [B["bw"]],
              lambda e, c=c: e.tensor_tensor(out=ck(bw, c), in0=ck(bxt, c),
                                             in1=bcB[0][:, 0:t], op=ALU.mult),
              cost=450, alts={"dve": 450.0, "gps": 1000.0})
            E("dve", [B["bw"], B["bcB1"]], [B["bwb"]],
              lambda e, c=c: e.tensor_tensor(out=ck(bwb, c), in0=ck(bw, c),
                                             in1=bcB[1][:, 0:t], op=ALU.add),
              cost=450, alts={"dve": 450.0, "gps": 1000.0})
        mmg_at(7, [(gwb[:, k * 8:(k + 1) * 8], ck(bwb, k)) for k in range(NK)],
               [B["gwb"], B["bwb"]], m_rows=8, note="gate logits")
        E("act", [Bbank[7]], [B["sp"]],
          lambda e: e.activation(out=sp, in_=pb[7][0:8, 0:t], func=AF.Exp),
          table="nlexp", cost=400)
        mmg_at(7, [(cvec[0:8, C_ONE:C_ONE + 1], sp)],
               [B["sp"], B["cvec"]], m_rows=1, note="gate sum")
        E("dve", [Bbank[7]], [B["st_r"]],
          lambda e: e.reciprocal(out=st_r[:], in_=pb[7][0:1, 0:t]))
        E("dve", [B["st_r"]], [B["st_r"]],
          lambda e: e.tensor_scalar(out=st_r[:], in0=st_r[:], scalar1=1.0 / 16.0,
                                    scalar2=None, op0=ALU.mult))
        mmg_at(7, [(bsel[:], sp)], [B["bsel"], B["sp"]], m_rows=128, note="gate bc")
        E("act", [Bbank[7]], [B["sg128"]],
          lambda e: e.activation(out=sg128[:], in_=pb[7][:, 0:t], func=AF.Copy))
        uat, uab = load_pair(poolB, "uall", 0, n2=1, note="uall")
        out_t, bbuf_t = mmg(rotB, (4, 5, 6, 7),
                            [(uat[:, k * 128:(k + 1) * 128], ck(bwb, k))
                             for k in range(NK)], [uab, B["bwb"]], note="t lora")
        out_r, bbuf_r = mmg(rotB, (4, 5, 6, 7),
                            [(onesrow[0:1, :], st_r[0:1, 0:t])],
                            [B["onesrow"], B["st_r"]], note="r bc")
        E("dve", [bbuf_t, B["sg128"]], [B["stgb"]],
          lambda e: e.tensor_tensor(out=stgb[:], in0=out_t, in1=sg128[:],
                                    op=ALU.mult))
        E("dve", [B["stgb"], bbuf_r], [B["stgb"]],
          lambda e: e.tensor_tensor(out=stgb[:], in0=stgb[:], in1=out_r,
                                    op=ALU.mult))
        v2t, v2b, v2slot = load_pair(poolB, "v2", 0, n2=1, note="v2", pin=True)
        for m0 in range(0, NK, 2):
            svt, svb = load_pair(poolB, "fw", m0, note=f"fw m{m0}")
            for d in range(2):
                m = m0 + d
                parts = [(svt[:, d * 1024 + k * 128:d * 1024 + (k + 1) * 128],
                          ck(bwb, k)) for k in range(NK)]
                parts.append((v2t[:, m * 128:(m + 1) * 128], stgb[:]))
                out, bbuf = mmg(rotB, (4, 5, 6, 7), parts, [svb, B["bwb"], v2b, B["stgb"]],
                                note=f"ffn2 m{m}")
                sti = m % 2
                E("dve", [bbuf, bxb], [B[f"bst{sti}"]],
                  lambda e, m=m, out=out, sti=sti: e.tensor_tensor(
                      out=bst[sti][:], in0=out, in1=ck(bxt, m), op=ALU.add))
                dma("qstore", ch_out[sti],
                    oh[b, m * 128:(m + 1) * 128, t0:t0 + t], bst[sti][:],
                    [B[f"bst{sti}"]], [Buf("_out")], note=f"out b{b} j{j} m{m}")
        pinned.discard(v2slot)

    with es:
        for i in range(total + 1):
            if i < total:
                b, j = divmod(i, nt)
                xload(i, b, j)
                A1a(i, b, j)
                A1b(i)
                gen = CH_(i)
            else:
                gen = iter(())

            def take(n=1, gen=gen):
                for _ in range(n):
                    if next(gen, "_done") == "_done":
                        return
            if i > 0:
                B1(i - 1, take)
                B2(i - 1, take)
            take(10 ** 6)
            if i > 0:
                bb, jb = divmod(i - 1, nt)
                B3a(i - 1, bb, jb)
            if i < total:
                A3a(i)
            if i > 0:
                B3b(i - 1, bb, jb)
            if i < total:
                A3b(i)
        est = P.schedule_and_replay()
        import os
        if os.environ.get("K_PRINT_EST"):
            print(f"[sched] estimated makespan: {est / 1e6:.3f} ms")
        for ch in ch_out:
            if ch.cnt:
                P.engs["qstore"].raw.wait_ge(ch.sem, ch.cnt)
    return nc



def kernel(**inputs):
    x = np.asarray(inputs["L0_attn_x"], np.float32)
    w = host_pack_weights(inputs)
    nc = build_core()
    in_maps = []
    for ci in range(N_CORES):
        xc = np.ascontiguousarray(x[ci * NB:(ci + 1) * NB].transpose(0, 2, 1))
        m = dict(w)
        m["x"] = xc
        in_maps.append(m)
    res = run_bass_kernel_spmd(nc, in_maps, core_ids=list(range(N_CORES)))
    outs = [res.results[i]["out"] for i in range(N_CORES)]
    full = np.concatenate(outs, axis=0)
    return np.ascontiguousarray(full.transpose(0, 2, 1)).astype(np.float32)
